# version 11
# baseline (speedup 1.0000x reference)
import numpy as np
import ml_dtypes
import concourse.bass as bass
import concourse.mybir as mybir
from concourse.bass_utils import run_bass_kernel_spmd

F32 = mybir.dt.float32
BF16 = mybir.dt.bfloat16
U8 = mybir.dt.uint8
AF = mybir.ActivationFunctionType
ALU = mybir.AluOpType
AX = mybir.AxisListType.X

L, D, DI, DIP = 4096, 1024, 2048, 4416
NE, CK, TPS = 8, 128, 4
CS = 1408
ENGS = ("tensor", "vector", "scalar", "sync", "gpsimd")
ZERO = ("pe", 0)
NT_OVERRIDE = 0
SCHED = "full"


NLANES = 8


class Bld:
    def __init__(self, nc):
        self.nc = nc
        self.ops = []
        self.cnt = {s: 0 for s in ("pe", "v", "s", "g")}
        self.lane_ord = {}
        self.ring_ctr = {"dsy": 0, "dsc": 0, "dgp": 0}
        for ring in ("dsy", "dsc", "dgp"):
            for ln in range(NLANES):
                self.cnt["%s%d" % (ring, ln)] = 0
        self.handles = {}

    def dma(self, ring, fn, waits=()):
        ln = self.ring_ctr[ring] % NLANES
        self.ring_ctr[ring] += 1
        sem = "%s%d" % (ring, ln)
        prev = self.cnt[sem]
        w = list(waits) + ([(sem, prev)] if prev else [])
        return self.op("sync" if ring == "dsy" else
                       ("scalar" if ring == "dsc" else "gpsimd"),
                       fn, w, sem, 16)

    def op(self, eng, fn, waits=(), inc=None, incn=1):
        w = [(s, v) for (s, v) in waits if s is not None and v > 0]
        tok = None
        if inc is not None:
            self.cnt[inc] += incn
            tok = (inc, self.cnt[inc])
        self.ops.append((eng, fn, w, inc, incn))
        return tok

    def t(self, fn, waits=()):
        return self.op("tensor", fn, waits, "pe")

    def v(self, fn, waits=()):
        return self.op("vector", fn, waits, "v")

    def s(self, fn, waits=()):
        return self.op("scalar", fn, waits, "s")

    def g(self, fn, waits=()):
        return self.op("gpsimd", fn, waits, "g")

    def dsync(self, fn, waits=()):
        return self.dma("dsy", fn, waits)

    def dscal(self, fn, waits=()):
        return self.dma("dsc", fn, waits)

    def dgp(self, fn, waits=()):
        return self.dma("dgp", fn, waits)

    def replay_into(self, blk):
        for name in ENGS:
            getattr(blk, name)(self._mk(name))

    def _mk(self, engname):
        def run(eng):
            waited = {}
            for e, fn, waits, inc, incn in self.ops:
                if e != engname:
                    continue
                for s, v in waits:
                    if waited.get(s, 0) < v:
                        eng.wait_ge(self.handles[s], v)
                        waited[s] = v
                ins = fn(eng)
                if inc is not None:
                    ins.then_inc(self.handles[inc], incn)
        return run


class Ring:

    def __init__(self, handle):
        self.h = handle

    def __getitem__(self, key):
        ap = self.h.ap()
        if isinstance(key, tuple):
            key = (key[1], key[0]) + tuple(key[2:])
            return ap[key]
        return ap[:, key]


class AliasRing:

    def __init__(self, *aps):
        self.aps = list(aps)

    def __getitem__(self, key):
        if isinstance(key, tuple):
            return self.aps[key[0]][tuple(key[1:])]
        return self.aps[key]


BCPERM = np.concatenate([np.arange(2048),
                         2048 + np.arange(64), 2176 + np.arange(64),
                         2112 + np.arange(64), 2240 + np.arange(64)])


def host_prep(inputs):
    A = {}
    wt = inputs["L0_attn_in_proj_w"].T
    colperm = np.concatenate([np.arange(2048), 2048 + BCPERM, np.arange(4352, 4416)])
    A["winT"] = np.ascontiguousarray(wt[:, colperm])
    cw = inputs["L0_attn_conv_w"][:, 0, :][BCPERM]
    A["wpool"] = np.ascontiguousarray(cw.reshape(18, 128, 7).transpose(1, 0, 2))
    A["convb"] = np.ascontiguousarray(
        inputs["L0_attn_conv_b"][BCPERM].reshape(18, 128).T)
    A["dtb"] = np.ascontiguousarray(
        np.broadcast_to(np.concatenate([inputs["L0_attn_dt_bias"]] * 2), (128, 64)))
    A["alog"] = np.ascontiguousarray(
        np.concatenate([inputs["L0_attn_A_log"]] * 2)[:, None])
    A["dpar"] = np.ascontiguousarray(np.broadcast_to(inputs["L0_attn_D_param"], (128, 32)))
    A["fcdT"] = np.ascontiguousarray(inputs["L0_attn_fc_D_w"].T)
    A["nw"] = np.ascontiguousarray(np.broadcast_to(inputs["L0_attn_norm_w"], (128, DI)))
    A["woutT"] = np.ascontiguousarray(inputs["L0_attn_out_proj_w"].T)
    A["w1T"] = np.ascontiguousarray(inputs["L0_ffn_W1"].transpose(0, 2, 1))
    A["b1c"] = np.ascontiguousarray(
        inputs["L0_ffn_b1"].reshape(8, 16, 128).transpose(2, 0, 1))
    A["w2T"] = np.ascontiguousarray(inputs["L0_ffn_W2"].transpose(0, 2, 1))
    A["b2"] = np.ascontiguousarray(inputs["L0_ffn_b2"])
    A["wgT"] = np.ascontiguousarray(inputs["L0_ffn_Wg"].T)
    A["bg"] = np.ascontiguousarray(np.broadcast_to(inputs["L0_ffn_bg"], (128, 8)))
    A["idf"] = np.eye(128, dtype=np.float32)
    idx = np.arange(128)
    mz = np.stack([(idx[None, :] < idx[:, None]),
                   (idx[None, :] > idx[:, None])], axis=0).astype(np.uint8)
    A["mskz"] = np.ascontiguousarray(
        np.tile(mz[None, :, :, :], (1, 1, 1, 4)).reshape(2, 128, 512)
        .transpose(1, 0, 2)).copy()
    A["ones1"] = np.ones((1, 128), np.float32)
    shm = np.zeros((128, 256), np.float32)
    shm[idx[1:], idx[:-1]] = 1.0
    shm[0, 255] = 1.0
    A["shm"] = np.ascontiguousarray(shm.astype(ml_dtypes.bfloat16))
    return A


def build(nt):
    nc = bass.Bass("TRN2", target_bir_lowering=False)
    b = Bld(nc)
    Ltot = nt * CK
    tps = min(TPS, nt)
    nst = max(1, (nt + tps - 1) // tps)

    dr = {}

    def din(name, shape, dt=F32):
        dr[name] = nc.dram_tensor(name, list(shape), dt, kind="ExternalInput").ap()

    din("u", (Ltot, D))
    for name, shape, dt in [
        ("winT", (D, DIP), F32), ("wpool", (128, 18, 7), F32),
        ("convb", (128, 18), F32), ("dtb", (128, 64), F32), ("alog", (64, 1), F32),
        ("dpar", (128, 32), F32), ("fcdT", (DI, 32), F32), ("nw", (128, DI), F32),
        ("woutT", (DI, D), F32), ("w1T", (NE, D, DI), F32), ("b1c", (128, NE, 16), F32),
        ("w2T", (NE, DI, D), F32), ("b2", (NE, D), F32), ("wgT", (D, NE), F32),
        ("bg", (128, NE), F32), ("idf", (128, 128), F32),
        ("mskz", (128, 2, 512), U8), ("ones1", (1, 128), F32),
        ("shm", (128, 256), BF16),
    ]:
        dr[name] = nc.dram_tensor(name, list(shape), dt, kind="ExternalInput").ap()
    out = nc.dram_tensor("out", [Ltot, D], F32, kind="ExternalOutput").ap()

    def scr(name, shape, dt=BF16):
        dr[name] = nc.dram_tensor(name, list(shape), dt, kind="Internal").ap()

    scr("win_b", (D, DIP))
    scr("w1_b", (NE, D, DI))
    scr("w2_b", (NE, DI, D))
    scr("wout_b", (DI, D))
    scr("z_sp", (Ltot, DI))
    scr("xi_sp", (Ltot, DI))
    scr("yfw_sp", (Ltot + 1, DI))
    scr("ybd_sp", (Ltot, DI))
    scr("csbw_sp", (nt, 64, 2048))
    scr("bc_sp", (Ltot, 256))
    scr("expal_sp", (Ltot, 64))
    scr("ffn_sp", (Ltot, D))

    sb = {}
    hnd = {}

    def asb(name, shape, dt=BF16):
        hnd[name] = nc.alloc_sbuf_tensor("sb_" + name, list(shape), dt)
        sb[name] = hnd[name]

    def aring(name, shape, dt=BF16):
        R, Pp = shape[0], shape[1]
        hnd[name] = nc.alloc_sbuf_tensor("sb_" + name, [Pp, R] + list(shape[2:]), dt)
        sb[name] = Ring(hnd[name])

    asb("idb", (128, 128))
    asb("shm", (128, 256))
    asb("idf", (128, 128), F32)
    asb("mskz", (128, 2, 512), U8)
    asb("zc", (128, 1))
    asb("ones1b", (1, 128))
    asb("cbc", (128, 18), F32)
    asb("dtbc", (128, 64), F32)
    asb("alogc", (64, 1), F32)
    asb("nega", (64, 1), F32)
    asb("dparc", (128, 32), F32)
    asb("bgc", (128, 8), F32)
    asb("wgc", (128, 8, 8), F32)
    asb("b1cc", (128, NE, 16), F32)
    asb("b2r", (NE, D))
    asb("fcdw", (128, 16, 32))
    asb("cdw", (128, 18, 7, 32))
    asb("drows", (1, nt, 64), F32)
    aring("drb", (2, 1, 64))
    asb("fcd", (128, nt, 32), F32)
    aring("comb", (2, 128, tps, 8), F32)
    asb("combT", (8, 128))
    asb("glt", (128, 8), F32)
    asb("gsc", (128, 12), F32)
    aring("dtr", (2, 128, 64), F32)
    asb("dscr", (128, 64), F32)
    aring("dts", (2, 128, 64), F32)
    aring("dtb16", (2, 128, 64))
    asb("abf", (64, CK), F32)
    aring("alf", (2, 64, CK), F32)
    aring("poshi", (2, 64, CK))
    aring("poslo", (2, 64, CK))
    aring("altm", (2, 128, 64), F32)
    aring("naltm", (2, 128, 64), F32)
    aring("expaltm", (2, 128, 64))
    aring("expdtm", (2, 128, 64))
    asb("expdfm", (64, CK), F32)
    asb("sumc", (64, 1), F32)
    aring("bcfm", (2, 128, 2, 128))
    aring("bct", (2, 128, 256))
    asb("stA", (64, 2048), F32)
    asb("stB", (64, 2048))
    asb("decrep", (64, 2048))
    asb("csbt", (64, 2048))
    aring("xt", (2, 128, D), F32)
    aring("win", (2, 128, 8, 512))
    aring("wA", (2, 128, 8, 1024))
    aring("asm", (2, 128, 18, 134))
    aring("xifm", (2, 128, 16, 128))
    aring("xitm", (2, 128, DI))
    aring("xall", (2, 128, 1024))
    asb("mall", (128, 2048))
    aring("bhat", (2, 128, 1024))
    aring("yt", (2, 128, 512))
    aring("zt", (2, 128, 512))
    asb("xmt", (128, D), F32)
    asb("xnt", (128, D), F32)
    asb("xnf32", (128, 8, 128), F32)
    aring("xnfm", (2, 128, 8, tps * CK))
    asb("acc", (128, tps, D))
    asb("hb", (128, 16, 512))
    aring("expb", (2, 128, 32))
    aring("cbw", (2, 128, 64))
    asb("cbwfm", (64, 128))

    def flat(h):
        ap = h.ap()
        nd = len(ap.shape)
        if nd == 2:
            return ap
        spec = "p " + " ".join("d%d" % k for k in range(nd - 1))
        return ap.rearrange(spec + " -> p (" + spec[2:] + ")")

    hbf = flat(hnd["hb"])
    sb["yfwt"] = AliasRing(hbf[:, 0:2048], hbf[:, 2048:4096])
    sb["ybshs"] = AliasRing(hbf[:, 4096:6144], hbf[:, 6144:8192])
    winf = flat(hnd["win"])
    sb["ybwt"] = AliasRing(winf[:, 0:2048], winf[:, 2048:4096])
    sb["ycmb"] = AliasRing(winf[:, 4096:6144], winf[:, 4096:6144])
    asmf = flat(hnd["asm"])
    sb["xirl"] = AliasRing(asmf[:, 0:2048], asmf[:, 2048:4096])
    xifmf = flat(hnd["xifm"])
    sb["zrl"] = AliasRing(xifmf[:, 0:2048], xifmf[:, 2048:4096])
    sb["ybdt"] = sb["xitm"]
    bhatf = flat(hnd["bhat"])
    sb["actt"] = AliasRing(hnd["mall"].ap(), bhatf)
    xallf = flat(hnd["xall"])
    sb["actf"] = AliasRing(
        xallf.rearrange("p (c n) -> p c n", n=128),
        winf[:, 6144:8192].rearrange("p (c n) -> p c n", n=128))
    sb["outt"] = AliasRing(hnd["xmt"].ap(), hnd["xnt"].ap())
    xnf32f = flat(hnd["xnf32"]).bitcast(BF16)
    sb["ffrl"] = AliasRing(xnf32f[:, 0:1024], xnf32f[:, 1024:2048])
    wAf = flat(hnd["wA"])
    sb["woutv"] = wAf.rearrange("p (c n) -> p c n", n=1024)
    cdwf = flat(hnd["cdw"])
    sb["nwb"] = cdwf[:, 0:2048]
    sb["wpool"] = cdwf[:, 3906:4032].rearrange("p (a k) -> p a k", k=7)
    sb["cva"] = cdwf[:, 0:1536].bitcast(F32).rearrange(
        "p (a k) -> p a k", k=128)
    sb["cvt"] = cdwf[:, 1536:3072].bitcast(F32).rearrange(
        "p (a k) -> p a k", k=128)

    pbig = nc.alloc_psum_tensor("pbig", [128, 2048], F32)
    pmed = nc.alloc_psum_tensor("pmed", [128, 512], F32)
    pyd = nc.alloc_psum_tensor("pyd", [128, 512], F32)
    pyo = nc.alloc_psum_tensor("pyo", [128, 512], F32)
    ptb = nc.alloc_psum_tensor("ptb", [128, 1024], BF16)

    bank_free = {}

    def bw(key):
        return list(bank_free.get(key, []))

    def bset(key, *toks):
        bank_free[key] = [t for t in toks if t is not None]

    def badd(key, tok):
        bank_free.setdefault(key, []).append(tok)

    S = {}

    def sget(k):
        return S.get(k, ZERO)

    def r3(ap, q):
        return ap.rearrange("p (h q) -> p h q", q=q)

    cv_win = b.dgp(lambda e: e.dma_start(dr["win_b"], dr["winT"]))
    cv_wout = b.dgp(lambda e: e.dma_start(dr["wout_b"], dr["woutT"]))
    cv_w1, cv_w2 = [], []
    for ei in range(NE):
        cv_w1.append(b.dgp(lambda e, ei=ei: e.dma_start(dr["w1_b"][ei], dr["w1T"][ei])))
        cv_w2.append(b.dgp(lambda e, ei=ei: e.dma_start(dr["w2_b"][ei], dr["w2T"][ei])))
    tok_idb = b.dgp(lambda e: e.dma_start(sb["idb"][:], dr["idf"]))
    tok_shm = b.dgp(lambda e: e.dma_start(sb["shm"][:], dr["shm"]))
    tok_b2r = b.dgp(lambda e: e.dma_start(sb["b2r"][:], dr["b2"]))
    tok_fcdw = b.dgp(lambda e: e.dma_start(
        sb["fcdw"][:], dr["fcdT"].rearrange("(c k) n -> k c n", k=128)))
    tok_cdw = b.dgp(lambda e: e.dma_start(sb["wpool"], dr["wpool"]))
    tok_o1b = b.dgp(lambda e: e.dma_start(sb["ones1b"][:], dr["ones1"]))
    tok_msk = b.dsync(lambda e: e.dma_start(sb["mskz"][:], dr["mskz"]))
    tok_idf = b.dsync(lambda e: e.dma_start(sb["idf"][:], dr["idf"]))
    tok_cbc = b.dsync(lambda e: e.dma_start(sb["cbc"][:], dr["convb"]))
    tok_dtb = b.dsync(lambda e: e.dma_start(sb["dtbc"][:], dr["dtb"]))
    tok_alog = b.dsync(lambda e: e.dma_start(sb["alogc"][:], dr["alog"]))
    tok_dpar = b.dsync(lambda e: e.dma_start(sb["dparc"][:], dr["dpar"]))
    tok_bg = b.dsync(lambda e: e.dma_start(sb["bgc"][:], dr["bg"]))
    tok_wg = b.dsync(lambda e: e.dma_start(
        sb["wgc"][:], dr["wgT"].rearrange("(c k) n -> k c n", k=128)))
    tok_b1c = b.dsync(lambda e: e.dma_start(sb["b1cc"][:], dr["b1c"]))

    t_ne = b.s(lambda e: e.activation(sb["nega"][:], sb["alogc"][:], AF.Exp),
               waits=[tok_alog])
    tok_nega = b.v(lambda e: e.tensor_scalar_mul(sb["nega"][:], sb["nega"][:], -1.0),
                   waits=[t_ne])
    tok_zc = b.v(lambda e: e.memset(sb["zc"][:], 0.0))
    tz = b.v(lambda e: e.memset(sb["outt"][0], 0.0))
    zr1 = b.dsync(lambda e: e.dma_start(
        dr["yfw_sp"][0:1, :], sb["outt"][0, 0:1, :].bitcast(BF16)[:, 0:DI]), waits=[tz])
    zrb = b.v(lambda e: e.engine_nop(), waits=[zr1])
    S[("outt_free", 0)] = zrb
    S["xmt_free"] = zrb
    S["sfw"] = b.v(lambda e: e.memset(sb["stA"][:], 0.0))
    S["sbf"] = b.v(lambda e: e.memset(sb["stB"][:], 0.0))
    S["asm_z0"] = b.v(lambda e: e.memset(sb["asm"][0, :, :, 0:3], 0.0))

    wctr = [0]

    def win_load(g):
        pp = wctr[0] % 2
        wctr[0] += 1
        n = 320 if g == 8 else 512
        tok = b.dsync(lambda e, g=g, pp=pp, n=n: e.dma_start(
            sb["win"][pp, :, :, 0:n],
            dr["win_b"][:, g * 512:g * 512 + n].rearrange("(c k) n -> k c n", k=128)),
            waits=[sget(("win_free", pp)), cv_win])
        return tok, pp

    S["win_next"] = win_load(0)

    def tile_head(i):
        st, it, r = i // tps, i % tps, i % 2
        stb = st % 2
        tok_x = b.dscal(lambda e, i=i, r=r: e.dma_start(
            sb["xt"][r], dr["u"][i * CK:(i + 1) * CK, :]),
            waits=[sget(("xt_free", r))])
        yield 2
        def trunc(tok):
            S[("asm_center", i)] = tok
            S[("dtr", i)] = tok
            S[("comb_tile", i // tps % 2, it)] = tok
            S[("xnfm_tile", i // tps % 2, it)] = tok
        if SCHED == "ln0":
            dmy = b.v(lambda e: e.engine_nop(), waits=[tok_x])
            trunc(dmy)
            return
        w0 = b.v(lambda e, r=r: e.tensor_reduce(sb["gsc"][:, 0:1], sb["xt"][r], AX,
                                                ALU.add),
                 waits=[tok_x, sget("gsc_free")])
        w1_ = b.v(lambda e: e.tensor_scalar_mul(sb["gsc"][:, 1:2], sb["gsc"][:, 0:1],
                                                1.0 / D), waits=[w0])
        w2_ = b.v(lambda e, r=r: e.tensor_scalar(sb["xmt"][:], sb["xt"][r],
                                                 sb["gsc"][:, 1:2], None, ALU.subtract),
                  waits=[w1_, sget("xmt_free")])
        S[("xt_free", r)] = w2_
        if SCHED == "ln1":
            trunc(w2_)
            S["xmt_free"] = w2_
            return
        w3 = b.s(lambda e: e.activation(sb["xnt"][:], sb["xmt"][:], AF.Square,
                                        accum_out=sb["gsc"][:, 2:3]),
                 waits=[w2_, sget("xnt_free")])
        w4a = b.v(lambda e: e.tensor_scalar(sb["gsc"][:, 3:4], sb["gsc"][:, 2:3],
                                            1.0 / D, 1e-5, ALU.mult, ALU.add),
                  waits=[w3])
        w4 = b.s(lambda e: e.activation(sb["gsc"][:, 3:4], sb["gsc"][:, 3:4], AF.Ln),
                 waits=[w4a])
        w5a = b.v(lambda e: e.tensor_scalar_mul(sb["gsc"][:, 4:5], sb["gsc"][:, 3:4],
                                                -0.5), waits=[w4])
        w5 = b.s(lambda e: e.activation(sb["gsc"][:, 4:5], sb["gsc"][:, 4:5], AF.Exp),
                 waits=[w5a])
        if SCHED == "ln2":
            trunc(w5)
            S["xmt_free"] = w5
            S["gsc_free"] = w5
            return
        w6 = b.v(lambda e: e.tensor_scalar_mul(sb["xnt"][:], sb["xmt"][:],
                                               sb["gsc"][:, 4:5]),
                 waits=[w5, sget("xnt_free")])
        S["xmt_free"] = w6
        if SCHED == "ln3":
            trunc(w6)
            S["gsc_free"] = w6
            return
        evb = None
        for kc in range(8):
            bank, pt = (("yd", pyd) if kc % 2 == 0 else ("yo", pyo))
            mmw = [w6, tok_idf] + bw(bank)
            if it == 0 and kc == 0:
                mmw.append(sget(("xnfm_free", stb)))
            if kc < 2:
                mmw.append(sget("xnf32_free"))
            tt = b.t(lambda e, kc=kc, pt=pt: e.transpose(
                pt[:, 0:128], sb["xnt"][:, kc * 128:(kc + 1) * 128], sb["idf"][:]),
                waits=mmw)
            ev1 = b.v(lambda e, kc=kc, it=it, pt=pt, stb=stb: e.tensor_copy(
                sb["xnfm"][stb, :, kc, it * CK:(it + 1) * CK], pt[:, 0:128]),
                waits=[tt])
            ev2 = b.s(lambda e, kc=kc, pt=pt: e.copy(
                sb["xnf32"][:, kc, :], pt[:, 0:128]), waits=[tt, ev1])
            bset(bank, ev1, ev2)
            S[("xnfm_k", kc)] = ev1
            S[("xnf32_k", kc)] = ev2
            evb = ev1
            if kc == 3:
                yield 0
        S["xnt_free"] = evb
        S[("xnfm_tile", stb, it)] = evb
        yield 0
        if SCHED == "ln":
            trunc(evb)
            return
        gm = None
        for kc in range(8):
            gm = b.t(lambda e, kc=kc: e.matmul(
                pmed[:, 128:136], sb["xnf32"][:, kc, :], sb["wgc"][:, kc, :],
                start=(kc == 0), stop=(kc == 7)),
                waits=[tok_wg, S[("xnf32_k", kc)]] + (bw("med") if kc == 0 else []))
        S["xnf32_free"] = gm
        g0 = b.v(lambda e: e.tensor_tensor(sb["glt"][:], pmed[:, 128:136], sb["bgc"][:],
                                           ALU.add),
                 waits=[gm, tok_bg, sget("glt_free")])
        bset("med", g0)
        m1 = b.v(lambda e: e.tensor_reduce(sb["gsc"][:, 5:6], sb["glt"][:], AX, ALU.max),
                 waits=[g0])
        nm1 = b.v(lambda e: e.tensor_reduce(sb["gsc"][:, 6:7], sb["glt"][:], AX, ALU.max,
                                            negate=True), waits=[g0])
        eq = b.v(lambda e: e.tensor_scalar(sb["gsc"][:, 8:12].bitcast(F32) if False
                                           else sb["dscr"][:, 0:8], sb["glt"][:],
                                           sb["gsc"][:, 5:6], None, ALU.is_equal),
                 waits=[m1, sget("dscr_free")])
        mk = b.v(lambda e: e.scalar_tensor_tensor(sb["dscr"][:, 0:8], sb["dscr"][:, 0:8],
                                                  -1e30, sb["glt"][:], ALU.mult, ALU.add),
                 waits=[eq])
        m2 = b.v(lambda e: e.tensor_reduce(sb["gsc"][:, 7:8], sb["dscr"][:, 0:8], AX,
                                           ALU.max), waits=[mk])
        ge = b.v(lambda e: e.tensor_scalar(sb["dscr"][:, 0:8], sb["glt"][:],
                                           sb["gsc"][:, 7:8], None, ALU.is_ge),
                 waits=[m2])
        dm = b.v(lambda e: e.tensor_tensor(sb["gsc"][:, 7:8], sb["gsc"][:, 7:8],
                                           sb["gsc"][:, 6:7], ALU.add),
                 waits=[m2, nm1, ge])
        ew = b.s(lambda e: e.activation(sb["glt"][:], sb["glt"][:], AF.Exp,
                                        bias=sb["gsc"][:, 6:7]), waits=[nm1, ge, mk])
        ed = b.s(lambda e: e.activation(sb["gsc"][:, 7:8], sb["gsc"][:, 7:8], AF.Exp),
                 waits=[dm])
        d1 = b.v(lambda e: e.tensor_scalar_add(sb["gsc"][:, 7:8], sb["gsc"][:, 7:8],
                                               1.0), waits=[ed])
        rc = b.v(lambda e: e.reciprocal(sb["gsc"][:, 7:8], sb["gsc"][:, 7:8]),
                 waits=[d1])
        c1 = b.v(lambda e: e.tensor_tensor(sb["glt"][:], sb["glt"][:], sb["dscr"][:, 0:8],
                                           ALU.mult), waits=[ew])
        c2 = b.v(lambda e, stb=stb, it=it: e.tensor_scalar_mul(
            sb["comb"][stb, :, it, :], sb["glt"][:], sb["gsc"][:, 7:8]),
            waits=[c1, rc, sget(("comb_free", stb))])
        S["glt_free"] = c2
        S["gsc_free"] = c2
        S["dscr_free"] = c2
        S[("comb_tile", stb, it)] = c2
        yield 0
        if SCHED == "lngate":
            S[("asm_center", i)] = c2
            S[("dtr", i)] = c2
            return

        ar = i % 2
        last_ctr = None
        for g in range(9):
            yield 1
            gtok, pp = S["win_next"]
            if g + 1 < 9:
                S["win_next"] = win_load(g + 1)
            elif i + 1 < nt:
                S["win_next"] = win_load(0)
            if g < 4:
                bank = "b%d" % (g % 4)
                mm = None
                for kc in range(8):
                    mm = b.t(lambda e, kc=kc, g=g, pp=pp, it=it: e.matmul(
                        pbig[:, (g % 4) * 512:(g % 4) * 512 + 512],
                        sb["xnfm"][stb, :, kc, it * CK:(it + 1) * CK],
                        sb["win"][pp, :, kc, :],
                        start=(kc == 0), stop=(kc == 7)),
                        waits=[gtok, S[("xnfm_k", kc)]] + (bw(bank) if kc == 0 else []))
                S[("win_free", pp)] = mm
                for hz in range(1):
                    pass
                ev = b.v(lambda e, g=g: e.tensor_copy(
                    sb["zt"][g % 2], pbig[:, (g % 4) * 512:(g % 4) * 512 + 512]),
                    waits=[mm, sget(("zt_free", g % 2))])
                bset(bank, ev)
                sp = b.dsync(lambda e, i=i, g=g: e.dma_start(
                    dr["z_sp"][i * CK:(i + 1) * CK, g * 512:(g + 1) * 512],
                    sb["zt"][g % 2]), waits=[ev])
                S[("zt_free", g % 2)] = sp
            else:
                nsub = 3 if g == 8 else 4
                for sub in range(nsub):
                    ch_abs = (g - 4) * 4 + sub
                    bank = "b%d" % (ch_abs % 4)
                    mm = None
                    if ch_abs < 18:
                        for kc in range(8):
                            mm = b.t(lambda e, kc=kc, pp=pp, sub=sub, it=it, stb=stb,
                                     ch_abs=ch_abs: e.matmul(
                                pbig[:, (ch_abs % 4) * 512:(ch_abs % 4) * 512 + 128],
                                sb["win"][pp, :, kc, sub * 128:sub * 128 + 128],
                                sb["xnfm"][stb, :, kc, it * CK:(it + 1) * CK],
                                start=(kc == 0), stop=(kc == 7)),
                                waits=[gtok, S[("xnfm_k", kc)]]
                                + (bw(bank) if kc == 0 else []))
                        ev = b.v(lambda e, ar=ar, ch_abs=ch_abs: e.tensor_copy(
                            sb["asm"][ar, :, ch_abs, 3:131],
                            pbig[:, (ch_abs % 4) * 512:(ch_abs % 4) * 512 + 128]),
                            waits=[mm, sget(("asm_free", ar))])
                        last_ctr = ev
                    else:
                        for kc in range(8):
                            mm = b.t(lambda e, kc=kc, pp=pp, it=it, stb=stb,
                                     ch_abs=ch_abs: e.matmul(
                                pbig[:, (ch_abs % 4) * 512:(ch_abs % 4) * 512 + 64],
                                sb["xnfm"][stb, :, kc, it * CK:(it + 1) * CK],
                                sb["win"][pp, :, kc, 256:320],
                                start=(kc == 0), stop=(kc == 7)),
                                waits=[gtok, S[("xnfm_k", kc)]]
                                + (bw(bank) if kc == 0 else []))
                        ev = b.v(lambda e, r=r, ch_abs=ch_abs: e.tensor_copy(
                            sb["dtr"][r],
                            pbig[:, (ch_abs % 4) * 512:(ch_abs % 4) * 512 + 64]),
                            waits=[mm, sget(("dtr_free", r))])
                        S[("dtr", i)] = ev
                    S[("win_free", pp)] = mm
                    bset(bank, ev)
        S[("asm_center", i)] = last_ctr
        if i > 0:
            pv = (i - 1) % 2
            e1 = b.v(lambda e, ar=ar, pv=pv: e.tensor_copy(
                sb["asm"][ar, :, :, 0:3], sb["asm"][pv, :, :, 128:131]),
                waits=[last_ctr, S[("asm_center", i - 1)]])
            e2 = b.v(lambda e, ar=ar, pv=pv: e.tensor_copy(
                sb["asm"][pv, :, :, 131:134], sb["asm"][ar, :, :, 3:6]),
                waits=[last_ctr, S[("asm_center", i - 1)]])
            S[("asm_ready", i - 1)] = e2
            S[("asm_left", i)] = e1
        else:
            S[("asm_left", 0)] = S["asm_z0"]

    def conv_ssd(c):
        rr = c % 2
        ar = c % 2
        asm_ok = [S[("asm_ready", c)], S[("asm_left", c)], tok_cdw, tok_cbc]
        ev = None
        at = None
        prev_evs = []
        for (ch0, ch1) in ((16, 18), (0, 6), (6, 12), (12, 16)):
            nch = ch1 - ch0
            pw = asm_ok + prev_evs + ([at] if at is not None else [])
            at = b.g(lambda e, ch0=ch0, nch=nch, ar=ar: e.tensor_tensor(
                sb["cva"][:, 0:nch, :],
                sb["asm"][ar][:, ch0:ch0 + nch, 0:128],
                sb["wpool"][:, ch0:ch0 + nch, 0:1]
                .broadcast_to([128, nch, 128]), ALU.mult), waits=pw)
            for k in range(1, 7):
                tk = b.g(lambda e, ch0=ch0, nch=nch, k=k, ar=ar: e.tensor_tensor(
                    sb["cvt"][:, 0:nch, :],
                    sb["asm"][ar][:, ch0:ch0 + nch, k:k + 128],
                    sb["wpool"][:, ch0:ch0 + nch, k:k + 1]
                    .broadcast_to([128, nch, 128]), ALU.mult),
                    waits=pw + [at])
                at = b.g(lambda e, nch=nch: e.tensor_tensor(
                    sb["cva"][:, 0:nch, :], sb["cva"][:, 0:nch, :],
                    sb["cvt"][:, 0:nch, :], ALU.add), waits=[tk, at])
            prev_evs = []
            for j in range(nch):
                ch = ch0 + j
                if ch < 16:
                    ev = b.s(lambda e, ch=ch, j=j, rr=rr: e.activation(
                        sb["xifm"][rr, :, ch, :], sb["cva"][:, j, :], AF.Silu,
                        bias=sb["cbc"][:, ch:ch + 1]),
                        waits=[at, sget(("xifm_free", rr))])
                    S[("xifm_ch", rr, ch)] = ev
                else:
                    ev = b.s(lambda e, ch=ch, j=j, rr=rr: e.activation(
                        sb["bcfm"][rr, :, ch - 16, :], sb["cva"][:, j, :], AF.Silu,
                        bias=sb["cbc"][:, ch:ch + 1]),
                        waits=[at, sget(("bcfm_free", rr))])
                    S[("bcfm_ch", rr, ch - 16)] = ev
                prev_evs.append(ev)
            yield 0
        S[("asm_free", ar)] = at

        tts = []
        for j in range(2):
            tts.append(b.t(lambda e, j=j, rr=rr: e.transpose(
                ptb[:, j * 128:(j + 1) * 128], sb["bcfm"][rr, :, j, :], sb["idb"][:]),
                waits=[S[("bcfm_ch", rr, j)], tok_idb] + bw("tb")))
        bc2 = None
        for j in range(2):
            bc2 = b.v(lambda e, j=j, rr=rr: e.tensor_copy(
                sb["bct"][rr, :, j * 128:(j + 1) * 128], ptb[:, j * 128:(j + 1) * 128]),
                waits=[tts[-1], sget(("bct_free", rr))])
        bset("tb", bc2)
        S[("bct", c)] = bc2
        yield 0
        spbc = b.dsync(lambda e, c=c, rr=rr: e.dma_start(
            dr["bc_sp"][c * CK:(c + 1) * CK, :], sb["bct"][rr]), waits=[bc2])

        evs = []
        for wave in range(2):
            tts = []
            for sl in range(8):
                ch = wave * 8 + sl
                tw = [S[("xifm_ch", rr, ch)], tok_idb]
                tw.extend(bw("tb") if ch == 0 else ([evs[-1]] if ch == 8 else []))
                tts.append(b.t(lambda e, ch=ch, sl=sl, rr=rr: e.transpose(
                    ptb[:, sl * 128:(sl + 1) * 128], sb["xifm"][rr, :, ch, :],
                    sb["idb"][:]), waits=tw))
            for sl in range(8):
                ch = wave * 8 + sl
                evx = b.v(lambda e, ch=ch, sl=sl, rr=rr: e.tensor_copy(
                    sb["xitm"][rr, :, ch * 128:(ch + 1) * 128],
                    ptb[:, sl * 128:(sl + 1) * 128]),
                    waits=[tts[-1], sget(("xitm_free", rr))])
                evs.append(evx)
            yield 1
        bset("tb", evs[-1])
        xitm_tok = evs[-1]
        S[("xitm", c)] = xitm_tok
        spxi = b.dsync(lambda e, c=c, rr=rr: e.dma_start(
            dr["xi_sp"][c * CK:(c + 1) * CK, :], sb["xitm"][rr]), waits=[xitm_tok])

        fm = None
        for kc in range(16):
            fm = b.t(lambda e, kc=kc, rr=rr: e.matmul(
                pmed[:, 256:288], sb["xifm"][rr, :, kc, :], sb["fcdw"][:, kc, :],
                start=(kc == 0), stop=(kc == 15)),
                waits=[S[("xifm_ch", rr, kc)], tok_fcdw] + (bw("med") if kc == 0 else []))
        S[("xifm_free", rr)] = fm
        fe = b.v(lambda e, c=c: e.tensor_tensor(
            sb["fcd"][:, c, :], pmed[:, 256:288], sb["dparc"][:], ALU.add),
            waits=[fm, tok_dpar])
        bset("med", fe)
        S[("fcd", c)] = fe
        yield 0

        a0 = b.v(lambda e, c=c: e.tensor_tensor(sb["dscr"][:, 0:64], sb["dtr"][c % 2],
                                                sb["dtbc"][:], ALU.add),
                 waits=[S[("dtr", c)], tok_dtb, sget("dscr_free")])
        a1 = b.s(lambda e, rr=rr: e.activation(sb["dts"][rr], sb["dscr"][:, 0:64],
                                               AF.Abs),
                 waits=[a0, sget(("dts_free", rr))])
        a2 = b.s(lambda e, rr=rr: e.activation(sb["dts"][rr], sb["dts"][rr], AF.Exp,
                                               scale=-1.0), waits=[a1])
        a3 = b.s(lambda e, rr=rr: e.activation(sb["dts"][rr], sb["dts"][rr], AF.Ln,
                                               bias=1.0), waits=[a2])
        a4 = b.v(lambda e, rr=rr: e.scalar_tensor_tensor(
            sb["dts"][rr], sb["dscr"][:, 0:64], 0.0, sb["dts"][rr], ALU.max, ALU.add),
            waits=[a3])
        S[("dtr_free", c % 2)] = a4
        S["dscr_free"] = a4
        a5 = b.v(lambda e, rr=rr: e.tensor_copy(sb["dtb16"][rr], sb["dts"][rr]),
                 waits=[a4, sget(("dtb16_free", rr))])
        yield 0
        td = b.t(lambda e, rr=rr: e.transpose(pyd[0:64, 0:128], sb["dts"][rr],
                                              sb["idf"][:]),
                 waits=[a4, tok_idf] + bw("yd"))
        ab0 = b.v(lambda e: e.tensor_scalar_mul(sb["abf"][:], pyd[0:64, 0:128],
                                                sb["nega"][:, 0:1]),
                  waits=[td, tok_nega, sget("abf_free")])
        bset("yd", ab0)
        sc0 = b.v(lambda e, rr=rr: e.tensor_tensor_scan(
            sb["alf"][rr], sb["abf"][:], sb["abf"][:], 0.0, ALU.add, ALU.bypass),
            waits=[ab0, sget(("alf_free", rr))])
        l0 = b.v(lambda e, rr=rr: e.tensor_scalar(
            sb["expdfm"][32:64, :], sb["alf"][rr][32:64, :],
            sb["alf"][rr][32:64, 127:128], -1.0, ALU.subtract, ALU.mult),
            waits=[sc0, sget("expdfm_free")])
        l1 = b.v(lambda e, rr=rr: e.tensor_tensor(
            sb["alf"][rr][32:64, :], sb["expdfm"][32:64, :], sb["abf"][32:64, :],
            ALU.add), waits=[l0])
        S["abf_free"] = l1
        alf_tok = l1
        p0 = b.v(lambda e, rr=rr: e.tensor_copy(sb["poshi"][rr], sb["alf"][rr]),
                 waits=[alf_tok, sget(("poshi_free", rr))])
        p1 = b.v(lambda e, rr=rr: e.scalar_tensor_tensor(
            sb["poslo"][rr], sb["alf"][rr], 1.0, sb["poshi"][rr], ALU.bypass,
            ALU.subtract), waits=[p0, sget(("poslo_free", rr))])
        ta = b.t(lambda e, rr=rr: e.transpose(pyo[0:128, 0:64], sb["alf"][rr],
                                              sb["idf"][0:64, 0:64]),
                 waits=[alf_tok, tok_idf] + bw("yo"))
        at0 = b.v(lambda e, rr=rr: e.tensor_copy(sb["altm"][rr], pyo[0:128, 0:64]),
                  waits=[ta, sget(("altm_free", rr))])
        bset("yo", at0)
        at1 = b.v(lambda e, rr=rr: e.tensor_scalar_mul(sb["naltm"][rr], sb["altm"][rr],
                                                       -1.0),
                  waits=[at0, sget(("naltm_free", rr))])
        ae = b.s(lambda e, rr=rr: e.activation(sb["expaltm"][rr], sb["altm"][rr],
                                               AF.Exp),
                 waits=[at0, sget(("expaltm_free", rr))])
        spal = b.dsync(lambda e, c=c, rr=rr: e.dma_start(
            dr["expal_sp"][c * CK:(c + 1) * CK, :], sb["expaltm"][rr]), waits=[ae])
        sm0 = b.v(lambda e, rr=rr: e.tensor_copy(sb["sumc"][0:32, :],
                                                 sb["alf"][rr][0:32, 127:128]),
                  waits=[sc0, sget("sumc_free")])
        sm1 = b.v(lambda e, rr=rr: e.tensor_copy(sb["sumc"][32:64, :],
                                                 sb["alf"][rr][32:64, 0:1]),
                  waits=[l1, sm0])
        ed0 = b.s(lambda e, rr=rr: e.activation(
            sb["expdfm"][:], sb["alf"][rr], AF.Exp, scale=-1.0,
            bias=sb["sumc"][:, 0:1]), waits=[l1, sm1])
        te = b.t(lambda e: e.transpose(pyd[0:128, 0:64], sb["expdfm"][:],
                                       sb["idf"][0:64, 0:64]),
                 waits=[ed0] + bw("yd"))
        et0 = b.v(lambda e, rr=rr: e.tensor_copy(sb["expdtm"][rr], pyd[0:128, 0:64]),
                  waits=[te, sget(("expdtm_free", rr))])
        bset("yd", et0)
        S["expdfm_free"] = et0
        tdr = b.t(lambda e: e.transpose(pyo[0:1, 0:64], sb["sumc"][:, 0:1],
                                        sb["idf"][0:64, 0:64]),
                  waits=[sm1] + bw("yo"))
        dr0 = b.s(lambda e, c=c: e.activation(sb["drows"][:, c, :], pyo[0:1, 0:64],
                                              AF.Exp), waits=[tdr])
        bset("yo", dr0)
        dr1 = b.v(lambda e, c=c, rr=rr: e.tensor_copy(sb["drb"][rr],
                                                      sb["drows"][:, c, :]),
                  waits=[dr0, sget(("drb_free", rr))])
        S["sumc_free"] = dr0
        yield 1

        yfree = None
        for d in (0, 1):
            h0 = 32 * d
            gmm = b.t(lambda e, d=d, rr=rr: e.matmul(
                pmed[:, 0:128], sb["bcfm"][rr, 64 * d:64 * d + 64, 0, :],
                sb["bcfm"][rr, 64 * d:64 * d + 64, 1, :],
                start=True, stop=True),
                waits=[S[("bcfm_ch", rr, 0)], S[("bcfm_ch", rr, 1)]] + bw("med"))
            yield 1
            for q in range(4):
                yield 0
                pair = q % 2
                cb = pair * 1024
                bkA, bkB = "b%d" % (2 * pair), "b%d" % (2 * pair + 1)
                stops = []
                for h8 in range(8):
                    h = 8 * q + h8
                    colw = bw(bkA) + bw(bkB) if h8 == 0 else []
                    b.t(lambda e, h=h, h8=h8, rr=rr, h0=h0, cb=cb: e.matmul(
                        pbig[:, cb + h8 * 128:cb + h8 * 128 + 128],
                        sb["idb"][0:64, h0 + h:h0 + h + 1].broadcast_to([64, 128]),
                        sb["poshi"][rr],
                        start=(h8 % 4 == 0), stop=False),
                        waits=[p0, tok_idb] + colw)
                    m2 = b.t(lambda e, h=h, h8=h8, rr=rr, h0=h0, cb=cb: e.matmul(
                        pbig[:, cb + h8 * 128:cb + h8 * 128 + 128],
                        sb["idb"][0:64, h0 + h:h0 + h + 1].broadcast_to([64, 128]),
                        sb["poslo"][rr],
                        start=False, stop=(h8 % 4 == 3)),
                        waits=[p1])
                    if h8 % 4 == 3:
                        stops.append(m2)
                etoks = []
                for h8 in range(8):
                    h = 8 * q + h8
                    ex = b.s(lambda e, h=h, h8=h8, rr=rr, h0=h0, cb=cb, pair=pair:
                             e.activation(
                                 sb["mall"][:, pair * 1024 + h8 * 128:
                                            pair * 1024 + h8 * 128 + 128],
                                 pbig[:, cb + h8 * 128:cb + h8 * 128 + 128], AF.Exp,
                                 bias=sb["naltm"][rr][:, h0 + h:h0 + h + 1]),
                             waits=[stops[h8 // 4], at1, sget(("mall_free", pair))])
                    etoks.append(ex)
                bset(bkA, etoks[3])
                bset(bkB, etoks[7])
                mzt = None
                for mq in range(2):
                    mzt = b.v(lambda e, d=d, mq=mq, pair=pair: e.copy_predicated(
                        sb["mall"][:, pair * 1024 + mq * 512:
                                   pair * 1024 + mq * 512 + 512],
                        sb["mskz"][:, d, :],
                        sb["zc"][:, 0:1].broadcast_to([128, 512])),
                        waits=etoks + [tok_msk, tok_zc])
                mg = b.v(lambda e, pair=pair: e.tensor_tensor(
                    r3(sb["mall"][:, pair * 1024:pair * 1024 + 1024], 128),
                    r3(sb["mall"][:, pair * 1024:pair * 1024 + 1024], 128),
                    pmed[:, 0:128].unsqueeze(1).broadcast_to([128, 8, 128]),
                    ALU.mult),
                    waits=[mzt, gmm])
                badd("med", mg)
                xb = b.v(lambda e, d=d, q=q, rr=rr, h0=h0, pair=pair: e.tensor_tensor(
                    r3(sb["xall"][pair], 64)[:, 0:8, :],
                    r3(sb["xitm"][rr][:, q * 512:(q + 1) * 512], 64),
                    sb["dtb16"][rr][:, h0 + 8 * q:h0 + 8 * q + 8]
                    .unsqueeze(2).broadcast_to([128, 8, 64]), ALU.mult),
                    waits=[S[("xitm", c)], a5, sget(("xall_free", pair))])
                bh = b.v(lambda e, d=d, q=q, rr=rr, h0=h0, pair=pair: e.tensor_tensor(
                    r3(sb["bhat"][pair], 64)[:, 0:8, :],
                    sb["bct"][rr][:, 64 * d:64 * d + 64]
                    .unsqueeze(1).broadcast_to([128, 8, 64]),
                    sb["expdtm"][rr][:, h0 + 8 * q:h0 + 8 * q + 8]
                    .unsqueeze(2).broadcast_to([128, 8, 64]), ALU.mult),
                    waits=[S[("bct", c)], et0, sget(("bhat_free", pair))])
                yield 2
                ydm = None
                for j in range(8):
                    ydm = b.t(lambda e, j=j, pair=pair: e.matmul(
                        pyd[:, j * 64:(j + 1) * 64],
                        sb["mall"][:, pair * 1024 + j * 128:pair * 1024 + j * 128 + 128],
                        sb["xall"][pair][:, j * 64:j * 64 + 64],
                        start=(j == 0), stop=(j == 7)),
                        waits=[mg, xb] + (bw("yd") if j == 0 else []))
                if d == 0:
                    yom = None
                    for j in range(8):
                        hq = 8 * q + j
                        yom = b.t(lambda e, hq=hq, j=j, rr=rr: e.matmul(
                            pyo[:, j * 64:(j + 1) * 64],
                            sb["bcfm"][rr, 0:64, 1, :],
                            sb["stB"][:, hq * 64:hq * 64 + 64],
                            start=(j == 0), stop=(j == 7)),
                            waits=[S[("bcfm_ch", rr, 1)], sget("sbf"), mg]
                            + (bw("yo") if j == 0 else []))
                    ye0 = b.v(lambda e, q=q, rr=rr, h0=h0: e.tensor_tensor(
                        r3(sb["yt"][q % 2], 64), r3(pyo[:, 0:512], 64),
                        sb["expaltm"][rr][:, h0 + 8 * q:h0 + 8 * q + 8]
                        .unsqueeze(2).broadcast_to([128, 8, 64]), ALU.mult),
                        waits=[yom, ae, sget(("yt_free", q % 2))])
                    bset("yo", ye0)
                    ye1 = b.v(lambda e, q=q: e.tensor_tensor(
                        sb["yt"][q % 2], sb["yt"][q % 2], pyd[:, 0:512], ALU.add),
                        waits=[ye0, ydm])
                    bset("yd", ye1)
                    spy = b.dsync(lambda e, c=c, q=q: e.dma_start(
                        dr["yfw_sp"][c * CK + 1:(c + 1) * CK + 1,
                                     q * 512:(q + 1) * 512],
                        sb["yt"][q % 2]), waits=[ye1])
                    S[("yt_free", q % 2)] = spy
                else:
                    ye1 = b.v(lambda e, q=q: e.tensor_copy(
                        sb["yt"][q % 2], pyd[:, 0:512]),
                        waits=[ydm, sget(("yt_free", q % 2))])
                    bset("yd", ye1)
                    spy = b.dsync(lambda e, c=c, q=q: e.dma_start(
                        dr["ybd_sp"][c * CK:(c + 1) * CK, q * 512:(q + 1) * 512],
                        sb["yt"][q % 2]), waits=[ye1])
                    S[("yt_free", q % 2)] = spy
                csm = None
                for j in range(8):
                    csm = b.t(lambda e, j=j, q=q, cb=cb, pair=pair: e.matmul(
                        pbig[0:64, cb + j * 64:cb + j * 64 + 64],
                        sb["bhat"][pair][:, j * 64:j * 64 + 64],
                        sb["xall"][pair][:, j * 64:j * 64 + 64],
                        start=(j == 0), stop=(j == 7)),
                        waits=[bh, xb, mg] + (bw(bkA) if j == 0 else []))
                S[("xall_free", pair)] = csm
                S[("bhat_free", pair)] = csm
                S[("mall_free", pair)] = csm
                if d == 0:
                    j = q
                    pt, bank = ((pyd, "yd") if j % 2 == 0 else (pyo, "yo"))
                    mm = b.t(lambda e, j=j, rr=rr, pt=pt: e.matmul(
                        pt[0:64, 0:512], sb["ones1b"][0:1, 0:64],
                        sb["drb"][rr][0:1, 8 * j:8 * j + 8]
                        .unsqueeze(2).broadcast_to([1, 8, 64]),
                        start=True, stop=True),
                        waits=[dr1, tok_o1b] + bw(bank))
                    evd = b.v(lambda e, j=j, pt=pt: e.tensor_copy(
                        sb["decrep"][:, j * 512:(j + 1) * 512], pt[0:64, 0:512]),
                        waits=[mm, sget("decrep_free")])
                    bset(bank, evd)
                    s_0 = b.v(lambda e, q=q: e.tensor_tensor(
                        sb["stA"][:, q * 512:(q + 1) * 512],
                        sb["stA"][:, q * 512:(q + 1) * 512],
                        sb["decrep"][:, q * 512:(q + 1) * 512], ALU.mult),
                        waits=[evd, sget("sfw"), yom])
                    s_1 = b.v(lambda e, q=q, cb=cb: e.tensor_tensor(
                        sb["stA"][:, q * 512:(q + 1) * 512],
                        sb["stA"][:, q * 512:(q + 1) * 512],
                        pbig[0:64, cb:cb + 512], ALU.add),
                        waits=[s_0, csm])
                    bset(bkA, s_1)
                    S[("sfw_h", q)] = s_1
                    s_2 = b.v(lambda e, q=q: e.tensor_copy(
                        sb["stB"][:, q * 512:(q + 1) * 512],
                        sb["stA"][:, q * 512:(q + 1) * 512]), waits=[s_1])
                    S[("sbf_h", q)] = s_2
                else:
                    cb0 = b.v(lambda e, q=q, cb=cb: e.tensor_copy(
                        sb["csbt"][:, q * 512:(q + 1) * 512],
                        pbig[0:64, cb:cb + 512]),
                        waits=[csm, sget("csbt_free")])
                    bset(bkA, cb0)
                    S[("csbt_h", q)] = cb0
            if d == 0:
                sfb = b.v(lambda e: e.engine_nop(),
                          waits=[S[("sfw_h", q)] for q in range(4)])
                S["sfw"] = sfb
                sbb = b.v(lambda e: e.engine_nop(),
                          waits=[S[("sbf_h", q)] for q in range(4)])
                S["sbf"] = sbb
                S["decrep_free"] = sfb
            else:
                spcs = b.dsync(lambda e, c=c: e.dma_start(dr["csbw_sp"][c],
                                                          sb["csbt"][:]),
                               waits=[S[("csbt_h", q)] for q in range(4)])
                S["csbt_free"] = spcs
                S[("csbw", c)] = spcs
                yfree = spcs
        S[("dts_free", rr)] = alf_tok
        S[("dtb16_free", rr)] = yfree
        S[("alf_free", rr)] = ed0
        S[("poshi_free", rr)] = yfree
        S[("poslo_free", rr)] = yfree
        S[("altm_free", rr)] = yfree
        S[("naltm_free", rr)] = yfree
        S[("expaltm_free", rr)] = yfree
        S[("expdtm_free", rr)] = yfree
        S[("drb_free", rr)] = yfree
        S[("bct_free", rr)] = yfree
        S[("xitm_free", rr)] = yfree
        S[("bcfm_free", rr)] = yfree

    def ffn(st):
        stb = st % 2
        TT = tps * CK
        xn_ready = [S[("xnfm_tile", stb, it)] for it in range(tps)]
        be = None
        for it in range(tps):
            tc = b.t(lambda e, stb=stb, it=it: e.transpose(
                pyd[0:8, 0:128], sb["comb"][stb, :, it, :], sb["idf"][:]),
                waits=[S[("comb_tile", stb, it)]] + bw("yd"))
            ce = b.v(lambda e: e.tensor_copy(sb["combT"][:], pyd[0:8, 0:128]),
                     waits=[tc, sget("combT_free")])
            bset("yd", ce)
            for nn2 in range(2):
                bm = b.t(lambda e, nn2=nn2: e.matmul(
                    pyo[:, 0:512], sb["combT"][:], sb["b2r"][:, nn2 * 512:(nn2 + 1) * 512],
                    start=True, stop=True),
                    waits=[ce, tok_b2r] + bw("yo"))
                be = b.v(lambda e, it=it, nn2=nn2: e.tensor_copy(
                    sb["acc"][:, it, nn2 * 512:(nn2 + 1) * 512], pyo[:, 0:512]),
                    waits=[bm, sget("acc_free")])
                bset("yo", be)
            S["combT_free"] = be
            yield
        acc_init = be
        prev_y = acc_init
        for ei in range(NE):
            for hf in range(2):
                pp = hf
                wtok = b.dscal(lambda e, ei=ei, hf=hf, pp=pp: e.dma_start(
                    sb["wA"][pp],
                    dr["w1_b"][ei][:, hf * 1024:(hf + 1) * 1024]
                    .rearrange("(c k) n -> k c n", k=128)),
                    waits=[sget(("wA_free", pp)), cv_w1[ei]])
                for hcl in range(8):
                    hc = 8 * hf + hcl
                    bank = "b%d" % (hcl % 4)
                    mm = None
                    for kc in range(8):
                        mm = b.t(lambda e, kc=kc, pp=pp, hcl=hcl, TT=TT: e.matmul(
                            pbig[:, (hcl % 4) * 512:(hcl % 4) * 512 + TT],
                            sb["wA"][pp, :, kc, hcl * 128:(hcl + 1) * 128],
                            sb["xnfm"][stb, :, kc, 0:TT],
                            start=(kc == 0), stop=(kc == 7)),
                            waits=[wtok] + xn_ready + (bw(bank) if kc == 0 else []))
                    hv = b.s(lambda e, hc=hc, ei=ei, hcl=hcl, TT=TT: e.activation(
                        sb["hb"][:, hc, 0:TT],
                        pbig[:, (hcl % 4) * 512:(hcl % 4) * 512 + TT], AF.Relu,
                        bias=sb["b1cc"][:, ei, hc:hc + 1]),
                        waits=[mm, tok_b1c, prev_y])
                    bset(bank, hv)
                    S[("hb_hc", hc)] = hv
                    if hcl % 2 == 1:
                        yield
                S[("wA_free", pp)] = mm
            ydone = None
            for chh in range(2):
                pp = chh
                w2tok = b.dscal(lambda e, ei=ei, chh=chh, pp=pp: e.dma_start(
                    sb["wA"][pp].rearrange("p a n -> p (a n)")
                    .rearrange("p (c n) -> p c n", n=512),
                    dr["w2_b"][ei][:, chh * 512:(chh + 1) * 512]
                    .rearrange("(c k) n -> k c n", k=128)),
                    waits=[sget(("wA_free", pp)), cv_w2[ei]])
                for tb in range(tps):
                    pt, bank = ((pyd, "yd") if tb % 2 == 0 else (pyo, "yo"))
                    mm = None
                    for kc in range(16):
                        mm = b.t(lambda e, kc=kc, pp=pp, tb=tb, pt=pt: e.matmul(
                            pt[:, 0:512],
                            sb["hb"][:, kc, tb * 128:(tb + 1) * 128],
                            sb["wA"][pp].rearrange("p a n -> p (a n)")
                            .rearrange("p (c n) -> p c n", n=512)[:, kc, :],
                            start=(kc == 0), stop=(kc == 15)),
                            waits=[w2tok, S[("hb_hc", kc)]] + (bw(bank) if kc == 0 else []))
                    ac = b.v(lambda e, tb=tb, chh=chh, stb=stb, ei=ei, pt=pt:
                             e.scalar_tensor_tensor(
                                 sb["acc"][:, tb, chh * 512:(chh + 1) * 512],
                                 pt[:, 0:512],
                                 sb["comb"][stb, :, tb, ei:ei + 1],
                                 sb["acc"][:, tb, chh * 512:(chh + 1) * 512],
                                 ALU.mult, ALU.add),
                             waits=[mm, S[("comb_tile", stb, tb)], acc_init])
                    bset(bank, ac)
                    ydone = ac
                    yield
                S[("wA_free", pp)] = mm
            prev_y = ydone
        spa = b.dsync(lambda e, st=st, TT=TT: e.dma_start(
            dr["ffn_sp"][st * TT:(st + 1) * TT, :].rearrange("(t p) c -> p t c", p=128),
            sb["acc"][:, 0:tps, :]), waits=[prev_y])
        S["acc_free"] = spa
        S[("comb_free", stb)] = prev_y
        S[("xnfm_free", stb)] = prev_y

    def tile_b(c):
        rb = c % 2
        if c < nt - 1:
            d0 = b.v(lambda e, c=c, rb=rb: e.tensor_copy(
                sb["drb"][rb], sb["drows"][:, c + 1, :]),
                waits=[sget(("drb_free", rb))])
            S[("drb_free", rb)] = d0
            dts_ = []
            for j in range(4):
                pt, bank = ((pyd, "yd") if j % 2 == 0 else (pyo, "yo"))
                mm = b.t(lambda e, j=j, rb=rb, pt=pt: e.matmul(
                    pt[0:64, 0:512], sb["ones1b"][0:1, 0:64],
                    sb["drb"][rb][0:1, 32 + 8 * j:32 + 8 * j + 8]
                    .unsqueeze(2).broadcast_to([1, 8, 64]),
                    start=True, stop=True),
                    waits=[d0, tok_o1b] + bw(bank))
                evd = b.s(lambda e, j=j, pt=pt: e.copy(
                    sb["decrep"][:, j * 512:(j + 1) * 512], pt[0:64, 0:512]),
                    waits=[mm, sget("decrep_a"), sget("decrep_b")])
                bset(bank, evd)
                dts_.append(evd)
            rl = b.dsync(lambda e, c=c: e.dma_start(
                sb["csbt"][:], dr["csbw_sp"][c + 1]),
                waits=[sget("csrl_a"), sget("csrl_b"), S[("csbw", c + 1)]])
            s_0a = b.v(lambda e: e.tensor_tensor(
                sb["stA"][:, 0:CS], sb["stA"][:, 0:CS],
                sb["decrep"][:, 0:CS], ALU.mult),
                waits=dts_[0:3] + [sget("sbw_a")])
            s_0b = b.g(lambda e: e.tensor_tensor(
                sb["stA"][:, CS:2048], sb["stA"][:, CS:2048],
                sb["decrep"][:, CS:2048], ALU.mult),
                waits=dts_[2:4] + [sget("sbw_b")])
            S["decrep_a"] = s_0a
            S["decrep_b"] = s_0b
            s_1a = b.v(lambda e: e.tensor_tensor(
                sb["stA"][:, 0:CS], sb["stA"][:, 0:CS], sb["csbt"][:, 0:CS],
                ALU.add), waits=[s_0a, rl])
            s_1b = b.g(lambda e: e.tensor_tensor(
                sb["stA"][:, CS:2048], sb["stA"][:, CS:2048],
                sb["csbt"][:, CS:2048], ALU.add), waits=[s_0b, rl])
            S["csrl_a"] = s_1a
            S["csrl_b"] = s_1b
            S["sbw_a"] = s_1a
            S["sbw_b"] = s_1b
            s_2 = b.v(lambda e: e.tensor_copy(sb["stB"][:], sb["stA"][:]),
                      waits=[s_1a, s_1b, sget("sbwbf_rd")])
            S["sbwbf"] = s_2
            S["sbwbf_g"] = s_2
        rl_exp = b.dsync(lambda e, c=c, rb=rb: e.dma_start(
            sb["expb"][rb], dr["expal_sp"][c * CK:(c + 1) * CK, 32:64]),
            waits=[sget(("expb_free", rb))])
        rl_cbw = b.dsync(lambda e, c=c, rb=rb: e.dma_start(
            sb["cbw"][rb], dr["bc_sp"][c * CK:(c + 1) * CK, 192:256]),
            waits=[sget(("cbw_free", rb))])
        rl_ybd = b.dsync(lambda e, c=c, rb=rb: e.dma_start(
            sb["ybdt"][rb], dr["ybd_sp"][c * CK:(c + 1) * CK, :]),
            waits=[sget(("ybdt_free", rb))])
        tcb = b.t(lambda e, rb=rb: e.transpose(ptb[0:64, 0:128], sb["cbw"][rb],
                                               sb["idb"][:]),
                  waits=[rl_cbw, sget("cbwfm_free")] + bw("tb"))
        ecb = b.v(lambda e: e.tensor_copy(sb["cbwfm"][:], ptb[0:64, 0:128]),
                  waits=[tcb])
        bset("tb", ecb)
        S[("cbw_free", rb)] = ecb
        ye1 = None
        for q in range(4):
            yom = None
            for j in range(8):
                hq = 8 * q + j
                yom = b.t(lambda e, hq=hq, j=j: e.matmul(
                    pyo[:, j * 64:(j + 1) * 64], sb["cbwfm"][:],
                    sb["stB"][:, hq * 64:hq * 64 + 64],
                    start=(j == 0), stop=(j == 7)),
                    waits=[ecb, sget("sbwbf"), sget("sbwbf_g")]
                    + (bw("yo") if j == 0 else []))
            ye0 = b.v(lambda e, q=q, rb=rb: e.tensor_tensor(
                r3(sb["ybwt"][rb][:, q * 512:(q + 1) * 512], 64),
                r3(pyo[:, 0:512], 64),
                sb["expb"][rb][:, 8 * q:8 * q + 8]
                .unsqueeze(2).broadcast_to([128, 8, 64]), ALU.mult),
                waits=[yom, rl_exp, sget(("ybwt_free", rb))])
            bset("yo", ye0)
            ye1 = b.v(lambda e, q=q, rb=rb: e.tensor_tensor(
                sb["ybwt"][rb][:, q * 512:(q + 1) * 512],
                sb["ybwt"][rb][:, q * 512:(q + 1) * 512],
                sb["ybdt"][rb][:, q * 512:(q + 1) * 512], ALU.add),
                waits=[ye0, rl_ybd])
        S[("ybdt_free", rb)] = ye1
        S["cbwfm_free"] = yom
        S["sbwbf_rd"] = yom
        S[("ybwt_ye", c)] = ye1
        smms = []
        for nb in range(4):
            bank = "b%d" % nb
            sm = b.t(lambda e, nb=nb, rb=rb, c=c: e.matmul(
                pbig[:, nb * 512:(nb + 1) * 512], sb["shm"][:, 0:128],
                sb["ybwt"][rb][:, nb * 512:(nb + 1) * 512],
                start=True, stop=(c == nt - 1)),
                waits=[ye1, tok_shm] + bw(bank))
            if c < nt - 1:
                sm = b.t(lambda e, nb=nb, rb=rb: e.matmul(
                    pbig[:, nb * 512:(nb + 1) * 512], sb["shm"][:, 128:256],
                    sb["ybwt"][rb ^ 1][:, nb * 512:(nb + 1) * 512],
                    start=False, stop=True),
                    waits=[S[("ybwt_ye", c + 1)]])
            smms.append(sm)
        if c < nt - 1:
            S[("ybwt_free", rb ^ 1)] = smms[-1]
        ybse = []
        for nb in range(4):
            ev = b.s(lambda e, nb=nb, rb=rb: e.copy(
                sb["ybshs"][rb][:, nb * 512:(nb + 1) * 512],
                pbig[:, nb * 512:(nb + 1) * 512]),
                waits=[smms[nb], sget(("ybshs_f", rb))])
            bset("b%d" % nb, ev)
            ybse.append(ev)
        S[("ybse", c)] = ybse

    def tile_c(c):
        rb = c % 2
        ybse = S[("ybse", c)]
        rl_yfw = b.dsync(lambda e, c=c, rb=rb: e.dma_start(
            sb["yfwt"][rb], dr["yfw_sp"][c * CK:(c + 1) * CK, :]),
            waits=[sget(("yfwt_free", rb)), sget(("yfwt_free2", rb))])
        rl_xi = b.dsync(lambda e, c=c, rb=rb: e.dma_start(
            sb["xirl"][rb], dr["xi_sp"][c * CK:(c + 1) * CK, :]),
            waits=[sget(("xirl_free", rb))])
        rl_z = b.dsync(lambda e, c=c, rb=rb: e.dma_start(
            sb["zrl"][rb], dr["z_sp"][c * CK:(c + 1) * CK, :]),
            waits=[sget(("zrl_free", rb)), sget(("zrl_free2", rb))])
        rl_ff = b.dsync(lambda e, c=c, rb=rb: e.dma_start(
            sb["ffrl"][rb], dr["ffn_sp"][c * CK:(c + 1) * CK, :]),
            waits=[sget(("ffrl_free", rb)), sget(("ffrl_free2", rb)),
                   sget("acc_free")])
        rl_x = b.dscal(lambda e, c=c, rb=rb: e.dma_start(
            sb["xt"][rb], dr["u"][c * CK:(c + 1) * CK, :]),
            waits=[sget(("xt_free", rb))])
        y0w = [rl_xi, S[("fcd", c)], sget(("ycmb_free", 0)),
               sget(("ycmb_free", 1))]
        y0a = b.v(lambda e, c=c, rb=rb: e.tensor_tensor(
            r3(sb["ycmb"][rb][:, 0:CS], 64), r3(sb["xirl"][rb][:, 0:CS], 64),
            sb["fcd"][:, c, 0:CS // 64].unsqueeze(2)
            .broadcast_to([128, CS // 64, 64]), ALU.mult), waits=y0w)
        y0b = b.g(lambda e, c=c, rb=rb: e.tensor_tensor(
            r3(sb["ycmb"][rb][:, CS:2048], 64), r3(sb["xirl"][rb][:, CS:2048], 64),
            sb["fcd"][:, c, CS // 64:32].unsqueeze(2)
            .broadcast_to([128, 32 - CS // 64, 64]), ALU.mult), waits=y0w)
        y1 = b.v(lambda e, rb=rb: e.tensor_tensor(
            sb["ycmb"][rb], sb["ycmb"][rb], sb["yfwt"][rb], ALU.add),
            waits=[y0a, y0b, rl_yfw])
        y2a = b.v(lambda e, rb=rb: e.tensor_tensor(
            sb["ycmb"][rb], sb["ycmb"][rb], sb["ybshs"][rb], ALU.add),
            waits=[y1] + ybse)
        y2b = y2a
        S[("ybshs_f", rb)] = y2a
        S[("yfwt_free", rb)] = y1
        S[("yfwt_free2", rb)] = y1
        sz = b.s(lambda e, rb=rb: e.activation(sb["zrl"][rb], sb["zrl"][rb], AF.Silu),
                 waits=[rl_z])
        ssq = b.s(lambda e, rb=rb: e.activation(sb["actt"][rb], sb["ycmb"][rb],
                                                AF.Square, accum_out=sb["gsc"][:, 0:1]),
                  waits=[y2a, y2b, sget("gsc_free"), sget(("actt_free", rb))])
        r0a = b.v(lambda e: e.tensor_scalar(sb["gsc"][:, 1:2], sb["gsc"][:, 0:1],
                                            1.0 / DI, 1e-5, ALU.mult, ALU.add),
                  waits=[ssq])
        r0 = b.s(lambda e: e.activation(sb["gsc"][:, 1:2], sb["gsc"][:, 1:2], AF.Ln),
                 waits=[r0a])
        r1a = b.v(lambda e: e.tensor_scalar_mul(sb["gsc"][:, 2:3], sb["gsc"][:, 1:2],
                                                -0.5), waits=[r0])
        r1 = b.s(lambda e: e.activation(sb["gsc"][:, 2:3], sb["gsc"][:, 2:3], AF.Exp),
                 waits=[r1a])
        v0 = b.v(lambda e, rb=rb: e.tensor_scalar_mul(sb["actt"][rb], sb["ycmb"][rb],
                                                      sb["gsc"][:, 2:3]),
                 waits=[r1, sget(("actt_free", rb))])
        S[("ycmb_free", 0)] = v0
        S[("ycmb_free", 1)] = v0
        S["gsc_free"] = v0
        v1 = b.v(lambda e, rb=rb: e.tensor_tensor(
            sb["actt"][rb], sb["actt"][rb], sb["nwb"][:], ALU.mult),
            waits=[v0, tok_nwb])
        v2a = b.v(lambda e, rb=rb: e.tensor_tensor(
            sb["actt"][rb], sb["actt"][rb], sb["zrl"][rb], ALU.mult),
            waits=[v1, sz])
        v2b = v2a
        S[("zrl_free", rb)] = v2a
        S[("zrl_free2", rb)] = v2a
        aevs = []
        for wave in range(2):
            tts2 = []
            for sl in range(8):
                chk = wave * 8 + sl
                tw = [v2a if chk < CS // 128 else v2b, tok_idb]
                tw.extend(bw("tb") if chk == 0 else ([aevs[-1]] if chk == 8 else []))
                tts2.append(b.t(lambda e, chk=chk, sl=sl, rb=rb: e.transpose(
                    ptb[:, sl * 128:(sl + 1) * 128],
                    sb["actt"][rb][:, chk * 128:(chk + 1) * 128], sb["idb"][:]),
                    waits=tw))
            for sl in range(8):
                chk = wave * 8 + sl
                mk = b.v if wave == 0 else b.s
                fn = ((lambda e, chk=chk, sl=sl, rb=rb: e.tensor_copy(
                    sb["actf"][rb][:, chk, :], ptb[:, sl * 128:(sl + 1) * 128]))
                    if wave == 0 else
                    (lambda e, chk=chk, sl=sl, rb=rb: e.copy(
                        sb["actf"][rb][:, chk, :], ptb[:, sl * 128:(sl + 1) * 128])))
                evx = mk(fn, waits=[tts2[-1], sget(("actf_free", rb))])
                aevs.append(evx)
        bset("tb", aevs[-1])
        S[("actt_free", rb)] = aevs[-1]
        o1s = []
        mm = None
        for chh in range(2):
            pt, bank = ((pyd, "yd") if chh == 0 else (pyo, "yo"))
            mm = None
            for kc in range(16):
                mm = b.t(lambda e, kc=kc, chh=chh, rb=rb, pt=pt: e.matmul(
                    pt[:, 0:512], sb["actf"][rb][:, kc, :],
                    sb["woutv"][:, kc, chh * 512:(chh + 1) * 512],
                    start=(kc == 0), stop=(kc == 15)),
                    waits=[aevs[kc], sget("wout_tok")] + (bw(bank) if kc == 0 else []))
            o0 = b.v(lambda e, chh=chh, rb=rb, pt=pt: e.scalar_tensor_tensor(
                sb["outt"][rb][:, chh * 512:(chh + 1) * 512], pt[:, 0:512], 1.0,
                sb["xt"][rb][:, chh * 512:(chh + 1) * 512], ALU.mult, ALU.add),
                waits=[mm, rl_x, sget(("outt_free", rb))])
            bset(bank, o0)
            mk1 = b.v if chh == 0 else b.g
            o1 = mk1(lambda e, chh=chh, rb=rb: e.tensor_tensor(
                sb["outt"][rb][:, chh * 512:(chh + 1) * 512],
                sb["outt"][rb][:, chh * 512:(chh + 1) * 512],
                sb["ffrl"][rb][:, chh * 512:(chh + 1) * 512], ALU.add),
                waits=[o0, rl_ff])
            o1s.append(o1)
        S[("actf_free", rb)] = mm
        S[("ffrl_free", rb)] = o1s[0]
        S[("ffrl_free2", rb)] = o1s[1]
        S[("xt_free", rb)] = o0
        st_out = b.dscal(lambda e, c=c, rb=rb: e.dma_start(
            out[c * CK:(c + 1) * CK, :], sb["outt"][rb]), waits=o1s)
        S[("outt_free", rb)] = st_out

    def drain(gen):
        if gen is not None:
            for _ in gen:
                pass

    def pump(gen, n=1):
        if gen is None:
            return None
        for _ in range(n):
            try:
                next(gen)
            except StopIteration:
                return None
        return gen

    if SCHED != "prepass":
        fg = None
        for i in range(nt):
            st = i // tps
            if i % tps == 0 and st > 0 and SCHED in ("ffn", "full"):
                drain(fg)
                fg = ffn(st - 1)
                fg = pump(fg)
            for h_ in tile_head(i):
                fg = pump(fg, h_)
            if i > 0 and SCHED in ("conv", "ffn", "full"):
                for h_ in conv_ssd(i - 1):
                    fg = pump(fg, h_)
        drain(fg)
        if SCHED in ("conv", "ffn", "full"):
            e2 = b.v(lambda e: e.memset(sb["asm"][(nt - 1) % 2, :, :, 131:134], 0.0),
                     waits=[S[("asm_center", nt - 1)]])
            S[("asm_ready", nt - 1)] = e2
            drain(conv_ssd(nt - 1))
        if SCHED in ("ffn", "full"):
            drain(ffn(nst - 1))
    if SCHED != "full":
        final_waits = []
        for ring in ("dsy", "dsc", "dgp"):
            for ln in range(NLANES):
                sem = "%s%d" % (ring, ln)
                if b.cnt[sem]:
                    final_waits.append((sem, b.cnt[sem]))
        b.op("sync", lambda e: e.nop(), waits=final_waits, inc=None)
        import contextlib
        with contextlib.ExitStack() as stk:
            for name in list(b.cnt):
                b.handles[name] = stk.enter_context(nc.semaphore(name))
            blk = stk.enter_context(nc.Block())
            b.replay_into(blk)
        return nc

    aw = [S[("xnfm_free", 0)], sget(("xnfm_free", 1)), S["acc_free"], S["xmt_free"],
          S["xnt_free"], S["xnf32_free"], S["gsc_free"]]
    aw.append(sget("csbt_free"))
    for r_ in (0, 1):
        for kk in ("xitm_free", "asm_free", "win_free", "bhat_free",
                   "xifm_free", "xall_free", "yt_free", "zt_free", "mall_free"):
            aw.append(sget((kk, r_)))
    bar = b.v(lambda e: e.engine_nop(), waits=aw)
    for r_ in (0, 1):
        for kk in ("yfwt_free", "yfwt_free2", "ybwt_free", "zrl_free",
                   "zrl_free2", "ffrl_free2",
                   "ybdt_free", "xirl_free", "ycmb_free", "actt_free",
                   "actf_free", "outt_free", "ffrl_free", "expb_free",
                   "cbw_free", "xt_free"):
            S[(kk, r_)] = bar
    S["gsc_free"] = bar
    S["csrl_free"] = bar
    S["wout_tok"] = b.dgp(lambda e: e.dma_start(
        sb["woutv"], dr["wout_b"].rearrange("(c k) n -> k c n", k=128)),
        waits=[bar, cv_wout])
    tok_nwb = b.dgp(lambda e: e.dma_start(sb["nwb"], dr["nw"]), waits=[bar])
    msA = b.v(lambda e: e.memset(sb["stA"][:], 0.0),
              waits=[bar, sget("sfw"), sget("sbf")])
    S["sbw_a"] = msA
    S["sbw_b"] = msA
    msB = b.v(lambda e: e.memset(sb["stB"][:], 0.0), waits=[bar, msA])
    S["sbwbf"] = msB
    S["sbwbf_g"] = msB
    S[("ybshs_f", 0)] = bar
    S[("ybshs_f", 1)] = bar
    S["csrl_a"] = bar
    S["csrl_b"] = bar
    S["decrep_a"] = sget("decrep_free")
    S["decrep_b"] = bar
    for c in range(nt - 1, -1, -1):
        tile_b(c)
        tile_c(c)
    final_waits = []
    for ring in ("dsy", "dsc", "dgp"):
        for ln in range(NLANES):
            sem = "%s%d" % (ring, ln)
            if b.cnt[sem]:
                final_waits.append((sem, b.cnt[sem]))
    b.op("sync", lambda e: e.nop(), waits=final_waits, inc=None)

    import contextlib
    with contextlib.ExitStack() as stk:
        for name in list(b.cnt):
            b.handles[name] = stk.enter_context(nc.semaphore(name))
        blk = stk.enter_context(nc.Block())
        b.replay_into(blk)
    return nc


def kernel(**inputs):
    A = host_prep(inputs)
    u = np.ascontiguousarray(inputs["L0_attn_u"], dtype=np.float32)
    nt = NT_OVERRIDE if NT_OVERRIDE else L // CK
    nc = build(nt)
    in_maps = []
    for c in range(8):
        m = {"u": np.ascontiguousarray(u[c][: nt * CK])}
        m.update(A)
        in_maps.append(m)
    res = run_bass_kernel_spmd(nc, in_maps, core_ids=list(range(8)))
    return np.stack([r["out"] for r in res.results])


# version 12
# speedup vs baseline: 1.1804x; 1.1804x over previous
import numpy as np
import ml_dtypes
import concourse.bass as bass
import concourse.mybir as mybir
from concourse.bass_utils import run_bass_kernel_spmd

F32 = mybir.dt.float32
BF16 = mybir.dt.bfloat16
U8 = mybir.dt.uint8
AF = mybir.ActivationFunctionType
ALU = mybir.AluOpType
AX = mybir.AxisListType.X

L, D, DI, DIP = 4096, 1024, 2048, 4416
NE, CK, TPS = 8, 128, 4
CS = 1408
ENGS = ("tensor", "vector", "scalar", "sync", "gpsimd")
ZERO = ("pe", 0)
NT_OVERRIDE = 0
SCHED = "full"


NLANES = 8


class Bld:
    def __init__(self, nc):
        self.nc = nc
        self.ops = []
        self.cnt = {s: 0 for s in ("pe", "v", "s", "g")}
        self.lane_ord = {}
        self.ring_ctr = {"dsy": 0, "dsc": 0, "dgp": 0}
        for ring in ("dsy", "dsc", "dgp"):
            for ln in range(NLANES):
                self.cnt["%s%d" % (ring, ln)] = 0
        self.handles = {}

    def dma(self, ring, fn, waits=()):
        ln = self.ring_ctr[ring] % NLANES
        self.ring_ctr[ring] += 1
        sem = "%s%d" % (ring, ln)
        prev = self.cnt[sem]
        w = list(waits) + ([(sem, prev)] if prev else [])
        return self.op("sync" if ring == "dsy" else
                       ("scalar" if ring == "dsc" else "gpsimd"),
                       fn, w, sem, 16)

    def op(self, eng, fn, waits=(), inc=None, incn=1):
        w = [(s, v) for (s, v) in waits if s is not None and v > 0]
        tok = None
        if inc is not None:
            self.cnt[inc] += incn
            tok = (inc, self.cnt[inc])
        self.ops.append((eng, fn, w, inc, incn))
        return tok

    def t(self, fn, waits=()):
        return self.op("tensor", fn, waits, "pe")

    def v(self, fn, waits=()):
        return self.op("vector", fn, waits, "v")

    def s(self, fn, waits=()):
        return self.op("scalar", fn, waits, "s")

    def g(self, fn, waits=()):
        return self.op("gpsimd", fn, waits, "g")

    def dsync(self, fn, waits=()):
        return self.dma("dsy", fn, waits)

    def dscal(self, fn, waits=()):
        return self.dma("dsc", fn, waits)

    def dgp(self, fn, waits=()):
        return self.dma("dgp", fn, waits)

    def replay_into(self, blk):
        for name in ENGS:
            getattr(blk, name)(self._mk(name))

    def _mk(self, engname):
        def run(eng):
            waited = {}
            for e, fn, waits, inc, incn in self.ops:
                if e != engname:
                    continue
                for s, v in waits:
                    if waited.get(s, 0) < v:
                        eng.wait_ge(self.handles[s], v)
                        waited[s] = v
                ins = fn(eng)
                if inc is not None:
                    ins.then_inc(self.handles[inc], incn)
        return run


class Ring:

    def __init__(self, handle):
        self.h = handle

    def __getitem__(self, key):
        ap = self.h.ap()
        if isinstance(key, tuple):
            key = (key[1], key[0]) + tuple(key[2:])
            return ap[key]
        return ap[:, key]


class AliasRing:

    def __init__(self, *aps):
        self.aps = list(aps)

    def __getitem__(self, key):
        if isinstance(key, tuple):
            return self.aps[key[0]][tuple(key[1:])]
        return self.aps[key]


BCPERM = np.concatenate([np.arange(2048),
                         2048 + np.arange(64), 2176 + np.arange(64),
                         2112 + np.arange(64), 2240 + np.arange(64)])


def host_prep(inputs):
    A = {}
    wt = inputs["L0_attn_in_proj_w"].T
    colperm = np.concatenate([np.arange(2048), 2048 + BCPERM, np.arange(4352, 4416)])
    A["winT"] = np.ascontiguousarray(wt[:, colperm])
    cw = inputs["L0_attn_conv_w"][:, 0, :][BCPERM]
    cd = np.zeros((128, 18, 7, 32), np.float32)
    cwr = cw.reshape(18, 4, 32, 7)
    for bb in range(4):
        for j in range(32):
            cd[32 * bb + j, :, :, j] = cwr[:, bb, j, :]
    A["convd"] = cd
    A["convb"] = np.ascontiguousarray(
        inputs["L0_attn_conv_b"][BCPERM].reshape(18, 128).T)
    A["dtb"] = np.ascontiguousarray(
        np.broadcast_to(np.concatenate([inputs["L0_attn_dt_bias"]] * 2), (128, 64)))
    A["alog"] = np.ascontiguousarray(
        np.concatenate([inputs["L0_attn_A_log"]] * 2)[:, None])
    A["dpar"] = np.ascontiguousarray(np.broadcast_to(inputs["L0_attn_D_param"], (128, 32)))
    A["fcdT"] = np.ascontiguousarray(inputs["L0_attn_fc_D_w"].T)
    A["nw"] = np.ascontiguousarray(np.broadcast_to(inputs["L0_attn_norm_w"], (128, DI)))
    A["woutT"] = np.ascontiguousarray(inputs["L0_attn_out_proj_w"].T)
    A["w1T"] = np.ascontiguousarray(inputs["L0_ffn_W1"].transpose(0, 2, 1))
    A["b1c"] = np.ascontiguousarray(
        inputs["L0_ffn_b1"].reshape(8, 16, 128).transpose(2, 0, 1))
    A["w2T"] = np.ascontiguousarray(inputs["L0_ffn_W2"].transpose(0, 2, 1))
    A["b2"] = np.ascontiguousarray(inputs["L0_ffn_b2"])
    A["wgT"] = np.ascontiguousarray(inputs["L0_ffn_Wg"].T)
    A["bg"] = np.ascontiguousarray(np.broadcast_to(inputs["L0_ffn_bg"], (128, 8)))
    A["idf"] = np.eye(128, dtype=np.float32)
    idx = np.arange(128)
    mz = np.stack([(idx[None, :] < idx[:, None]),
                   (idx[None, :] > idx[:, None])], axis=0).astype(np.uint8)
    A["mskz"] = np.ascontiguousarray(
        np.tile(mz[None, :, :, :], (1, 1, 1, 4)).reshape(2, 128, 512)
        .transpose(1, 0, 2)).copy()
    A["ones1"] = np.ones((1, 128), np.float32)
    shm = np.zeros((128, 256), np.float32)
    shm[idx[1:], idx[:-1]] = 1.0
    shm[0, 255] = 1.0
    A["shm"] = np.ascontiguousarray(shm.astype(ml_dtypes.bfloat16))
    return A


def build(nt):
    nc = bass.Bass("TRN2", target_bir_lowering=False)
    b = Bld(nc)
    Ltot = nt * CK
    tps = min(TPS, nt)
    nst = max(1, (nt + tps - 1) // tps)

    dr = {}

    def din(name, shape, dt=F32):
        dr[name] = nc.dram_tensor(name, list(shape), dt, kind="ExternalInput").ap()

    din("u", (Ltot, D))
    for name, shape, dt in [
        ("winT", (D, DIP), F32), ("convd", (128, 18, 7, 32), F32),
        ("convb", (128, 18), F32), ("dtb", (128, 64), F32), ("alog", (64, 1), F32),
        ("dpar", (128, 32), F32), ("fcdT", (DI, 32), F32), ("nw", (128, DI), F32),
        ("woutT", (DI, D), F32), ("w1T", (NE, D, DI), F32), ("b1c", (128, NE, 16), F32),
        ("w2T", (NE, DI, D), F32), ("b2", (NE, D), F32), ("wgT", (D, NE), F32),
        ("bg", (128, NE), F32), ("idf", (128, 128), F32),
        ("mskz", (128, 2, 512), U8), ("ones1", (1, 128), F32),
        ("shm", (128, 256), BF16),
    ]:
        dr[name] = nc.dram_tensor(name, list(shape), dt, kind="ExternalInput").ap()
    out = nc.dram_tensor("out", [Ltot, D], F32, kind="ExternalOutput").ap()

    def scr(name, shape, dt=BF16):
        dr[name] = nc.dram_tensor(name, list(shape), dt, kind="Internal").ap()

    scr("win_b", (D, DIP))
    scr("w1_b", (NE, D, DI))
    scr("w2_b", (NE, DI, D))
    scr("wout_b", (DI, D))
    scr("z_sp", (Ltot, DI))
    scr("xi_sp", (Ltot, DI))
    scr("yfw_sp", (Ltot + 1, DI))
    scr("ybd_sp", (Ltot, DI))
    scr("csbw_sp", (nt, 64, 2048))
    scr("bc_sp", (Ltot, 256))
    scr("expal_sp", (Ltot, 64))
    scr("ffn_sp", (Ltot, D))

    sb = {}
    hnd = {}

    def asb(name, shape, dt=BF16):
        hnd[name] = nc.alloc_sbuf_tensor("sb_" + name, list(shape), dt)
        sb[name] = hnd[name]

    def aring(name, shape, dt=BF16):
        R, Pp = shape[0], shape[1]
        hnd[name] = nc.alloc_sbuf_tensor("sb_" + name, [Pp, R] + list(shape[2:]), dt)
        sb[name] = Ring(hnd[name])

    asb("idb", (128, 128))
    asb("shm", (128, 256))
    asb("idf", (128, 128), F32)
    asb("mskz", (128, 2, 512), U8)
    asb("zc", (128, 1))
    asb("ones1b", (1, 128))
    asb("cbc", (128, 18), F32)
    asb("dtbc", (128, 64), F32)
    asb("alogc", (64, 1), F32)
    asb("nega", (64, 1), F32)
    asb("dparc", (128, 32), F32)
    asb("bgc", (128, 8), F32)
    asb("wgc", (128, 8, 8), F32)
    asb("b1cc", (128, NE, 16), F32)
    asb("b2r", (NE, D))
    asb("fcdw", (128, 16, 32))
    asb("cdw", (128, 18, 7, 32))
    asb("drows", (1, nt, 64), F32)
    aring("drb", (2, 1, 64))
    asb("fcd", (128, nt, 32), F32)
    aring("comb", (2, 128, tps, 8), F32)
    asb("combT", (8, 128))
    asb("glt", (128, 8), F32)
    asb("gsc", (128, 12), F32)
    aring("dtr", (2, 128, 64), F32)
    asb("dscr", (128, 64), F32)
    aring("dts", (2, 128, 64), F32)
    aring("dtb16", (2, 128, 64))
    asb("abf", (64, CK), F32)
    aring("alf", (2, 64, CK), F32)
    aring("poshi", (2, 64, CK))
    aring("poslo", (2, 64, CK))
    aring("altm", (2, 128, 64), F32)
    aring("naltm", (2, 128, 64), F32)
    aring("expaltm", (2, 128, 64))
    aring("expdtm", (2, 128, 64))
    asb("expdfm", (64, CK), F32)
    asb("sumc", (64, 1), F32)
    aring("bcfm", (2, 128, 2, 128))
    aring("bct", (2, 128, 256))
    asb("stA", (64, 2048), F32)
    asb("stB", (64, 2048))
    asb("decrep", (64, 2048))
    asb("csbt", (64, 2048))
    aring("xt", (2, 128, D), F32)
    aring("win", (2, 128, 8, 512))
    aring("wA", (2, 128, 8, 1024))
    aring("asm", (2, 128, 18, 134))
    aring("xifm", (2, 128, 16, 128))
    aring("xitm", (2, 128, DI))
    aring("xall", (2, 128, 1024))
    asb("mall", (128, 2048))
    aring("bhat", (2, 128, 1024))
    aring("yt", (2, 128, 512))
    aring("zt", (2, 128, 512))
    asb("xmt", (128, D), F32)
    asb("xnt", (128, D), F32)
    asb("xnf32", (128, 8, 128), F32)
    aring("xnfm", (2, 128, 8, tps * CK))
    asb("acc", (128, tps, D))
    asb("hb", (128, 16, 512))
    aring("expb", (2, 128, 32))
    aring("cbw", (2, 128, 64))
    asb("cbwfm", (64, 128))

    def flat(h):
        ap = h.ap()
        nd = len(ap.shape)
        if nd == 2:
            return ap
        spec = "p " + " ".join("d%d" % k for k in range(nd - 1))
        return ap.rearrange(spec + " -> p (" + spec[2:] + ")")

    hbf = flat(hnd["hb"])
    sb["yfwt"] = AliasRing(hbf[:, 0:2048], hbf[:, 2048:4096])
    sb["ybshs"] = AliasRing(hbf[:, 4096:6144], hbf[:, 6144:8192])
    winf = flat(hnd["win"])
    sb["ybwt"] = AliasRing(winf[:, 0:2048], winf[:, 2048:4096])
    sb["ycmb"] = AliasRing(winf[:, 4096:6144], winf[:, 4096:6144])
    asmf = flat(hnd["asm"])
    sb["xirl"] = AliasRing(asmf[:, 0:2048], asmf[:, 2048:4096])
    xifmf = flat(hnd["xifm"])
    sb["zrl"] = AliasRing(xifmf[:, 0:2048], xifmf[:, 2048:4096])
    sb["ybdt"] = sb["xitm"]
    bhatf = flat(hnd["bhat"])
    sb["actt"] = AliasRing(hnd["mall"].ap(), bhatf)
    xallf = flat(hnd["xall"])
    sb["actf"] = AliasRing(
        xallf.rearrange("p (c n) -> p c n", n=128),
        winf[:, 6144:8192].rearrange("p (c n) -> p c n", n=128))
    sb["outt"] = AliasRing(hnd["xmt"].ap(), hnd["xnt"].ap())
    xnf32f = flat(hnd["xnf32"]).bitcast(BF16)
    sb["ffrl"] = AliasRing(xnf32f[:, 0:1024], xnf32f[:, 1024:2048])
    wAf = flat(hnd["wA"])
    sb["woutv"] = wAf.rearrange("p (c n) -> p c n", n=1024)
    cdwf = flat(hnd["cdw"])
    sb["nwb"] = cdwf[:, 0:2048]

    pbig = nc.alloc_psum_tensor("pbig", [128, 2048], F32)
    pmed = nc.alloc_psum_tensor("pmed", [128, 512], F32)
    pyd = nc.alloc_psum_tensor("pyd", [128, 512], F32)
    pyo = nc.alloc_psum_tensor("pyo", [128, 512], F32)
    ptb = nc.alloc_psum_tensor("ptb", [128, 1024], BF16)

    bank_free = {}

    def bw(key):
        return list(bank_free.get(key, []))

    def bset(key, *toks):
        bank_free[key] = [t for t in toks if t is not None]

    def badd(key, tok):
        bank_free.setdefault(key, []).append(tok)

    S = {}

    def sget(k):
        return S.get(k, ZERO)

    def r3(ap, q):
        return ap.rearrange("p (h q) -> p h q", q=q)

    cv_win = b.dgp(lambda e: e.dma_start(dr["win_b"], dr["winT"]))
    cv_wout = b.dgp(lambda e: e.dma_start(dr["wout_b"], dr["woutT"]))
    cv_w1, cv_w2 = [], []
    for ei in range(NE):
        cv_w1.append(b.dgp(lambda e, ei=ei: e.dma_start(dr["w1_b"][ei], dr["w1T"][ei])))
        cv_w2.append(b.dgp(lambda e, ei=ei: e.dma_start(dr["w2_b"][ei], dr["w2T"][ei])))
    tok_idb = b.dgp(lambda e: e.dma_start(sb["idb"][:], dr["idf"]))
    tok_shm = b.dgp(lambda e: e.dma_start(sb["shm"][:], dr["shm"]))
    tok_b2r = b.dgp(lambda e: e.dma_start(sb["b2r"][:], dr["b2"]))
    tok_fcdw = b.dgp(lambda e: e.dma_start(
        sb["fcdw"][:], dr["fcdT"].rearrange("(c k) n -> k c n", k=128)))
    tok_cdw = b.dgp(lambda e: e.dma_start(sb["cdw"][:], dr["convd"]))
    tok_o1b = b.dgp(lambda e: e.dma_start(sb["ones1b"][:], dr["ones1"]))
    tok_msk = b.dsync(lambda e: e.dma_start(sb["mskz"][:], dr["mskz"]))
    tok_idf = b.dsync(lambda e: e.dma_start(sb["idf"][:], dr["idf"]))
    tok_cbc = b.dsync(lambda e: e.dma_start(sb["cbc"][:], dr["convb"]))
    tok_dtb = b.dsync(lambda e: e.dma_start(sb["dtbc"][:], dr["dtb"]))
    tok_alog = b.dsync(lambda e: e.dma_start(sb["alogc"][:], dr["alog"]))
    tok_dpar = b.dsync(lambda e: e.dma_start(sb["dparc"][:], dr["dpar"]))
    tok_bg = b.dsync(lambda e: e.dma_start(sb["bgc"][:], dr["bg"]))
    tok_wg = b.dsync(lambda e: e.dma_start(
        sb["wgc"][:], dr["wgT"].rearrange("(c k) n -> k c n", k=128)))
    tok_b1c = b.dsync(lambda e: e.dma_start(sb["b1cc"][:], dr["b1c"]))

    t_ne = b.s(lambda e: e.activation(sb["nega"][:], sb["alogc"][:], AF.Exp),
               waits=[tok_alog])
    tok_nega = b.v(lambda e: e.tensor_scalar_mul(sb["nega"][:], sb["nega"][:], -1.0),
                   waits=[t_ne])
    tok_zc = b.v(lambda e: e.memset(sb["zc"][:], 0.0))
    tz = b.v(lambda e: e.memset(sb["outt"][0], 0.0))
    zr1 = b.dsync(lambda e: e.dma_start(
        dr["yfw_sp"][0:1, :], sb["outt"][0, 0:1, :].bitcast(BF16)[:, 0:DI]), waits=[tz])
    zrb = b.v(lambda e: e.engine_nop(), waits=[zr1])
    S[("outt_free", 0)] = zrb
    S["xmt_free"] = zrb
    S["sfw"] = b.v(lambda e: e.memset(sb["stA"][:], 0.0))
    S["sbf"] = b.v(lambda e: e.memset(sb["stB"][:], 0.0))
    S["asm_z0"] = b.v(lambda e: e.memset(sb["asm"][0, :, :, 0:3], 0.0))

    wctr = [0]

    def win_load(g):
        pp = wctr[0] % 2
        wctr[0] += 1
        n = 320 if g == 8 else 512
        tok = b.dsync(lambda e, g=g, pp=pp, n=n: e.dma_start(
            sb["win"][pp, :, :, 0:n],
            dr["win_b"][:, g * 512:g * 512 + n].rearrange("(c k) n -> k c n", k=128)),
            waits=[sget(("win_free", pp)), cv_win])
        return tok, pp

    S["win_next"] = win_load(0)

    def tile_head(i):
        st, it, r = i // tps, i % tps, i % 2
        stb = st % 2
        tok_x = b.dscal(lambda e, i=i, r=r: e.dma_start(
            sb["xt"][r], dr["u"][i * CK:(i + 1) * CK, :]),
            waits=[sget(("xt_free", r))])
        yield 2
        def trunc(tok):
            S[("asm_center", i)] = tok
            S[("dtr", i)] = tok
            S[("comb_tile", i // tps % 2, it)] = tok
            S[("xnfm_tile", i // tps % 2, it)] = tok
        if SCHED == "ln0":
            dmy = b.v(lambda e: e.engine_nop(), waits=[tok_x])
            trunc(dmy)
            return
        w0 = b.v(lambda e, r=r: e.tensor_reduce(sb["gsc"][:, 0:1], sb["xt"][r], AX,
                                                ALU.add),
                 waits=[tok_x, sget("gsc_free")])
        w1_ = b.v(lambda e: e.tensor_scalar_mul(sb["gsc"][:, 1:2], sb["gsc"][:, 0:1],
                                                1.0 / D), waits=[w0])
        w2_ = b.v(lambda e, r=r: e.tensor_scalar(sb["xmt"][:], sb["xt"][r],
                                                 sb["gsc"][:, 1:2], None, ALU.subtract),
                  waits=[w1_, sget("xmt_free")])
        S[("xt_free", r)] = w2_
        if SCHED == "ln1":
            trunc(w2_)
            S["xmt_free"] = w2_
            return
        w3 = b.s(lambda e: e.activation(sb["xnt"][:], sb["xmt"][:], AF.Square,
                                        accum_out=sb["gsc"][:, 2:3]),
                 waits=[w2_, sget("xnt_free")])
        w4a = b.v(lambda e: e.tensor_scalar(sb["gsc"][:, 3:4], sb["gsc"][:, 2:3],
                                            1.0 / D, 1e-5, ALU.mult, ALU.add),
                  waits=[w3])
        w4 = b.s(lambda e: e.activation(sb["gsc"][:, 3:4], sb["gsc"][:, 3:4], AF.Ln),
                 waits=[w4a])
        w5a = b.v(lambda e: e.tensor_scalar_mul(sb["gsc"][:, 4:5], sb["gsc"][:, 3:4],
                                                -0.5), waits=[w4])
        w5 = b.s(lambda e: e.activation(sb["gsc"][:, 4:5], sb["gsc"][:, 4:5], AF.Exp),
                 waits=[w5a])
        if SCHED == "ln2":
            trunc(w5)
            S["xmt_free"] = w5
            S["gsc_free"] = w5
            return
        w6 = b.v(lambda e: e.tensor_scalar_mul(sb["xnt"][:], sb["xmt"][:],
                                               sb["gsc"][:, 4:5]),
                 waits=[w5, sget("xnt_free")])
        S["xmt_free"] = w6
        if SCHED == "ln3":
            trunc(w6)
            S["gsc_free"] = w6
            return
        evb = None
        for kc in range(8):
            bank, pt = (("yd", pyd) if kc % 2 == 0 else ("yo", pyo))
            mmw = [w6, tok_idf] + bw(bank)
            if it == 0 and kc == 0:
                mmw.append(sget(("xnfm_free", stb)))
            if kc < 2:
                mmw.append(sget("xnf32_free"))
            tt = b.t(lambda e, kc=kc, pt=pt: e.transpose(
                pt[:, 0:128], sb["xnt"][:, kc * 128:(kc + 1) * 128], sb["idf"][:]),
                waits=mmw)
            ev1 = b.v(lambda e, kc=kc, it=it, pt=pt, stb=stb: e.tensor_copy(
                sb["xnfm"][stb, :, kc, it * CK:(it + 1) * CK], pt[:, 0:128]),
                waits=[tt])
            ev2 = b.s(lambda e, kc=kc, pt=pt: e.copy(
                sb["xnf32"][:, kc, :], pt[:, 0:128]), waits=[tt, ev1])
            bset(bank, ev1, ev2)
            S[("xnfm_k", kc)] = ev1
            S[("xnf32_k", kc)] = ev2
            evb = ev1
            if kc == 3:
                yield 0
        S["xnt_free"] = evb
        S[("xnfm_tile", stb, it)] = evb
        yield 0
        if SCHED == "ln":
            trunc(evb)
            return
        gm = None
        for kc in range(8):
            gm = b.t(lambda e, kc=kc: e.matmul(
                pmed[:, 128:136], sb["xnf32"][:, kc, :], sb["wgc"][:, kc, :],
                start=(kc == 0), stop=(kc == 7)),
                waits=[tok_wg, S[("xnf32_k", kc)]] + (bw("med") if kc == 0 else []))
        S["xnf32_free"] = gm
        g0 = b.v(lambda e: e.tensor_tensor(sb["glt"][:], pmed[:, 128:136], sb["bgc"][:],
                                           ALU.add),
                 waits=[gm, tok_bg, sget("glt_free")])
        bset("med", g0)
        m1 = b.v(lambda e: e.tensor_reduce(sb["gsc"][:, 5:6], sb["glt"][:], AX, ALU.max),
                 waits=[g0])
        nm1 = b.v(lambda e: e.tensor_reduce(sb["gsc"][:, 6:7], sb["glt"][:], AX, ALU.max,
                                            negate=True), waits=[g0])
        eq = b.v(lambda e: e.tensor_scalar(sb["gsc"][:, 8:12].bitcast(F32) if False
                                           else sb["dscr"][:, 0:8], sb["glt"][:],
                                           sb["gsc"][:, 5:6], None, ALU.is_equal),
                 waits=[m1, sget("dscr_free")])
        mk = b.v(lambda e: e.scalar_tensor_tensor(sb["dscr"][:, 0:8], sb["dscr"][:, 0:8],
                                                  -1e30, sb["glt"][:], ALU.mult, ALU.add),
                 waits=[eq])
        m2 = b.v(lambda e: e.tensor_reduce(sb["gsc"][:, 7:8], sb["dscr"][:, 0:8], AX,
                                           ALU.max), waits=[mk])
        ge = b.v(lambda e: e.tensor_scalar(sb["dscr"][:, 0:8], sb["glt"][:],
                                           sb["gsc"][:, 7:8], None, ALU.is_ge),
                 waits=[m2])
        dm = b.v(lambda e: e.tensor_tensor(sb["gsc"][:, 7:8], sb["gsc"][:, 7:8],
                                           sb["gsc"][:, 6:7], ALU.add),
                 waits=[m2, nm1, ge])
        ew = b.s(lambda e: e.activation(sb["glt"][:], sb["glt"][:], AF.Exp,
                                        bias=sb["gsc"][:, 6:7]), waits=[nm1, ge, mk])
        ed = b.s(lambda e: e.activation(sb["gsc"][:, 7:8], sb["gsc"][:, 7:8], AF.Exp),
                 waits=[dm])
        d1 = b.v(lambda e: e.tensor_scalar_add(sb["gsc"][:, 7:8], sb["gsc"][:, 7:8],
                                               1.0), waits=[ed])
        rc = b.v(lambda e: e.reciprocal(sb["gsc"][:, 7:8], sb["gsc"][:, 7:8]),
                 waits=[d1])
        c1 = b.v(lambda e: e.tensor_tensor(sb["glt"][:], sb["glt"][:], sb["dscr"][:, 0:8],
                                           ALU.mult), waits=[ew])
        c2 = b.v(lambda e, stb=stb, it=it: e.tensor_scalar_mul(
            sb["comb"][stb, :, it, :], sb["glt"][:], sb["gsc"][:, 7:8]),
            waits=[c1, rc, sget(("comb_free", stb))])
        S["glt_free"] = c2
        S["gsc_free"] = c2
        S["dscr_free"] = c2
        S[("comb_tile", stb, it)] = c2
        yield 0
        if SCHED == "lngate":
            S[("asm_center", i)] = c2
            S[("dtr", i)] = c2
            return

        ar = i % 2
        last_ctr = None
        for g in range(9):
            yield 1
            gtok, pp = S["win_next"]
            if g + 1 < 9:
                S["win_next"] = win_load(g + 1)
            elif i + 1 < nt:
                S["win_next"] = win_load(0)
            if g < 4:
                bank = "b%d" % (g % 4)
                mm = None
                for kc in range(8):
                    mm = b.t(lambda e, kc=kc, g=g, pp=pp, it=it: e.matmul(
                        pbig[:, (g % 4) * 512:(g % 4) * 512 + 512],
                        sb["xnfm"][stb, :, kc, it * CK:(it + 1) * CK],
                        sb["win"][pp, :, kc, :],
                        start=(kc == 0), stop=(kc == 7)),
                        waits=[gtok, S[("xnfm_k", kc)]] + (bw(bank) if kc == 0 else []))
                S[("win_free", pp)] = mm
                for hz in range(1):
                    pass
                ev = b.v(lambda e, g=g: e.tensor_copy(
                    sb["zt"][g % 2], pbig[:, (g % 4) * 512:(g % 4) * 512 + 512]),
                    waits=[mm, sget(("zt_free", g % 2))])
                bset(bank, ev)
                sp = b.dsync(lambda e, i=i, g=g: e.dma_start(
                    dr["z_sp"][i * CK:(i + 1) * CK, g * 512:(g + 1) * 512],
                    sb["zt"][g % 2]), waits=[ev])
                S[("zt_free", g % 2)] = sp
            else:
                nsub = 3 if g == 8 else 4
                for sub in range(nsub):
                    ch_abs = (g - 4) * 4 + sub
                    bank = "b%d" % (ch_abs % 4)
                    mm = None
                    if ch_abs < 18:
                        for kc in range(8):
                            mm = b.t(lambda e, kc=kc, pp=pp, sub=sub, it=it, stb=stb,
                                     ch_abs=ch_abs: e.matmul(
                                pbig[:, (ch_abs % 4) * 512:(ch_abs % 4) * 512 + 128],
                                sb["win"][pp, :, kc, sub * 128:sub * 128 + 128],
                                sb["xnfm"][stb, :, kc, it * CK:(it + 1) * CK],
                                start=(kc == 0), stop=(kc == 7)),
                                waits=[gtok, S[("xnfm_k", kc)]]
                                + (bw(bank) if kc == 0 else []))
                        ev = b.v(lambda e, ar=ar, ch_abs=ch_abs: e.tensor_copy(
                            sb["asm"][ar, :, ch_abs, 3:131],
                            pbig[:, (ch_abs % 4) * 512:(ch_abs % 4) * 512 + 128]),
                            waits=[mm, sget(("asm_free", ar))])
                        last_ctr = ev
                    else:
                        for kc in range(8):
                            mm = b.t(lambda e, kc=kc, pp=pp, it=it, stb=stb,
                                     ch_abs=ch_abs: e.matmul(
                                pbig[:, (ch_abs % 4) * 512:(ch_abs % 4) * 512 + 64],
                                sb["xnfm"][stb, :, kc, it * CK:(it + 1) * CK],
                                sb["win"][pp, :, kc, 256:320],
                                start=(kc == 0), stop=(kc == 7)),
                                waits=[gtok, S[("xnfm_k", kc)]]
                                + (bw(bank) if kc == 0 else []))
                        ev = b.v(lambda e, r=r, ch_abs=ch_abs: e.tensor_copy(
                            sb["dtr"][r],
                            pbig[:, (ch_abs % 4) * 512:(ch_abs % 4) * 512 + 64]),
                            waits=[mm, sget(("dtr_free", r))])
                        S[("dtr", i)] = ev
                    S[("win_free", pp)] = mm
                    bset(bank, ev)
        S[("asm_center", i)] = last_ctr
        if i > 0:
            pv = (i - 1) % 2
            e1 = b.v(lambda e, ar=ar, pv=pv: e.tensor_copy(
                sb["asm"][ar, :, :, 0:3], sb["asm"][pv, :, :, 128:131]),
                waits=[last_ctr, S[("asm_center", i - 1)]])
            e2 = b.v(lambda e, ar=ar, pv=pv: e.tensor_copy(
                sb["asm"][pv, :, :, 131:134], sb["asm"][ar, :, :, 3:6]),
                waits=[last_ctr, S[("asm_center", i - 1)]])
            S[("asm_ready", i - 1)] = e2
            S[("asm_left", i)] = e1
        else:
            S[("asm_left", 0)] = S["asm_z0"]

    def conv_ssd(c):
        rr = c % 2
        ar = c % 2
        asm_ok = [S[("asm_ready", c)], S[("asm_left", c)], tok_cdw, tok_cbc]
        ev = None
        for ch in list(range(16, 18)) + list(range(16)):
            bank = "b%d" % (ch % 4)
            mm = None
            for bb in range(4):
                for k in range(7):
                    mm = b.t(lambda e, ch=ch, k=k, bb=bb, ar=ar: e.matmul(
                        pbig[32 * bb:32 * bb + 32,
                             (ch % 4) * 512:(ch % 4) * 512 + 128],
                        sb["cdw"][32 * bb:32 * bb + 32, ch, k, :],
                        sb["asm"][ar, 32 * bb:32 * bb + 32, ch, k:k + 128],
                        start=(k == 0), stop=(k == 6),
                        tile_position=(32 * bb, 32 * bb)),
                        waits=asm_ok + (bw(bank) if (k == 0 and bb == 0) else []))
            if ch < 16:
                ev = b.s(lambda e, ch=ch, rr=rr: e.activation(
                    sb["xifm"][rr, :, ch, :],
                    pbig[:, (ch % 4) * 512:(ch % 4) * 512 + 128], AF.Silu,
                    bias=sb["cbc"][:, ch:ch + 1]),
                    waits=[mm, sget(("xifm_free", rr))])
                S[("xifm_ch", rr, ch)] = ev
            else:
                ev = b.s(lambda e, ch=ch, rr=rr: e.activation(
                    sb["bcfm"][rr, :, ch - 16, :],
                    pbig[:, (ch % 4) * 512:(ch % 4) * 512 + 128], AF.Silu,
                    bias=sb["cbc"][:, ch:ch + 1]),
                    waits=[mm, sget(("bcfm_free", rr))])
                S[("bcfm_ch", rr, ch - 16)] = ev
            bset(bank, ev)
            if ch % 3 == 2:
                yield 0
        S[("asm_free", ar)] = ev

        tts = []
        for j in range(2):
            tts.append(b.t(lambda e, j=j, rr=rr: e.transpose(
                ptb[:, j * 128:(j + 1) * 128], sb["bcfm"][rr, :, j, :], sb["idb"][:]),
                waits=[S[("bcfm_ch", rr, j)], tok_idb] + bw("tb")))
        bc2 = None
        for j in range(2):
            bc2 = b.v(lambda e, j=j, rr=rr: e.tensor_copy(
                sb["bct"][rr, :, j * 128:(j + 1) * 128], ptb[:, j * 128:(j + 1) * 128]),
                waits=[tts[-1], sget(("bct_free", rr))])
        bset("tb", bc2)
        S[("bct", c)] = bc2
        yield 0
        spbc = b.dsync(lambda e, c=c, rr=rr: e.dma_start(
            dr["bc_sp"][c * CK:(c + 1) * CK, :], sb["bct"][rr]), waits=[bc2])

        evs = []
        for wave in range(2):
            tts = []
            for sl in range(8):
                ch = wave * 8 + sl
                tw = [S[("xifm_ch", rr, ch)], tok_idb]
                tw.extend(bw("tb") if ch == 0 else ([evs[-1]] if ch == 8 else []))
                tts.append(b.t(lambda e, ch=ch, sl=sl, rr=rr: e.transpose(
                    ptb[:, sl * 128:(sl + 1) * 128], sb["xifm"][rr, :, ch, :],
                    sb["idb"][:]), waits=tw))
            for sl in range(8):
                ch = wave * 8 + sl
                evx = b.v(lambda e, ch=ch, sl=sl, rr=rr: e.tensor_copy(
                    sb["xitm"][rr, :, ch * 128:(ch + 1) * 128],
                    ptb[:, sl * 128:(sl + 1) * 128]),
                    waits=[tts[-1], sget(("xitm_free", rr))])
                evs.append(evx)
            yield 1
        bset("tb", evs[-1])
        xitm_tok = evs[-1]
        S[("xitm", c)] = xitm_tok
        spxi = b.dsync(lambda e, c=c, rr=rr: e.dma_start(
            dr["xi_sp"][c * CK:(c + 1) * CK, :], sb["xitm"][rr]), waits=[xitm_tok])

        fm = None
        for kc in range(16):
            fm = b.t(lambda e, kc=kc, rr=rr: e.matmul(
                pmed[:, 256:288], sb["xifm"][rr, :, kc, :], sb["fcdw"][:, kc, :],
                start=(kc == 0), stop=(kc == 15)),
                waits=[S[("xifm_ch", rr, kc)], tok_fcdw] + (bw("med") if kc == 0 else []))
        S[("xifm_free", rr)] = fm
        fe = b.v(lambda e, c=c: e.tensor_tensor(
            sb["fcd"][:, c, :], pmed[:, 256:288], sb["dparc"][:], ALU.add),
            waits=[fm, tok_dpar])
        bset("med", fe)
        S[("fcd", c)] = fe
        yield 0

        a0 = b.v(lambda e, c=c: e.tensor_tensor(sb["dscr"][:, 0:64], sb["dtr"][c % 2],
                                                sb["dtbc"][:], ALU.add),
                 waits=[S[("dtr", c)], tok_dtb, sget("dscr_free")])
        a1 = b.s(lambda e, rr=rr: e.activation(sb["dts"][rr], sb["dscr"][:, 0:64],
                                               AF.Abs),
                 waits=[a0, sget(("dts_free", rr))])
        a2 = b.s(lambda e, rr=rr: e.activation(sb["dts"][rr], sb["dts"][rr], AF.Exp,
                                               scale=-1.0), waits=[a1])
        a3 = b.s(lambda e, rr=rr: e.activation(sb["dts"][rr], sb["dts"][rr], AF.Ln,
                                               bias=1.0), waits=[a2])
        a4 = b.v(lambda e, rr=rr: e.scalar_tensor_tensor(
            sb["dts"][rr], sb["dscr"][:, 0:64], 0.0, sb["dts"][rr], ALU.max, ALU.add),
            waits=[a3])
        S[("dtr_free", c % 2)] = a4
        S["dscr_free"] = a4
        a5 = b.v(lambda e, rr=rr: e.tensor_copy(sb["dtb16"][rr], sb["dts"][rr]),
                 waits=[a4, sget(("dtb16_free", rr))])
        yield 0
        td = b.t(lambda e, rr=rr: e.transpose(pyd[0:64, 0:128], sb["dts"][rr],
                                              sb["idf"][:]),
                 waits=[a4, tok_idf] + bw("yd"))
        ab0 = b.v(lambda e: e.tensor_scalar_mul(sb["abf"][:], pyd[0:64, 0:128],
                                                sb["nega"][:, 0:1]),
                  waits=[td, tok_nega, sget("abf_free")])
        bset("yd", ab0)
        sc0 = b.v(lambda e, rr=rr: e.tensor_tensor_scan(
            sb["alf"][rr], sb["abf"][:], sb["abf"][:], 0.0, ALU.add, ALU.bypass),
            waits=[ab0, sget(("alf_free", rr))])
        l0 = b.v(lambda e, rr=rr: e.tensor_scalar(
            sb["expdfm"][32:64, :], sb["alf"][rr][32:64, :],
            sb["alf"][rr][32:64, 127:128], -1.0, ALU.subtract, ALU.mult),
            waits=[sc0, sget("expdfm_free")])
        l1 = b.v(lambda e, rr=rr: e.tensor_tensor(
            sb["alf"][rr][32:64, :], sb["expdfm"][32:64, :], sb["abf"][32:64, :],
            ALU.add), waits=[l0])
        S["abf_free"] = l1
        alf_tok = l1
        p0 = b.v(lambda e, rr=rr: e.tensor_copy(sb["poshi"][rr], sb["alf"][rr]),
                 waits=[alf_tok, sget(("poshi_free", rr))])
        p1 = b.v(lambda e, rr=rr: e.scalar_tensor_tensor(
            sb["poslo"][rr], sb["alf"][rr], 1.0, sb["poshi"][rr], ALU.bypass,
            ALU.subtract), waits=[p0, sget(("poslo_free", rr))])
        ta = b.t(lambda e, rr=rr: e.transpose(pyo[0:128, 0:64], sb["alf"][rr],
                                              sb["idf"][0:64, 0:64]),
                 waits=[alf_tok, tok_idf] + bw("yo"))
        at0 = b.v(lambda e, rr=rr: e.tensor_copy(sb["altm"][rr], pyo[0:128, 0:64]),
                  waits=[ta, sget(("altm_free", rr))])
        bset("yo", at0)
        at1 = b.v(lambda e, rr=rr: e.tensor_scalar_mul(sb["naltm"][rr], sb["altm"][rr],
                                                       -1.0),
                  waits=[at0, sget(("naltm_free", rr))])
        ae = b.s(lambda e, rr=rr: e.activation(sb["expaltm"][rr], sb["altm"][rr],
                                               AF.Exp),
                 waits=[at0, sget(("expaltm_free", rr))])
        spal = b.dsync(lambda e, c=c, rr=rr: e.dma_start(
            dr["expal_sp"][c * CK:(c + 1) * CK, :], sb["expaltm"][rr]), waits=[ae])
        sm0 = b.v(lambda e, rr=rr: e.tensor_copy(sb["sumc"][0:32, :],
                                                 sb["alf"][rr][0:32, 127:128]),
                  waits=[sc0, sget("sumc_free")])
        sm1 = b.v(lambda e, rr=rr: e.tensor_copy(sb["sumc"][32:64, :],
                                                 sb["alf"][rr][32:64, 0:1]),
                  waits=[l1, sm0])
        ed0 = b.s(lambda e, rr=rr: e.activation(
            sb["expdfm"][:], sb["alf"][rr], AF.Exp, scale=-1.0,
            bias=sb["sumc"][:, 0:1]), waits=[l1, sm1])
        te = b.t(lambda e: e.transpose(pyd[0:128, 0:64], sb["expdfm"][:],
                                       sb["idf"][0:64, 0:64]),
                 waits=[ed0] + bw("yd"))
        et0 = b.v(lambda e, rr=rr: e.tensor_copy(sb["expdtm"][rr], pyd[0:128, 0:64]),
                  waits=[te, sget(("expdtm_free", rr))])
        bset("yd", et0)
        S["expdfm_free"] = et0
        tdr = b.t(lambda e: e.transpose(pyo[0:1, 0:64], sb["sumc"][:, 0:1],
                                        sb["idf"][0:64, 0:64]),
                  waits=[sm1] + bw("yo"))
        dr0 = b.s(lambda e, c=c: e.activation(sb["drows"][:, c, :], pyo[0:1, 0:64],
                                              AF.Exp), waits=[tdr])
        bset("yo", dr0)
        dr1 = b.v(lambda e, c=c, rr=rr: e.tensor_copy(sb["drb"][rr],
                                                      sb["drows"][:, c, :]),
                  waits=[dr0, sget(("drb_free", rr))])
        S["sumc_free"] = dr0
        yield 1

        yfree = None
        for d in (0, 1):
            h0 = 32 * d
            gmm = b.t(lambda e, d=d, rr=rr: e.matmul(
                pmed[:, 0:128], sb["bcfm"][rr, 64 * d:64 * d + 64, 0, :],
                sb["bcfm"][rr, 64 * d:64 * d + 64, 1, :],
                start=True, stop=True),
                waits=[S[("bcfm_ch", rr, 0)], S[("bcfm_ch", rr, 1)]] + bw("med"))
            yield 1
            for q in range(4):
                yield 0
                pair = q % 2
                cb = pair * 1024
                bkA, bkB = "b%d" % (2 * pair), "b%d" % (2 * pair + 1)
                stops = []
                for h8 in range(8):
                    h = 8 * q + h8
                    colw = bw(bkA) + bw(bkB) if h8 == 0 else []
                    b.t(lambda e, h=h, h8=h8, rr=rr, h0=h0, cb=cb: e.matmul(
                        pbig[:, cb + h8 * 128:cb + h8 * 128 + 128],
                        sb["idb"][0:64, h0 + h:h0 + h + 1].broadcast_to([64, 128]),
                        sb["poshi"][rr],
                        start=(h8 % 4 == 0), stop=False),
                        waits=[p0, tok_idb] + colw)
                    m2 = b.t(lambda e, h=h, h8=h8, rr=rr, h0=h0, cb=cb: e.matmul(
                        pbig[:, cb + h8 * 128:cb + h8 * 128 + 128],
                        sb["idb"][0:64, h0 + h:h0 + h + 1].broadcast_to([64, 128]),
                        sb["poslo"][rr],
                        start=False, stop=(h8 % 4 == 3)),
                        waits=[p1])
                    if h8 % 4 == 3:
                        stops.append(m2)
                etoks = []
                for h8 in range(8):
                    h = 8 * q + h8
                    ex = b.s(lambda e, h=h, h8=h8, rr=rr, h0=h0, cb=cb, pair=pair:
                             e.activation(
                                 sb["mall"][:, pair * 1024 + h8 * 128:
                                            pair * 1024 + h8 * 128 + 128],
                                 pbig[:, cb + h8 * 128:cb + h8 * 128 + 128], AF.Exp,
                                 bias=sb["naltm"][rr][:, h0 + h:h0 + h + 1]),
                             waits=[stops[h8 // 4], at1, sget(("mall_free", pair))])
                    etoks.append(ex)
                bset(bkA, etoks[3])
                bset(bkB, etoks[7])
                mzt = None
                for mq in range(2):
                    mzt = b.v(lambda e, d=d, mq=mq, pair=pair: e.copy_predicated(
                        sb["mall"][:, pair * 1024 + mq * 512:
                                   pair * 1024 + mq * 512 + 512],
                        sb["mskz"][:, d, :],
                        sb["zc"][:, 0:1].broadcast_to([128, 512])),
                        waits=etoks + [tok_msk, tok_zc])
                mg = b.v(lambda e, pair=pair: e.tensor_tensor(
                    r3(sb["mall"][:, pair * 1024:pair * 1024 + 1024], 128),
                    r3(sb["mall"][:, pair * 1024:pair * 1024 + 1024], 128),
                    pmed[:, 0:128].unsqueeze(1).broadcast_to([128, 8, 128]),
                    ALU.mult),
                    waits=[mzt, gmm])
                badd("med", mg)
                xb = b.v(lambda e, d=d, q=q, rr=rr, h0=h0, pair=pair: e.tensor_tensor(
                    r3(sb["xall"][pair], 64)[:, 0:8, :],
                    r3(sb["xitm"][rr][:, q * 512:(q + 1) * 512], 64),
                    sb["dtb16"][rr][:, h0 + 8 * q:h0 + 8 * q + 8]
                    .unsqueeze(2).broadcast_to([128, 8, 64]), ALU.mult),
                    waits=[S[("xitm", c)], a5, sget(("xall_free", pair))])
                bh = b.v(lambda e, d=d, q=q, rr=rr, h0=h0, pair=pair: e.tensor_tensor(
                    r3(sb["bhat"][pair], 64)[:, 0:8, :],
                    sb["bct"][rr][:, 64 * d:64 * d + 64]
                    .unsqueeze(1).broadcast_to([128, 8, 64]),
                    sb["expdtm"][rr][:, h0 + 8 * q:h0 + 8 * q + 8]
                    .unsqueeze(2).broadcast_to([128, 8, 64]), ALU.mult),
                    waits=[S[("bct", c)], et0, sget(("bhat_free", pair))])
                yield 2
                ydm = None
                for j in range(8):
                    ydm = b.t(lambda e, j=j, pair=pair: e.matmul(
                        pyd[:, j * 64:(j + 1) * 64],
                        sb["mall"][:, pair * 1024 + j * 128:pair * 1024 + j * 128 + 128],
                        sb["xall"][pair][:, j * 64:j * 64 + 64],
                        start=(j == 0), stop=(j == 7)),
                        waits=[mg, xb] + (bw("yd") if j == 0 else []))
                if d == 0:
                    yom = None
                    for j in range(8):
                        hq = 8 * q + j
                        yom = b.t(lambda e, hq=hq, j=j, rr=rr: e.matmul(
                            pyo[:, j * 64:(j + 1) * 64],
                            sb["bcfm"][rr, 0:64, 1, :],
                            sb["stB"][:, hq * 64:hq * 64 + 64],
                            start=(j == 0), stop=(j == 7)),
                            waits=[S[("bcfm_ch", rr, 1)], sget("sbf"), mg]
                            + (bw("yo") if j == 0 else []))
                    ye0 = b.v(lambda e, q=q, rr=rr, h0=h0: e.tensor_tensor(
                        r3(sb["yt"][q % 2], 64), r3(pyo[:, 0:512], 64),
                        sb["expaltm"][rr][:, h0 + 8 * q:h0 + 8 * q + 8]
                        .unsqueeze(2).broadcast_to([128, 8, 64]), ALU.mult),
                        waits=[yom, ae, sget(("yt_free", q % 2))])
                    bset("yo", ye0)
                    ye1 = b.v(lambda e, q=q: e.tensor_tensor(
                        sb["yt"][q % 2], sb["yt"][q % 2], pyd[:, 0:512], ALU.add),
                        waits=[ye0, ydm])
                    bset("yd", ye1)
                    spy = b.dsync(lambda e, c=c, q=q: e.dma_start(
                        dr["yfw_sp"][c * CK + 1:(c + 1) * CK + 1,
                                     q * 512:(q + 1) * 512],
                        sb["yt"][q % 2]), waits=[ye1])
                    S[("yt_free", q % 2)] = spy
                else:
                    ye1 = b.v(lambda e, q=q: e.tensor_copy(
                        sb["yt"][q % 2], pyd[:, 0:512]),
                        waits=[ydm, sget(("yt_free", q % 2))])
                    bset("yd", ye1)
                    spy = b.dsync(lambda e, c=c, q=q: e.dma_start(
                        dr["ybd_sp"][c * CK:(c + 1) * CK, q * 512:(q + 1) * 512],
                        sb["yt"][q % 2]), waits=[ye1])
                    S[("yt_free", q % 2)] = spy
                csm = None
                for j in range(8):
                    csm = b.t(lambda e, j=j, q=q, cb=cb, pair=pair: e.matmul(
                        pbig[0:64, cb + j * 64:cb + j * 64 + 64],
                        sb["bhat"][pair][:, j * 64:j * 64 + 64],
                        sb["xall"][pair][:, j * 64:j * 64 + 64],
                        start=(j == 0), stop=(j == 7)),
                        waits=[bh, xb, mg] + (bw(bkA) if j == 0 else []))
                S[("xall_free", pair)] = csm
                S[("bhat_free", pair)] = csm
                S[("mall_free", pair)] = csm
                if d == 0:
                    j = q
                    pt, bank = ((pyd, "yd") if j % 2 == 0 else (pyo, "yo"))
                    mm = b.t(lambda e, j=j, rr=rr, pt=pt: e.matmul(
                        pt[0:64, 0:512], sb["ones1b"][0:1, 0:64],
                        sb["drb"][rr][0:1, 8 * j:8 * j + 8]
                        .unsqueeze(2).broadcast_to([1, 8, 64]),
                        start=True, stop=True),
                        waits=[dr1, tok_o1b] + bw(bank))
                    evd = b.v(lambda e, j=j, pt=pt: e.tensor_copy(
                        sb["decrep"][:, j * 512:(j + 1) * 512], pt[0:64, 0:512]),
                        waits=[mm, sget("decrep_free")])
                    bset(bank, evd)
                    s_0 = b.v(lambda e, q=q: e.tensor_tensor(
                        sb["stA"][:, q * 512:(q + 1) * 512],
                        sb["stA"][:, q * 512:(q + 1) * 512],
                        sb["decrep"][:, q * 512:(q + 1) * 512], ALU.mult),
                        waits=[evd, sget("sfw"), yom])
                    s_1 = b.v(lambda e, q=q, cb=cb: e.tensor_tensor(
                        sb["stA"][:, q * 512:(q + 1) * 512],
                        sb["stA"][:, q * 512:(q + 1) * 512],
                        pbig[0:64, cb:cb + 512], ALU.add),
                        waits=[s_0, csm])
                    bset(bkA, s_1)
                    S[("sfw_h", q)] = s_1
                    s_2 = b.v(lambda e, q=q: e.tensor_copy(
                        sb["stB"][:, q * 512:(q + 1) * 512],
                        sb["stA"][:, q * 512:(q + 1) * 512]), waits=[s_1])
                    S[("sbf_h", q)] = s_2
                else:
                    cb0 = b.v(lambda e, q=q, cb=cb: e.tensor_copy(
                        sb["csbt"][:, q * 512:(q + 1) * 512],
                        pbig[0:64, cb:cb + 512]),
                        waits=[csm, sget("csbt_free")])
                    bset(bkA, cb0)
                    S[("csbt_h", q)] = cb0
            if d == 0:
                sfb = b.v(lambda e: e.engine_nop(),
                          waits=[S[("sfw_h", q)] for q in range(4)])
                S["sfw"] = sfb
                sbb = b.v(lambda e: e.engine_nop(),
                          waits=[S[("sbf_h", q)] for q in range(4)])
                S["sbf"] = sbb
                S["decrep_free"] = sfb
            else:
                spcs = b.dsync(lambda e, c=c: e.dma_start(dr["csbw_sp"][c],
                                                          sb["csbt"][:]),
                               waits=[S[("csbt_h", q)] for q in range(4)])
                S["csbt_free"] = spcs
                S[("csbw", c)] = spcs
                yfree = spcs
        S[("dts_free", rr)] = alf_tok
        S[("dtb16_free", rr)] = yfree
        S[("alf_free", rr)] = ed0
        S[("poshi_free", rr)] = yfree
        S[("poslo_free", rr)] = yfree
        S[("altm_free", rr)] = yfree
        S[("naltm_free", rr)] = yfree
        S[("expaltm_free", rr)] = yfree
        S[("expdtm_free", rr)] = yfree
        S[("drb_free", rr)] = yfree
        S[("bct_free", rr)] = yfree
        S[("xitm_free", rr)] = yfree
        S[("bcfm_free", rr)] = yfree

    def ffn(st):
        stb = st % 2
        TT = tps * CK
        xn_ready = [S[("xnfm_tile", stb, it)] for it in range(tps)]
        be = None
        for it in range(tps):
            tc = b.t(lambda e, stb=stb, it=it: e.transpose(
                pyd[0:8, 0:128], sb["comb"][stb, :, it, :], sb["idf"][:]),
                waits=[S[("comb_tile", stb, it)]] + bw("yd"))
            ce = b.v(lambda e: e.tensor_copy(sb["combT"][:], pyd[0:8, 0:128]),
                     waits=[tc, sget("combT_free")])
            bset("yd", ce)
            for nn2 in range(2):
                bm = b.t(lambda e, nn2=nn2: e.matmul(
                    pyo[:, 0:512], sb["combT"][:], sb["b2r"][:, nn2 * 512:(nn2 + 1) * 512],
                    start=True, stop=True),
                    waits=[ce, tok_b2r] + bw("yo"))
                be = b.v(lambda e, it=it, nn2=nn2: e.tensor_copy(
                    sb["acc"][:, it, nn2 * 512:(nn2 + 1) * 512], pyo[:, 0:512]),
                    waits=[bm, sget("acc_free")])
                bset("yo", be)
            S["combT_free"] = be
            yield
        acc_init = be
        prev_y = acc_init
        for ei in range(NE):
            for hf in range(2):
                pp = hf
                wtok = b.dscal(lambda e, ei=ei, hf=hf, pp=pp: e.dma_start(
                    sb["wA"][pp],
                    dr["w1_b"][ei][:, hf * 1024:(hf + 1) * 1024]
                    .rearrange("(c k) n -> k c n", k=128)),
                    waits=[sget(("wA_free", pp)), cv_w1[ei]])
                for hcl in range(8):
                    hc = 8 * hf + hcl
                    bank = "b%d" % (hcl % 4)
                    mm = None
                    for kc in range(8):
                        mm = b.t(lambda e, kc=kc, pp=pp, hcl=hcl, TT=TT: e.matmul(
                            pbig[:, (hcl % 4) * 512:(hcl % 4) * 512 + TT],
                            sb["wA"][pp, :, kc, hcl * 128:(hcl + 1) * 128],
                            sb["xnfm"][stb, :, kc, 0:TT],
                            start=(kc == 0), stop=(kc == 7)),
                            waits=[wtok] + xn_ready + (bw(bank) if kc == 0 else []))
                    hv = b.s(lambda e, hc=hc, ei=ei, hcl=hcl, TT=TT: e.activation(
                        sb["hb"][:, hc, 0:TT],
                        pbig[:, (hcl % 4) * 512:(hcl % 4) * 512 + TT], AF.Relu,
                        bias=sb["b1cc"][:, ei, hc:hc + 1]),
                        waits=[mm, tok_b1c, prev_y])
                    bset(bank, hv)
                    S[("hb_hc", hc)] = hv
                    if hcl % 2 == 1:
                        yield
                S[("wA_free", pp)] = mm
            ydone = None
            for chh in range(2):
                pp = chh
                w2tok = b.dscal(lambda e, ei=ei, chh=chh, pp=pp: e.dma_start(
                    sb["wA"][pp].rearrange("p a n -> p (a n)")
                    .rearrange("p (c n) -> p c n", n=512),
                    dr["w2_b"][ei][:, chh * 512:(chh + 1) * 512]
                    .rearrange("(c k) n -> k c n", k=128)),
                    waits=[sget(("wA_free", pp)), cv_w2[ei]])
                for tb in range(tps):
                    pt, bank = ((pyd, "yd") if tb % 2 == 0 else (pyo, "yo"))
                    mm = None
                    for kc in range(16):
                        mm = b.t(lambda e, kc=kc, pp=pp, tb=tb, pt=pt: e.matmul(
                            pt[:, 0:512],
                            sb["hb"][:, kc, tb * 128:(tb + 1) * 128],
                            sb["wA"][pp].rearrange("p a n -> p (a n)")
                            .rearrange("p (c n) -> p c n", n=512)[:, kc, :],
                            start=(kc == 0), stop=(kc == 15)),
                            waits=[w2tok, S[("hb_hc", kc)]] + (bw(bank) if kc == 0 else []))
                    ac = b.v(lambda e, tb=tb, chh=chh, stb=stb, ei=ei, pt=pt:
                             e.scalar_tensor_tensor(
                                 sb["acc"][:, tb, chh * 512:(chh + 1) * 512],
                                 pt[:, 0:512],
                                 sb["comb"][stb, :, tb, ei:ei + 1],
                                 sb["acc"][:, tb, chh * 512:(chh + 1) * 512],
                                 ALU.mult, ALU.add),
                             waits=[mm, S[("comb_tile", stb, tb)], acc_init])
                    bset(bank, ac)
                    ydone = ac
                    yield
                S[("wA_free", pp)] = mm
            prev_y = ydone
        spa = b.dsync(lambda e, st=st, TT=TT: e.dma_start(
            dr["ffn_sp"][st * TT:(st + 1) * TT, :].rearrange("(t p) c -> p t c", p=128),
            sb["acc"][:, 0:tps, :]), waits=[prev_y])
        S["acc_free"] = spa
        S[("comb_free", stb)] = prev_y
        S[("xnfm_free", stb)] = prev_y

    def tile_b(c):
        rb = c % 2
        if c < nt - 1:
            d0 = b.v(lambda e, c=c, rb=rb: e.tensor_copy(
                sb["drb"][rb], sb["drows"][:, c + 1, :]),
                waits=[sget(("drb_free", rb))])
            S[("drb_free", rb)] = d0
            dts_ = []
            for j in range(4):
                pt, bank = ((pyd, "yd") if j % 2 == 0 else (pyo, "yo"))
                mm = b.t(lambda e, j=j, rb=rb, pt=pt: e.matmul(
                    pt[0:64, 0:512], sb["ones1b"][0:1, 0:64],
                    sb["drb"][rb][0:1, 32 + 8 * j:32 + 8 * j + 8]
                    .unsqueeze(2).broadcast_to([1, 8, 64]),
                    start=True, stop=True),
                    waits=[d0, tok_o1b] + bw(bank))
                evd = b.s(lambda e, j=j, pt=pt: e.copy(
                    sb["decrep"][:, j * 512:(j + 1) * 512], pt[0:64, 0:512]),
                    waits=[mm, sget("decrep_a"), sget("decrep_b")])
                bset(bank, evd)
                dts_.append(evd)
            rl = b.dsync(lambda e, c=c: e.dma_start(
                sb["csbt"][:], dr["csbw_sp"][c + 1]),
                waits=[sget("csrl_a"), sget("csrl_b"), S[("csbw", c + 1)]])
            s_0a = b.v(lambda e: e.tensor_tensor(
                sb["stA"][:, 0:CS], sb["stA"][:, 0:CS],
                sb["decrep"][:, 0:CS], ALU.mult),
                waits=dts_[0:3] + [sget("sbw_a")])
            s_0b = b.g(lambda e: e.tensor_tensor(
                sb["stA"][:, CS:2048], sb["stA"][:, CS:2048],
                sb["decrep"][:, CS:2048], ALU.mult),
                waits=dts_[2:4] + [sget("sbw_b")])
            S["decrep_a"] = s_0a
            S["decrep_b"] = s_0b
            s_1a = b.v(lambda e: e.tensor_tensor(
                sb["stA"][:, 0:CS], sb["stA"][:, 0:CS], sb["csbt"][:, 0:CS],
                ALU.add), waits=[s_0a, rl])
            s_1b = b.g(lambda e: e.tensor_tensor(
                sb["stA"][:, CS:2048], sb["stA"][:, CS:2048],
                sb["csbt"][:, CS:2048], ALU.add), waits=[s_0b, rl])
            S["csrl_a"] = s_1a
            S["csrl_b"] = s_1b
            S["sbw_a"] = s_1a
            S["sbw_b"] = s_1b
            s_2 = b.v(lambda e: e.tensor_copy(sb["stB"][:], sb["stA"][:]),
                      waits=[s_1a, s_1b, sget("sbwbf_rd")])
            S["sbwbf"] = s_2
            S["sbwbf_g"] = s_2
        rl_exp = b.dsync(lambda e, c=c, rb=rb: e.dma_start(
            sb["expb"][rb], dr["expal_sp"][c * CK:(c + 1) * CK, 32:64]),
            waits=[sget(("expb_free", rb))])
        rl_cbw = b.dsync(lambda e, c=c, rb=rb: e.dma_start(
            sb["cbw"][rb], dr["bc_sp"][c * CK:(c + 1) * CK, 192:256]),
            waits=[sget(("cbw_free", rb))])
        rl_ybd = b.dsync(lambda e, c=c, rb=rb: e.dma_start(
            sb["ybdt"][rb], dr["ybd_sp"][c * CK:(c + 1) * CK, :]),
            waits=[sget(("ybdt_free", rb))])
        tcb = b.t(lambda e, rb=rb: e.transpose(ptb[0:64, 0:128], sb["cbw"][rb],
                                               sb["idb"][:]),
                  waits=[rl_cbw, sget("cbwfm_free")] + bw("tb"))
        ecb = b.v(lambda e: e.tensor_copy(sb["cbwfm"][:], ptb[0:64, 0:128]),
                  waits=[tcb])
        bset("tb", ecb)
        S[("cbw_free", rb)] = ecb
        ye1 = None
        for q in range(4):
            yom = None
            for j in range(8):
                hq = 8 * q + j
                yom = b.t(lambda e, hq=hq, j=j: e.matmul(
                    pyo[:, j * 64:(j + 1) * 64], sb["cbwfm"][:],
                    sb["stB"][:, hq * 64:hq * 64 + 64],
                    start=(j == 0), stop=(j == 7)),
                    waits=[ecb, sget("sbwbf"), sget("sbwbf_g")]
                    + (bw("yo") if j == 0 else []))
            ye0 = b.v(lambda e, q=q, rb=rb: e.tensor_tensor(
                r3(sb["ybwt"][rb][:, q * 512:(q + 1) * 512], 64),
                r3(pyo[:, 0:512], 64),
                sb["expb"][rb][:, 8 * q:8 * q + 8]
                .unsqueeze(2).broadcast_to([128, 8, 64]), ALU.mult),
                waits=[yom, rl_exp, sget(("ybwt_free", rb))])
            bset("yo", ye0)
            ye1 = b.v(lambda e, q=q, rb=rb: e.tensor_tensor(
                sb["ybwt"][rb][:, q * 512:(q + 1) * 512],
                sb["ybwt"][rb][:, q * 512:(q + 1) * 512],
                sb["ybdt"][rb][:, q * 512:(q + 1) * 512], ALU.add),
                waits=[ye0, rl_ybd])
        S[("ybdt_free", rb)] = ye1
        S["cbwfm_free"] = yom
        S["sbwbf_rd"] = yom
        S[("ybwt_ye", c)] = ye1
        smms = []
        for nb in range(4):
            bank = "b%d" % nb
            sm = b.t(lambda e, nb=nb, rb=rb, c=c: e.matmul(
                pbig[:, nb * 512:(nb + 1) * 512], sb["shm"][:, 0:128],
                sb["ybwt"][rb][:, nb * 512:(nb + 1) * 512],
                start=True, stop=(c == nt - 1)),
                waits=[ye1, tok_shm] + bw(bank))
            if c < nt - 1:
                sm = b.t(lambda e, nb=nb, rb=rb: e.matmul(
                    pbig[:, nb * 512:(nb + 1) * 512], sb["shm"][:, 128:256],
                    sb["ybwt"][rb ^ 1][:, nb * 512:(nb + 1) * 512],
                    start=False, stop=True),
                    waits=[S[("ybwt_ye", c + 1)]])
            smms.append(sm)
        if c < nt - 1:
            S[("ybwt_free", rb ^ 1)] = smms[-1]
        ybse = []
        for nb in range(4):
            ev = b.s(lambda e, nb=nb, rb=rb: e.copy(
                sb["ybshs"][rb][:, nb * 512:(nb + 1) * 512],
                pbig[:, nb * 512:(nb + 1) * 512]),
                waits=[smms[nb], sget(("ybshs_f", rb))])
            bset("b%d" % nb, ev)
            ybse.append(ev)
        S[("ybse", c)] = ybse

    def tile_c(c):
        rb = c % 2
        ybse = S[("ybse", c)]
        rl_yfw = b.dsync(lambda e, c=c, rb=rb: e.dma_start(
            sb["yfwt"][rb], dr["yfw_sp"][c * CK:(c + 1) * CK, :]),
            waits=[sget(("yfwt_free", rb)), sget(("yfwt_free2", rb))])
        rl_xi = b.dsync(lambda e, c=c, rb=rb: e.dma_start(
            sb["xirl"][rb], dr["xi_sp"][c * CK:(c + 1) * CK, :]),
            waits=[sget(("xirl_free", rb))])
        rl_z = b.dsync(lambda e, c=c, rb=rb: e.dma_start(
            sb["zrl"][rb], dr["z_sp"][c * CK:(c + 1) * CK, :]),
            waits=[sget(("zrl_free", rb)), sget(("zrl_free2", rb))])
        rl_ff = b.dsync(lambda e, c=c, rb=rb: e.dma_start(
            sb["ffrl"][rb], dr["ffn_sp"][c * CK:(c + 1) * CK, :]),
            waits=[sget(("ffrl_free", rb)), sget(("ffrl_free2", rb)),
                   sget("acc_free")])
        rl_x = b.dscal(lambda e, c=c, rb=rb: e.dma_start(
            sb["xt"][rb], dr["u"][c * CK:(c + 1) * CK, :]),
            waits=[sget(("xt_free", rb))])
        y0w = [rl_xi, S[("fcd", c)], sget(("ycmb_free", 0)),
               sget(("ycmb_free", 1))]
        y0a = b.v(lambda e, c=c, rb=rb: e.tensor_tensor(
            r3(sb["ycmb"][rb][:, 0:CS], 64), r3(sb["xirl"][rb][:, 0:CS], 64),
            sb["fcd"][:, c, 0:CS // 64].unsqueeze(2)
            .broadcast_to([128, CS // 64, 64]), ALU.mult), waits=y0w)
        y0b = b.g(lambda e, c=c, rb=rb: e.tensor_tensor(
            r3(sb["ycmb"][rb][:, CS:2048], 64), r3(sb["xirl"][rb][:, CS:2048], 64),
            sb["fcd"][:, c, CS // 64:32].unsqueeze(2)
            .broadcast_to([128, 32 - CS // 64, 64]), ALU.mult), waits=y0w)
        y1 = b.v(lambda e, rb=rb: e.tensor_tensor(
            sb["ycmb"][rb], sb["ycmb"][rb], sb["yfwt"][rb], ALU.add),
            waits=[y0a, y0b, rl_yfw])
        y2a = b.v(lambda e, rb=rb: e.tensor_tensor(
            sb["ycmb"][rb], sb["ycmb"][rb], sb["ybshs"][rb], ALU.add),
            waits=[y1] + ybse)
        y2b = y2a
        S[("ybshs_f", rb)] = y2a
        S[("yfwt_free", rb)] = y1
        S[("yfwt_free2", rb)] = y1
        sz = b.s(lambda e, rb=rb: e.activation(sb["zrl"][rb], sb["zrl"][rb], AF.Silu),
                 waits=[rl_z])
        ssq = b.s(lambda e, rb=rb: e.activation(sb["actt"][rb], sb["ycmb"][rb],
                                                AF.Square, accum_out=sb["gsc"][:, 0:1]),
                  waits=[y2a, y2b, sget("gsc_free"), sget(("actt_free", rb))])
        r0a = b.v(lambda e: e.tensor_scalar(sb["gsc"][:, 1:2], sb["gsc"][:, 0:1],
                                            1.0 / DI, 1e-5, ALU.mult, ALU.add),
                  waits=[ssq])
        r0 = b.s(lambda e: e.activation(sb["gsc"][:, 1:2], sb["gsc"][:, 1:2], AF.Ln),
                 waits=[r0a])
        r1a = b.v(lambda e: e.tensor_scalar_mul(sb["gsc"][:, 2:3], sb["gsc"][:, 1:2],
                                                -0.5), waits=[r0])
        r1 = b.s(lambda e: e.activation(sb["gsc"][:, 2:3], sb["gsc"][:, 2:3], AF.Exp),
                 waits=[r1a])
        v0 = b.v(lambda e, rb=rb: e.tensor_scalar_mul(sb["actt"][rb], sb["ycmb"][rb],
                                                      sb["gsc"][:, 2:3]),
                 waits=[r1, sget(("actt_free", rb))])
        S[("ycmb_free", 0)] = v0
        S[("ycmb_free", 1)] = v0
        S["gsc_free"] = v0
        v1 = b.v(lambda e, rb=rb: e.tensor_tensor(
            sb["actt"][rb], sb["actt"][rb], sb["nwb"][:], ALU.mult),
            waits=[v0, tok_nwb])
        v2a = b.v(lambda e, rb=rb: e.tensor_tensor(
            sb["actt"][rb], sb["actt"][rb], sb["zrl"][rb], ALU.mult),
            waits=[v1, sz])
        v2b = v2a
        S[("zrl_free", rb)] = v2a
        S[("zrl_free2", rb)] = v2a
        aevs = []
        for wave in range(2):
            tts2 = []
            for sl in range(8):
                chk = wave * 8 + sl
                tw = [v2a if chk < CS // 128 else v2b, tok_idb]
                tw.extend(bw("tb") if chk == 0 else ([aevs[-1]] if chk == 8 else []))
                tts2.append(b.t(lambda e, chk=chk, sl=sl, rb=rb: e.transpose(
                    ptb[:, sl * 128:(sl + 1) * 128],
                    sb["actt"][rb][:, chk * 128:(chk + 1) * 128], sb["idb"][:]),
                    waits=tw))
            for sl in range(8):
                chk = wave * 8 + sl
                mk = b.v if wave == 0 else b.s
                fn = ((lambda e, chk=chk, sl=sl, rb=rb: e.tensor_copy(
                    sb["actf"][rb][:, chk, :], ptb[:, sl * 128:(sl + 1) * 128]))
                    if wave == 0 else
                    (lambda e, chk=chk, sl=sl, rb=rb: e.copy(
                        sb["actf"][rb][:, chk, :], ptb[:, sl * 128:(sl + 1) * 128])))
                evx = mk(fn, waits=[tts2[-1], sget(("actf_free", rb))])
                aevs.append(evx)
        bset("tb", aevs[-1])
        S[("actt_free", rb)] = aevs[-1]
        o1s = []
        mm = None
        for chh in range(2):
            pt, bank = ((pyd, "yd") if chh == 0 else (pyo, "yo"))
            mm = None
            for kc in range(16):
                mm = b.t(lambda e, kc=kc, chh=chh, rb=rb, pt=pt: e.matmul(
                    pt[:, 0:512], sb["actf"][rb][:, kc, :],
                    sb["woutv"][:, kc, chh * 512:(chh + 1) * 512],
                    start=(kc == 0), stop=(kc == 15)),
                    waits=[aevs[kc], sget("wout_tok")] + (bw(bank) if kc == 0 else []))
            o0 = b.v(lambda e, chh=chh, rb=rb, pt=pt: e.scalar_tensor_tensor(
                sb["outt"][rb][:, chh * 512:(chh + 1) * 512], pt[:, 0:512], 1.0,
                sb["xt"][rb][:, chh * 512:(chh + 1) * 512], ALU.mult, ALU.add),
                waits=[mm, rl_x, sget(("outt_free", rb))])
            bset(bank, o0)
            mk1 = b.v if chh == 0 else b.g
            o1 = mk1(lambda e, chh=chh, rb=rb: e.tensor_tensor(
                sb["outt"][rb][:, chh * 512:(chh + 1) * 512],
                sb["outt"][rb][:, chh * 512:(chh + 1) * 512],
                sb["ffrl"][rb][:, chh * 512:(chh + 1) * 512], ALU.add),
                waits=[o0, rl_ff])
            o1s.append(o1)
        S[("actf_free", rb)] = mm
        S[("ffrl_free", rb)] = o1s[0]
        S[("ffrl_free2", rb)] = o1s[1]
        S[("xt_free", rb)] = o0
        st_out = b.dscal(lambda e, c=c, rb=rb: e.dma_start(
            out[c * CK:(c + 1) * CK, :], sb["outt"][rb]), waits=o1s)
        S[("outt_free", rb)] = st_out

    def drain(gen):
        if gen is not None:
            for _ in gen:
                pass

    def pump(gen, n=1):
        if gen is None:
            return None
        for _ in range(n):
            try:
                next(gen)
            except StopIteration:
                return None
        return gen

    if SCHED != "prepass":
        fg = None
        for i in range(nt):
            st = i // tps
            if i % tps == 0 and st > 0 and SCHED in ("ffn", "full"):
                drain(fg)
                fg = ffn(st - 1)
                fg = pump(fg)
            for h_ in tile_head(i):
                fg = pump(fg, h_)
            if i > 0 and SCHED in ("conv", "ffn", "full"):
                for h_ in conv_ssd(i - 1):
                    fg = pump(fg, h_)
        drain(fg)
        if SCHED in ("conv", "ffn", "full"):
            e2 = b.v(lambda e: e.memset(sb["asm"][(nt - 1) % 2, :, :, 131:134], 0.0),
                     waits=[S[("asm_center", nt - 1)]])
            S[("asm_ready", nt - 1)] = e2
            drain(conv_ssd(nt - 1))
        if SCHED in ("ffn", "full"):
            drain(ffn(nst - 1))
    if SCHED != "full":
        final_waits = []
        for ring in ("dsy", "dsc", "dgp"):
            for ln in range(NLANES):
                sem = "%s%d" % (ring, ln)
                if b.cnt[sem]:
                    final_waits.append((sem, b.cnt[sem]))
        b.op("sync", lambda e: e.nop(), waits=final_waits, inc=None)
        import contextlib
        with contextlib.ExitStack() as stk:
            for name in list(b.cnt):
                b.handles[name] = stk.enter_context(nc.semaphore(name))
            blk = stk.enter_context(nc.Block())
            b.replay_into(blk)
        return nc

    aw = [S[("xnfm_free", 0)], sget(("xnfm_free", 1)), S["acc_free"], S["xmt_free"],
          S["xnt_free"], S["xnf32_free"], S["gsc_free"]]
    aw.append(sget("csbt_free"))
    for r_ in (0, 1):
        for kk in ("xitm_free", "asm_free", "win_free", "bhat_free",
                   "xifm_free", "xall_free", "yt_free", "zt_free", "mall_free"):
            aw.append(sget((kk, r_)))
    bar = b.v(lambda e: e.engine_nop(), waits=aw)
    for r_ in (0, 1):
        for kk in ("yfwt_free", "yfwt_free2", "ybwt_free", "zrl_free",
                   "zrl_free2", "ffrl_free2",
                   "ybdt_free", "xirl_free", "ycmb_free", "actt_free",
                   "actf_free", "outt_free", "ffrl_free", "expb_free",
                   "cbw_free", "xt_free"):
            S[(kk, r_)] = bar
    S["gsc_free"] = bar
    S["csrl_free"] = bar
    S["wout_tok"] = b.dgp(lambda e: e.dma_start(
        sb["woutv"], dr["wout_b"].rearrange("(c k) n -> k c n", k=128)),
        waits=[bar, cv_wout])
    tok_nwb = b.dgp(lambda e: e.dma_start(sb["nwb"], dr["nw"]), waits=[bar])
    msA = b.v(lambda e: e.memset(sb["stA"][:], 0.0),
              waits=[bar, sget("sfw"), sget("sbf")])
    S["sbw_a"] = msA
    S["sbw_b"] = msA
    msB = b.v(lambda e: e.memset(sb["stB"][:], 0.0), waits=[bar, msA])
    S["sbwbf"] = msB
    S["sbwbf_g"] = msB
    S[("ybshs_f", 0)] = bar
    S[("ybshs_f", 1)] = bar
    S["csrl_a"] = bar
    S["csrl_b"] = bar
    S["decrep_a"] = sget("decrep_free")
    S["decrep_b"] = bar
    for c in range(nt - 1, -1, -1):
        tile_b(c)
        tile_c(c)
    final_waits = []
    for ring in ("dsy", "dsc", "dgp"):
        for ln in range(NLANES):
            sem = "%s%d" % (ring, ln)
            if b.cnt[sem]:
                final_waits.append((sem, b.cnt[sem]))
    b.op("sync", lambda e: e.nop(), waits=final_waits, inc=None)

    import contextlib
    with contextlib.ExitStack() as stk:
        for name in list(b.cnt):
            b.handles[name] = stk.enter_context(nc.semaphore(name))
        blk = stk.enter_context(nc.Block())
        b.replay_into(blk)
    return nc


def kernel(**inputs):
    A = host_prep(inputs)
    u = np.ascontiguousarray(inputs["L0_attn_u"], dtype=np.float32)
    nt = NT_OVERRIDE if NT_OVERRIDE else L // CK
    nc = build(nt)
    in_maps = []
    for c in range(8):
        m = {"u": np.ascontiguousarray(u[c][: nt * CK])}
        m.update(A)
        in_maps.append(m)
    res = run_bass_kernel_spmd(nc, in_maps, core_ids=list(range(8)))
    return np.stack([r["out"] for r in res.results])


# version 13
# speedup vs baseline: 1.4172x; 1.2006x over previous
import numpy as np
import ml_dtypes
import concourse.bass as bass
import concourse.mybir as mybir
from concourse.bass_utils import run_bass_kernel_spmd

F32 = mybir.dt.float32
BF16 = mybir.dt.bfloat16
U8 = mybir.dt.uint8
AF = mybir.ActivationFunctionType
ALU = mybir.AluOpType
AX = mybir.AxisListType.X

L, D, DI, DIP = 4096, 1024, 2048, 4416
NE, CK, TPS = 8, 128, 4
CS = 1408
ENGS = ("tensor", "vector", "scalar", "sync", "gpsimd")
ZERO = ("pe", 0)
NT_OVERRIDE = 0
SCHED = "full"


NLANES = 8


class Bld:
    def __init__(self, nc):
        self.nc = nc
        self.ops = []
        self.cnt = {s: 0 for s in ("pe", "v", "s", "g")}
        self.lane_ord = {}
        self.ring_ctr = {"dsy": 0, "dsc": 0, "dgp": 0}
        for ring in ("dsy", "dsc", "dgp"):
            for ln in range(NLANES):
                self.cnt["%s%d" % (ring, ln)] = 0
        self.handles = {}

    def dma(self, ring, fn, waits=()):
        ln = self.ring_ctr[ring] % NLANES
        self.ring_ctr[ring] += 1
        sem = "%s%d" % (ring, ln)
        prev = self.cnt[sem]
        w = list(waits) + ([(sem, prev)] if prev else [])
        return self.op("sync" if ring == "dsy" else
                       ("scalar" if ring == "dsc" else "gpsimd"),
                       fn, w, sem, 16)

    def op(self, eng, fn, waits=(), inc=None, incn=1):
        w = [(s, v) for (s, v) in waits if s is not None and v > 0]
        tok = None
        if inc is not None:
            self.cnt[inc] += incn
            tok = (inc, self.cnt[inc])
        self.ops.append((eng, fn, w, inc, incn))
        return tok

    def t(self, fn, waits=()):
        return self.op("tensor", fn, waits, "pe")

    def v(self, fn, waits=()):
        return self.op("vector", fn, waits, "v")

    def s(self, fn, waits=()):
        return self.op("scalar", fn, waits, "s")

    def g(self, fn, waits=()):
        return self.op("gpsimd", fn, waits, "g")

    def dsync(self, fn, waits=()):
        return self.dma("dsy", fn, waits)

    def dscal(self, fn, waits=()):
        return self.dma("dsc", fn, waits)

    def dgp(self, fn, waits=()):
        return self.dma("dgp", fn, waits)

    def replay_into(self, blk):
        for name in ENGS:
            getattr(blk, name)(self._mk(name))

    def _mk(self, engname):
        def run(eng):
            waited = {}
            for e, fn, waits, inc, incn in self.ops:
                if e != engname:
                    continue
                for s, v in waits:
                    if waited.get(s, 0) < v:
                        eng.wait_ge(self.handles[s], v)
                        waited[s] = v
                ins = fn(eng)
                if inc is not None:
                    ins.then_inc(self.handles[inc], incn)
        return run


class Ring:

    def __init__(self, handle):
        self.h = handle

    def __getitem__(self, key):
        ap = self.h.ap()
        if isinstance(key, tuple):
            key = (key[1], key[0]) + tuple(key[2:])
            return ap[key]
        return ap[:, key]


class AliasRing:

    def __init__(self, *aps):
        self.aps = list(aps)

    def __getitem__(self, key):
        if isinstance(key, tuple):
            return self.aps[key[0]][tuple(key[1:])]
        return self.aps[key]


BCPERM = np.concatenate([np.arange(2048),
                         2048 + np.arange(64), 2176 + np.arange(64),
                         2112 + np.arange(64), 2240 + np.arange(64)])


def host_prep(inputs):
    A = {}
    wt = inputs["L0_attn_in_proj_w"].T
    colperm = np.concatenate([np.arange(2048), 2048 + BCPERM, np.arange(4352, 4416)])
    A["winT"] = np.ascontiguousarray(wt[:, colperm])
    cw = inputs["L0_attn_conv_w"][:, 0, :][BCPERM]
    cd = np.zeros((128, 18, 7, 32), np.float32)
    cwr = cw.reshape(18, 4, 32, 7)
    for bb in range(4):
        for j in range(32):
            cd[32 * bb + j, :, :, j] = cwr[:, bb, j, :]
    A["convd"] = cd
    A["convb"] = np.ascontiguousarray(
        inputs["L0_attn_conv_b"][BCPERM].reshape(18, 128).T)
    A["dtb"] = np.ascontiguousarray(
        np.broadcast_to(np.concatenate([inputs["L0_attn_dt_bias"]] * 2), (128, 64)))
    A["alog"] = np.ascontiguousarray(
        np.concatenate([inputs["L0_attn_A_log"]] * 2)[:, None])
    A["dpar"] = np.ascontiguousarray(np.broadcast_to(inputs["L0_attn_D_param"], (128, 32)))
    A["fcdT"] = np.ascontiguousarray(inputs["L0_attn_fc_D_w"].T)
    A["nw"] = np.ascontiguousarray(np.broadcast_to(inputs["L0_attn_norm_w"], (128, DI)))
    A["woutT"] = np.ascontiguousarray(inputs["L0_attn_out_proj_w"].T)
    A["w1T"] = np.ascontiguousarray(inputs["L0_ffn_W1"].transpose(0, 2, 1))
    A["b1c"] = np.ascontiguousarray(
        inputs["L0_ffn_b1"].reshape(8, 16, 128).transpose(2, 0, 1))
    A["w2T"] = np.ascontiguousarray(inputs["L0_ffn_W2"].transpose(0, 2, 1))
    A["b2"] = np.ascontiguousarray(inputs["L0_ffn_b2"])
    A["wgT"] = np.ascontiguousarray(inputs["L0_ffn_Wg"].T)
    A["bg"] = np.ascontiguousarray(np.broadcast_to(inputs["L0_ffn_bg"], (128, 8)))
    A["idf"] = np.eye(128, dtype=np.float32)
    idx = np.arange(128)
    mz = np.stack([(idx[None, :] < idx[:, None]),
                   (idx[None, :] > idx[:, None])], axis=0).astype(np.uint8)
    A["mskz"] = np.ascontiguousarray(
        np.tile(mz[None, :, :, :], (1, 1, 1, 4)).reshape(2, 128, 512)
        .transpose(1, 0, 2)).copy()
    A["ones1"] = np.ones((1, 128), np.float32)
    shm = np.zeros((128, 256), np.float32)
    shm[idx[1:], idx[:-1]] = 1.0
    shm[0, 255] = 1.0
    A["shm"] = np.ascontiguousarray(shm.astype(ml_dtypes.bfloat16))
    return A


def build(nt):
    nc = bass.Bass("TRN2", target_bir_lowering=False)
    b = Bld(nc)
    Ltot = nt * CK
    tps = min(TPS, nt)
    nst = max(1, (nt + tps - 1) // tps)

    dr = {}

    def din(name, shape, dt=F32):
        dr[name] = nc.dram_tensor(name, list(shape), dt, kind="ExternalInput").ap()

    din("u", (Ltot, D))
    for name, shape, dt in [
        ("winT", (D, DIP), F32), ("convd", (128, 18, 7, 32), F32),
        ("convb", (128, 18), F32), ("dtb", (128, 64), F32), ("alog", (64, 1), F32),
        ("dpar", (128, 32), F32), ("fcdT", (DI, 32), F32), ("nw", (128, DI), F32),
        ("woutT", (DI, D), F32), ("w1T", (NE, D, DI), F32), ("b1c", (128, NE, 16), F32),
        ("w2T", (NE, DI, D), F32), ("b2", (NE, D), F32), ("wgT", (D, NE), F32),
        ("bg", (128, NE), F32), ("idf", (128, 128), F32),
        ("mskz", (128, 2, 512), U8), ("ones1", (1, 128), F32),
        ("shm", (128, 256), BF16),
    ]:
        dr[name] = nc.dram_tensor(name, list(shape), dt, kind="ExternalInput").ap()
    out = nc.dram_tensor("out", [Ltot, D], F32, kind="ExternalOutput").ap()

    def scr(name, shape, dt=BF16):
        dr[name] = nc.dram_tensor(name, list(shape), dt, kind="Internal").ap()

    scr("win_b", (D, DIP))
    scr("w1_b", (NE, D, DI))
    scr("w2_b", (NE, DI, D))
    scr("wout_b", (DI, D))
    scr("z_sp", (Ltot, DI))
    scr("xi_sp", (Ltot, DI))
    scr("yfw_sp", (Ltot + 1, DI))
    scr("ybd_sp", (Ltot, DI))
    scr("csbw_sp", (nt, 64, 2048))
    scr("bc_sp", (Ltot, 256))
    scr("expal_sp", (Ltot, 64))
    scr("ffn_sp", (Ltot, D))

    sb = {}
    hnd = {}

    def asb(name, shape, dt=BF16):
        hnd[name] = nc.alloc_sbuf_tensor("sb_" + name, list(shape), dt)
        sb[name] = hnd[name]

    def aring(name, shape, dt=BF16):
        R, Pp = shape[0], shape[1]
        hnd[name] = nc.alloc_sbuf_tensor("sb_" + name, [Pp, R] + list(shape[2:]), dt)
        sb[name] = Ring(hnd[name])

    asb("idb", (128, 128))
    asb("shm", (128, 256))
    asb("idf", (128, 128), F32)
    asb("mskz", (128, 2, 512), U8)
    asb("zc", (128, 1))
    asb("ones1b", (1, 128))
    asb("cbc", (128, 18), F32)
    asb("dtbc", (128, 64), F32)
    asb("alogc", (64, 1), F32)
    asb("nega", (64, 1), F32)
    asb("dparc", (128, 32), F32)
    asb("bgc", (128, 8), F32)
    asb("wgc", (128, 8, 8), F32)
    asb("b1cc", (128, NE, 16), F32)
    asb("b2r", (NE, D))
    asb("fcdw", (128, 16, 32))
    asb("cdw", (128, 18, 7, 32))
    asb("drows", (1, nt, 64), F32)
    aring("drb", (2, 1, 64))
    asb("fcd", (128, nt, 32), F32)
    aring("comb", (2, 128, tps, 8), F32)
    asb("combT", (8, 128))
    asb("glt", (128, 8), F32)
    asb("gsc", (128, 12), F32)
    aring("dtr", (2, 128, 64), F32)
    asb("dscr", (128, 64), F32)
    aring("dts", (2, 128, 64), F32)
    aring("dtb16", (2, 128, 64))
    asb("abf", (64, CK), F32)
    aring("alf", (2, 64, CK), F32)
    aring("poshi", (2, 64, CK))
    aring("poslo", (2, 64, CK))
    aring("altm", (2, 128, 64), F32)
    aring("naltm", (2, 128, 64), F32)
    aring("expaltm", (2, 128, 64))
    aring("expdtm", (2, 128, 64))
    asb("expdfm", (64, CK), F32)
    asb("sumc", (64, 1), F32)
    aring("bcfm", (2, 128, 2, 128))
    aring("bct", (2, 128, 256))
    asb("stA", (64, 2048), F32)
    asb("stB", (64, 2048))
    asb("decrep", (64, 2048))
    asb("csbt", (64, 2048))
    aring("xt", (2, 128, D), F32)
    aring("win", (2, 128, 8, 512))
    aring("wA", (2, 128, 8, 1024))
    aring("asm", (2, 128, 18, 134))
    aring("xifm", (2, 128, 16, 128))
    aring("xitm", (2, 128, DI))
    aring("xall", (2, 128, 1024))
    asb("mall", (128, 2048))
    aring("bhat", (2, 128, 1024))
    aring("yt", (2, 128, 512))
    aring("zt", (2, 128, 512))
    asb("xmt", (128, D), F32)
    asb("xnt", (128, D), F32)
    asb("xnf32", (128, 8, 128), F32)
    aring("xnfm", (2, 128, 8, tps * CK))
    asb("acc", (128, tps, D))
    asb("hb", (128, 16, 512))
    aring("expb", (2, 128, 32))
    aring("cbw", (2, 128, 64))
    asb("cbwfm", (64, 128))

    def flat(h):
        ap = h.ap()
        nd = len(ap.shape)
        if nd == 2:
            return ap
        spec = "p " + " ".join("d%d" % k for k in range(nd - 1))
        return ap.rearrange(spec + " -> p (" + spec[2:] + ")")

    hbf = flat(hnd["hb"])
    sb["yfwt"] = AliasRing(hbf[:, 0:2048], hbf[:, 2048:4096])
    sb["ybshs"] = AliasRing(hbf[:, 4096:6144], hbf[:, 6144:8192])
    winf = flat(hnd["win"])
    sb["ybwt"] = AliasRing(winf[:, 0:2048], winf[:, 2048:4096])
    sb["ycmb"] = AliasRing(winf[:, 4096:6144], winf[:, 4096:6144])
    asmf = flat(hnd["asm"])
    sb["xirl"] = AliasRing(asmf[:, 0:2048], asmf[:, 2048:4096])
    xifmf = flat(hnd["xifm"])
    sb["zrl"] = AliasRing(xifmf[:, 0:2048], xifmf[:, 2048:4096])
    sb["ybdt"] = sb["xitm"]
    bhatf = flat(hnd["bhat"])
    sb["actt"] = AliasRing(hnd["mall"].ap(), bhatf)
    xallf = flat(hnd["xall"])
    sb["actf"] = AliasRing(
        xallf.rearrange("p (c n) -> p c n", n=128),
        winf[:, 6144:8192].rearrange("p (c n) -> p c n", n=128))
    sb["outt"] = AliasRing(hnd["xmt"].ap(), hnd["xnt"].ap())
    xnf32f = flat(hnd["xnf32"]).bitcast(BF16)
    sb["ffrl"] = AliasRing(xnf32f[:, 0:1024], xnf32f[:, 1024:2048])
    wAf = flat(hnd["wA"])
    sb["woutv"] = wAf.rearrange("p (c n) -> p c n", n=1024)
    cdwf = flat(hnd["cdw"])
    sb["nwb"] = cdwf[:, 0:2048]

    pbig = nc.alloc_psum_tensor("pbig", [128, 2048], F32)
    pmed = nc.alloc_psum_tensor("pmed", [128, 512], F32)
    pyd = nc.alloc_psum_tensor("pyd", [128, 512], F32)
    pyo = nc.alloc_psum_tensor("pyo", [128, 512], F32)
    ptb = nc.alloc_psum_tensor("ptb", [128, 1024], BF16)

    bank_free = {}

    def bw(key):
        return list(bank_free.get(key, []))

    def bset(key, *toks):
        bank_free[key] = [t for t in toks if t is not None]

    def badd(key, tok):
        bank_free.setdefault(key, []).append(tok)

    S = {}

    def sget(k):
        return S.get(k, ZERO)

    def r3(ap, q):
        return ap.rearrange("p (h q) -> p h q", q=q)

    cv_win = b.dgp(lambda e: e.dma_start(dr["win_b"], dr["winT"]))
    cv_wout = b.dgp(lambda e: e.dma_start(dr["wout_b"], dr["woutT"]))
    cv_w1, cv_w2 = [], []
    for ei in range(NE):
        cv_w1.append(b.dgp(lambda e, ei=ei: e.dma_start(dr["w1_b"][ei], dr["w1T"][ei])))
        cv_w2.append(b.dgp(lambda e, ei=ei: e.dma_start(dr["w2_b"][ei], dr["w2T"][ei])))
    tok_idb = b.dgp(lambda e: e.dma_start(sb["idb"][:], dr["idf"]))
    tok_shm = b.dgp(lambda e: e.dma_start(sb["shm"][:], dr["shm"]))
    tok_b2r = b.dgp(lambda e: e.dma_start(sb["b2r"][:], dr["b2"]))
    tok_fcdw = b.dgp(lambda e: e.dma_start(
        sb["fcdw"][:], dr["fcdT"].rearrange("(c k) n -> k c n", k=128)))
    tok_cdw = b.dgp(lambda e: e.dma_start(sb["cdw"][:], dr["convd"]))
    tok_o1b = b.dgp(lambda e: e.dma_start(sb["ones1b"][:], dr["ones1"]))
    tok_msk = b.dsync(lambda e: e.dma_start(sb["mskz"][:], dr["mskz"]))
    tok_idf = b.dsync(lambda e: e.dma_start(sb["idf"][:], dr["idf"]))
    tok_cbc = b.dsync(lambda e: e.dma_start(sb["cbc"][:], dr["convb"]))
    tok_dtb = b.dsync(lambda e: e.dma_start(sb["dtbc"][:], dr["dtb"]))
    tok_alog = b.dsync(lambda e: e.dma_start(sb["alogc"][:], dr["alog"]))
    tok_dpar = b.dsync(lambda e: e.dma_start(sb["dparc"][:], dr["dpar"]))
    tok_bg = b.dsync(lambda e: e.dma_start(sb["bgc"][:], dr["bg"]))
    tok_wg = b.dsync(lambda e: e.dma_start(
        sb["wgc"][:], dr["wgT"].rearrange("(c k) n -> k c n", k=128)))
    tok_b1c = b.dsync(lambda e: e.dma_start(sb["b1cc"][:], dr["b1c"]))

    t_ne = b.s(lambda e: e.activation(sb["nega"][:], sb["alogc"][:], AF.Exp),
               waits=[tok_alog])
    tok_nega = b.v(lambda e: e.tensor_scalar_mul(sb["nega"][:], sb["nega"][:], -1.0),
                   waits=[t_ne])
    tok_zc = b.v(lambda e: e.memset(sb["zc"][:], 0.0))
    tz = b.v(lambda e: e.memset(sb["outt"][0], 0.0))
    zr1 = b.dsync(lambda e: e.dma_start(
        dr["yfw_sp"][0:1, :], sb["outt"][0, 0:1, :].bitcast(BF16)[:, 0:DI]), waits=[tz])
    zrb = b.v(lambda e: e.engine_nop(), waits=[zr1])
    S[("outt_free", 0)] = zrb
    S["xmt_free"] = zrb
    S["sfw"] = b.v(lambda e: e.memset(sb["stA"][:], 0.0))
    S["sbf"] = b.v(lambda e: e.memset(sb["stB"][:], 0.0))
    S["asm_z0"] = b.v(lambda e: e.memset(sb["asm"][0, :, :, 0:3], 0.0))

    wctr = [0]

    def win_load(g):
        pp = wctr[0] % 2
        wctr[0] += 1
        n = 320 if g == 8 else 512
        tok = b.dsync(lambda e, g=g, pp=pp, n=n: e.dma_start(
            sb["win"][pp, :, :, 0:n],
            dr["win_b"][:, g * 512:g * 512 + n].rearrange("(c k) n -> k c n", k=128)),
            waits=[sget(("win_free", pp)), cv_win])
        return tok, pp

    S["win_next"] = win_load(0)

    def tile_head(i):
        st, it, r = i // tps, i % tps, i % 2
        stb = st % 2
        tok_x = b.dscal(lambda e, i=i, r=r: e.dma_start(
            sb["xt"][r], dr["u"][i * CK:(i + 1) * CK, :]),
            waits=[sget(("xt_free", r))])
        yield 2
        def trunc(tok):
            S[("asm_center", i)] = tok
            S[("dtr", i)] = tok
            S[("comb_tile", i // tps % 2, it)] = tok
            S[("xnfm_tile", i // tps % 2, it)] = tok
        if SCHED == "ln0":
            dmy = b.v(lambda e: e.engine_nop(), waits=[tok_x])
            trunc(dmy)
            return
        w0 = b.v(lambda e, r=r: e.tensor_reduce(sb["gsc"][:, 0:1], sb["xt"][r], AX,
                                                ALU.add),
                 waits=[tok_x, sget("gsc_free")])
        w1_ = b.v(lambda e: e.tensor_scalar_mul(sb["gsc"][:, 1:2], sb["gsc"][:, 0:1],
                                                1.0 / D), waits=[w0])
        w2_ = b.v(lambda e, r=r: e.tensor_scalar(sb["xmt"][:], sb["xt"][r],
                                                 sb["gsc"][:, 1:2], None, ALU.subtract),
                  waits=[w1_, sget("xmt_free")])
        S[("xt_free", r)] = w2_
        if SCHED == "ln1":
            trunc(w2_)
            S["xmt_free"] = w2_
            return
        w3 = b.s(lambda e: e.activation(sb["xnt"][:], sb["xmt"][:], AF.Square,
                                        accum_out=sb["gsc"][:, 2:3]),
                 waits=[w2_, sget("xnt_free")])
        w4a = b.v(lambda e: e.tensor_scalar(sb["gsc"][:, 3:4], sb["gsc"][:, 2:3],
                                            1.0 / D, 1e-5, ALU.mult, ALU.add),
                  waits=[w3])
        w4 = b.s(lambda e: e.activation(sb["gsc"][:, 3:4], sb["gsc"][:, 3:4], AF.Ln),
                 waits=[w4a])
        w5a = b.v(lambda e: e.tensor_scalar_mul(sb["gsc"][:, 4:5], sb["gsc"][:, 3:4],
                                                -0.5), waits=[w4])
        w5 = b.s(lambda e: e.activation(sb["gsc"][:, 4:5], sb["gsc"][:, 4:5], AF.Exp),
                 waits=[w5a])
        if SCHED == "ln2":
            trunc(w5)
            S["xmt_free"] = w5
            S["gsc_free"] = w5
            return
        w6 = b.v(lambda e: e.tensor_scalar_mul(sb["xnt"][:], sb["xmt"][:],
                                               sb["gsc"][:, 4:5]),
                 waits=[w5, sget("xnt_free")])
        S["xmt_free"] = w6
        if SCHED == "ln3":
            trunc(w6)
            S["gsc_free"] = w6
            return
        evb = None
        for kc in range(8):
            bank, pt = (("yd", pyd) if kc % 2 == 0 else ("yo", pyo))
            mmw = [w6, tok_idf] + bw(bank)
            if it == 0 and kc == 0:
                mmw.append(sget(("xnfm_free", stb)))
            if kc < 2:
                mmw.append(sget("xnf32_free"))
            tt = b.t(lambda e, kc=kc, pt=pt: e.transpose(
                pt[:, 0:128], sb["xnt"][:, kc * 128:(kc + 1) * 128], sb["idf"][:]),
                waits=mmw)
            ev1 = b.v(lambda e, kc=kc, it=it, pt=pt, stb=stb: e.tensor_copy(
                sb["xnfm"][stb, :, kc, it * CK:(it + 1) * CK], pt[:, 0:128]),
                waits=[tt])
            ev2 = b.s(lambda e, kc=kc, pt=pt: e.copy(
                sb["xnf32"][:, kc, :], pt[:, 0:128]), waits=[tt, ev1])
            bset(bank, ev1, ev2)
            S[("xnfm_k", kc)] = ev1
            S[("xnf32_k", kc)] = ev2
            evb = ev1
            if kc == 3:
                yield 0
        S["xnt_free"] = evb
        S[("xnfm_tile", stb, it)] = evb
        yield 0
        if SCHED == "ln":
            trunc(evb)
            return
        gm = None
        for kc in range(8):
            gm = b.t(lambda e, kc=kc: e.matmul(
                pmed[:, 128:136], sb["xnf32"][:, kc, :], sb["wgc"][:, kc, :],
                start=(kc == 0), stop=(kc == 7)),
                waits=[tok_wg, S[("xnf32_k", kc)]] + (bw("med") if kc == 0 else []))
        S["xnf32_free"] = gm
        g0 = b.v(lambda e: e.tensor_tensor(sb["glt"][:], pmed[:, 128:136], sb["bgc"][:],
                                           ALU.add),
                 waits=[gm, tok_bg, sget("glt_free")])
        bset("med", g0)
        m1 = b.v(lambda e: e.tensor_reduce(sb["gsc"][:, 5:6], sb["glt"][:], AX, ALU.max),
                 waits=[g0])
        nm1 = b.v(lambda e: e.tensor_reduce(sb["gsc"][:, 6:7], sb["glt"][:], AX, ALU.max,
                                            negate=True), waits=[g0])
        eq = b.v(lambda e: e.tensor_scalar(sb["gsc"][:, 8:12].bitcast(F32) if False
                                           else sb["dscr"][:, 0:8], sb["glt"][:],
                                           sb["gsc"][:, 5:6], None, ALU.is_equal),
                 waits=[m1, sget("dscr_free")])
        mk = b.v(lambda e: e.scalar_tensor_tensor(sb["dscr"][:, 0:8], sb["dscr"][:, 0:8],
                                                  -1e30, sb["glt"][:], ALU.mult, ALU.add),
                 waits=[eq])
        m2 = b.v(lambda e: e.tensor_reduce(sb["gsc"][:, 7:8], sb["dscr"][:, 0:8], AX,
                                           ALU.max), waits=[mk])
        ge = b.v(lambda e: e.tensor_scalar(sb["dscr"][:, 0:8], sb["glt"][:],
                                           sb["gsc"][:, 7:8], None, ALU.is_ge),
                 waits=[m2])
        dm = b.v(lambda e: e.tensor_tensor(sb["gsc"][:, 7:8], sb["gsc"][:, 7:8],
                                           sb["gsc"][:, 6:7], ALU.add),
                 waits=[m2, nm1, ge])
        ew = b.s(lambda e: e.activation(sb["glt"][:], sb["glt"][:], AF.Exp,
                                        bias=sb["gsc"][:, 6:7]), waits=[nm1, ge, mk])
        ed = b.s(lambda e: e.activation(sb["gsc"][:, 7:8], sb["gsc"][:, 7:8], AF.Exp),
                 waits=[dm])
        d1 = b.v(lambda e: e.tensor_scalar_add(sb["gsc"][:, 7:8], sb["gsc"][:, 7:8],
                                               1.0), waits=[ed])
        rc = b.v(lambda e: e.reciprocal(sb["gsc"][:, 7:8], sb["gsc"][:, 7:8]),
                 waits=[d1])
        c1 = b.v(lambda e: e.tensor_tensor(sb["glt"][:], sb["glt"][:], sb["dscr"][:, 0:8],
                                           ALU.mult), waits=[ew])
        c2 = b.v(lambda e, stb=stb, it=it: e.tensor_scalar_mul(
            sb["comb"][stb, :, it, :], sb["glt"][:], sb["gsc"][:, 7:8]),
            waits=[c1, rc, sget(("comb_free", stb))])
        S["glt_free"] = c2
        S["gsc_free"] = c2
        S["dscr_free"] = c2
        S[("comb_tile", stb, it)] = c2
        yield 0
        if SCHED == "lngate":
            S[("asm_center", i)] = c2
            S[("dtr", i)] = c2
            return

        ar = i % 2
        last_ctr = None
        for g in range(9):
            yield 1
            gtok, pp = S["win_next"]
            if g + 1 < 9:
                S["win_next"] = win_load(g + 1)
            elif i + 1 < nt:
                S["win_next"] = win_load(0)
            if g < 4:
                bank = "b%d" % (g % 4)
                mm = None
                for kc in range(8):
                    mm = b.t(lambda e, kc=kc, g=g, pp=pp, it=it: e.matmul(
                        pbig[:, (g % 4) * 512:(g % 4) * 512 + 512],
                        sb["xnfm"][stb, :, kc, it * CK:(it + 1) * CK],
                        sb["win"][pp, :, kc, :],
                        start=(kc == 0), stop=(kc == 7)),
                        waits=[gtok, S[("xnfm_k", kc)]] + (bw(bank) if kc == 0 else []))
                S[("win_free", pp)] = mm
                for hz in range(1):
                    pass
                ev = b.v(lambda e, g=g: e.tensor_copy(
                    sb["zt"][g % 2], pbig[:, (g % 4) * 512:(g % 4) * 512 + 512]),
                    waits=[mm, sget(("zt_free", g % 2))])
                bset(bank, ev)
                sp = b.dsync(lambda e, i=i, g=g: e.dma_start(
                    dr["z_sp"][i * CK:(i + 1) * CK, g * 512:(g + 1) * 512],
                    sb["zt"][g % 2]), waits=[ev])
                S[("zt_free", g % 2)] = sp
            else:
                nsub = 3 if g == 8 else 4
                for sub in range(nsub):
                    ch_abs = (g - 4) * 4 + sub
                    bank = "b%d" % (ch_abs % 4)
                    mm = None
                    if ch_abs < 18:
                        for kc in range(8):
                            mm = b.t(lambda e, kc=kc, pp=pp, sub=sub, it=it, stb=stb,
                                     ch_abs=ch_abs: e.matmul(
                                pbig[:, (ch_abs % 4) * 512:(ch_abs % 4) * 512 + 128],
                                sb["win"][pp, :, kc, sub * 128:sub * 128 + 128],
                                sb["xnfm"][stb, :, kc, it * CK:(it + 1) * CK],
                                start=(kc == 0), stop=(kc == 7)),
                                waits=[gtok, S[("xnfm_k", kc)]]
                                + (bw(bank) if kc == 0 else []))
                        ev = b.v(lambda e, ar=ar, ch_abs=ch_abs: e.tensor_copy(
                            sb["asm"][ar, :, ch_abs, 3:131],
                            pbig[:, (ch_abs % 4) * 512:(ch_abs % 4) * 512 + 128]),
                            waits=[mm, sget(("asm_free", ar))])
                        last_ctr = ev
                    else:
                        for kc in range(8):
                            mm = b.t(lambda e, kc=kc, pp=pp, it=it, stb=stb,
                                     ch_abs=ch_abs: e.matmul(
                                pbig[:, (ch_abs % 4) * 512:(ch_abs % 4) * 512 + 64],
                                sb["xnfm"][stb, :, kc, it * CK:(it + 1) * CK],
                                sb["win"][pp, :, kc, 256:320],
                                start=(kc == 0), stop=(kc == 7)),
                                waits=[gtok, S[("xnfm_k", kc)]]
                                + (bw(bank) if kc == 0 else []))
                        ev = b.v(lambda e, r=r, ch_abs=ch_abs: e.tensor_copy(
                            sb["dtr"][r],
                            pbig[:, (ch_abs % 4) * 512:(ch_abs % 4) * 512 + 64]),
                            waits=[mm, sget(("dtr_free", r))])
                        S[("dtr", i)] = ev
                    S[("win_free", pp)] = mm
                    bset(bank, ev)
        S[("asm_center", i)] = last_ctr
        if i > 0:
            pv = (i - 1) % 2
            e1 = b.v(lambda e, ar=ar, pv=pv: e.tensor_copy(
                sb["asm"][ar, :, :, 0:3], sb["asm"][pv, :, :, 128:131]),
                waits=[last_ctr, S[("asm_center", i - 1)]])
            e2 = b.v(lambda e, ar=ar, pv=pv: e.tensor_copy(
                sb["asm"][pv, :, :, 131:134], sb["asm"][ar, :, :, 3:6]),
                waits=[last_ctr, S[("asm_center", i - 1)]])
            S[("asm_ready", i - 1)] = e2
            S[("asm_left", i)] = e1
        else:
            S[("asm_left", 0)] = S["asm_z0"]

    def conv_ssd(c):
        rr = c % 2
        ar = c % 2
        asm_ok = [S[("asm_ready", c)], S[("asm_left", c)], tok_cdw, tok_cbc]
        ev = None
        for ch in list(range(16, 18)) + list(range(16)):
            bank = "b%d" % (ch % 4)
            mm = None
            for bb in range(4):
                for k in range(7):
                    mm = b.t(lambda e, ch=ch, k=k, bb=bb, ar=ar: e.matmul(
                        pbig[32 * bb:32 * bb + 32,
                             (ch % 4) * 512:(ch % 4) * 512 + 128],
                        sb["cdw"][32 * bb:32 * bb + 32, ch, k, :],
                        sb["asm"][ar, 32 * bb:32 * bb + 32, ch, k:k + 128],
                        start=(k == 0), stop=(k == 6),
                        tile_position=(32 * bb, 32 * bb)),
                        waits=asm_ok + (bw(bank) if (k == 0 and bb == 0) else []))
            if ch < 16:
                ev = b.s(lambda e, ch=ch, rr=rr: e.activation(
                    sb["xifm"][rr, :, ch, :],
                    pbig[:, (ch % 4) * 512:(ch % 4) * 512 + 128], AF.Silu,
                    bias=sb["cbc"][:, ch:ch + 1]),
                    waits=[mm, sget(("xifm_free", rr))])
                S[("xifm_ch", rr, ch)] = ev
            else:
                ev = b.s(lambda e, ch=ch, rr=rr: e.activation(
                    sb["bcfm"][rr, :, ch - 16, :],
                    pbig[:, (ch % 4) * 512:(ch % 4) * 512 + 128], AF.Silu,
                    bias=sb["cbc"][:, ch:ch + 1]),
                    waits=[mm, sget(("bcfm_free", rr))])
                S[("bcfm_ch", rr, ch - 16)] = ev
            bset(bank, ev)
            if ch % 3 == 2:
                yield 0
        S[("asm_free", ar)] = ev

        tts = []
        for j in range(2):
            tts.append(b.t(lambda e, j=j, rr=rr: e.transpose(
                ptb[:, j * 128:(j + 1) * 128], sb["bcfm"][rr, :, j, :], sb["idb"][:]),
                waits=[S[("bcfm_ch", rr, j)], tok_idb] + bw("tb")))
        bc2 = None
        for j in range(2):
            bc2 = b.v(lambda e, j=j, rr=rr: e.tensor_copy(
                sb["bct"][rr, :, j * 128:(j + 1) * 128], ptb[:, j * 128:(j + 1) * 128]),
                waits=[tts[-1], sget(("bct_free", rr))])
        bset("tb", bc2)
        S[("bct", c)] = bc2
        yield 0
        spbc = b.dsync(lambda e, c=c, rr=rr: e.dma_start(
            dr["bc_sp"][c * CK:(c + 1) * CK, :], sb["bct"][rr]), waits=[bc2])

        evs = []
        for wave in range(2):
            tts = []
            for sl in range(8):
                ch = wave * 8 + sl
                tw = [S[("xifm_ch", rr, ch)], tok_idb]
                tw.extend(bw("tb") if ch == 0 else ([evs[-1]] if ch == 8 else []))
                tts.append(b.t(lambda e, ch=ch, sl=sl, rr=rr: e.transpose(
                    ptb[:, sl * 128:(sl + 1) * 128], sb["xifm"][rr, :, ch, :],
                    sb["idb"][:]), waits=tw))
            for sl in range(8):
                ch = wave * 8 + sl
                evx = b.v(lambda e, ch=ch, sl=sl, rr=rr: e.tensor_copy(
                    sb["xitm"][rr, :, ch * 128:(ch + 1) * 128],
                    ptb[:, sl * 128:(sl + 1) * 128]),
                    waits=[tts[-1], sget(("xitm_free", rr))])
                evs.append(evx)
            yield 1
        bset("tb", evs[-1])
        xitm_tok = evs[-1]
        S[("xitm", c)] = xitm_tok
        spxi = b.dsync(lambda e, c=c, rr=rr: e.dma_start(
            dr["xi_sp"][c * CK:(c + 1) * CK, :], sb["xitm"][rr]), waits=[xitm_tok])

        fm = None
        for kc in range(16):
            fm = b.t(lambda e, kc=kc, rr=rr: e.matmul(
                pmed[:, 256:288], sb["xifm"][rr, :, kc, :], sb["fcdw"][:, kc, :],
                start=(kc == 0), stop=(kc == 15)),
                waits=[S[("xifm_ch", rr, kc)], tok_fcdw] + (bw("med") if kc == 0 else []))
        S[("xifm_free", rr)] = fm
        fe = b.v(lambda e, c=c: e.tensor_tensor(
            sb["fcd"][:, c, :], pmed[:, 256:288], sb["dparc"][:], ALU.add),
            waits=[fm, tok_dpar])
        bset("med", fe)
        S[("fcd", c)] = fe
        yield 0

        a0 = b.v(lambda e, c=c: e.tensor_tensor(sb["dscr"][:, 0:64], sb["dtr"][c % 2],
                                                sb["dtbc"][:], ALU.add),
                 waits=[S[("dtr", c)], tok_dtb, sget("dscr_free")])
        a1 = b.s(lambda e, rr=rr: e.activation(sb["dts"][rr], sb["dscr"][:, 0:64],
                                               AF.Abs),
                 waits=[a0, sget(("dts_free", rr))])
        a2 = b.s(lambda e, rr=rr: e.activation(sb["dts"][rr], sb["dts"][rr], AF.Exp,
                                               scale=-1.0), waits=[a1])
        a3 = b.s(lambda e, rr=rr: e.activation(sb["dts"][rr], sb["dts"][rr], AF.Ln,
                                               bias=1.0), waits=[a2])
        a4 = b.v(lambda e, rr=rr: e.scalar_tensor_tensor(
            sb["dts"][rr], sb["dscr"][:, 0:64], 0.0, sb["dts"][rr], ALU.max, ALU.add),
            waits=[a3])
        S[("dtr_free", c % 2)] = a4
        S["dscr_free"] = a4
        a5 = b.v(lambda e, rr=rr: e.tensor_copy(sb["dtb16"][rr], sb["dts"][rr]),
                 waits=[a4, sget(("dtb16_free", rr))])
        yield 0
        td = b.t(lambda e, rr=rr: e.transpose(pyd[0:64, 0:128], sb["dts"][rr],
                                              sb["idf"][:]),
                 waits=[a4, tok_idf] + bw("yd"))
        ab0 = b.v(lambda e: e.tensor_scalar_mul(sb["abf"][:], pyd[0:64, 0:128],
                                                sb["nega"][:, 0:1]),
                  waits=[td, tok_nega, sget("abf_free")])
        bset("yd", ab0)
        sc0 = b.v(lambda e, rr=rr: e.tensor_tensor_scan(
            sb["alf"][rr], sb["abf"][:], sb["abf"][:], 0.0, ALU.add, ALU.bypass),
            waits=[ab0, sget(("alf_free", rr))])
        l0 = b.v(lambda e, rr=rr: e.tensor_scalar(
            sb["expdfm"][32:64, :], sb["alf"][rr][32:64, :],
            sb["alf"][rr][32:64, 127:128], -1.0, ALU.subtract, ALU.mult),
            waits=[sc0, sget("expdfm_free")])
        l1 = b.v(lambda e, rr=rr: e.tensor_tensor(
            sb["alf"][rr][32:64, :], sb["expdfm"][32:64, :], sb["abf"][32:64, :],
            ALU.add), waits=[l0])
        S["abf_free"] = l1
        alf_tok = l1
        p0 = b.v(lambda e, rr=rr: e.tensor_copy(sb["poshi"][rr], sb["alf"][rr]),
                 waits=[alf_tok, sget(("poshi_free", rr))])
        p1 = b.v(lambda e, rr=rr: e.scalar_tensor_tensor(
            sb["poslo"][rr], sb["alf"][rr], 1.0, sb["poshi"][rr], ALU.bypass,
            ALU.subtract), waits=[p0, sget(("poslo_free", rr))])
        ta = b.t(lambda e, rr=rr: e.transpose(pyo[0:128, 0:64], sb["alf"][rr],
                                              sb["idf"][0:64, 0:64]),
                 waits=[alf_tok, tok_idf] + bw("yo"))
        at0 = b.v(lambda e, rr=rr: e.tensor_copy(sb["altm"][rr], pyo[0:128, 0:64]),
                  waits=[ta, sget(("altm_free", rr))])
        bset("yo", at0)
        at1 = b.v(lambda e, rr=rr: e.tensor_scalar_mul(sb["naltm"][rr], sb["altm"][rr],
                                                       -1.0),
                  waits=[at0, sget(("naltm_free", rr))])
        ae = b.s(lambda e, rr=rr: e.activation(sb["expaltm"][rr], sb["altm"][rr],
                                               AF.Exp),
                 waits=[at0, sget(("expaltm_free", rr))])
        spal = b.dsync(lambda e, c=c, rr=rr: e.dma_start(
            dr["expal_sp"][c * CK:(c + 1) * CK, :], sb["expaltm"][rr]), waits=[ae])
        sm0 = b.v(lambda e, rr=rr: e.tensor_copy(sb["sumc"][0:32, :],
                                                 sb["alf"][rr][0:32, 127:128]),
                  waits=[sc0, sget("sumc_free")])
        sm1 = b.v(lambda e, rr=rr: e.tensor_copy(sb["sumc"][32:64, :],
                                                 sb["alf"][rr][32:64, 0:1]),
                  waits=[l1, sm0])
        ed0 = b.s(lambda e, rr=rr: e.activation(
            sb["expdfm"][:], sb["alf"][rr], AF.Exp, scale=-1.0,
            bias=sb["sumc"][:, 0:1]), waits=[l1, sm1])
        te = b.t(lambda e: e.transpose(pyd[0:128, 0:64], sb["expdfm"][:],
                                       sb["idf"][0:64, 0:64]),
                 waits=[ed0] + bw("yd"))
        et0 = b.v(lambda e, rr=rr: e.tensor_copy(sb["expdtm"][rr], pyd[0:128, 0:64]),
                  waits=[te, sget(("expdtm_free", rr))])
        bset("yd", et0)
        S["expdfm_free"] = et0
        tdr = b.t(lambda e: e.transpose(pyo[0:1, 0:64], sb["sumc"][:, 0:1],
                                        sb["idf"][0:64, 0:64]),
                  waits=[sm1] + bw("yo"))
        dr0 = b.s(lambda e, c=c: e.activation(sb["drows"][:, c, :], pyo[0:1, 0:64],
                                              AF.Exp), waits=[tdr])
        bset("yo", dr0)
        dr1 = b.v(lambda e, c=c, rr=rr: e.tensor_copy(sb["drb"][rr],
                                                      sb["drows"][:, c, :]),
                  waits=[dr0, sget(("drb_free", rr))])
        S["sumc_free"] = dr0
        yield 1

        yfree = None
        for d in (0, 1):
            h0 = 32 * d
            gmm = b.t(lambda e, d=d, rr=rr: e.matmul(
                pmed[:, 0:128], sb["bcfm"][rr, 64 * d:64 * d + 64, 0, :],
                sb["bcfm"][rr, 64 * d:64 * d + 64, 1, :],
                start=True, stop=True),
                waits=[S[("bcfm_ch", rr, 0)], S[("bcfm_ch", rr, 1)]] + bw("med"))
            yield 1
            for q in range(4):
                yield 0
                pair = q % 2
                cb = pair * 1024
                bkA, bkB = "b%d" % (2 * pair), "b%d" % (2 * pair + 1)
                stops = []
                for h8 in range(8):
                    h = 8 * q + h8
                    colw = bw(bkA) + bw(bkB) if h8 == 0 else []
                    b.t(lambda e, h=h, h8=h8, rr=rr, h0=h0, cb=cb: e.matmul(
                        pbig[:, cb + h8 * 128:cb + h8 * 128 + 128],
                        sb["idb"][0:64, h0 + h:h0 + h + 1].broadcast_to([64, 128]),
                        sb["poshi"][rr],
                        start=(h8 % 4 == 0), stop=False),
                        waits=[p0, tok_idb] + colw)
                    m2 = b.t(lambda e, h=h, h8=h8, rr=rr, h0=h0, cb=cb: e.matmul(
                        pbig[:, cb + h8 * 128:cb + h8 * 128 + 128],
                        sb["idb"][0:64, h0 + h:h0 + h + 1].broadcast_to([64, 128]),
                        sb["poslo"][rr],
                        start=False, stop=(h8 % 4 == 3)),
                        waits=[p1])
                    if h8 % 4 == 3:
                        stops.append(m2)
                etoks = []
                for h8 in range(8):
                    h = 8 * q + h8
                    ex = b.s(lambda e, h=h, h8=h8, rr=rr, h0=h0, cb=cb, pair=pair:
                             e.activation(
                                 sb["mall"][:, pair * 1024 + h8 * 128:
                                            pair * 1024 + h8 * 128 + 128],
                                 pbig[:, cb + h8 * 128:cb + h8 * 128 + 128], AF.Exp,
                                 bias=sb["naltm"][rr][:, h0 + h:h0 + h + 1]),
                             waits=[stops[h8 // 4], at1, sget(("mall_free", pair))])
                    etoks.append(ex)
                bset(bkA, etoks[3])
                bset(bkB, etoks[7])
                mzt = None
                for mq in range(2):
                    mzt = b.v(lambda e, d=d, mq=mq, pair=pair: e.copy_predicated(
                        sb["mall"][:, pair * 1024 + mq * 512:
                                   pair * 1024 + mq * 512 + 512],
                        sb["mskz"][:, d, :],
                        sb["zc"][:, 0:1].broadcast_to([128, 512])),
                        waits=etoks + [tok_msk, tok_zc])
                mg = b.v(lambda e, pair=pair: e.tensor_tensor(
                    r3(sb["mall"][:, pair * 1024:pair * 1024 + 1024], 128),
                    r3(sb["mall"][:, pair * 1024:pair * 1024 + 1024], 128),
                    pmed[:, 0:128].unsqueeze(1).broadcast_to([128, 8, 128]),
                    ALU.mult),
                    waits=[mzt, gmm])
                badd("med", mg)
                xb = b.v(lambda e, d=d, q=q, rr=rr, h0=h0, pair=pair: e.tensor_tensor(
                    r3(sb["xall"][pair], 64)[:, 0:8, :],
                    r3(sb["xitm"][rr][:, q * 512:(q + 1) * 512], 64),
                    sb["dtb16"][rr][:, h0 + 8 * q:h0 + 8 * q + 8]
                    .unsqueeze(2).broadcast_to([128, 8, 64]), ALU.mult),
                    waits=[S[("xitm", c)], a5, sget(("xall_free", pair))])
                bh = b.v(lambda e, d=d, q=q, rr=rr, h0=h0, pair=pair: e.tensor_tensor(
                    r3(sb["bhat"][pair], 64)[:, 0:8, :],
                    sb["bct"][rr][:, 64 * d:64 * d + 64]
                    .unsqueeze(1).broadcast_to([128, 8, 64]),
                    sb["expdtm"][rr][:, h0 + 8 * q:h0 + 8 * q + 8]
                    .unsqueeze(2).broadcast_to([128, 8, 64]), ALU.mult),
                    waits=[S[("bct", c)], et0, sget(("bhat_free", pair))])
                yield 2
                ydm = None
                for j in range(8):
                    ydm = b.t(lambda e, j=j, pair=pair: e.matmul(
                        pyd[:, j * 64:(j + 1) * 64],
                        sb["mall"][:, pair * 1024 + j * 128:pair * 1024 + j * 128 + 128],
                        sb["xall"][pair][:, j * 64:j * 64 + 64],
                        start=(j == 0), stop=(j == 7)),
                        waits=[mg, xb] + (bw("yd") if j == 0 else []))
                if d == 0:
                    yom = None
                    for j in range(8):
                        hq = 8 * q + j
                        yom = b.t(lambda e, hq=hq, j=j, rr=rr: e.matmul(
                            pyo[:, j * 64:(j + 1) * 64],
                            sb["bcfm"][rr, 0:64, 1, :],
                            sb["stB"][:, hq * 64:hq * 64 + 64],
                            start=(j == 0), stop=(j == 7)),
                            waits=[S[("bcfm_ch", rr, 1)], sget("sbf"), mg]
                            + (bw("yo") if j == 0 else []))
                    ye0 = b.v(lambda e, q=q, rr=rr, h0=h0: e.tensor_tensor(
                        r3(sb["yt"][q % 2], 64), r3(pyo[:, 0:512], 64),
                        sb["expaltm"][rr][:, h0 + 8 * q:h0 + 8 * q + 8]
                        .unsqueeze(2).broadcast_to([128, 8, 64]), ALU.mult),
                        waits=[yom, ae, sget(("yt_free", q % 2))])
                    bset("yo", ye0)
                    ye1 = b.v(lambda e, q=q: e.tensor_tensor(
                        sb["yt"][q % 2], sb["yt"][q % 2], pyd[:, 0:512], ALU.add),
                        waits=[ye0, ydm])
                    bset("yd", ye1)
                    spy = b.dsync(lambda e, c=c, q=q: e.dma_start(
                        dr["yfw_sp"][c * CK + 1:(c + 1) * CK + 1,
                                     q * 512:(q + 1) * 512],
                        sb["yt"][q % 2]), waits=[ye1])
                    S[("yt_free", q % 2)] = spy
                else:
                    ye1 = b.v(lambda e, q=q: e.tensor_copy(
                        sb["yt"][q % 2], pyd[:, 0:512]),
                        waits=[ydm, sget(("yt_free", q % 2))])
                    bset("yd", ye1)
                    spy = b.dsync(lambda e, c=c, q=q: e.dma_start(
                        dr["ybd_sp"][c * CK:(c + 1) * CK, q * 512:(q + 1) * 512],
                        sb["yt"][q % 2]), waits=[ye1])
                    S[("yt_free", q % 2)] = spy
                csm = None
                for j in range(8):
                    csm = b.t(lambda e, j=j, q=q, cb=cb, pair=pair: e.matmul(
                        pbig[0:64, cb + j * 64:cb + j * 64 + 64],
                        sb["bhat"][pair][:, j * 64:j * 64 + 64],
                        sb["xall"][pair][:, j * 64:j * 64 + 64],
                        start=(j == 0), stop=(j == 7)),
                        waits=[bh, xb, mg] + (bw(bkA) if j == 0 else []))
                S[("xall_free", pair)] = csm
                S[("bhat_free", pair)] = csm
                S[("mall_free", pair)] = csm
                if d == 0:
                    j = q
                    pt, bank = ((pyd, "yd") if j % 2 == 0 else (pyo, "yo"))
                    mm = b.t(lambda e, j=j, rr=rr, pt=pt: e.matmul(
                        pt[0:64, 0:512], sb["ones1b"][0:1, 0:64],
                        sb["drb"][rr][0:1, 8 * j:8 * j + 8]
                        .unsqueeze(2).broadcast_to([1, 8, 64]),
                        start=True, stop=True),
                        waits=[dr1, tok_o1b] + bw(bank))
                    evd = b.v(lambda e, j=j, pt=pt: e.tensor_copy(
                        sb["decrep"][:, j * 512:(j + 1) * 512], pt[0:64, 0:512]),
                        waits=[mm, sget("decrep_free")])
                    bset(bank, evd)
                    s_0 = b.v(lambda e, q=q: e.tensor_tensor(
                        sb["stA"][:, q * 512:(q + 1) * 512],
                        sb["stA"][:, q * 512:(q + 1) * 512],
                        sb["decrep"][:, q * 512:(q + 1) * 512], ALU.mult),
                        waits=[evd, sget("sfw"), yom])
                    s_1 = b.v(lambda e, q=q, cb=cb: e.tensor_tensor(
                        sb["stA"][:, q * 512:(q + 1) * 512],
                        sb["stA"][:, q * 512:(q + 1) * 512],
                        pbig[0:64, cb:cb + 512], ALU.add),
                        waits=[s_0, csm])
                    bset(bkA, s_1)
                    S[("sfw_h", q)] = s_1
                    s_2 = b.v(lambda e, q=q: e.tensor_copy(
                        sb["stB"][:, q * 512:(q + 1) * 512],
                        sb["stA"][:, q * 512:(q + 1) * 512]), waits=[s_1])
                    S[("sbf_h", q)] = s_2
                else:
                    cb0 = b.v(lambda e, q=q, cb=cb: e.tensor_copy(
                        sb["csbt"][:, q * 512:(q + 1) * 512],
                        pbig[0:64, cb:cb + 512]),
                        waits=[csm, sget("csbt_free")])
                    bset(bkA, cb0)
                    S[("csbt_h", q)] = cb0
            if d == 0:
                sfb = b.v(lambda e: e.engine_nop(),
                          waits=[S[("sfw_h", q)] for q in range(4)])
                S["sfw"] = sfb
                sbb = b.v(lambda e: e.engine_nop(),
                          waits=[S[("sbf_h", q)] for q in range(4)])
                S["sbf"] = sbb
                S["decrep_free"] = sfb
            else:
                spcs = b.dsync(lambda e, c=c: e.dma_start(dr["csbw_sp"][c],
                                                          sb["csbt"][:]),
                               waits=[S[("csbt_h", q)] for q in range(4)])
                S["csbt_free"] = spcs
                S[("csbw", c)] = spcs
                yfree = spcs
        S[("dts_free", rr)] = alf_tok
        S[("dtb16_free", rr)] = yfree
        S[("alf_free", rr)] = ed0
        S[("poshi_free", rr)] = yfree
        S[("poslo_free", rr)] = yfree
        S[("altm_free", rr)] = yfree
        S[("naltm_free", rr)] = yfree
        S[("expaltm_free", rr)] = yfree
        S[("expdtm_free", rr)] = yfree
        S[("drb_free", rr)] = yfree
        S[("bct_free", rr)] = yfree
        S[("xitm_free", rr)] = yfree
        S[("bcfm_free", rr)] = yfree

    def ffn(st):
        stb = st % 2
        TT = tps * CK
        xn_ready = [S[("xnfm_tile", stb, it)] for it in range(tps)]
        be = None
        for it in range(tps):
            tc = b.t(lambda e, stb=stb, it=it: e.transpose(
                pyd[0:8, 0:128], sb["comb"][stb, :, it, :], sb["idf"][:]),
                waits=[S[("comb_tile", stb, it)]] + bw("yd"))
            ce = b.v(lambda e: e.tensor_copy(sb["combT"][:], pyd[0:8, 0:128]),
                     waits=[tc, sget("combT_free")])
            bset("yd", ce)
            for nn2 in range(2):
                bm = b.t(lambda e, nn2=nn2: e.matmul(
                    pyo[:, 0:512], sb["combT"][:], sb["b2r"][:, nn2 * 512:(nn2 + 1) * 512],
                    start=True, stop=True),
                    waits=[ce, tok_b2r] + bw("yo"))
                be = b.v(lambda e, it=it, nn2=nn2: e.tensor_copy(
                    sb["acc"][:, it, nn2 * 512:(nn2 + 1) * 512], pyo[:, 0:512]),
                    waits=[bm, sget("acc_free")])
                bset("yo", be)
            S["combT_free"] = be
            yield
        acc_init = be
        prev_y = acc_init
        for ei in range(NE):
            for hf in range(2):
                pp = hf
                wtok = b.dgp(lambda e, ei=ei, hf=hf, pp=pp: e.dma_start(
                    sb["wA"][pp],
                    dr["w1_b"][ei][:, hf * 1024:(hf + 1) * 1024]
                    .rearrange("(c k) n -> k c n", k=128)),
                    waits=[sget(("wA_free", pp)), cv_w1[ei]])
                for hcl in range(8):
                    hc = 8 * hf + hcl
                    bank = "b%d" % (hcl % 4)
                    mm = None
                    for kc in range(8):
                        mm = b.t(lambda e, kc=kc, pp=pp, hcl=hcl, TT=TT: e.matmul(
                            pbig[:, (hcl % 4) * 512:(hcl % 4) * 512 + TT],
                            sb["wA"][pp, :, kc, hcl * 128:(hcl + 1) * 128],
                            sb["xnfm"][stb, :, kc, 0:TT],
                            start=(kc == 0), stop=(kc == 7)),
                            waits=[wtok] + xn_ready + (bw(bank) if kc == 0 else []))
                    hv = b.s(lambda e, hc=hc, ei=ei, hcl=hcl, TT=TT: e.activation(
                        sb["hb"][:, hc, 0:TT],
                        pbig[:, (hcl % 4) * 512:(hcl % 4) * 512 + TT], AF.Relu,
                        bias=sb["b1cc"][:, ei, hc:hc + 1]),
                        waits=[mm, tok_b1c, prev_y])
                    bset(bank, hv)
                    S[("hb_hc", hc)] = hv
                    if hcl % 2 == 1:
                        yield
                S[("wA_free", pp)] = mm
            ydone = None
            for chh in range(2):
                pp = chh
                w2tok = b.dgp(lambda e, ei=ei, chh=chh, pp=pp: e.dma_start(
                    sb["wA"][pp].rearrange("p a n -> p (a n)")
                    .rearrange("p (c n) -> p c n", n=512),
                    dr["w2_b"][ei][:, chh * 512:(chh + 1) * 512]
                    .rearrange("(c k) n -> k c n", k=128)),
                    waits=[sget(("wA_free", pp)), cv_w2[ei]])
                for tb in range(tps):
                    pt, bank = ((pyd, "yd") if tb % 2 == 0 else (pyo, "yo"))
                    mm = None
                    for kc in range(16):
                        mm = b.t(lambda e, kc=kc, pp=pp, tb=tb, pt=pt: e.matmul(
                            pt[:, 0:512],
                            sb["hb"][:, kc, tb * 128:(tb + 1) * 128],
                            sb["wA"][pp].rearrange("p a n -> p (a n)")
                            .rearrange("p (c n) -> p c n", n=512)[:, kc, :],
                            start=(kc == 0), stop=(kc == 15)),
                            waits=[w2tok, S[("hb_hc", kc)]] + (bw(bank) if kc == 0 else []))
                    ac = b.v(lambda e, tb=tb, chh=chh, stb=stb, ei=ei, pt=pt:
                             e.scalar_tensor_tensor(
                                 sb["acc"][:, tb, chh * 512:(chh + 1) * 512],
                                 pt[:, 0:512],
                                 sb["comb"][stb, :, tb, ei:ei + 1],
                                 sb["acc"][:, tb, chh * 512:(chh + 1) * 512],
                                 ALU.mult, ALU.add),
                             waits=[mm, S[("comb_tile", stb, tb)], acc_init])
                    bset(bank, ac)
                    ydone = ac
                    yield
                S[("wA_free", pp)] = mm
            prev_y = ydone
        spa = b.dsync(lambda e, st=st, TT=TT: e.dma_start(
            dr["ffn_sp"][st * TT:(st + 1) * TT, :].rearrange("(t p) c -> p t c", p=128),
            sb["acc"][:, 0:tps, :]), waits=[prev_y])
        S["acc_free"] = spa
        S[("comb_free", stb)] = prev_y
        S[("xnfm_free", stb)] = prev_y

    def tile_b(c):
        rb = c % 2
        if c < nt - 1:
            d0 = b.v(lambda e, c=c, rb=rb: e.tensor_copy(
                sb["drb"][rb], sb["drows"][:, c + 1, :]),
                waits=[sget(("drb_free", rb))])
            S[("drb_free", rb)] = d0
            dts_ = []
            for j in range(4):
                pt, bank = ((pyd, "yd") if j % 2 == 0 else (pyo, "yo"))
                mm = b.t(lambda e, j=j, rb=rb, pt=pt: e.matmul(
                    pt[0:64, 0:512], sb["ones1b"][0:1, 0:64],
                    sb["drb"][rb][0:1, 32 + 8 * j:32 + 8 * j + 8]
                    .unsqueeze(2).broadcast_to([1, 8, 64]),
                    start=True, stop=True),
                    waits=[d0, tok_o1b] + bw(bank))
                evd = b.s(lambda e, j=j, pt=pt: e.copy(
                    sb["decrep"][:, j * 512:(j + 1) * 512], pt[0:64, 0:512]),
                    waits=[mm, sget("decrep_a"), sget("decrep_b")])
                bset(bank, evd)
                dts_.append(evd)
            rl = b.dsync(lambda e, c=c: e.dma_start(
                sb["csbt"][:], dr["csbw_sp"][c + 1]),
                waits=[sget("csrl_a"), sget("csrl_b"), S[("csbw", c + 1)]])
            s_0a = b.v(lambda e: e.tensor_tensor(
                sb["stA"][:, 0:CS], sb["stA"][:, 0:CS],
                sb["decrep"][:, 0:CS], ALU.mult),
                waits=dts_[0:3] + [sget("sbw_a")])
            s_0b = b.g(lambda e: e.tensor_tensor(
                sb["stA"][:, CS:2048], sb["stA"][:, CS:2048],
                sb["decrep"][:, CS:2048], ALU.mult),
                waits=dts_[2:4] + [sget("sbw_b")])
            S["decrep_a"] = s_0a
            S["decrep_b"] = s_0b
            s_1a = b.v(lambda e: e.tensor_tensor(
                sb["stA"][:, 0:CS], sb["stA"][:, 0:CS], sb["csbt"][:, 0:CS],
                ALU.add), waits=[s_0a, rl])
            s_1b = b.g(lambda e: e.tensor_tensor(
                sb["stA"][:, CS:2048], sb["stA"][:, CS:2048],
                sb["csbt"][:, CS:2048], ALU.add), waits=[s_0b, rl])
            S["csrl_a"] = s_1a
            S["csrl_b"] = s_1b
            S["sbw_a"] = s_1a
            S["sbw_b"] = s_1b
            s_2 = b.v(lambda e: e.tensor_copy(sb["stB"][:], sb["stA"][:]),
                      waits=[s_1a, s_1b, sget("sbwbf_rd")])
            S["sbwbf"] = s_2
            S["sbwbf_g"] = s_2
        rl_exp = b.dsync(lambda e, c=c, rb=rb: e.dma_start(
            sb["expb"][rb], dr["expal_sp"][c * CK:(c + 1) * CK, 32:64]),
            waits=[sget(("expb_free", rb))])
        rl_cbw = b.dsync(lambda e, c=c, rb=rb: e.dma_start(
            sb["cbw"][rb], dr["bc_sp"][c * CK:(c + 1) * CK, 192:256]),
            waits=[sget(("cbw_free", rb))])
        rl_ybd = b.dsync(lambda e, c=c, rb=rb: e.dma_start(
            sb["ybdt"][rb], dr["ybd_sp"][c * CK:(c + 1) * CK, :]),
            waits=[sget(("ybdt_free", rb))])
        tcb = b.t(lambda e, rb=rb: e.transpose(ptb[0:64, 0:128], sb["cbw"][rb],
                                               sb["idb"][:]),
                  waits=[rl_cbw, sget("cbwfm_free")] + bw("tb"))
        ecb = b.v(lambda e: e.tensor_copy(sb["cbwfm"][:], ptb[0:64, 0:128]),
                  waits=[tcb])
        bset("tb", ecb)
        S[("cbw_free", rb)] = ecb
        ye1 = None
        for q in range(4):
            yom = None
            for j in range(8):
                hq = 8 * q + j
                yom = b.t(lambda e, hq=hq, j=j: e.matmul(
                    pyo[:, j * 64:(j + 1) * 64], sb["cbwfm"][:],
                    sb["stB"][:, hq * 64:hq * 64 + 64],
                    start=(j == 0), stop=(j == 7)),
                    waits=[ecb, sget("sbwbf"), sget("sbwbf_g")]
                    + (bw("yo") if j == 0 else []))
            ye0 = b.v(lambda e, q=q, rb=rb: e.tensor_tensor(
                r3(sb["ybwt"][rb][:, q * 512:(q + 1) * 512], 64),
                r3(pyo[:, 0:512], 64),
                sb["expb"][rb][:, 8 * q:8 * q + 8]
                .unsqueeze(2).broadcast_to([128, 8, 64]), ALU.mult),
                waits=[yom, rl_exp, sget(("ybwt_free", rb))])
            bset("yo", ye0)
            ye1 = b.v(lambda e, q=q, rb=rb: e.tensor_tensor(
                sb["ybwt"][rb][:, q * 512:(q + 1) * 512],
                sb["ybwt"][rb][:, q * 512:(q + 1) * 512],
                sb["ybdt"][rb][:, q * 512:(q + 1) * 512], ALU.add),
                waits=[ye0, rl_ybd])
        S[("ybdt_free", rb)] = ye1
        S["cbwfm_free"] = yom
        S["sbwbf_rd"] = yom
        S[("ybwt_ye", c)] = ye1
        smms = []
        for nb in range(4):
            bank = "b%d" % nb
            sm = b.t(lambda e, nb=nb, rb=rb, c=c: e.matmul(
                pbig[:, nb * 512:(nb + 1) * 512], sb["shm"][:, 0:128],
                sb["ybwt"][rb][:, nb * 512:(nb + 1) * 512],
                start=True, stop=(c == nt - 1)),
                waits=[ye1, tok_shm] + bw(bank))
            if c < nt - 1:
                sm = b.t(lambda e, nb=nb, rb=rb: e.matmul(
                    pbig[:, nb * 512:(nb + 1) * 512], sb["shm"][:, 128:256],
                    sb["ybwt"][rb ^ 1][:, nb * 512:(nb + 1) * 512],
                    start=False, stop=True),
                    waits=[S[("ybwt_ye", c + 1)]])
            smms.append(sm)
        if c < nt - 1:
            S[("ybwt_free", rb ^ 1)] = smms[-1]
        ybse = []
        for nb in range(4):
            ev = b.s(lambda e, nb=nb, rb=rb: e.copy(
                sb["ybshs"][rb][:, nb * 512:(nb + 1) * 512],
                pbig[:, nb * 512:(nb + 1) * 512]),
                waits=[smms[nb], sget(("ybshs_f", rb))])
            bset("b%d" % nb, ev)
            ybse.append(ev)
        S[("ybse", c)] = ybse

    def tile_c(c):
        rb = c % 2
        ybse = S[("ybse", c)]
        rl_yfw = b.dsync(lambda e, c=c, rb=rb: e.dma_start(
            sb["yfwt"][rb], dr["yfw_sp"][c * CK:(c + 1) * CK, :]),
            waits=[sget(("yfwt_free", rb)), sget(("yfwt_free2", rb))])
        rl_xi = b.dsync(lambda e, c=c, rb=rb: e.dma_start(
            sb["xirl"][rb], dr["xi_sp"][c * CK:(c + 1) * CK, :]),
            waits=[sget(("xirl_free", rb))])
        rl_z = b.dsync(lambda e, c=c, rb=rb: e.dma_start(
            sb["zrl"][rb], dr["z_sp"][c * CK:(c + 1) * CK, :]),
            waits=[sget(("zrl_free", rb)), sget(("zrl_free2", rb))])
        rl_ff = b.dsync(lambda e, c=c, rb=rb: e.dma_start(
            sb["ffrl"][rb], dr["ffn_sp"][c * CK:(c + 1) * CK, :]),
            waits=[sget(("ffrl_free", rb)), sget(("ffrl_free2", rb)),
                   sget("acc_free")])
        rl_x = b.dscal(lambda e, c=c, rb=rb: e.dma_start(
            sb["xt"][rb], dr["u"][c * CK:(c + 1) * CK, :]),
            waits=[sget(("xt_free", rb))])
        y0w = [rl_xi, S[("fcd", c)], sget(("ycmb_free", 0)),
               sget(("ycmb_free", 1))]
        y0a = b.v(lambda e, c=c, rb=rb: e.tensor_tensor(
            r3(sb["ycmb"][rb][:, 0:CS], 64), r3(sb["xirl"][rb][:, 0:CS], 64),
            sb["fcd"][:, c, 0:CS // 64].unsqueeze(2)
            .broadcast_to([128, CS // 64, 64]), ALU.mult), waits=y0w)
        y0b = b.g(lambda e, c=c, rb=rb: e.tensor_tensor(
            r3(sb["ycmb"][rb][:, CS:2048], 64), r3(sb["xirl"][rb][:, CS:2048], 64),
            sb["fcd"][:, c, CS // 64:32].unsqueeze(2)
            .broadcast_to([128, 32 - CS // 64, 64]), ALU.mult), waits=y0w)
        y1 = b.v(lambda e, rb=rb: e.tensor_tensor(
            sb["ycmb"][rb], sb["ycmb"][rb], sb["yfwt"][rb], ALU.add),
            waits=[y0a, y0b, rl_yfw])
        y2a = b.v(lambda e, rb=rb: e.tensor_tensor(
            sb["ycmb"][rb], sb["ycmb"][rb], sb["ybshs"][rb], ALU.add),
            waits=[y1] + ybse)
        y2b = y2a
        S[("ybshs_f", rb)] = y2a
        S[("yfwt_free", rb)] = y1
        S[("yfwt_free2", rb)] = y1
        sz = b.s(lambda e, rb=rb: e.activation(sb["zrl"][rb], sb["zrl"][rb], AF.Silu),
                 waits=[rl_z])
        ssq = b.s(lambda e, rb=rb: e.activation(sb["actt"][rb], sb["ycmb"][rb],
                                                AF.Square, accum_out=sb["gsc"][:, 0:1]),
                  waits=[y2a, y2b, sget("gsc_free"), sget(("actt_free", rb))])
        r0a = b.v(lambda e: e.tensor_scalar(sb["gsc"][:, 1:2], sb["gsc"][:, 0:1],
                                            1.0 / DI, 1e-5, ALU.mult, ALU.add),
                  waits=[ssq])
        r0 = b.s(lambda e: e.activation(sb["gsc"][:, 1:2], sb["gsc"][:, 1:2], AF.Ln),
                 waits=[r0a])
        r1a = b.v(lambda e: e.tensor_scalar_mul(sb["gsc"][:, 2:3], sb["gsc"][:, 1:2],
                                                -0.5), waits=[r0])
        r1 = b.s(lambda e: e.activation(sb["gsc"][:, 2:3], sb["gsc"][:, 2:3], AF.Exp),
                 waits=[r1a])
        v0 = b.v(lambda e, rb=rb: e.tensor_scalar_mul(sb["actt"][rb], sb["ycmb"][rb],
                                                      sb["gsc"][:, 2:3]),
                 waits=[r1, sget(("actt_free", rb))])
        S[("ycmb_free", 0)] = v0
        S[("ycmb_free", 1)] = v0
        S["gsc_free"] = v0
        v1 = b.v(lambda e, rb=rb: e.tensor_tensor(
            sb["actt"][rb], sb["actt"][rb], sb["nwb"][:], ALU.mult),
            waits=[v0, tok_nwb])
        v2a = b.v(lambda e, rb=rb: e.tensor_tensor(
            sb["actt"][rb], sb["actt"][rb], sb["zrl"][rb], ALU.mult),
            waits=[v1, sz])
        v2b = v2a
        S[("zrl_free", rb)] = v2a
        S[("zrl_free2", rb)] = v2a
        aevs = []
        for wave in range(2):
            tts2 = []
            for sl in range(8):
                chk = wave * 8 + sl
                tw = [v2a if chk < CS // 128 else v2b, tok_idb]
                tw.extend(bw("tb") if chk == 0 else ([aevs[-1]] if chk == 8 else []))
                tts2.append(b.t(lambda e, chk=chk, sl=sl, rb=rb: e.transpose(
                    ptb[:, sl * 128:(sl + 1) * 128],
                    sb["actt"][rb][:, chk * 128:(chk + 1) * 128], sb["idb"][:]),
                    waits=tw))
            for sl in range(8):
                chk = wave * 8 + sl
                mk = b.v if wave == 0 else b.s
                fn = ((lambda e, chk=chk, sl=sl, rb=rb: e.tensor_copy(
                    sb["actf"][rb][:, chk, :], ptb[:, sl * 128:(sl + 1) * 128]))
                    if wave == 0 else
                    (lambda e, chk=chk, sl=sl, rb=rb: e.copy(
                        sb["actf"][rb][:, chk, :], ptb[:, sl * 128:(sl + 1) * 128])))
                evx = mk(fn, waits=[tts2[-1], sget(("actf_free", rb))])
                aevs.append(evx)
        bset("tb", aevs[-1])
        S[("actt_free", rb)] = aevs[-1]
        o1s = []
        mm = None
        for chh in range(2):
            pt, bank = ((pyd, "yd") if chh == 0 else (pyo, "yo"))
            mm = None
            for kc in range(16):
                mm = b.t(lambda e, kc=kc, chh=chh, rb=rb, pt=pt: e.matmul(
                    pt[:, 0:512], sb["actf"][rb][:, kc, :],
                    sb["woutv"][:, kc, chh * 512:(chh + 1) * 512],
                    start=(kc == 0), stop=(kc == 15)),
                    waits=[aevs[kc], sget("wout_tok")] + (bw(bank) if kc == 0 else []))
            o0 = b.v(lambda e, chh=chh, rb=rb, pt=pt: e.scalar_tensor_tensor(
                sb["outt"][rb][:, chh * 512:(chh + 1) * 512], pt[:, 0:512], 1.0,
                sb["xt"][rb][:, chh * 512:(chh + 1) * 512], ALU.mult, ALU.add),
                waits=[mm, rl_x, sget(("outt_free", rb))])
            bset(bank, o0)
            mk1 = b.v if chh == 0 else b.g
            o1 = mk1(lambda e, chh=chh, rb=rb: e.tensor_tensor(
                sb["outt"][rb][:, chh * 512:(chh + 1) * 512],
                sb["outt"][rb][:, chh * 512:(chh + 1) * 512],
                sb["ffrl"][rb][:, chh * 512:(chh + 1) * 512], ALU.add),
                waits=[o0, rl_ff])
            o1s.append(o1)
        S[("actf_free", rb)] = mm
        S[("ffrl_free", rb)] = o1s[0]
        S[("ffrl_free2", rb)] = o1s[1]
        S[("xt_free", rb)] = o0
        st_out = b.dscal(lambda e, c=c, rb=rb: e.dma_start(
            out[c * CK:(c + 1) * CK, :], sb["outt"][rb]), waits=o1s)
        S[("outt_free", rb)] = st_out

    def drain(gen):
        if gen is not None:
            for _ in gen:
                pass

    def pump(gen, n=1):
        if gen is None:
            return None
        for _ in range(n):
            try:
                next(gen)
            except StopIteration:
                return None
        return gen

    if SCHED != "prepass":
        fg = None
        for i in range(nt):
            st = i // tps
            if i % tps == 0 and st > 0 and SCHED in ("ffn", "full"):
                drain(fg)
                fg = ffn(st - 1)
                fg = pump(fg)
            for h_ in tile_head(i):
                fg = pump(fg, h_)
            if i > 0 and SCHED in ("conv", "ffn", "full"):
                for h_ in conv_ssd(i - 1):
                    fg = pump(fg, h_)
        drain(fg)
        if SCHED in ("conv", "ffn", "full"):
            e2 = b.v(lambda e: e.memset(sb["asm"][(nt - 1) % 2, :, :, 131:134], 0.0),
                     waits=[S[("asm_center", nt - 1)]])
            S[("asm_ready", nt - 1)] = e2
            drain(conv_ssd(nt - 1))
        if SCHED in ("ffn", "full"):
            drain(ffn(nst - 1))
    if SCHED != "full":
        final_waits = []
        for ring in ("dsy", "dsc", "dgp"):
            for ln in range(NLANES):
                sem = "%s%d" % (ring, ln)
                if b.cnt[sem]:
                    final_waits.append((sem, b.cnt[sem]))
        b.op("sync", lambda e: e.nop(), waits=final_waits, inc=None)
        import contextlib
        with contextlib.ExitStack() as stk:
            for name in list(b.cnt):
                b.handles[name] = stk.enter_context(nc.semaphore(name))
            blk = stk.enter_context(nc.Block())
            b.replay_into(blk)
        return nc

    aw = [S[("xnfm_free", 0)], sget(("xnfm_free", 1)), S["acc_free"], S["xmt_free"],
          S["xnt_free"], S["xnf32_free"], S["gsc_free"]]
    aw.append(sget("csbt_free"))
    for r_ in (0, 1):
        for kk in ("xitm_free", "asm_free", "win_free", "bhat_free",
                   "xifm_free", "xall_free", "yt_free", "zt_free", "mall_free"):
            aw.append(sget((kk, r_)))
    bar = b.v(lambda e: e.engine_nop(), waits=aw)
    for r_ in (0, 1):
        for kk in ("yfwt_free", "yfwt_free2", "ybwt_free", "zrl_free",
                   "zrl_free2", "ffrl_free2",
                   "ybdt_free", "xirl_free", "ycmb_free", "actt_free",
                   "actf_free", "outt_free", "ffrl_free", "expb_free",
                   "cbw_free", "xt_free"):
            S[(kk, r_)] = bar
    S["gsc_free"] = bar
    S["csrl_free"] = bar
    S["wout_tok"] = b.dgp(lambda e: e.dma_start(
        sb["woutv"], dr["wout_b"].rearrange("(c k) n -> k c n", k=128)),
        waits=[bar, cv_wout])
    tok_nwb = b.dgp(lambda e: e.dma_start(sb["nwb"], dr["nw"]), waits=[bar])
    msA = b.v(lambda e: e.memset(sb["stA"][:], 0.0),
              waits=[bar, sget("sfw"), sget("sbf")])
    S["sbw_a"] = msA
    S["sbw_b"] = msA
    msB = b.v(lambda e: e.memset(sb["stB"][:], 0.0), waits=[bar, msA])
    S["sbwbf"] = msB
    S["sbwbf_g"] = msB
    S[("ybshs_f", 0)] = bar
    S[("ybshs_f", 1)] = bar
    S["csrl_a"] = bar
    S["csrl_b"] = bar
    S["decrep_a"] = sget("decrep_free")
    S["decrep_b"] = bar
    for c in range(nt - 1, -1, -1):
        tile_b(c)
        tile_c(c)
    final_waits = []
    for ring in ("dsy", "dsc", "dgp"):
        for ln in range(NLANES):
            sem = "%s%d" % (ring, ln)
            if b.cnt[sem]:
                final_waits.append((sem, b.cnt[sem]))
    b.op("sync", lambda e: e.nop(), waits=final_waits, inc=None)

    import contextlib
    with contextlib.ExitStack() as stk:
        for name in list(b.cnt):
            b.handles[name] = stk.enter_context(nc.semaphore(name))
        blk = stk.enter_context(nc.Block())
        b.replay_into(blk)
    return nc


def kernel(**inputs):
    A = host_prep(inputs)
    u = np.ascontiguousarray(inputs["L0_attn_u"], dtype=np.float32)
    nt = NT_OVERRIDE if NT_OVERRIDE else L // CK
    nc = build(nt)
    in_maps = []
    for c in range(8):
        m = {"u": np.ascontiguousarray(u[c][: nt * CK])}
        m.update(A)
        in_maps.append(m)
    res = run_bass_kernel_spmd(nc, in_maps, core_ids=list(range(8)))
    return np.stack([r["out"] for r in res.results])


# version 14
# speedup vs baseline: 1.4260x; 1.0062x over previous
import numpy as np
import ml_dtypes
import concourse.bass as bass
import concourse.mybir as mybir
from concourse.bass_utils import run_bass_kernel_spmd

F32 = mybir.dt.float32
BF16 = mybir.dt.bfloat16
U8 = mybir.dt.uint8
AF = mybir.ActivationFunctionType
ALU = mybir.AluOpType
AX = mybir.AxisListType.X

L, D, DI, DIP = 4096, 1024, 2048, 4416
NE, CK, TPS = 8, 128, 4
ENGS = ("tensor", "vector", "scalar", "sync", "gpsimd")
ZERO = ("pe", 0)
NT_OVERRIDE = 0
SCHED = "full"


NLANES = 8


class Bld:
    def __init__(self, nc):
        self.nc = nc
        self.ops = []
        self.cnt = {s: 0 for s in ("pe", "v", "s")}
        self.lane_ord = {}
        self.ring_ctr = {"dsy": 0, "dsc": 0, "dgp": 0}
        for ring in ("dsy", "dsc", "dgp"):
            for ln in range(NLANES):
                self.cnt["%s%d" % (ring, ln)] = 0
        self.handles = {}

    def dma(self, ring, fn, waits=()):
        ln = self.ring_ctr[ring] % NLANES
        self.ring_ctr[ring] += 1
        sem = "%s%d" % (ring, ln)
        prev = self.cnt[sem]
        w = list(waits) + ([(sem, prev)] if prev else [])
        return self.op("sync" if ring == "dsy" else
                       ("scalar" if ring == "dsc" else "gpsimd"),
                       fn, w, sem, 16)

    def op(self, eng, fn, waits=(), inc=None, incn=1):
        w = [(s, v) for (s, v) in waits if s is not None and v > 0]
        tok = None
        if inc is not None:
            self.cnt[inc] += incn
            tok = (inc, self.cnt[inc])
        self.ops.append((eng, fn, w, inc, incn))
        return tok

    def t(self, fn, waits=()):
        return self.op("tensor", fn, waits, "pe")

    def v(self, fn, waits=()):
        return self.op("vector", fn, waits, "v")

    def s(self, fn, waits=()):
        return self.op("scalar", fn, waits, "s")

    def dsync(self, fn, waits=()):
        return self.dma("dsy", fn, waits)

    def dscal(self, fn, waits=()):
        return self.dma("dsc", fn, waits)

    def dgp(self, fn, waits=()):
        return self.dma("dgp", fn, waits)

    def replay_into(self, blk):
        for name in ENGS:
            getattr(blk, name)(self._mk(name))

    def _mk(self, engname):
        def run(eng):
            waited = {}
            for e, fn, waits, inc, incn in self.ops:
                if e != engname:
                    continue
                for s, v in waits:
                    if waited.get(s, 0) < v:
                        eng.wait_ge(self.handles[s], v)
                        waited[s] = v
                ins = fn(eng)
                if inc is not None:
                    ins.then_inc(self.handles[inc], incn)
        return run


class Ring:

    def __init__(self, handle):
        self.h = handle

    def __getitem__(self, key):
        ap = self.h.ap()
        if isinstance(key, tuple):
            key = (key[1], key[0]) + tuple(key[2:])
            return ap[key]
        return ap[:, key]


class AliasRing:

    def __init__(self, *aps):
        self.aps = list(aps)

    def __getitem__(self, key):
        if isinstance(key, tuple):
            return self.aps[key[0]][tuple(key[1:])]
        return self.aps[key]


BCPERM = np.concatenate([np.arange(2048),
                         2048 + np.arange(64), 2176 + np.arange(64),
                         2112 + np.arange(64), 2240 + np.arange(64)])


def host_prep(inputs):
    A = {}
    wt = inputs["L0_attn_in_proj_w"].T
    colperm = np.concatenate([np.arange(2048), 2048 + BCPERM, np.arange(4352, 4416)])
    A["winT"] = np.ascontiguousarray(wt[:, colperm])
    cw = inputs["L0_attn_conv_w"][:, 0, :][BCPERM]
    cd = np.zeros((128, 18, 7, 32), np.float32)
    cwr = cw.reshape(18, 4, 32, 7)
    for bb in range(4):
        for j in range(32):
            cd[32 * bb + j, :, :, j] = cwr[:, bb, j, :]
    A["convd"] = cd
    A["convb"] = np.ascontiguousarray(
        inputs["L0_attn_conv_b"][BCPERM].reshape(18, 128).T)
    A["dtb"] = np.ascontiguousarray(
        np.broadcast_to(np.concatenate([inputs["L0_attn_dt_bias"]] * 2), (128, 64)))
    A["alog"] = np.ascontiguousarray(
        np.concatenate([inputs["L0_attn_A_log"]] * 2)[:, None])
    A["dpar"] = np.ascontiguousarray(np.broadcast_to(inputs["L0_attn_D_param"], (128, 32)))
    A["fcdT"] = np.ascontiguousarray(inputs["L0_attn_fc_D_w"].T)
    A["nw"] = np.ascontiguousarray(np.broadcast_to(inputs["L0_attn_norm_w"], (128, DI)))
    A["woutT"] = np.ascontiguousarray(inputs["L0_attn_out_proj_w"].T)
    A["w1T"] = np.ascontiguousarray(inputs["L0_ffn_W1"].transpose(0, 2, 1))
    A["b1c"] = np.ascontiguousarray(
        inputs["L0_ffn_b1"].reshape(8, 16, 128).transpose(2, 0, 1))
    A["w2T"] = np.ascontiguousarray(inputs["L0_ffn_W2"].transpose(0, 2, 1))
    A["b2"] = np.ascontiguousarray(inputs["L0_ffn_b2"])
    A["wgT"] = np.ascontiguousarray(inputs["L0_ffn_Wg"].T)
    A["bg"] = np.ascontiguousarray(np.broadcast_to(inputs["L0_ffn_bg"], (128, 8)))
    A["idf"] = np.eye(128, dtype=np.float32)
    idx = np.arange(128)
    mz = np.stack([(idx[None, :] < idx[:, None]),
                   (idx[None, :] > idx[:, None])], axis=0).astype(np.uint8)
    A["mskz"] = np.ascontiguousarray(
        np.tile(mz[None, :, :, :], (1, 1, 1, 4)).reshape(2, 128, 512)
        .transpose(1, 0, 2)).copy()
    A["ones1"] = np.ones((1, 128), np.float32)
    shm = np.zeros((128, 256), np.float32)
    shm[idx[1:], idx[:-1]] = 1.0
    shm[0, 255] = 1.0
    A["shm"] = np.ascontiguousarray(shm.astype(ml_dtypes.bfloat16))
    return A


def build(nt):
    nc = bass.Bass("TRN2", target_bir_lowering=False)
    b = Bld(nc)
    Ltot = nt * CK
    tps = min(TPS, nt)
    nst = max(1, (nt + tps - 1) // tps)

    dr = {}

    def din(name, shape, dt=F32):
        dr[name] = nc.dram_tensor(name, list(shape), dt, kind="ExternalInput").ap()

    din("u", (Ltot, D))
    for name, shape, dt in [
        ("winT", (D, DIP), F32), ("convd", (128, 18, 7, 32), F32),
        ("convb", (128, 18), F32), ("dtb", (128, 64), F32), ("alog", (64, 1), F32),
        ("dpar", (128, 32), F32), ("fcdT", (DI, 32), F32), ("nw", (128, DI), F32),
        ("woutT", (DI, D), F32), ("w1T", (NE, D, DI), F32), ("b1c", (128, NE, 16), F32),
        ("w2T", (NE, DI, D), F32), ("b2", (NE, D), F32), ("wgT", (D, NE), F32),
        ("bg", (128, NE), F32), ("idf", (128, 128), F32),
        ("mskz", (128, 2, 512), U8), ("ones1", (1, 128), F32),
        ("shm", (128, 256), BF16),
    ]:
        dr[name] = nc.dram_tensor(name, list(shape), dt, kind="ExternalInput").ap()
    out = nc.dram_tensor("out", [Ltot, D], F32, kind="ExternalOutput").ap()

    def scr(name, shape, dt=BF16):
        dr[name] = nc.dram_tensor(name, list(shape), dt, kind="Internal").ap()

    scr("win_b", (D, DIP))
    scr("w1_b", (NE, D, DI))
    scr("w2_b", (NE, DI, D))
    scr("wout_b", (DI, D))
    scr("z_sp", (Ltot, DI))
    scr("xi_sp", (Ltot, DI))
    scr("yfw_sp", (Ltot + 1, DI))
    scr("ybd_sp", (Ltot, DI))
    scr("csbw_sp", (nt, 64, 2048))
    scr("bc_sp", (Ltot, 256))
    scr("expal_sp", (Ltot, 64))
    scr("ffn_sp", (Ltot, D))

    sb = {}
    hnd = {}

    def asb(name, shape, dt=BF16):
        hnd[name] = nc.alloc_sbuf_tensor("sb_" + name, list(shape), dt)
        sb[name] = hnd[name]

    def aring(name, shape, dt=BF16):
        R, Pp = shape[0], shape[1]
        hnd[name] = nc.alloc_sbuf_tensor("sb_" + name, [Pp, R] + list(shape[2:]), dt)
        sb[name] = Ring(hnd[name])

    asb("idb", (128, 128))
    asb("shm", (128, 256))
    asb("idf", (128, 128), F32)
    asb("mskz", (128, 2, 512), U8)
    asb("zc", (128, 1))
    asb("ones1b", (1, 128))
    asb("cbc", (128, 18), F32)
    asb("dtbc", (128, 64), F32)
    asb("alogc", (64, 1), F32)
    asb("nega", (64, 1), F32)
    asb("dparc", (128, 32), F32)
    asb("bgc", (128, 8), F32)
    asb("wgc", (128, 8, 8), F32)
    asb("b1cc", (128, NE, 16), F32)
    asb("b2r", (NE, D))
    asb("fcdw", (128, 16, 32))
    asb("cdw", (128, 18, 7, 32))
    asb("drows", (1, nt, 64), F32)
    aring("drb", (2, 1, 64))
    asb("fcd", (128, nt, 32), F32)
    aring("comb", (2, 128, tps, 8), F32)
    asb("combT", (8, 128))
    asb("glt", (128, 8), F32)
    asb("gsc", (128, 12), F32)
    aring("dtr", (2, 128, 64), F32)
    asb("dscr", (128, 64), F32)
    aring("dts", (2, 128, 64), F32)
    aring("dtb16", (2, 128, 64))
    asb("abf", (64, CK), F32)
    aring("alf", (2, 64, CK), F32)
    aring("poshi", (2, 64, CK))
    aring("poslo", (2, 64, CK))
    aring("altm", (2, 128, 64), F32)
    aring("naltm", (2, 128, 64), F32)
    aring("expaltm", (2, 128, 64))
    aring("expdtm", (2, 128, 64))
    asb("expdfm", (64, CK), F32)
    asb("sumc", (64, 1), F32)
    aring("bcfm", (2, 128, 2, 128))
    aring("bct", (2, 128, 256))
    asb("stA", (64, 2048), F32)
    asb("stB", (64, 2048))
    asb("decrep", (64, 2048))
    asb("csbt", (64, 2048))
    aring("xt", (2, 128, D), F32)
    aring("win", (2, 128, 8, 512))
    aring("wA", (2, 128, 8, 1024))
    aring("asm", (2, 128, 18, 134))
    aring("xifm", (2, 128, 16, 128))
    aring("xitm", (2, 128, DI))
    aring("xall", (2, 128, 1024))
    asb("mall", (128, 2048))
    aring("bhat", (2, 128, 1024))
    aring("yt", (2, 128, 512))
    aring("zt", (2, 128, 512))
    asb("xmt", (128, D), F32)
    asb("xnt", (128, D), F32)
    asb("xnf32", (128, 8, 128), F32)
    aring("xnfm", (2, 128, 8, tps * CK))
    asb("acc", (128, tps, D))
    asb("hb", (128, 16, 512))
    aring("expb", (2, 128, 32))
    aring("cbw", (2, 128, 64))
    asb("cbwfm", (64, 128))

    def flat(h):
        ap = h.ap()
        nd = len(ap.shape)
        if nd == 2:
            return ap
        spec = "p " + " ".join("d%d" % k for k in range(nd - 1))
        return ap.rearrange(spec + " -> p (" + spec[2:] + ")")

    hbf = flat(hnd["hb"])
    sb["yfwt"] = AliasRing(hbf[:, 0:2048], hbf[:, 2048:4096])
    sb["ybshs"] = AliasRing(hbf[:, 4096:6144], hbf[:, 6144:8192])
    winf = flat(hnd["win"])
    sb["ybwt"] = AliasRing(winf[:, 0:2048], winf[:, 2048:4096])
    sb["ycmb"] = AliasRing(winf[:, 4096:6144], winf[:, 4096:6144])
    asmf = flat(hnd["asm"])
    sb["xirl"] = AliasRing(asmf[:, 0:2048], asmf[:, 2048:4096])
    xifmf = flat(hnd["xifm"])
    sb["zrl"] = AliasRing(xifmf[:, 0:2048], xifmf[:, 2048:4096])
    sb["ybdt"] = sb["xitm"]
    bhatf = flat(hnd["bhat"])
    sb["actt"] = AliasRing(hnd["mall"].ap(), bhatf)
    xallf = flat(hnd["xall"])
    sb["actf"] = AliasRing(
        xallf.rearrange("p (c n) -> p c n", n=128),
        winf[:, 6144:8192].rearrange("p (c n) -> p c n", n=128))
    sb["outt"] = AliasRing(hnd["xmt"].ap(), hnd["xnt"].ap())
    xnf32f = flat(hnd["xnf32"]).bitcast(BF16)
    sb["ffrl"] = AliasRing(xnf32f[:, 0:1024], xnf32f[:, 1024:2048])
    wAf = flat(hnd["wA"])
    sb["woutv"] = wAf.rearrange("p (c n) -> p c n", n=1024)
    cdwf = flat(hnd["cdw"])
    sb["nwb"] = cdwf[:, 0:2048]

    pbig = nc.alloc_psum_tensor("pbig", [128, 2048], F32)
    pmed = nc.alloc_psum_tensor("pmed", [128, 512], F32)
    pyd = nc.alloc_psum_tensor("pyd", [128, 512], F32)
    pyo = nc.alloc_psum_tensor("pyo", [128, 512], F32)
    ptb = nc.alloc_psum_tensor("ptb", [128, 1024], BF16)

    bank_free = {}

    def bw(key):
        return list(bank_free.get(key, []))

    def bset(key, *toks):
        bank_free[key] = [t for t in toks if t is not None]

    def badd(key, tok):
        bank_free.setdefault(key, []).append(tok)

    S = {}

    def sget(k):
        return S.get(k, ZERO)

    def r3(ap, q):
        return ap.rearrange("p (h q) -> p h q", q=q)

    cv_win = b.dgp(lambda e: e.dma_start(dr["win_b"], dr["winT"]))
    cv_wout = b.dgp(lambda e: e.dma_start(dr["wout_b"], dr["woutT"]))
    cv_w1, cv_w2 = [], []
    for ei in range(NE):
        cv_w1.append(b.dgp(lambda e, ei=ei: e.dma_start(dr["w1_b"][ei], dr["w1T"][ei])))
        cv_w2.append(b.dgp(lambda e, ei=ei: e.dma_start(dr["w2_b"][ei], dr["w2T"][ei])))
    tok_idb = b.dgp(lambda e: e.dma_start(sb["idb"][:], dr["idf"]))
    tok_shm = b.dgp(lambda e: e.dma_start(sb["shm"][:], dr["shm"]))
    tok_b2r = b.dgp(lambda e: e.dma_start(sb["b2r"][:], dr["b2"]))
    tok_fcdw = b.dgp(lambda e: e.dma_start(
        sb["fcdw"][:], dr["fcdT"].rearrange("(c k) n -> k c n", k=128)))
    tok_cdw = b.dgp(lambda e: e.dma_start(sb["cdw"][:], dr["convd"]))
    tok_o1b = b.dgp(lambda e: e.dma_start(sb["ones1b"][:], dr["ones1"]))
    tok_msk = b.dsync(lambda e: e.dma_start(sb["mskz"][:], dr["mskz"]))
    tok_idf = b.dsync(lambda e: e.dma_start(sb["idf"][:], dr["idf"]))
    tok_cbc = b.dsync(lambda e: e.dma_start(sb["cbc"][:], dr["convb"]))
    tok_dtb = b.dsync(lambda e: e.dma_start(sb["dtbc"][:], dr["dtb"]))
    tok_alog = b.dsync(lambda e: e.dma_start(sb["alogc"][:], dr["alog"]))
    tok_dpar = b.dsync(lambda e: e.dma_start(sb["dparc"][:], dr["dpar"]))
    tok_bg = b.dsync(lambda e: e.dma_start(sb["bgc"][:], dr["bg"]))
    tok_wg = b.dsync(lambda e: e.dma_start(
        sb["wgc"][:], dr["wgT"].rearrange("(c k) n -> k c n", k=128)))
    tok_b1c = b.dsync(lambda e: e.dma_start(sb["b1cc"][:], dr["b1c"]))

    t_ne = b.s(lambda e: e.activation(sb["nega"][:], sb["alogc"][:], AF.Exp),
               waits=[tok_alog])
    tok_nega = b.v(lambda e: e.tensor_scalar_mul(sb["nega"][:], sb["nega"][:], -1.0),
                   waits=[t_ne])
    tok_zc = b.v(lambda e: e.memset(sb["zc"][:], 0.0))
    tz = b.v(lambda e: e.memset(sb["outt"][0], 0.0))
    zr1 = b.dsync(lambda e: e.dma_start(
        dr["yfw_sp"][0:1, :], sb["outt"][0, 0:1, :].bitcast(BF16)[:, 0:DI]), waits=[tz])
    zrb = b.v(lambda e: e.engine_nop(), waits=[zr1])
    S[("outt_free", 0)] = zrb
    S["xmt_free"] = zrb
    S["sfw"] = b.v(lambda e: e.memset(sb["stA"][:], 0.0))
    S["sbf"] = b.v(lambda e: e.memset(sb["stB"][:], 0.0))
    S["asm_z0"] = b.v(lambda e: e.memset(sb["asm"][0, :, :, 0:3], 0.0))

    wctr = [0]

    def win_load(g):
        pp = wctr[0] % 2
        wctr[0] += 1
        n = 320 if g == 8 else 512
        tok = b.dsync(lambda e, g=g, pp=pp, n=n: e.dma_start(
            sb["win"][pp, :, :, 0:n],
            dr["win_b"][:, g * 512:g * 512 + n].rearrange("(c k) n -> k c n", k=128)),
            waits=[sget(("win_free", pp)), cv_win])
        return tok, pp

    S["win_next"] = win_load(0)

    def tile_head(i):
        st, it, r = i // tps, i % tps, i % 2
        stb = st % 2
        tok_x = b.dscal(lambda e, i=i, r=r: e.dma_start(
            sb["xt"][r], dr["u"][i * CK:(i + 1) * CK, :]),
            waits=[sget(("xt_free", r))])
        yield 2
        def trunc(tok):
            S[("asm_center", i)] = tok
            S[("dtr", i)] = tok
            S[("comb_tile", i // tps % 2, it)] = tok
            S[("xnfm_tile", i // tps % 2, it)] = tok
        if SCHED == "ln0":
            dmy = b.v(lambda e: e.engine_nop(), waits=[tok_x])
            trunc(dmy)
            return
        w0 = b.v(lambda e, r=r: e.tensor_reduce(sb["gsc"][:, 0:1], sb["xt"][r], AX,
                                                ALU.add),
                 waits=[tok_x, sget("gsc_free")])
        w1_ = b.v(lambda e: e.tensor_scalar_mul(sb["gsc"][:, 1:2], sb["gsc"][:, 0:1],
                                                1.0 / D), waits=[w0])
        w2_ = b.v(lambda e, r=r: e.tensor_scalar(sb["xmt"][:], sb["xt"][r],
                                                 sb["gsc"][:, 1:2], None, ALU.subtract),
                  waits=[w1_, sget("xmt_free")])
        S[("xt_free", r)] = w2_
        if SCHED == "ln1":
            trunc(w2_)
            S["xmt_free"] = w2_
            return
        w3 = b.s(lambda e: e.activation(sb["xnt"][:], sb["xmt"][:], AF.Square,
                                        accum_out=sb["gsc"][:, 2:3]),
                 waits=[w2_, sget("xnt_free")])
        w4a = b.v(lambda e: e.tensor_scalar(sb["gsc"][:, 3:4], sb["gsc"][:, 2:3],
                                            1.0 / D, 1e-5, ALU.mult, ALU.add),
                  waits=[w3])
        w4 = b.s(lambda e: e.activation(sb["gsc"][:, 3:4], sb["gsc"][:, 3:4], AF.Ln),
                 waits=[w4a])
        w5a = b.v(lambda e: e.tensor_scalar_mul(sb["gsc"][:, 4:5], sb["gsc"][:, 3:4],
                                                -0.5), waits=[w4])
        w5 = b.s(lambda e: e.activation(sb["gsc"][:, 4:5], sb["gsc"][:, 4:5], AF.Exp),
                 waits=[w5a])
        if SCHED == "ln2":
            trunc(w5)
            S["xmt_free"] = w5
            S["gsc_free"] = w5
            return
        w6 = b.v(lambda e: e.tensor_scalar_mul(sb["xnt"][:], sb["xmt"][:],
                                               sb["gsc"][:, 4:5]),
                 waits=[w5, sget("xnt_free")])
        S["xmt_free"] = w6
        if SCHED == "ln3":
            trunc(w6)
            S["gsc_free"] = w6
            return
        evb = None
        for kc in range(8):
            bank, pt = (("yd", pyd) if kc % 2 == 0 else ("yo", pyo))
            mmw = [w6, tok_idf] + bw(bank)
            if it == 0 and kc == 0:
                mmw.append(sget(("xnfm_free", stb)))
            if kc < 2:
                mmw.append(sget("xnf32_free"))
            tt = b.t(lambda e, kc=kc, pt=pt: e.transpose(
                pt[:, 0:128], sb["xnt"][:, kc * 128:(kc + 1) * 128], sb["idf"][:]),
                waits=mmw)
            ev1 = b.v(lambda e, kc=kc, it=it, pt=pt, stb=stb: e.tensor_copy(
                sb["xnfm"][stb, :, kc, it * CK:(it + 1) * CK], pt[:, 0:128]),
                waits=[tt])
            ev2 = b.s(lambda e, kc=kc, pt=pt: e.copy(
                sb["xnf32"][:, kc, :], pt[:, 0:128]), waits=[tt, ev1])
            bset(bank, ev1, ev2)
            S[("xnfm_k", kc)] = ev1
            S[("xnf32_k", kc)] = ev2
            evb = ev1
            if kc == 3:
                yield 0
        S["xnt_free"] = evb
        S[("xnfm_tile", stb, it)] = evb
        yield 0
        if SCHED == "ln":
            trunc(evb)
            return
        gm = None
        for kc in range(8):
            gm = b.t(lambda e, kc=kc: e.matmul(
                pmed[:, 128:136], sb["xnf32"][:, kc, :], sb["wgc"][:, kc, :],
                start=(kc == 0), stop=(kc == 7)),
                waits=[tok_wg, S[("xnf32_k", kc)]] + (bw("med") if kc == 0 else []))
        S["xnf32_free"] = gm
        g0 = b.v(lambda e: e.tensor_tensor(sb["glt"][:], pmed[:, 128:136], sb["bgc"][:],
                                           ALU.add),
                 waits=[gm, tok_bg, sget("glt_free")])
        bset("med", g0)
        m1 = b.v(lambda e: e.tensor_reduce(sb["gsc"][:, 5:6], sb["glt"][:], AX, ALU.max),
                 waits=[g0])
        nm1 = b.v(lambda e: e.tensor_reduce(sb["gsc"][:, 6:7], sb["glt"][:], AX, ALU.max,
                                            negate=True), waits=[g0])
        eq = b.v(lambda e: e.tensor_scalar(sb["gsc"][:, 8:12].bitcast(F32) if False
                                           else sb["dscr"][:, 0:8], sb["glt"][:],
                                           sb["gsc"][:, 5:6], None, ALU.is_equal),
                 waits=[m1, sget("dscr_free")])
        mk = b.v(lambda e: e.scalar_tensor_tensor(sb["dscr"][:, 0:8], sb["dscr"][:, 0:8],
                                                  -1e30, sb["glt"][:], ALU.mult, ALU.add),
                 waits=[eq])
        m2 = b.v(lambda e: e.tensor_reduce(sb["gsc"][:, 7:8], sb["dscr"][:, 0:8], AX,
                                           ALU.max), waits=[mk])
        ge = b.v(lambda e: e.tensor_scalar(sb["dscr"][:, 0:8], sb["glt"][:],
                                           sb["gsc"][:, 7:8], None, ALU.is_ge),
                 waits=[m2])
        dm = b.v(lambda e: e.tensor_tensor(sb["gsc"][:, 7:8], sb["gsc"][:, 7:8],
                                           sb["gsc"][:, 6:7], ALU.add),
                 waits=[m2, nm1, ge])
        ew = b.s(lambda e: e.activation(sb["glt"][:], sb["glt"][:], AF.Exp,
                                        bias=sb["gsc"][:, 6:7]), waits=[nm1, ge, mk])
        ed = b.s(lambda e: e.activation(sb["gsc"][:, 7:8], sb["gsc"][:, 7:8], AF.Exp),
                 waits=[dm])
        d1 = b.v(lambda e: e.tensor_scalar_add(sb["gsc"][:, 7:8], sb["gsc"][:, 7:8],
                                               1.0), waits=[ed])
        rc = b.v(lambda e: e.reciprocal(sb["gsc"][:, 7:8], sb["gsc"][:, 7:8]),
                 waits=[d1])
        c1 = b.v(lambda e: e.tensor_tensor(sb["glt"][:], sb["glt"][:], sb["dscr"][:, 0:8],
                                           ALU.mult), waits=[ew])
        c2 = b.v(lambda e, stb=stb, it=it: e.tensor_scalar_mul(
            sb["comb"][stb, :, it, :], sb["glt"][:], sb["gsc"][:, 7:8]),
            waits=[c1, rc, sget(("comb_free", stb))])
        S["glt_free"] = c2
        S["gsc_free"] = c2
        S["dscr_free"] = c2
        S[("comb_tile", stb, it)] = c2
        yield 0
        if SCHED == "lngate":
            S[("asm_center", i)] = c2
            S[("dtr", i)] = c2
            return

        ar = i % 2
        ctrV, ctrS = [None], [None]
        for g in range(9):
            yield 1
            gtok, pp = S["win_next"]
            if g + 1 < 9:
                S["win_next"] = win_load(g + 1)
            elif i + 1 < nt:
                S["win_next"] = win_load(0)
            if g < 4:
                bank = "b%d" % (g % 4)
                mm = None
                for kc in range(8):
                    mm = b.t(lambda e, kc=kc, g=g, pp=pp, it=it: e.matmul(
                        pbig[:, (g % 4) * 512:(g % 4) * 512 + 512],
                        sb["xnfm"][stb, :, kc, it * CK:(it + 1) * CK],
                        sb["win"][pp, :, kc, :],
                        start=(kc == 0), stop=(kc == 7)),
                        waits=[gtok, S[("xnfm_k", kc)]] + (bw(bank) if kc == 0 else []))
                S[("win_free", pp)] = mm
                for hz in range(1):
                    pass
                mke = b.v if g % 2 == 0 else b.s
                fne = ((lambda e, g=g: e.tensor_copy(
                    sb["zt"][g % 2], pbig[:, (g % 4) * 512:(g % 4) * 512 + 512]))
                    if g % 2 == 0 else
                    (lambda e, g=g: e.copy(
                        sb["zt"][g % 2], pbig[:, (g % 4) * 512:(g % 4) * 512 + 512])))
                ev = mke(fne, waits=[mm, sget(("zt_free", g % 2))])
                bset(bank, ev)
                sp = b.dsync(lambda e, i=i, g=g: e.dma_start(
                    dr["z_sp"][i * CK:(i + 1) * CK, g * 512:(g + 1) * 512],
                    sb["zt"][g % 2]), waits=[ev])
                S[("zt_free", g % 2)] = sp
            else:
                nsub = 3 if g == 8 else 4
                for sub in range(nsub):
                    ch_abs = (g - 4) * 4 + sub
                    bank = "b%d" % (ch_abs % 4)
                    mm = None
                    if ch_abs < 18:
                        for kc in range(8):
                            mm = b.t(lambda e, kc=kc, pp=pp, sub=sub, it=it, stb=stb,
                                     ch_abs=ch_abs: e.matmul(
                                pbig[:, (ch_abs % 4) * 512:(ch_abs % 4) * 512 + 128],
                                sb["win"][pp, :, kc, sub * 128:sub * 128 + 128],
                                sb["xnfm"][stb, :, kc, it * CK:(it + 1) * CK],
                                start=(kc == 0), stop=(kc == 7)),
                                waits=[gtok, S[("xnfm_k", kc)]]
                                + (bw(bank) if kc == 0 else []))
                        mke = b.v if ch_abs % 2 == 0 else b.s
                        fne = ((lambda e, ar=ar, ch_abs=ch_abs: e.tensor_copy(
                            sb["asm"][ar, :, ch_abs, 3:131],
                            pbig[:, (ch_abs % 4) * 512:(ch_abs % 4) * 512 + 128]))
                            if ch_abs % 2 == 0 else
                            (lambda e, ar=ar, ch_abs=ch_abs: e.copy(
                                sb["asm"][ar, :, ch_abs, 3:131],
                                pbig[:, (ch_abs % 4) * 512:(ch_abs % 4) * 512 + 128])))
                        ev = mke(fne, waits=[mm, sget(("asm_free", ar))])
                        if ch_abs % 2 == 0:
                            ctrV[0] = ev
                        else:
                            ctrS[0] = ev
                    else:
                        for kc in range(8):
                            mm = b.t(lambda e, kc=kc, pp=pp, it=it, stb=stb,
                                     ch_abs=ch_abs: e.matmul(
                                pbig[:, (ch_abs % 4) * 512:(ch_abs % 4) * 512 + 64],
                                sb["xnfm"][stb, :, kc, it * CK:(it + 1) * CK],
                                sb["win"][pp, :, kc, 256:320],
                                start=(kc == 0), stop=(kc == 7)),
                                waits=[gtok, S[("xnfm_k", kc)]]
                                + (bw(bank) if kc == 0 else []))
                        ev = b.v(lambda e, r=r, ch_abs=ch_abs: e.tensor_copy(
                            sb["dtr"][r],
                            pbig[:, (ch_abs % 4) * 512:(ch_abs % 4) * 512 + 64]),
                            waits=[mm, sget(("dtr_free", r))])
                        S[("dtr", i)] = ev
                    S[("win_free", pp)] = mm
                    bset(bank, ev)
        last_ctr = b.v(lambda e: e.engine_nop(), waits=[ctrV[0], ctrS[0]])
        S[("asm_center", i)] = last_ctr
        if i > 0:
            pv = (i - 1) % 2
            e1 = b.v(lambda e, ar=ar, pv=pv: e.tensor_copy(
                sb["asm"][ar, :, :, 0:3], sb["asm"][pv, :, :, 128:131]),
                waits=[last_ctr, S[("asm_center", i - 1)]])
            e2 = b.v(lambda e, ar=ar, pv=pv: e.tensor_copy(
                sb["asm"][pv, :, :, 131:134], sb["asm"][ar, :, :, 3:6]),
                waits=[last_ctr, S[("asm_center", i - 1)]])
            S[("asm_ready", i - 1)] = e2
            S[("asm_left", i)] = e1
        else:
            S[("asm_left", 0)] = S["asm_z0"]

    def conv_ssd(c):
        rr = c % 2
        ar = c % 2
        asm_ok = [S[("asm_ready", c)], S[("asm_left", c)], tok_cdw, tok_cbc]
        ev = None
        for ch in range(18):
            bank = "b%d" % (ch % 4)
            mm = None
            for bb in range(4):
                for k in range(7):
                    mm = b.t(lambda e, ch=ch, k=k, bb=bb, ar=ar: e.matmul(
                        pbig[32 * bb:32 * bb + 32,
                             (ch % 4) * 512:(ch % 4) * 512 + 128],
                        sb["cdw"][32 * bb:32 * bb + 32, ch, k, :],
                        sb["asm"][ar, 32 * bb:32 * bb + 32, ch, k:k + 128],
                        start=(k == 0), stop=(k == 6),
                        tile_position=(32 * bb, 32 * bb)),
                        waits=asm_ok + (bw(bank) if (k == 0 and bb == 0) else []))
            if ch < 16:
                ev = b.s(lambda e, ch=ch, rr=rr: e.activation(
                    sb["xifm"][rr, :, ch, :],
                    pbig[:, (ch % 4) * 512:(ch % 4) * 512 + 128], AF.Silu,
                    bias=sb["cbc"][:, ch:ch + 1]),
                    waits=[mm, sget(("xifm_free", rr))])
                S[("xifm_ch", rr, ch)] = ev
            else:
                ev = b.s(lambda e, ch=ch, rr=rr: e.activation(
                    sb["bcfm"][rr, :, ch - 16, :],
                    pbig[:, (ch % 4) * 512:(ch % 4) * 512 + 128], AF.Silu,
                    bias=sb["cbc"][:, ch:ch + 1]),
                    waits=[mm, sget(("bcfm_free", rr))])
                S[("bcfm_ch", rr, ch - 16)] = ev
            bset(bank, ev)
            if ch % 3 == 2:
                yield 0
        S[("asm_free", ar)] = ev

        tts = []
        for j in range(2):
            tts.append(b.t(lambda e, j=j, rr=rr: e.transpose(
                ptb[:, j * 128:(j + 1) * 128], sb["bcfm"][rr, :, j, :], sb["idb"][:]),
                waits=[S[("bcfm_ch", rr, j)], tok_idb] + bw("tb")))
        bc2 = None
        for j in range(2):
            bc2 = b.v(lambda e, j=j, rr=rr: e.tensor_copy(
                sb["bct"][rr, :, j * 128:(j + 1) * 128], ptb[:, j * 128:(j + 1) * 128]),
                waits=[tts[-1], sget(("bct_free", rr))])
        bset("tb", bc2)
        S[("bct", c)] = bc2
        yield 0
        spbc = b.dsync(lambda e, c=c, rr=rr: e.dma_start(
            dr["bc_sp"][c * CK:(c + 1) * CK, :], sb["bct"][rr]), waits=[bc2])

        evs = []
        for wave in range(2):
            tts = []
            for sl in range(8):
                ch = wave * 8 + sl
                tw = [S[("xifm_ch", rr, ch)], tok_idb]
                tw.extend(bw("tb") if ch == 0 else ([evs[-1]] if ch == 8 else []))
                tts.append(b.t(lambda e, ch=ch, sl=sl, rr=rr: e.transpose(
                    ptb[:, sl * 128:(sl + 1) * 128], sb["xifm"][rr, :, ch, :],
                    sb["idb"][:]), waits=tw))
            for sl in range(8):
                ch = wave * 8 + sl
                evx = b.v(lambda e, ch=ch, sl=sl, rr=rr: e.tensor_copy(
                    sb["xitm"][rr, :, ch * 128:(ch + 1) * 128],
                    ptb[:, sl * 128:(sl + 1) * 128]),
                    waits=[tts[-1], sget(("xitm_free", rr))])
                evs.append(evx)
            yield 1
        bset("tb", evs[-1])
        xitm_tok = evs[-1]
        S[("xitm", c)] = xitm_tok
        spxi = b.dsync(lambda e, c=c, rr=rr: e.dma_start(
            dr["xi_sp"][c * CK:(c + 1) * CK, :], sb["xitm"][rr]), waits=[xitm_tok])

        fm = None
        for kc in range(16):
            fm = b.t(lambda e, kc=kc, rr=rr: e.matmul(
                pmed[:, 256:288], sb["xifm"][rr, :, kc, :], sb["fcdw"][:, kc, :],
                start=(kc == 0), stop=(kc == 15)),
                waits=[S[("xifm_ch", rr, kc)], tok_fcdw] + (bw("med") if kc == 0 else []))
        S[("xifm_free", rr)] = fm
        fe = b.v(lambda e, c=c: e.tensor_tensor(
            sb["fcd"][:, c, :], pmed[:, 256:288], sb["dparc"][:], ALU.add),
            waits=[fm, tok_dpar])
        bset("med", fe)
        S[("fcd", c)] = fe
        yield 0

        a0 = b.v(lambda e, c=c: e.tensor_tensor(sb["dscr"][:, 0:64], sb["dtr"][c % 2],
                                                sb["dtbc"][:], ALU.add),
                 waits=[S[("dtr", c)], tok_dtb, sget("dscr_free")])
        a1 = b.s(lambda e, rr=rr: e.activation(sb["dts"][rr], sb["dscr"][:, 0:64],
                                               AF.Abs),
                 waits=[a0, sget(("dts_free", rr))])
        a2 = b.s(lambda e, rr=rr: e.activation(sb["dts"][rr], sb["dts"][rr], AF.Exp,
                                               scale=-1.0), waits=[a1])
        a3 = b.s(lambda e, rr=rr: e.activation(sb["dts"][rr], sb["dts"][rr], AF.Ln,
                                               bias=1.0), waits=[a2])
        a4 = b.v(lambda e, rr=rr: e.scalar_tensor_tensor(
            sb["dts"][rr], sb["dscr"][:, 0:64], 0.0, sb["dts"][rr], ALU.max, ALU.add),
            waits=[a3])
        S[("dtr_free", c % 2)] = a4
        S["dscr_free"] = a4
        a5 = b.v(lambda e, rr=rr: e.tensor_copy(sb["dtb16"][rr], sb["dts"][rr]),
                 waits=[a4, sget(("dtb16_free", rr))])
        yield 0
        td = b.t(lambda e, rr=rr: e.transpose(pyd[0:64, 0:128], sb["dts"][rr],
                                              sb["idf"][:]),
                 waits=[a4, tok_idf] + bw("yd"))
        ab0 = b.v(lambda e: e.tensor_scalar_mul(sb["abf"][:], pyd[0:64, 0:128],
                                                sb["nega"][:, 0:1]),
                  waits=[td, tok_nega, sget("abf_free")])
        bset("yd", ab0)
        sc0 = b.v(lambda e, rr=rr: e.tensor_tensor_scan(
            sb["alf"][rr], sb["abf"][:], sb["abf"][:], 0.0, ALU.add, ALU.bypass),
            waits=[ab0, sget(("alf_free", rr))])
        l0 = b.v(lambda e, rr=rr: e.tensor_scalar(
            sb["expdfm"][32:64, :], sb["alf"][rr][32:64, :],
            sb["alf"][rr][32:64, 127:128], -1.0, ALU.subtract, ALU.mult),
            waits=[sc0, sget("expdfm_free")])
        l1 = b.v(lambda e, rr=rr: e.tensor_tensor(
            sb["alf"][rr][32:64, :], sb["expdfm"][32:64, :], sb["abf"][32:64, :],
            ALU.add), waits=[l0])
        S["abf_free"] = l1
        alf_tok = l1
        p0 = b.v(lambda e, rr=rr: e.tensor_copy(sb["poshi"][rr], sb["alf"][rr]),
                 waits=[alf_tok, sget(("poshi_free", rr))])
        p1 = b.v(lambda e, rr=rr: e.scalar_tensor_tensor(
            sb["poslo"][rr], sb["alf"][rr], 1.0, sb["poshi"][rr], ALU.bypass,
            ALU.subtract), waits=[p0, sget(("poslo_free", rr))])
        ta = b.t(lambda e, rr=rr: e.transpose(pyo[0:128, 0:64], sb["alf"][rr],
                                              sb["idf"][0:64, 0:64]),
                 waits=[alf_tok, tok_idf] + bw("yo"))
        at0 = b.v(lambda e, rr=rr: e.tensor_copy(sb["altm"][rr], pyo[0:128, 0:64]),
                  waits=[ta, sget(("altm_free", rr))])
        bset("yo", at0)
        at1 = b.v(lambda e, rr=rr: e.tensor_scalar_mul(sb["naltm"][rr], sb["altm"][rr],
                                                       -1.0),
                  waits=[at0, sget(("naltm_free", rr))])
        ae = b.s(lambda e, rr=rr: e.activation(sb["expaltm"][rr], sb["altm"][rr],
                                               AF.Exp),
                 waits=[at0, sget(("expaltm_free", rr))])
        spal = b.dsync(lambda e, c=c, rr=rr: e.dma_start(
            dr["expal_sp"][c * CK:(c + 1) * CK, :], sb["expaltm"][rr]), waits=[ae])
        sm0 = b.v(lambda e, rr=rr: e.tensor_copy(sb["sumc"][0:32, :],
                                                 sb["alf"][rr][0:32, 127:128]),
                  waits=[sc0, sget("sumc_free")])
        sm1 = b.v(lambda e, rr=rr: e.tensor_copy(sb["sumc"][32:64, :],
                                                 sb["alf"][rr][32:64, 0:1]),
                  waits=[l1, sm0])
        ed0 = b.s(lambda e, rr=rr: e.activation(
            sb["expdfm"][:], sb["alf"][rr], AF.Exp, scale=-1.0,
            bias=sb["sumc"][:, 0:1]), waits=[l1, sm1])
        te = b.t(lambda e: e.transpose(pyd[0:128, 0:64], sb["expdfm"][:],
                                       sb["idf"][0:64, 0:64]),
                 waits=[ed0] + bw("yd"))
        et0 = b.v(lambda e, rr=rr: e.tensor_copy(sb["expdtm"][rr], pyd[0:128, 0:64]),
                  waits=[te, sget(("expdtm_free", rr))])
        bset("yd", et0)
        S["expdfm_free"] = et0
        tdr = b.t(lambda e: e.transpose(pyo[0:1, 0:64], sb["sumc"][:, 0:1],
                                        sb["idf"][0:64, 0:64]),
                  waits=[sm1] + bw("yo"))
        dr0 = b.s(lambda e, c=c: e.activation(sb["drows"][:, c, :], pyo[0:1, 0:64],
                                              AF.Exp), waits=[tdr])
        bset("yo", dr0)
        dr1 = b.v(lambda e, c=c, rr=rr: e.tensor_copy(sb["drb"][rr],
                                                      sb["drows"][:, c, :]),
                  waits=[dr0, sget(("drb_free", rr))])
        S["sumc_free"] = dr0
        yield 1

        yfree = None
        for d in (0, 1):
            h0 = 32 * d
            gmm = b.t(lambda e, d=d, rr=rr: e.matmul(
                pmed[:, 0:128], sb["bcfm"][rr, 64 * d:64 * d + 64, 0, :],
                sb["bcfm"][rr, 64 * d:64 * d + 64, 1, :],
                start=True, stop=True),
                waits=[S[("bcfm_ch", rr, 0)], S[("bcfm_ch", rr, 1)]] + bw("med"))
            yield 1
            for q in range(4):
                yield 0
                pair = q % 2
                cb = pair * 1024
                bkA, bkB = "b%d" % (2 * pair), "b%d" % (2 * pair + 1)
                stops = []
                for h8 in range(8):
                    h = 8 * q + h8
                    colw = bw(bkA) + bw(bkB) if h8 == 0 else []
                    b.t(lambda e, h=h, h8=h8, rr=rr, h0=h0, cb=cb: e.matmul(
                        pbig[:, cb + h8 * 128:cb + h8 * 128 + 128],
                        sb["idb"][0:64, h0 + h:h0 + h + 1].broadcast_to([64, 128]),
                        sb["poshi"][rr],
                        start=(h8 % 4 == 0), stop=False),
                        waits=[p0, tok_idb] + colw)
                    m2 = b.t(lambda e, h=h, h8=h8, rr=rr, h0=h0, cb=cb: e.matmul(
                        pbig[:, cb + h8 * 128:cb + h8 * 128 + 128],
                        sb["idb"][0:64, h0 + h:h0 + h + 1].broadcast_to([64, 128]),
                        sb["poslo"][rr],
                        start=False, stop=(h8 % 4 == 3)),
                        waits=[p1])
                    if h8 % 4 == 3:
                        stops.append(m2)
                etoks = []
                for h8 in range(8):
                    h = 8 * q + h8
                    ex = b.s(lambda e, h=h, h8=h8, rr=rr, h0=h0, cb=cb, pair=pair:
                             e.activation(
                                 sb["mall"][:, pair * 1024 + h8 * 128:
                                            pair * 1024 + h8 * 128 + 128],
                                 pbig[:, cb + h8 * 128:cb + h8 * 128 + 128], AF.Exp,
                                 bias=sb["naltm"][rr][:, h0 + h:h0 + h + 1]),
                             waits=[stops[h8 // 4], at1, sget(("mall_free", pair))])
                    etoks.append(ex)
                bset(bkA, etoks[3])
                bset(bkB, etoks[7])
                mzt = None
                for mq in range(2):
                    mzt = b.v(lambda e, d=d, mq=mq, pair=pair: e.copy_predicated(
                        sb["mall"][:, pair * 1024 + mq * 512:
                                   pair * 1024 + mq * 512 + 512],
                        sb["mskz"][:, d, :],
                        sb["zc"][:, 0:1].broadcast_to([128, 512])),
                        waits=etoks + [tok_msk, tok_zc])
                mg = b.v(lambda e, pair=pair: e.tensor_tensor(
                    r3(sb["mall"][:, pair * 1024:pair * 1024 + 1024], 128),
                    r3(sb["mall"][:, pair * 1024:pair * 1024 + 1024], 128),
                    pmed[:, 0:128].unsqueeze(1).broadcast_to([128, 8, 128]),
                    ALU.mult),
                    waits=[mzt, gmm])
                badd("med", mg)
                xb = b.v(lambda e, d=d, q=q, rr=rr, h0=h0, pair=pair: e.tensor_tensor(
                    r3(sb["xall"][pair], 64)[:, 0:8, :],
                    r3(sb["xitm"][rr][:, q * 512:(q + 1) * 512], 64),
                    sb["dtb16"][rr][:, h0 + 8 * q:h0 + 8 * q + 8]
                    .unsqueeze(2).broadcast_to([128, 8, 64]), ALU.mult),
                    waits=[S[("xitm", c)], a5, sget(("xall_free", pair))])
                bh = b.v(lambda e, d=d, q=q, rr=rr, h0=h0, pair=pair: e.tensor_tensor(
                    r3(sb["bhat"][pair], 64)[:, 0:8, :],
                    sb["bct"][rr][:, 64 * d:64 * d + 64]
                    .unsqueeze(1).broadcast_to([128, 8, 64]),
                    sb["expdtm"][rr][:, h0 + 8 * q:h0 + 8 * q + 8]
                    .unsqueeze(2).broadcast_to([128, 8, 64]), ALU.mult),
                    waits=[S[("bct", c)], et0, sget(("bhat_free", pair))])
                yield 2
                ydm = None
                for j in range(8):
                    ydm = b.t(lambda e, j=j, pair=pair: e.matmul(
                        pyd[:, j * 64:(j + 1) * 64],
                        sb["mall"][:, pair * 1024 + j * 128:pair * 1024 + j * 128 + 128],
                        sb["xall"][pair][:, j * 64:j * 64 + 64],
                        start=(j == 0), stop=(j == 7)),
                        waits=[mg, xb] + (bw("yd") if j == 0 else []))
                if d == 0:
                    yom = None
                    for j in range(8):
                        hq = 8 * q + j
                        yom = b.t(lambda e, hq=hq, j=j, rr=rr: e.matmul(
                            pyo[:, j * 64:(j + 1) * 64],
                            sb["bcfm"][rr, 0:64, 1, :],
                            sb["stB"][:, hq * 64:hq * 64 + 64],
                            start=(j == 0), stop=(j == 7)),
                            waits=[S[("bcfm_ch", rr, 1)], sget("sbf"), mg]
                            + (bw("yo") if j == 0 else []))
                    ye0 = b.v(lambda e, q=q, rr=rr, h0=h0: e.tensor_tensor(
                        r3(sb["yt"][q % 2], 64), r3(pyo[:, 0:512], 64),
                        sb["expaltm"][rr][:, h0 + 8 * q:h0 + 8 * q + 8]
                        .unsqueeze(2).broadcast_to([128, 8, 64]), ALU.mult),
                        waits=[yom, ae, sget(("yt_free", q % 2))])
                    bset("yo", ye0)
                    ye1 = b.v(lambda e, q=q: e.tensor_tensor(
                        sb["yt"][q % 2], sb["yt"][q % 2], pyd[:, 0:512], ALU.add),
                        waits=[ye0, ydm])
                    bset("yd", ye1)
                    spy = b.dsync(lambda e, c=c, q=q: e.dma_start(
                        dr["yfw_sp"][c * CK + 1:(c + 1) * CK + 1,
                                     q * 512:(q + 1) * 512],
                        sb["yt"][q % 2]), waits=[ye1])
                    S[("yt_free", q % 2)] = spy
                else:
                    ye1 = b.v(lambda e, q=q: e.tensor_copy(
                        sb["yt"][q % 2], pyd[:, 0:512]),
                        waits=[ydm, sget(("yt_free", q % 2))])
                    bset("yd", ye1)
                    spy = b.dsync(lambda e, c=c, q=q: e.dma_start(
                        dr["ybd_sp"][c * CK:(c + 1) * CK, q * 512:(q + 1) * 512],
                        sb["yt"][q % 2]), waits=[ye1])
                    S[("yt_free", q % 2)] = spy
                csm = None
                for j in range(8):
                    csm = b.t(lambda e, j=j, q=q, cb=cb, pair=pair: e.matmul(
                        pbig[0:64, cb + j * 64:cb + j * 64 + 64],
                        sb["bhat"][pair][:, j * 64:j * 64 + 64],
                        sb["xall"][pair][:, j * 64:j * 64 + 64],
                        start=(j == 0), stop=(j == 7)),
                        waits=[bh, xb, mg] + (bw(bkA) if j == 0 else []))
                S[("xall_free", pair)] = csm
                S[("bhat_free", pair)] = csm
                S[("mall_free", pair)] = csm
                if d == 0:
                    j = q
                    pt, bank = ((pyd, "yd") if j % 2 == 0 else (pyo, "yo"))
                    mm = b.t(lambda e, j=j, rr=rr, pt=pt: e.matmul(
                        pt[0:64, 0:512], sb["ones1b"][0:1, 0:64],
                        sb["drb"][rr][0:1, 8 * j:8 * j + 8]
                        .unsqueeze(2).broadcast_to([1, 8, 64]),
                        start=True, stop=True),
                        waits=[dr1, tok_o1b] + bw(bank))
                    evd = b.v(lambda e, j=j, pt=pt: e.tensor_copy(
                        sb["decrep"][:, j * 512:(j + 1) * 512], pt[0:64, 0:512]),
                        waits=[mm, sget("decrep_free")])
                    bset(bank, evd)
                    s_0 = b.v(lambda e, q=q: e.tensor_tensor(
                        sb["stA"][:, q * 512:(q + 1) * 512],
                        sb["stA"][:, q * 512:(q + 1) * 512],
                        sb["decrep"][:, q * 512:(q + 1) * 512], ALU.mult),
                        waits=[evd, sget("sfw"), yom])
                    s_1 = b.v(lambda e, q=q, cb=cb: e.tensor_tensor(
                        sb["stA"][:, q * 512:(q + 1) * 512],
                        sb["stA"][:, q * 512:(q + 1) * 512],
                        pbig[0:64, cb:cb + 512], ALU.add),
                        waits=[s_0, csm])
                    bset(bkA, s_1)
                    S[("sfw_h", q)] = s_1
                    s_2 = b.v(lambda e, q=q: e.tensor_copy(
                        sb["stB"][:, q * 512:(q + 1) * 512],
                        sb["stA"][:, q * 512:(q + 1) * 512]), waits=[s_1])
                    S[("sbf_h", q)] = s_2
                else:
                    cb0 = b.v(lambda e, q=q, cb=cb: e.tensor_copy(
                        sb["csbt"][:, q * 512:(q + 1) * 512],
                        pbig[0:64, cb:cb + 512]),
                        waits=[csm, sget("csbt_free")])
                    bset(bkA, cb0)
                    S[("csbt_h", q)] = cb0
            if d == 0:
                sfb = b.v(lambda e: e.engine_nop(),
                          waits=[S[("sfw_h", q)] for q in range(4)])
                S["sfw"] = sfb
                sbb = b.v(lambda e: e.engine_nop(),
                          waits=[S[("sbf_h", q)] for q in range(4)])
                S["sbf"] = sbb
                S["decrep_free"] = sfb
            else:
                spcs = b.dsync(lambda e, c=c: e.dma_start(dr["csbw_sp"][c],
                                                          sb["csbt"][:]),
                               waits=[S[("csbt_h", q)] for q in range(4)])
                S["csbt_free"] = spcs
                S[("csbw", c)] = spcs
                yfree = spcs
        S[("dts_free", rr)] = alf_tok
        S[("dtb16_free", rr)] = yfree
        S[("alf_free", rr)] = ed0
        S[("poshi_free", rr)] = yfree
        S[("poslo_free", rr)] = yfree
        S[("altm_free", rr)] = yfree
        S[("naltm_free", rr)] = yfree
        S[("expaltm_free", rr)] = yfree
        S[("expdtm_free", rr)] = yfree
        S[("drb_free", rr)] = yfree
        S[("bct_free", rr)] = yfree
        S[("xitm_free", rr)] = yfree
        S[("bcfm_free", rr)] = yfree

    def ffn(st):
        stb = st % 2
        TT = tps * CK
        xn_ready = [S[("xnfm_tile", stb, it)] for it in range(tps)]
        be = None
        for it in range(tps):
            tc = b.t(lambda e, stb=stb, it=it: e.transpose(
                pyd[0:8, 0:128], sb["comb"][stb, :, it, :], sb["idf"][:]),
                waits=[S[("comb_tile", stb, it)]] + bw("yd"))
            ce = b.v(lambda e: e.tensor_copy(sb["combT"][:], pyd[0:8, 0:128]),
                     waits=[tc, sget("combT_free")])
            bset("yd", ce)
            for nn2 in range(2):
                bm = b.t(lambda e, nn2=nn2: e.matmul(
                    pyo[:, 0:512], sb["combT"][:], sb["b2r"][:, nn2 * 512:(nn2 + 1) * 512],
                    start=True, stop=True),
                    waits=[ce, tok_b2r] + bw("yo"))
                be = b.v(lambda e, it=it, nn2=nn2: e.tensor_copy(
                    sb["acc"][:, it, nn2 * 512:(nn2 + 1) * 512], pyo[:, 0:512]),
                    waits=[bm, sget("acc_free")])
                bset("yo", be)
            S["combT_free"] = be
            yield
        acc_init = be
        prev_y = acc_init
        for ei in range(NE):
            for hf in range(2):
                pp = hf
                wtok = b.dgp(lambda e, ei=ei, hf=hf, pp=pp: e.dma_start(
                    sb["wA"][pp],
                    dr["w1_b"][ei][:, hf * 1024:(hf + 1) * 1024]
                    .rearrange("(c k) n -> k c n", k=128)),
                    waits=[sget(("wA_free", pp)), cv_w1[ei]])
                for hcl in range(8):
                    hc = 8 * hf + hcl
                    bank = "b%d" % (hcl % 4)
                    mm = None
                    for kc in range(8):
                        mm = b.t(lambda e, kc=kc, pp=pp, hcl=hcl, TT=TT: e.matmul(
                            pbig[:, (hcl % 4) * 512:(hcl % 4) * 512 + TT],
                            sb["wA"][pp, :, kc, hcl * 128:(hcl + 1) * 128],
                            sb["xnfm"][stb, :, kc, 0:TT],
                            start=(kc == 0), stop=(kc == 7)),
                            waits=[wtok] + xn_ready + (bw(bank) if kc == 0 else []))
                    hv = b.s(lambda e, hc=hc, ei=ei, hcl=hcl, TT=TT: e.activation(
                        sb["hb"][:, hc, 0:TT],
                        pbig[:, (hcl % 4) * 512:(hcl % 4) * 512 + TT], AF.Relu,
                        bias=sb["b1cc"][:, ei, hc:hc + 1]),
                        waits=[mm, tok_b1c, prev_y])
                    bset(bank, hv)
                    S[("hb_hc", hc)] = hv
                    if hcl % 2 == 1:
                        yield
                S[("wA_free", pp)] = mm
            ydone = None
            for chh in range(2):
                pp = chh
                w2tok = b.dgp(lambda e, ei=ei, chh=chh, pp=pp: e.dma_start(
                    sb["wA"][pp].rearrange("p a n -> p (a n)")
                    .rearrange("p (c n) -> p c n", n=512),
                    dr["w2_b"][ei][:, chh * 512:(chh + 1) * 512]
                    .rearrange("(c k) n -> k c n", k=128)),
                    waits=[sget(("wA_free", pp)), cv_w2[ei]])
                for tb in range(tps):
                    pt, bank = ((pyd, "yd") if tb % 2 == 0 else (pyo, "yo"))
                    mm = None
                    for kc in range(16):
                        mm = b.t(lambda e, kc=kc, pp=pp, tb=tb, pt=pt: e.matmul(
                            pt[:, 0:512],
                            sb["hb"][:, kc, tb * 128:(tb + 1) * 128],
                            sb["wA"][pp].rearrange("p a n -> p (a n)")
                            .rearrange("p (c n) -> p c n", n=512)[:, kc, :],
                            start=(kc == 0), stop=(kc == 15)),
                            waits=[w2tok, S[("hb_hc", kc)]] + (bw(bank) if kc == 0 else []))
                    ac = b.v(lambda e, tb=tb, chh=chh, stb=stb, ei=ei, pt=pt:
                             e.scalar_tensor_tensor(
                                 sb["acc"][:, tb, chh * 512:(chh + 1) * 512],
                                 pt[:, 0:512],
                                 sb["comb"][stb, :, tb, ei:ei + 1],
                                 sb["acc"][:, tb, chh * 512:(chh + 1) * 512],
                                 ALU.mult, ALU.add),
                             waits=[mm, S[("comb_tile", stb, tb)], acc_init])
                    bset(bank, ac)
                    ydone = ac
                    yield
                S[("wA_free", pp)] = mm
            prev_y = ydone
        spa = b.dsync(lambda e, st=st, TT=TT: e.dma_start(
            dr["ffn_sp"][st * TT:(st + 1) * TT, :].rearrange("(t p) c -> p t c", p=128),
            sb["acc"][:, 0:tps, :]), waits=[prev_y])
        S["acc_free"] = spa
        S[("comb_free", stb)] = prev_y
        S[("xnfm_free", stb)] = prev_y

    def tile_bc(c):
        rb = c % 2
        if c < nt - 1:
            d0 = b.v(lambda e, c=c, rb=rb: e.tensor_copy(
                sb["drb"][rb], sb["drows"][:, c + 1, :]),
                waits=[sget(("drb_free", rb))])
            S[("drb_free", rb)] = d0
            dts_ = []
            for j in range(4):
                pt, bank = ((pyd, "yd") if j % 2 == 0 else (pyo, "yo"))
                mm = b.t(lambda e, j=j, rb=rb, pt=pt: e.matmul(
                    pt[0:64, 0:512], sb["ones1b"][0:1, 0:64],
                    sb["drb"][rb][0:1, 32 + 8 * j:32 + 8 * j + 8]
                    .unsqueeze(2).broadcast_to([1, 8, 64]),
                    start=True, stop=True),
                    waits=[d0, tok_o1b] + bw(bank))
                evd = b.s(lambda e, j=j, pt=pt: e.copy(
                    sb["decrep"][:, j * 512:(j + 1) * 512], pt[0:64, 0:512]),
                    waits=[mm, sget("decrep_free")])
                bset(bank, evd)
                dts_.append(evd)
            rl = b.dsync(lambda e, c=c: e.dma_start(
                sb["csbt"][:], dr["csbw_sp"][c + 1]),
                waits=[sget("csrl_free"), S[("csbw", c + 1)]])
            s_0 = b.v(lambda e: e.tensor_tensor(sb["stA"][:],
                                                sb["stA"][:],
                                                sb["decrep"][:], ALU.mult),
                      waits=dts_ + [sget("sbw")])
            s_1 = b.v(lambda e: e.tensor_tensor(
                sb["stA"][:], sb["stA"][:], sb["csbt"][:], ALU.add),
                waits=[s_0, rl])
            S["csrl_free"] = s_1
            S["sbw"] = s_1
            S["decrep_free"] = s_1
            s_2 = b.v(lambda e: e.tensor_copy(sb["stB"][:], sb["stA"][:]),
                      waits=[s_1, sget("sbwbf_rd")])
            S["sbwbf"] = s_2
        rl_exp = b.dsync(lambda e, c=c, rb=rb: e.dma_start(
            sb["expb"][rb], dr["expal_sp"][c * CK:(c + 1) * CK, 32:64]),
            waits=[sget(("expb_free", rb))])
        rl_cbw = b.dsync(lambda e, c=c, rb=rb: e.dma_start(
            sb["cbw"][rb], dr["bc_sp"][c * CK:(c + 1) * CK, 192:256]),
            waits=[sget(("cbw_free", rb))])
        rl_ybd = b.dsync(lambda e, c=c, rb=rb: e.dma_start(
            sb["ybdt"][rb], dr["ybd_sp"][c * CK:(c + 1) * CK, :]),
            waits=[sget(("ybdt_free", rb))])
        tcb = b.t(lambda e, rb=rb: e.transpose(ptb[0:64, 0:128], sb["cbw"][rb],
                                               sb["idb"][:]),
                  waits=[rl_cbw, sget("cbwfm_free")] + bw("tb"))
        ecb = b.v(lambda e: e.tensor_copy(sb["cbwfm"][:], ptb[0:64, 0:128]),
                  waits=[tcb])
        bset("tb", ecb)
        S[("cbw_free", rb)] = ecb
        ye1 = None
        for q in range(4):
            yom = None
            for j in range(8):
                hq = 8 * q + j
                yom = b.t(lambda e, hq=hq, j=j: e.matmul(
                    pyo[:, j * 64:(j + 1) * 64], sb["cbwfm"][:],
                    sb["stB"][:, hq * 64:hq * 64 + 64],
                    start=(j == 0), stop=(j == 7)),
                    waits=[ecb, sget("sbwbf")] + (bw("yo") if j == 0 else []))
            ye0 = b.v(lambda e, q=q, rb=rb: e.tensor_tensor(
                r3(sb["ybwt"][rb][:, q * 512:(q + 1) * 512], 64),
                r3(pyo[:, 0:512], 64),
                sb["expb"][rb][:, 8 * q:8 * q + 8]
                .unsqueeze(2).broadcast_to([128, 8, 64]), ALU.mult),
                waits=[yom, rl_exp, sget(("ybwt_free", rb))])
            bset("yo", ye0)
            ye1 = b.v(lambda e, q=q, rb=rb: e.tensor_tensor(
                sb["ybwt"][rb][:, q * 512:(q + 1) * 512],
                sb["ybwt"][rb][:, q * 512:(q + 1) * 512],
                sb["ybdt"][rb][:, q * 512:(q + 1) * 512], ALU.add),
                waits=[ye0, rl_ybd])
        S[("ybdt_free", rb)] = ye1
        S["cbwfm_free"] = yom
        S["sbwbf_rd"] = yom
        S[("ybwt_ye", c)] = ye1
        smms = []
        for nb in range(4):
            bank = "b%d" % nb
            sm = b.t(lambda e, nb=nb, rb=rb, c=c: e.matmul(
                pbig[:, nb * 512:(nb + 1) * 512], sb["shm"][:, 0:128],
                sb["ybwt"][rb][:, nb * 512:(nb + 1) * 512],
                start=True, stop=(c == nt - 1)),
                waits=[ye1, tok_shm] + bw(bank))
            if c < nt - 1:
                sm = b.t(lambda e, nb=nb, rb=rb: e.matmul(
                    pbig[:, nb * 512:(nb + 1) * 512], sb["shm"][:, 128:256],
                    sb["ybwt"][rb ^ 1][:, nb * 512:(nb + 1) * 512],
                    start=False, stop=True),
                    waits=[S[("ybwt_ye", c + 1)]])
            smms.append(sm)
        if c < nt - 1:
            S[("ybwt_free", rb ^ 1)] = smms[-1]
        ybse = []
        for nb in range(4):
            ev = b.s(lambda e, nb=nb, rb=rb: e.copy(
                sb["ybshs"][rb][:, nb * 512:(nb + 1) * 512],
                pbig[:, nb * 512:(nb + 1) * 512]),
                waits=[smms[nb], sget(("ybshs_f", rb))])
            bset("b%d" % nb, ev)
            ybse.append(ev)
        rl_yfw = b.dsync(lambda e, c=c, rb=rb: e.dma_start(
            sb["yfwt"][rb], dr["yfw_sp"][c * CK:(c + 1) * CK, :]),
            waits=[sget(("yfwt_free", rb))])
        rl_xi = b.dsync(lambda e, c=c, rb=rb: e.dma_start(
            sb["xirl"][rb], dr["xi_sp"][c * CK:(c + 1) * CK, :]),
            waits=[sget(("xirl_free", rb))])
        rl_z = b.dsync(lambda e, c=c, rb=rb: e.dma_start(
            sb["zrl"][rb], dr["z_sp"][c * CK:(c + 1) * CK, :]),
            waits=[sget(("zrl_free", rb))])
        rl_ff = b.dsync(lambda e, c=c, rb=rb: e.dma_start(
            sb["ffrl"][rb], dr["ffn_sp"][c * CK:(c + 1) * CK, :]),
            waits=[sget(("ffrl_free", rb)), sget("acc_free")])
        rl_x = b.dscal(lambda e, c=c, rb=rb: e.dma_start(
            sb["xt"][rb], dr["u"][c * CK:(c + 1) * CK, :]),
            waits=[sget(("xt_free", rb))])
        y0 = b.v(lambda e, c=c, rb=rb: e.tensor_tensor(
            r3(sb["ycmb"][rb], 64), r3(sb["xirl"][rb], 64),
            sb["fcd"][:, c, :].unsqueeze(2).broadcast_to([128, 32, 64]), ALU.mult),
            waits=[rl_xi, S[("fcd", c)], sget(("ycmb_free", 0)),
                   sget(("ycmb_free", 1))])
        y1 = b.v(lambda e, rb=rb: e.tensor_tensor(sb["ycmb"][rb], sb["ycmb"][rb],
                                                  sb["yfwt"][rb], ALU.add),
                 waits=[y0, rl_yfw])
        y2 = b.v(lambda e, rb=rb: e.tensor_tensor(
            sb["ycmb"][rb], sb["ycmb"][rb], sb["ybshs"][rb], ALU.add),
            waits=[y1] + ybse)
        S[("ybshs_f", rb)] = y2
        S[("yfwt_free", rb)] = y2
        sz = b.s(lambda e, rb=rb: e.activation(sb["zrl"][rb], sb["zrl"][rb], AF.Silu),
                 waits=[rl_z])
        ssq = b.s(lambda e, rb=rb: e.activation(sb["actt"][rb], sb["ycmb"][rb],
                                                AF.Square, accum_out=sb["gsc"][:, 0:1]),
                  waits=[y2, sget("gsc_free"), sget(("actt_free", rb))])
        r0a = b.v(lambda e: e.tensor_scalar(sb["gsc"][:, 1:2], sb["gsc"][:, 0:1],
                                            1.0 / DI, 1e-5, ALU.mult, ALU.add),
                  waits=[ssq])
        r0 = b.s(lambda e: e.activation(sb["gsc"][:, 1:2], sb["gsc"][:, 1:2], AF.Ln),
                 waits=[r0a])
        r1a = b.v(lambda e: e.tensor_scalar_mul(sb["gsc"][:, 2:3], sb["gsc"][:, 1:2],
                                                -0.5), waits=[r0])
        r1 = b.s(lambda e: e.activation(sb["gsc"][:, 2:3], sb["gsc"][:, 2:3], AF.Exp),
                 waits=[r1a])
        v0 = b.v(lambda e, rb=rb: e.tensor_scalar_mul(sb["actt"][rb], sb["ycmb"][rb],
                                                      sb["gsc"][:, 2:3]),
                 waits=[r1, sget(("actt_free", rb))])
        S[("ycmb_free", 0)] = v0
        S[("ycmb_free", 1)] = v0
        S["gsc_free"] = v0
        v1 = b.v(lambda e, rb=rb: e.tensor_tensor(sb["actt"][rb], sb["actt"][rb],
                                                  sb["nwb"][:], ALU.mult),
                 waits=[v0, tok_nwb])
        v2 = b.v(lambda e, rb=rb: e.tensor_tensor(sb["actt"][rb], sb["actt"][rb],
                                                  sb["zrl"][rb], ALU.mult),
                 waits=[v1, sz])
        S[("zrl_free", rb)] = v2
        aevs = []
        for wave in range(2):
            tts2 = []
            for sl in range(8):
                chk = wave * 8 + sl
                tw = [v2, tok_idb]
                tw.extend(bw("tb") if chk == 0 else ([aevs[-1]] if chk == 8 else []))
                tts2.append(b.t(lambda e, chk=chk, sl=sl, rb=rb: e.transpose(
                    ptb[:, sl * 128:(sl + 1) * 128],
                    sb["actt"][rb][:, chk * 128:(chk + 1) * 128], sb["idb"][:]),
                    waits=tw))
            for sl in range(8):
                chk = wave * 8 + sl
                mk = b.v if wave == 0 else b.s
                fn = ((lambda e, chk=chk, sl=sl, rb=rb: e.tensor_copy(
                    sb["actf"][rb][:, chk, :], ptb[:, sl * 128:(sl + 1) * 128]))
                    if wave == 0 else
                    (lambda e, chk=chk, sl=sl, rb=rb: e.copy(
                        sb["actf"][rb][:, chk, :], ptb[:, sl * 128:(sl + 1) * 128])))
                evx = mk(fn, waits=[tts2[-1], sget(("actf_free", rb))])
                aevs.append(evx)
        bset("tb", aevs[-1])
        S[("actt_free", rb)] = aevs[-1]
        o1 = None
        mm = None
        for chh in range(2):
            pt, bank = ((pyd, "yd") if chh == 0 else (pyo, "yo"))
            mm = None
            for kc in range(16):
                mm = b.t(lambda e, kc=kc, chh=chh, rb=rb, pt=pt: e.matmul(
                    pt[:, 0:512], sb["actf"][rb][:, kc, :],
                    sb["woutv"][:, kc, chh * 512:(chh + 1) * 512],
                    start=(kc == 0), stop=(kc == 15)),
                    waits=[aevs[kc], sget("wout_tok")] + (bw(bank) if kc == 0 else []))
            o0 = b.v(lambda e, chh=chh, rb=rb, pt=pt: e.scalar_tensor_tensor(
                sb["outt"][rb][:, chh * 512:(chh + 1) * 512], pt[:, 0:512], 1.0,
                sb["xt"][rb][:, chh * 512:(chh + 1) * 512], ALU.mult, ALU.add),
                waits=[mm, rl_x, sget(("outt_free", rb))])
            bset(bank, o0)
            o1 = b.v(lambda e, chh=chh, rb=rb: e.tensor_tensor(
                sb["outt"][rb][:, chh * 512:(chh + 1) * 512],
                sb["outt"][rb][:, chh * 512:(chh + 1) * 512],
                sb["ffrl"][rb][:, chh * 512:(chh + 1) * 512], ALU.add),
                waits=[o0, rl_ff])
        S[("actf_free", rb)] = mm
        S[("ffrl_free", rb)] = o1
        S[("xt_free", rb)] = o1
        st_out = b.dscal(lambda e, c=c, rb=rb: e.dma_start(
            out[c * CK:(c + 1) * CK, :], sb["outt"][rb]), waits=[o1])
        S[("outt_free", rb)] = st_out

    def drain(gen):
        if gen is not None:
            for _ in gen:
                pass

    def pump(gen, n=1):
        if gen is None:
            return None
        for _ in range(n):
            try:
                next(gen)
            except StopIteration:
                return None
        return gen

    if SCHED != "prepass":
        fg = None
        for i in range(nt):
            st = i // tps
            if i % tps == 0 and st > 0 and SCHED in ("ffn", "full"):
                drain(fg)
                fg = ffn(st - 1)
                fg = pump(fg)
            for h_ in tile_head(i):
                fg = pump(fg, h_)
            if i > 0 and SCHED in ("conv", "ffn", "full"):
                for h_ in conv_ssd(i - 1):
                    fg = pump(fg, h_)
        drain(fg)
        if SCHED in ("conv", "ffn", "full"):
            e2 = b.v(lambda e: e.memset(sb["asm"][(nt - 1) % 2, :, :, 131:134], 0.0),
                     waits=[S[("asm_center", nt - 1)]])
            S[("asm_ready", nt - 1)] = e2
            drain(conv_ssd(nt - 1))
        if SCHED in ("ffn", "full"):
            drain(ffn(nst - 1))
    if SCHED != "full":
        final_waits = []
        for ring in ("dsy", "dsc", "dgp"):
            for ln in range(NLANES):
                sem = "%s%d" % (ring, ln)
                if b.cnt[sem]:
                    final_waits.append((sem, b.cnt[sem]))
        b.op("sync", lambda e: e.nop(), waits=final_waits, inc=None)
        import contextlib
        with contextlib.ExitStack() as stk:
            for name in list(b.cnt):
                b.handles[name] = stk.enter_context(nc.semaphore(name))
            blk = stk.enter_context(nc.Block())
            b.replay_into(blk)
        return nc

    aw = [S[("xnfm_free", 0)], sget(("xnfm_free", 1)), S["acc_free"], S["xmt_free"],
          S["xnt_free"], S["xnf32_free"], S["gsc_free"]]
    aw.append(sget("csbt_free"))
    for r_ in (0, 1):
        for kk in ("xitm_free", "asm_free", "win_free", "bhat_free",
                   "xifm_free", "xall_free", "yt_free", "zt_free", "mall_free"):
            aw.append(sget((kk, r_)))
    bar = b.v(lambda e: e.engine_nop(), waits=aw)
    for r_ in (0, 1):
        for kk in ("yfwt_free", "ybwt_free", "zrl_free",
                   "ybdt_free", "xirl_free", "ycmb_free", "actt_free",
                   "actf_free", "outt_free", "ffrl_free", "expb_free",
                   "cbw_free", "xt_free"):
            S[(kk, r_)] = bar
    S["gsc_free"] = bar
    S["csrl_free"] = bar
    S[("ybshs_f", 0)] = bar
    S[("ybshs_f", 1)] = bar
    S["wout_tok"] = b.dgp(lambda e: e.dma_start(
        sb["woutv"], dr["wout_b"].rearrange("(c k) n -> k c n", k=128)),
        waits=[bar, cv_wout])
    tok_nwb = b.dgp(lambda e: e.dma_start(sb["nwb"], dr["nw"]), waits=[bar])
    S["sbw"] = b.v(lambda e: e.memset(sb["stA"][:], 0.0),
                   waits=[bar, sget("sfw"), sget("sbf")])
    S["sbwbf"] = b.v(lambda e: e.memset(sb["stB"][:], 0.0), waits=[bar, S["sbw"]])
    for c in range(nt - 1, -1, -1):
        tile_bc(c)
    final_waits = []
    for ring in ("dsy", "dsc", "dgp"):
        for ln in range(NLANES):
            sem = "%s%d" % (ring, ln)
            if b.cnt[sem]:
                final_waits.append((sem, b.cnt[sem]))
    b.op("sync", lambda e: e.nop(), waits=final_waits, inc=None)

    import contextlib
    with contextlib.ExitStack() as stk:
        for name in list(b.cnt):
            b.handles[name] = stk.enter_context(nc.semaphore(name))
        blk = stk.enter_context(nc.Block())
        b.replay_into(blk)
    return nc


def kernel(**inputs):
    A = host_prep(inputs)
    u = np.ascontiguousarray(inputs["L0_attn_u"], dtype=np.float32)
    nt = NT_OVERRIDE if NT_OVERRIDE else L // CK
    nc = build(nt)
    in_maps = []
    for c in range(8):
        m = {"u": np.ascontiguousarray(u[c][: nt * CK])}
        m.update(A)
        in_maps.append(m)
    res = run_bass_kernel_spmd(nc, in_maps, core_ids=list(range(8)))
    return np.stack([r["out"] for r in res.results])


# version 18
# speedup vs baseline: 1.4267x; 1.0005x over previous
import numpy as np
import ml_dtypes
import concourse.bass as bass
import concourse.mybir as mybir
from concourse.bass_utils import run_bass_kernel_spmd

F32 = mybir.dt.float32
BF16 = mybir.dt.bfloat16
U8 = mybir.dt.uint8
AF = mybir.ActivationFunctionType
ALU = mybir.AluOpType
AX = mybir.AxisListType.X

L, D, DI, DIP = 4096, 1024, 2048, 4416
NE, CK, TPS = 8, 128, 4
ENGS = ("tensor", "vector", "scalar", "sync", "gpsimd")
ZERO = ("pe", 0)
NT_OVERRIDE = 0
SCHED = "full"


NLANES = 8


class Bld:
    def __init__(self, nc):
        self.nc = nc
        self.ops = []
        self.cnt = {s: 0 for s in ("pe", "v", "s")}
        self.lane_ord = {}
        self.ring_ctr = {"dsy": 0, "dsc": 0, "dgp": 0}
        for ring in ("dsy", "dsc", "dgp"):
            for ln in range(NLANES):
                self.cnt["%s%d" % (ring, ln)] = 0
        self.handles = {}

    def dma(self, ring, fn, waits=()):
        ln = self.ring_ctr[ring] % NLANES
        self.ring_ctr[ring] += 1
        sem = "%s%d" % (ring, ln)
        prev = self.cnt[sem]
        w = list(waits) + ([(sem, prev)] if prev else [])
        return self.op("sync" if ring == "dsy" else
                       ("scalar" if ring == "dsc" else "gpsimd"),
                       fn, w, sem, 16)

    def op(self, eng, fn, waits=(), inc=None, incn=1):
        w = [(s, v) for (s, v) in waits if s is not None and v > 0]
        tok = None
        if inc is not None:
            self.cnt[inc] += incn
            tok = (inc, self.cnt[inc])
        self.ops.append((eng, fn, w, inc, incn))
        return tok

    def t(self, fn, waits=()):
        return self.op("tensor", fn, waits, "pe")

    def v(self, fn, waits=()):
        return self.op("vector", fn, waits, "v")

    def s(self, fn, waits=()):
        return self.op("scalar", fn, waits, "s")

    def dsync(self, fn, waits=()):
        return self.dma("dsy", fn, waits)

    def dscal(self, fn, waits=()):
        return self.dma("dsc", fn, waits)

    def dgp(self, fn, waits=()):
        return self.dma("dgp", fn, waits)

    def replay_into(self, blk):
        for name in ENGS:
            getattr(blk, name)(self._mk(name))

    def _mk(self, engname):
        def run(eng):
            waited = {}
            for e, fn, waits, inc, incn in self.ops:
                if e != engname:
                    continue
                for s, v in waits:
                    if waited.get(s, 0) < v:
                        eng.wait_ge(self.handles[s], v)
                        waited[s] = v
                ins = fn(eng)
                if inc is not None:
                    ins.then_inc(self.handles[inc], incn)
        return run


class Ring:

    def __init__(self, handle):
        self.h = handle

    def __getitem__(self, key):
        ap = self.h.ap()
        if isinstance(key, tuple):
            key = (key[1], key[0]) + tuple(key[2:])
            return ap[key]
        return ap[:, key]


class AliasRing:

    def __init__(self, *aps):
        self.aps = list(aps)

    def __getitem__(self, key):
        if isinstance(key, tuple):
            return self.aps[key[0]][tuple(key[1:])]
        return self.aps[key]


BCPERM = np.concatenate([np.arange(2048),
                         2048 + np.arange(64), 2176 + np.arange(64),
                         2112 + np.arange(64), 2240 + np.arange(64)])


def host_prep(inputs):
    A = {}
    wt = inputs["L0_attn_in_proj_w"].T
    colperm = np.concatenate([np.arange(2048), 2048 + BCPERM, np.arange(4352, 4416)])
    A["winT"] = np.ascontiguousarray(wt[:, colperm])
    cw = inputs["L0_attn_conv_w"][:, 0, :][BCPERM]
    cd = np.zeros((128, 18, 7, 32), np.float32)
    cwr = cw.reshape(18, 4, 32, 7)
    for bb in range(4):
        for j in range(32):
            cd[32 * bb + j, :, :, j] = cwr[:, bb, j, :]
    A["convd"] = cd
    A["convb"] = np.ascontiguousarray(
        inputs["L0_attn_conv_b"][BCPERM].reshape(18, 128).T)
    A["dtb"] = np.ascontiguousarray(
        np.broadcast_to(np.concatenate([inputs["L0_attn_dt_bias"]] * 2), (128, 64)))
    A["alog"] = np.ascontiguousarray(
        np.concatenate([inputs["L0_attn_A_log"]] * 2)[:, None])
    A["dpar"] = np.ascontiguousarray(np.broadcast_to(inputs["L0_attn_D_param"], (128, 32)))
    A["fcdT"] = np.ascontiguousarray(inputs["L0_attn_fc_D_w"].T)
    A["nw"] = np.ascontiguousarray(np.broadcast_to(inputs["L0_attn_norm_w"], (128, DI)))
    A["woutT"] = np.ascontiguousarray(inputs["L0_attn_out_proj_w"].T)
    A["w1T"] = np.ascontiguousarray(inputs["L0_ffn_W1"].transpose(0, 2, 1))
    A["b1c"] = np.ascontiguousarray(
        inputs["L0_ffn_b1"].reshape(8, 16, 128).transpose(2, 0, 1))
    A["w2T"] = np.ascontiguousarray(inputs["L0_ffn_W2"].transpose(0, 2, 1))
    A["b2"] = np.ascontiguousarray(inputs["L0_ffn_b2"])
    A["wgT"] = np.ascontiguousarray(inputs["L0_ffn_Wg"].T)
    A["bg"] = np.ascontiguousarray(np.broadcast_to(inputs["L0_ffn_bg"], (128, 8)))
    A["idf"] = np.eye(128, dtype=np.float32)
    idx = np.arange(128)
    mz = np.stack([(idx[None, :] < idx[:, None]),
                   (idx[None, :] > idx[:, None])], axis=0).astype(np.uint8)
    A["mskz"] = np.ascontiguousarray(
        np.tile(mz[None, :, :, :], (1, 1, 1, 4)).reshape(2, 128, 512)
        .transpose(1, 0, 2)).copy()
    A["ones1"] = np.ones((1, 128), np.float32)
    shm = np.zeros((128, 256), np.float32)
    shm[idx[1:], idx[:-1]] = 1.0
    shm[0, 255] = 1.0
    A["shm"] = np.ascontiguousarray(shm.astype(ml_dtypes.bfloat16))
    return A


def build(nt):
    nc = bass.Bass("TRN2", target_bir_lowering=False)
    b = Bld(nc)
    Ltot = nt * CK
    tps = min(TPS, nt)
    nst = max(1, (nt + tps - 1) // tps)

    dr = {}

    def din(name, shape, dt=F32):
        dr[name] = nc.dram_tensor(name, list(shape), dt, kind="ExternalInput").ap()

    din("u", (Ltot, D))
    for name, shape, dt in [
        ("winT", (D, DIP), F32), ("convd", (128, 18, 7, 32), F32),
        ("convb", (128, 18), F32), ("dtb", (128, 64), F32), ("alog", (64, 1), F32),
        ("dpar", (128, 32), F32), ("fcdT", (DI, 32), F32), ("nw", (128, DI), F32),
        ("woutT", (DI, D), F32), ("w1T", (NE, D, DI), F32), ("b1c", (128, NE, 16), F32),
        ("w2T", (NE, DI, D), F32), ("b2", (NE, D), F32), ("wgT", (D, NE), F32),
        ("bg", (128, NE), F32), ("idf", (128, 128), F32),
        ("mskz", (128, 2, 512), U8), ("ones1", (1, 128), F32),
        ("shm", (128, 256), BF16),
    ]:
        dr[name] = nc.dram_tensor(name, list(shape), dt, kind="ExternalInput").ap()
    out = nc.dram_tensor("out", [Ltot, D], F32, kind="ExternalOutput").ap()

    def scr(name, shape, dt=BF16):
        dr[name] = nc.dram_tensor(name, list(shape), dt, kind="Internal").ap()

    scr("win_b", (D, DIP))
    scr("w1_b", (NE, D, DI))
    scr("w2_b", (NE, DI, D))
    scr("wout_b", (DI, D))
    scr("z_sp", (Ltot, DI))
    scr("xi_sp", (Ltot, DI))
    scr("yfw_sp", (Ltot + 1, DI))
    scr("ybd_sp", (Ltot, DI))
    scr("csbw_sp", (nt, 64, 2048))
    scr("bc_sp", (Ltot, 256))
    scr("expal_sp", (Ltot, 64))
    scr("ffn_sp", (Ltot, D))

    sb = {}
    hnd = {}

    def asb(name, shape, dt=BF16):
        hnd[name] = nc.alloc_sbuf_tensor("sb_" + name, list(shape), dt)
        sb[name] = hnd[name]

    def aring(name, shape, dt=BF16):
        R, Pp = shape[0], shape[1]
        hnd[name] = nc.alloc_sbuf_tensor("sb_" + name, [Pp, R] + list(shape[2:]), dt)
        sb[name] = Ring(hnd[name])

    asb("idb", (128, 128))
    asb("shm", (128, 256))
    asb("idf", (128, 128), F32)
    asb("mskz", (128, 2, 512), U8)
    asb("zc", (128, 1))
    asb("ones1b", (1, 128))
    asb("cbc", (128, 18), F32)
    asb("dtbc", (128, 64), F32)
    asb("alogc", (64, 1), F32)
    asb("nega", (64, 1), F32)
    asb("dparc", (128, 32), F32)
    asb("bgc", (128, 8), F32)
    asb("wgc", (128, 8, 8), F32)
    asb("b1cc", (128, NE, 16), F32)
    asb("b2r", (NE, D))
    asb("fcdw", (128, 16, 32))
    asb("cdw", (128, 18, 7, 32))
    asb("drows", (1, nt, 64), F32)
    aring("drb", (2, 1, 64))
    asb("fcd", (128, nt, 32), F32)
    aring("comb", (2, 128, tps, 8), F32)
    asb("combT", (8, 128))
    asb("glt", (128, 8), F32)
    asb("gsc", (128, 12), F32)
    aring("dtr", (2, 128, 64), F32)
    asb("dscr", (128, 64), F32)
    aring("dts", (2, 128, 64), F32)
    aring("dtb16", (2, 128, 64))
    asb("abf", (64, CK), F32)
    aring("alf", (2, 64, CK), F32)
    aring("poshi", (2, 64, CK))
    aring("poslo", (2, 64, CK))
    aring("altm", (2, 128, 64), F32)
    aring("naltm", (2, 128, 64), F32)
    aring("expaltm", (2, 128, 64))
    aring("expdtm", (2, 128, 64))
    asb("expdfm", (64, CK), F32)
    asb("sumc", (64, 1), F32)
    aring("bcfm", (2, 128, 2, 128))
    aring("bct", (2, 128, 256))
    asb("stA", (64, 2048), F32)
    asb("stB", (64, 2048))
    asb("decrep", (64, 2048))
    asb("csbt", (64, 2048))
    aring("xt", (2, 128, D), F32)
    aring("win", (2, 128, 8, 512))
    aring("wA", (2, 128, 8, 1024))
    aring("asm", (2, 128, 18, 134))
    aring("xifm", (2, 128, 16, 128))
    aring("xitm", (2, 128, DI))
    aring("xall", (2, 128, 1024))
    asb("mall", (128, 2048))
    aring("bhat", (2, 128, 1024))
    aring("yt", (2, 128, 512))
    aring("zt", (2, 128, 512))
    asb("xmt", (128, D), F32)
    asb("xnt", (128, D), F32)
    asb("xnf32", (128, 8, 128), F32)
    aring("xnfm", (2, 128, 8, tps * CK))
    asb("acc", (128, tps, D))
    asb("hb", (128, 16, 512))
    aring("expb", (2, 128, 32))
    aring("cbw", (2, 128, 64))
    asb("cbwfm", (64, 128))

    def flat(h):
        ap = h.ap()
        nd = len(ap.shape)
        if nd == 2:
            return ap
        spec = "p " + " ".join("d%d" % k for k in range(nd - 1))
        return ap.rearrange(spec + " -> p (" + spec[2:] + ")")

    hbf = flat(hnd["hb"])
    sb["yfwt"] = AliasRing(hbf[:, 0:2048], hbf[:, 2048:4096])
    sb["ybshs"] = AliasRing(hbf[:, 4096:6144], hbf[:, 6144:8192])
    winf = flat(hnd["win"])
    sb["ybwt"] = AliasRing(winf[:, 0:2048], winf[:, 2048:4096])
    sb["ycmb"] = AliasRing(winf[:, 4096:6144], winf[:, 4096:6144])
    asmf = flat(hnd["asm"])
    sb["xirl"] = AliasRing(asmf[:, 0:2048], asmf[:, 2048:4096])
    xifmf = flat(hnd["xifm"])
    sb["zrl"] = AliasRing(xifmf[:, 0:2048], xifmf[:, 2048:4096])
    sb["ybdt"] = sb["xitm"]
    bhatf = flat(hnd["bhat"])
    sb["actt"] = AliasRing(hnd["mall"].ap(), bhatf)
    xallf = flat(hnd["xall"])
    sb["actf"] = AliasRing(
        xallf.rearrange("p (c n) -> p c n", n=128),
        winf[:, 6144:8192].rearrange("p (c n) -> p c n", n=128))
    sb["outt"] = AliasRing(hnd["xmt"].ap(), hnd["xnt"].ap())
    xnf32f = flat(hnd["xnf32"]).bitcast(BF16)
    sb["ffrl"] = AliasRing(xnf32f[:, 0:1024], xnf32f[:, 1024:2048])
    wAf = flat(hnd["wA"])
    sb["woutv"] = wAf.rearrange("p (c n) -> p c n", n=1024)
    cdwf = flat(hnd["cdw"])
    sb["nwb"] = cdwf[:, 0:2048]

    pbig = nc.alloc_psum_tensor("pbig", [128, 2048], F32)
    pmed = nc.alloc_psum_tensor("pmed", [128, 512], F32)
    pyd = nc.alloc_psum_tensor("pyd", [128, 512], F32)
    pyo = nc.alloc_psum_tensor("pyo", [128, 512], F32)
    ptb = nc.alloc_psum_tensor("ptb", [128, 1024], BF16)

    bank_free = {}

    def bw(key):
        return list(bank_free.get(key, []))

    def bset(key, *toks):
        bank_free[key] = [t for t in toks if t is not None]

    def badd(key, tok):
        bank_free.setdefault(key, []).append(tok)

    S = {}

    def sget(k):
        return S.get(k, ZERO)

    def r3(ap, q):
        return ap.rearrange("p (h q) -> p h q", q=q)

    cv_win = b.dgp(lambda e: e.dma_start(dr["win_b"], dr["winT"]))
    cv_wout = b.dgp(lambda e: e.dma_start(dr["wout_b"], dr["woutT"]))
    cv_w1, cv_w2 = [], []
    for ei in range(NE):
        cv_w1.append(b.dgp(lambda e, ei=ei: e.dma_start(dr["w1_b"][ei], dr["w1T"][ei])))
        cv_w2.append(b.dgp(lambda e, ei=ei: e.dma_start(dr["w2_b"][ei], dr["w2T"][ei])))
    tok_idb = b.dgp(lambda e: e.dma_start(sb["idb"][:], dr["idf"]))
    tok_shm = b.dgp(lambda e: e.dma_start(sb["shm"][:], dr["shm"]))
    tok_b2r = b.dgp(lambda e: e.dma_start(sb["b2r"][:], dr["b2"]))
    tok_fcdw = b.dgp(lambda e: e.dma_start(
        sb["fcdw"][:], dr["fcdT"].rearrange("(c k) n -> k c n", k=128)))
    tok_cdw = b.dgp(lambda e: e.dma_start(sb["cdw"][:], dr["convd"]))
    tok_o1b = b.dgp(lambda e: e.dma_start(sb["ones1b"][:], dr["ones1"]))
    tok_msk = b.dsync(lambda e: e.dma_start(sb["mskz"][:], dr["mskz"]))
    tok_idf = b.dsync(lambda e: e.dma_start(sb["idf"][:], dr["idf"]))
    tok_cbc = b.dsync(lambda e: e.dma_start(sb["cbc"][:], dr["convb"]))
    tok_dtb = b.dsync(lambda e: e.dma_start(sb["dtbc"][:], dr["dtb"]))
    tok_alog = b.dsync(lambda e: e.dma_start(sb["alogc"][:], dr["alog"]))
    tok_dpar = b.dsync(lambda e: e.dma_start(sb["dparc"][:], dr["dpar"]))
    tok_bg = b.dsync(lambda e: e.dma_start(sb["bgc"][:], dr["bg"]))
    tok_wg = b.dsync(lambda e: e.dma_start(
        sb["wgc"][:], dr["wgT"].rearrange("(c k) n -> k c n", k=128)))
    tok_b1c = b.dsync(lambda e: e.dma_start(sb["b1cc"][:], dr["b1c"]))

    t_ne = b.s(lambda e: e.activation(sb["nega"][:], sb["alogc"][:], AF.Exp),
               waits=[tok_alog])
    tok_nega = b.v(lambda e: e.tensor_scalar_mul(sb["nega"][:], sb["nega"][:], -1.0),
                   waits=[t_ne])
    tok_zc = b.v(lambda e: e.memset(sb["zc"][:], 0.0))
    tz = b.v(lambda e: e.memset(sb["outt"][0], 0.0))
    zr1 = b.dsync(lambda e: e.dma_start(
        dr["yfw_sp"][0:1, :], sb["outt"][0, 0:1, :].bitcast(BF16)[:, 0:DI]), waits=[tz])
    zrb = b.v(lambda e: e.engine_nop(), waits=[zr1])
    S[("outt_free", 0)] = zrb
    S["xmt_free"] = zrb
    S["sfw"] = b.v(lambda e: e.memset(sb["stA"][:], 0.0))
    S["sbf"] = b.v(lambda e: e.memset(sb["stB"][:], 0.0))
    S["asm_z0"] = b.v(lambda e: e.memset(sb["asm"][0, :, :, 0:3], 0.0))

    wctr = [0]

    def win_load(g):
        pp = wctr[0] % 2
        wctr[0] += 1
        n = 320 if g == 8 else 512
        tok = b.dsync(lambda e, g=g, pp=pp, n=n: e.dma_start(
            sb["win"][pp, :, :, 0:n],
            dr["win_b"][:, g * 512:g * 512 + n].rearrange("(c k) n -> k c n", k=128)),
            waits=[sget(("win_free", pp)), cv_win])
        return tok, pp

    S["win_next"] = win_load(0)

    def tile_head(i):
        st, it, r = i // tps, i % tps, i % 2
        stb = st % 2
        tok_x = b.dscal(lambda e, i=i, r=r: e.dma_start(
            sb["xt"][r], dr["u"][i * CK:(i + 1) * CK, :]),
            waits=[sget(("xt_free", r))])
        yield 3
        def trunc(tok):
            S[("asm_center", i)] = tok
            S[("dtr", i)] = tok
            S[("comb_tile", i // tps % 2, it)] = tok
            S[("xnfm_tile", i // tps % 2, it)] = tok
        if SCHED == "ln0":
            dmy = b.v(lambda e: e.engine_nop(), waits=[tok_x])
            trunc(dmy)
            return
        w0 = b.v(lambda e, r=r: e.tensor_reduce(sb["gsc"][:, 0:1], sb["xt"][r], AX,
                                                ALU.add),
                 waits=[tok_x, sget("gsc_free")])
        w1_ = b.v(lambda e: e.tensor_scalar_mul(sb["gsc"][:, 1:2], sb["gsc"][:, 0:1],
                                                1.0 / D), waits=[w0])
        w2_ = b.v(lambda e, r=r: e.tensor_scalar(sb["xmt"][:], sb["xt"][r],
                                                 sb["gsc"][:, 1:2], None, ALU.subtract),
                  waits=[w1_, sget("xmt_free")])
        S[("xt_free", r)] = w2_
        if SCHED == "ln1":
            trunc(w2_)
            S["xmt_free"] = w2_
            return
        w3 = b.s(lambda e: e.activation(sb["xnt"][:], sb["xmt"][:], AF.Square,
                                        accum_out=sb["gsc"][:, 2:3]),
                 waits=[w2_, sget("xnt_free")])
        w4a = b.v(lambda e: e.tensor_scalar(sb["gsc"][:, 3:4], sb["gsc"][:, 2:3],
                                            1.0 / D, 1e-5, ALU.mult, ALU.add),
                  waits=[w3])
        w4 = b.s(lambda e: e.activation(sb["gsc"][:, 3:4], sb["gsc"][:, 3:4], AF.Ln),
                 waits=[w4a])
        w5a = b.v(lambda e: e.tensor_scalar_mul(sb["gsc"][:, 4:5], sb["gsc"][:, 3:4],
                                                -0.5), waits=[w4])
        w5 = b.s(lambda e: e.activation(sb["gsc"][:, 4:5], sb["gsc"][:, 4:5], AF.Exp),
                 waits=[w5a])
        if SCHED == "ln2":
            trunc(w5)
            S["xmt_free"] = w5
            S["gsc_free"] = w5
            return
        w6 = b.v(lambda e: e.tensor_scalar_mul(sb["xnt"][:], sb["xmt"][:],
                                               sb["gsc"][:, 4:5]),
                 waits=[w5, sget("xnt_free")])
        S["xmt_free"] = w6
        if SCHED == "ln3":
            trunc(w6)
            S["gsc_free"] = w6
            return
        evb = None
        for kc in range(8):
            bank, pt = (("yd", pyd) if kc % 2 == 0 else ("yo", pyo))
            mmw = [w6, tok_idf] + bw(bank)
            if it == 0 and kc == 0:
                mmw.append(sget(("xnfm_free", stb)))
            if kc < 2:
                mmw.append(sget("xnf32_free"))
            tt = b.t(lambda e, kc=kc, pt=pt: e.transpose(
                pt[:, 0:128], sb["xnt"][:, kc * 128:(kc + 1) * 128], sb["idf"][:]),
                waits=mmw)
            ev1 = b.v(lambda e, kc=kc, it=it, pt=pt, stb=stb: e.tensor_copy(
                sb["xnfm"][stb, :, kc, it * CK:(it + 1) * CK], pt[:, 0:128]),
                waits=[tt])
            ev2 = b.s(lambda e, kc=kc, pt=pt: e.copy(
                sb["xnf32"][:, kc, :], pt[:, 0:128]), waits=[tt, ev1])
            bset(bank, ev1, ev2)
            S[("xnfm_k", kc)] = ev1
            S[("xnf32_k", kc)] = ev2
            evb = ev1
            if kc == 3:
                yield 0
        S["xnt_free"] = evb
        S[("xnfm_tile", stb, it)] = evb
        yield 0
        if SCHED == "ln":
            trunc(evb)
            return
        gm = None
        for kc in range(8):
            gm = b.t(lambda e, kc=kc: e.matmul(
                pmed[:, 128:136], sb["xnf32"][:, kc, :], sb["wgc"][:, kc, :],
                start=(kc == 0), stop=(kc == 7)),
                waits=[tok_wg, S[("xnf32_k", kc)]] + (bw("med") if kc == 0 else []))
        S["xnf32_free"] = gm
        g0 = b.v(lambda e: e.tensor_tensor(sb["glt"][:], pmed[:, 128:136], sb["bgc"][:],
                                           ALU.add),
                 waits=[gm, tok_bg, sget("glt_free")])
        bset("med", g0)
        m1 = b.v(lambda e: e.tensor_reduce(sb["gsc"][:, 5:6], sb["glt"][:], AX, ALU.max),
                 waits=[g0])
        nm1 = b.v(lambda e: e.tensor_reduce(sb["gsc"][:, 6:7], sb["glt"][:], AX, ALU.max,
                                            negate=True), waits=[g0])
        eq = b.v(lambda e: e.tensor_scalar(sb["gsc"][:, 8:12].bitcast(F32) if False
                                           else sb["dscr"][:, 0:8], sb["glt"][:],
                                           sb["gsc"][:, 5:6], None, ALU.is_equal),
                 waits=[m1, sget("dscr_free")])
        mk = b.v(lambda e: e.scalar_tensor_tensor(sb["dscr"][:, 0:8], sb["dscr"][:, 0:8],
                                                  -1e30, sb["glt"][:], ALU.mult, ALU.add),
                 waits=[eq])
        m2 = b.v(lambda e: e.tensor_reduce(sb["gsc"][:, 7:8], sb["dscr"][:, 0:8], AX,
                                           ALU.max), waits=[mk])
        ge = b.v(lambda e: e.tensor_scalar(sb["dscr"][:, 0:8], sb["glt"][:],
                                           sb["gsc"][:, 7:8], None, ALU.is_ge),
                 waits=[m2])
        dm = b.v(lambda e: e.tensor_tensor(sb["gsc"][:, 7:8], sb["gsc"][:, 7:8],
                                           sb["gsc"][:, 6:7], ALU.add),
                 waits=[m2, nm1, ge])
        ew = b.s(lambda e: e.activation(sb["glt"][:], sb["glt"][:], AF.Exp,
                                        bias=sb["gsc"][:, 6:7]), waits=[nm1, ge, mk])
        ed = b.s(lambda e: e.activation(sb["gsc"][:, 7:8], sb["gsc"][:, 7:8], AF.Exp),
                 waits=[dm])
        d1 = b.v(lambda e: e.tensor_scalar_add(sb["gsc"][:, 7:8], sb["gsc"][:, 7:8],
                                               1.0), waits=[ed])
        rc = b.v(lambda e: e.reciprocal(sb["gsc"][:, 7:8], sb["gsc"][:, 7:8]),
                 waits=[d1])
        c1 = b.v(lambda e: e.tensor_tensor(sb["glt"][:], sb["glt"][:], sb["dscr"][:, 0:8],
                                           ALU.mult), waits=[ew])
        c2 = b.v(lambda e, stb=stb, it=it: e.tensor_scalar_mul(
            sb["comb"][stb, :, it, :], sb["glt"][:], sb["gsc"][:, 7:8]),
            waits=[c1, rc, sget(("comb_free", stb))])
        S["glt_free"] = c2
        S["gsc_free"] = c2
        S["dscr_free"] = c2
        S[("comb_tile", stb, it)] = c2
        yield 0
        if SCHED == "lngate":
            S[("asm_center", i)] = c2
            S[("dtr", i)] = c2
            return

        ar = i % 2
        ctrV, ctrS = [None], [None]
        for g in range(9):
            yield 1 if g < 4 else 2
            gtok, pp = S["win_next"]
            if g + 1 < 9:
                S["win_next"] = win_load(g + 1)
            elif i + 1 < nt:
                S["win_next"] = win_load(0)
            if g < 4:
                bank = "b%d" % (g % 4)
                mm = None
                for kc in range(8):
                    mm = b.t(lambda e, kc=kc, g=g, pp=pp, it=it: e.matmul(
                        pbig[:, (g % 4) * 512:(g % 4) * 512 + 512],
                        sb["xnfm"][stb, :, kc, it * CK:(it + 1) * CK],
                        sb["win"][pp, :, kc, :],
                        start=(kc == 0), stop=(kc == 7)),
                        waits=[gtok, S[("xnfm_k", kc)]] + (bw(bank) if kc == 0 else []))
                S[("win_free", pp)] = mm
                for hz in range(1):
                    pass
                mke = b.v if g % 2 == 0 else b.s
                fne = ((lambda e, g=g: e.tensor_copy(
                    sb["zt"][g % 2], pbig[:, (g % 4) * 512:(g % 4) * 512 + 512]))
                    if g % 2 == 0 else
                    (lambda e, g=g: e.copy(
                        sb["zt"][g % 2], pbig[:, (g % 4) * 512:(g % 4) * 512 + 512])))
                ev = mke(fne, waits=[mm, sget(("zt_free", g % 2))])
                bset(bank, ev)
                sp = b.dsync(lambda e, i=i, g=g: e.dma_start(
                    dr["z_sp"][i * CK:(i + 1) * CK, g * 512:(g + 1) * 512],
                    sb["zt"][g % 2]), waits=[ev])
                S[("zt_free", g % 2)] = sp
            else:
                nsub = 3 if g == 8 else 4
                for sub in range(nsub):
                    ch_abs = (g - 4) * 4 + sub
                    bank = "b%d" % (ch_abs % 4)
                    mm = None
                    if ch_abs < 18:
                        for kc in range(8):
                            mm = b.t(lambda e, kc=kc, pp=pp, sub=sub, it=it, stb=stb,
                                     ch_abs=ch_abs: e.matmul(
                                pbig[:, (ch_abs % 4) * 512:(ch_abs % 4) * 512 + 128],
                                sb["win"][pp, :, kc, sub * 128:sub * 128 + 128],
                                sb["xnfm"][stb, :, kc, it * CK:(it + 1) * CK],
                                start=(kc == 0), stop=(kc == 7)),
                                waits=[gtok, S[("xnfm_k", kc)]]
                                + (bw(bank) if kc == 0 else []))
                        mke = b.v if ch_abs % 2 == 0 else b.s
                        fne = ((lambda e, ar=ar, ch_abs=ch_abs: e.tensor_copy(
                            sb["asm"][ar, :, ch_abs, 3:131],
                            pbig[:, (ch_abs % 4) * 512:(ch_abs % 4) * 512 + 128]))
                            if ch_abs % 2 == 0 else
                            (lambda e, ar=ar, ch_abs=ch_abs: e.copy(
                                sb["asm"][ar, :, ch_abs, 3:131],
                                pbig[:, (ch_abs % 4) * 512:(ch_abs % 4) * 512 + 128])))
                        ev = mke(fne, waits=[mm, sget(("asm_free", ar))])
                        if ch_abs % 2 == 0:
                            ctrV[0] = ev
                        else:
                            ctrS[0] = ev
                    else:
                        for kc in range(8):
                            mm = b.t(lambda e, kc=kc, pp=pp, it=it, stb=stb,
                                     ch_abs=ch_abs: e.matmul(
                                pbig[:, (ch_abs % 4) * 512:(ch_abs % 4) * 512 + 64],
                                sb["xnfm"][stb, :, kc, it * CK:(it + 1) * CK],
                                sb["win"][pp, :, kc, 256:320],
                                start=(kc == 0), stop=(kc == 7)),
                                waits=[gtok, S[("xnfm_k", kc)]]
                                + (bw(bank) if kc == 0 else []))
                        ev = b.v(lambda e, r=r, ch_abs=ch_abs: e.tensor_copy(
                            sb["dtr"][r],
                            pbig[:, (ch_abs % 4) * 512:(ch_abs % 4) * 512 + 64]),
                            waits=[mm, sget(("dtr_free", r))])
                        S[("dtr", i)] = ev
                    S[("win_free", pp)] = mm
                    bset(bank, ev)
        last_ctr = b.v(lambda e: e.engine_nop(), waits=[ctrV[0], ctrS[0]])
        S[("asm_center", i)] = last_ctr
        if i > 0:
            pv = (i - 1) % 2
            e1 = b.v(lambda e, ar=ar, pv=pv: e.tensor_copy(
                sb["asm"][ar, :, :, 0:3], sb["asm"][pv, :, :, 128:131]),
                waits=[last_ctr, S[("asm_center", i - 1)]])
            e2 = b.v(lambda e, ar=ar, pv=pv: e.tensor_copy(
                sb["asm"][pv, :, :, 131:134], sb["asm"][ar, :, :, 3:6]),
                waits=[last_ctr, S[("asm_center", i - 1)]])
            S[("asm_ready", i - 1)] = e2
            S[("asm_left", i)] = e1
        else:
            S[("asm_left", 0)] = S["asm_z0"]

    def conv_ssd(c):
        rr = c % 2
        ar = c % 2
        asm_ok = [S[("asm_ready", c)], S[("asm_left", c)], tok_cdw, tok_cbc]
        ev = None
        for ch in range(18):
            bank = "b%d" % (ch % 4)
            mm = None
            for bb in range(4):
                for k in range(7):
                    mm = b.t(lambda e, ch=ch, k=k, bb=bb, ar=ar: e.matmul(
                        pbig[32 * bb:32 * bb + 32,
                             (ch % 4) * 512:(ch % 4) * 512 + 128],
                        sb["cdw"][32 * bb:32 * bb + 32, ch, k, :],
                        sb["asm"][ar, 32 * bb:32 * bb + 32, ch, k:k + 128],
                        start=(k == 0), stop=(k == 6),
                        tile_position=(32 * bb, 32 * bb)),
                        waits=asm_ok + (bw(bank) if (k == 0 and bb == 0) else []))
            if ch < 16:
                ev = b.s(lambda e, ch=ch, rr=rr: e.activation(
                    sb["xifm"][rr, :, ch, :],
                    pbig[:, (ch % 4) * 512:(ch % 4) * 512 + 128], AF.Silu,
                    bias=sb["cbc"][:, ch:ch + 1]),
                    waits=[mm, sget(("xifm_free", rr))])
                S[("xifm_ch", rr, ch)] = ev
            else:
                ev = b.s(lambda e, ch=ch, rr=rr: e.activation(
                    sb["bcfm"][rr, :, ch - 16, :],
                    pbig[:, (ch % 4) * 512:(ch % 4) * 512 + 128], AF.Silu,
                    bias=sb["cbc"][:, ch:ch + 1]),
                    waits=[mm, sget(("bcfm_free", rr))])
                S[("bcfm_ch", rr, ch - 16)] = ev
            bset(bank, ev)
            if ch % 3 == 2:
                yield 0
        S[("asm_free", ar)] = ev

        tts = []
        for j in range(2):
            tts.append(b.t(lambda e, j=j, rr=rr: e.transpose(
                ptb[:, j * 128:(j + 1) * 128], sb["bcfm"][rr, :, j, :], sb["idb"][:]),
                waits=[S[("bcfm_ch", rr, j)], tok_idb] + bw("tb")))
        bc2 = None
        for j in range(2):
            bc2 = b.v(lambda e, j=j, rr=rr: e.tensor_copy(
                sb["bct"][rr, :, j * 128:(j + 1) * 128], ptb[:, j * 128:(j + 1) * 128]),
                waits=[tts[-1], sget(("bct_free", rr))])
        bset("tb", bc2)
        S[("bct", c)] = bc2
        yield 0
        spbc = b.dsync(lambda e, c=c, rr=rr: e.dma_start(
            dr["bc_sp"][c * CK:(c + 1) * CK, :], sb["bct"][rr]), waits=[bc2])

        evs = []
        for wave in range(2):
            tts = []
            for sl in range(8):
                ch = wave * 8 + sl
                tw = [S[("xifm_ch", rr, ch)], tok_idb]
                tw.extend(bw("tb") if ch == 0 else ([evs[-1]] if ch == 8 else []))
                tts.append(b.t(lambda e, ch=ch, sl=sl, rr=rr: e.transpose(
                    ptb[:, sl * 128:(sl + 1) * 128], sb["xifm"][rr, :, ch, :],
                    sb["idb"][:]), waits=tw))
            for sl in range(8):
                ch = wave * 8 + sl
                evx = b.v(lambda e, ch=ch, sl=sl, rr=rr: e.tensor_copy(
                    sb["xitm"][rr, :, ch * 128:(ch + 1) * 128],
                    ptb[:, sl * 128:(sl + 1) * 128]),
                    waits=[tts[-1], sget(("xitm_free", rr))])
                evs.append(evx)
            yield 1
        bset("tb", evs[-1])
        xitm_tok = evs[-1]
        S[("xitm", c)] = xitm_tok
        spxi = b.dsync(lambda e, c=c, rr=rr: e.dma_start(
            dr["xi_sp"][c * CK:(c + 1) * CK, :], sb["xitm"][rr]), waits=[xitm_tok])

        fm = None
        for kc in range(16):
            fm = b.t(lambda e, kc=kc, rr=rr: e.matmul(
                pmed[:, 256:288], sb["xifm"][rr, :, kc, :], sb["fcdw"][:, kc, :],
                start=(kc == 0), stop=(kc == 15)),
                waits=[S[("xifm_ch", rr, kc)], tok_fcdw] + (bw("med") if kc == 0 else []))
        S[("xifm_free", rr)] = fm
        fe = b.v(lambda e, c=c: e.tensor_tensor(
            sb["fcd"][:, c, :], pmed[:, 256:288], sb["dparc"][:], ALU.add),
            waits=[fm, tok_dpar])
        bset("med", fe)
        S[("fcd", c)] = fe
        yield 1

        a0 = b.v(lambda e, c=c: e.tensor_tensor(sb["dscr"][:, 0:64], sb["dtr"][c % 2],
                                                sb["dtbc"][:], ALU.add),
                 waits=[S[("dtr", c)], tok_dtb, sget("dscr_free")])
        a1 = b.s(lambda e, rr=rr: e.activation(sb["dts"][rr], sb["dscr"][:, 0:64],
                                               AF.Abs),
                 waits=[a0, sget(("dts_free", rr))])
        a2 = b.s(lambda e, rr=rr: e.activation(sb["dts"][rr], sb["dts"][rr], AF.Exp,
                                               scale=-1.0), waits=[a1])
        a3 = b.s(lambda e, rr=rr: e.activation(sb["dts"][rr], sb["dts"][rr], AF.Ln,
                                               bias=1.0), waits=[a2])
        a4 = b.v(lambda e, rr=rr: e.scalar_tensor_tensor(
            sb["dts"][rr], sb["dscr"][:, 0:64], 0.0, sb["dts"][rr], ALU.max, ALU.add),
            waits=[a3])
        S[("dtr_free", c % 2)] = a4
        S["dscr_free"] = a4
        a5 = b.v(lambda e, rr=rr: e.tensor_copy(sb["dtb16"][rr], sb["dts"][rr]),
                 waits=[a4, sget(("dtb16_free", rr))])
        yield 1
        td = b.t(lambda e, rr=rr: e.transpose(pyd[0:64, 0:128], sb["dts"][rr],
                                              sb["idf"][:]),
                 waits=[a4, tok_idf] + bw("yd"))
        ab0 = b.v(lambda e: e.tensor_scalar_mul(sb["abf"][:], pyd[0:64, 0:128],
                                                sb["nega"][:, 0:1]),
                  waits=[td, tok_nega, sget("abf_free")])
        bset("yd", ab0)
        sc0 = b.v(lambda e, rr=rr: e.tensor_tensor_scan(
            sb["alf"][rr], sb["abf"][:], sb["abf"][:], 0.0, ALU.add, ALU.bypass),
            waits=[ab0, sget(("alf_free", rr))])
        l0 = b.v(lambda e, rr=rr: e.tensor_scalar(
            sb["expdfm"][32:64, :], sb["alf"][rr][32:64, :],
            sb["alf"][rr][32:64, 127:128], -1.0, ALU.subtract, ALU.mult),
            waits=[sc0, sget("expdfm_free")])
        l1 = b.v(lambda e, rr=rr: e.tensor_tensor(
            sb["alf"][rr][32:64, :], sb["expdfm"][32:64, :], sb["abf"][32:64, :],
            ALU.add), waits=[l0])
        S["abf_free"] = l1
        alf_tok = l1
        p0 = b.v(lambda e, rr=rr: e.tensor_copy(sb["poshi"][rr], sb["alf"][rr]),
                 waits=[alf_tok, sget(("poshi_free", rr))])
        p1 = b.v(lambda e, rr=rr: e.scalar_tensor_tensor(
            sb["poslo"][rr], sb["alf"][rr], 1.0, sb["poshi"][rr], ALU.bypass,
            ALU.subtract), waits=[p0, sget(("poslo_free", rr))])
        ta = b.t(lambda e, rr=rr: e.transpose(pyo[0:128, 0:64], sb["alf"][rr],
                                              sb["idf"][0:64, 0:64]),
                 waits=[alf_tok, tok_idf] + bw("yo"))
        at0 = b.v(lambda e, rr=rr: e.tensor_copy(sb["altm"][rr], pyo[0:128, 0:64]),
                  waits=[ta, sget(("altm_free", rr))])
        bset("yo", at0)
        at1 = b.v(lambda e, rr=rr: e.tensor_scalar_mul(sb["naltm"][rr], sb["altm"][rr],
                                                       -1.0),
                  waits=[at0, sget(("naltm_free", rr))])
        ae = b.s(lambda e, rr=rr: e.activation(sb["expaltm"][rr], sb["altm"][rr],
                                               AF.Exp),
                 waits=[at0, sget(("expaltm_free", rr))])
        spal = b.dsync(lambda e, c=c, rr=rr: e.dma_start(
            dr["expal_sp"][c * CK:(c + 1) * CK, :], sb["expaltm"][rr]), waits=[ae])
        sm0 = b.v(lambda e, rr=rr: e.tensor_copy(sb["sumc"][0:32, :],
                                                 sb["alf"][rr][0:32, 127:128]),
                  waits=[sc0, sget("sumc_free")])
        sm1 = b.v(lambda e, rr=rr: e.tensor_copy(sb["sumc"][32:64, :],
                                                 sb["alf"][rr][32:64, 0:1]),
                  waits=[l1, sm0])
        ed0 = b.s(lambda e, rr=rr: e.activation(
            sb["expdfm"][:], sb["alf"][rr], AF.Exp, scale=-1.0,
            bias=sb["sumc"][:, 0:1]), waits=[l1, sm1])
        te = b.t(lambda e: e.transpose(pyd[0:128, 0:64], sb["expdfm"][:],
                                       sb["idf"][0:64, 0:64]),
                 waits=[ed0] + bw("yd"))
        et0 = b.v(lambda e, rr=rr: e.tensor_copy(sb["expdtm"][rr], pyd[0:128, 0:64]),
                  waits=[te, sget(("expdtm_free", rr))])
        bset("yd", et0)
        S["expdfm_free"] = et0
        tdr = b.t(lambda e: e.transpose(pyo[0:1, 0:64], sb["sumc"][:, 0:1],
                                        sb["idf"][0:64, 0:64]),
                  waits=[sm1] + bw("yo"))
        dr0 = b.s(lambda e, c=c: e.activation(sb["drows"][:, c, :], pyo[0:1, 0:64],
                                              AF.Exp), waits=[tdr])
        bset("yo", dr0)
        dr1 = b.v(lambda e, c=c, rr=rr: e.tensor_copy(sb["drb"][rr],
                                                      sb["drows"][:, c, :]),
                  waits=[dr0, sget(("drb_free", rr))])
        S["sumc_free"] = dr0
        yield 1

        yfree = None
        for d in (0, 1):
            h0 = 32 * d
            gmm = b.t(lambda e, d=d, rr=rr: e.matmul(
                pmed[:, 0:128], sb["bcfm"][rr, 64 * d:64 * d + 64, 0, :],
                sb["bcfm"][rr, 64 * d:64 * d + 64, 1, :],
                start=True, stop=True),
                waits=[S[("bcfm_ch", rr, 0)], S[("bcfm_ch", rr, 1)]] + bw("med"))
            yield 1
            for q in range(4):
                yield 0
                pair = q % 2
                cb = pair * 1024
                bkA, bkB = "b%d" % (2 * pair), "b%d" % (2 * pair + 1)
                stops = []
                for h8 in range(8):
                    h = 8 * q + h8
                    colw = bw(bkA) + bw(bkB) if h8 == 0 else []
                    b.t(lambda e, h=h, h8=h8, rr=rr, h0=h0, cb=cb: e.matmul(
                        pbig[:, cb + h8 * 128:cb + h8 * 128 + 128],
                        sb["idb"][0:64, h0 + h:h0 + h + 1].broadcast_to([64, 128]),
                        sb["poshi"][rr],
                        start=(h8 % 4 == 0), stop=False),
                        waits=[p0, tok_idb] + colw)
                    m2 = b.t(lambda e, h=h, h8=h8, rr=rr, h0=h0, cb=cb: e.matmul(
                        pbig[:, cb + h8 * 128:cb + h8 * 128 + 128],
                        sb["idb"][0:64, h0 + h:h0 + h + 1].broadcast_to([64, 128]),
                        sb["poslo"][rr],
                        start=False, stop=(h8 % 4 == 3)),
                        waits=[p1])
                    if h8 % 4 == 3:
                        stops.append(m2)
                etoks = []
                for h8 in range(8):
                    h = 8 * q + h8
                    ex = b.s(lambda e, h=h, h8=h8, rr=rr, h0=h0, cb=cb, pair=pair:
                             e.activation(
                                 sb["mall"][:, pair * 1024 + h8 * 128:
                                            pair * 1024 + h8 * 128 + 128],
                                 pbig[:, cb + h8 * 128:cb + h8 * 128 + 128], AF.Exp,
                                 bias=sb["naltm"][rr][:, h0 + h:h0 + h + 1]),
                             waits=[stops[h8 // 4], at1, sget(("mall_free", pair))])
                    etoks.append(ex)
                bset(bkA, etoks[3])
                bset(bkB, etoks[7])
                mzt = None
                for mq in range(2):
                    mzt = b.v(lambda e, d=d, mq=mq, pair=pair: e.copy_predicated(
                        sb["mall"][:, pair * 1024 + mq * 512:
                                   pair * 1024 + mq * 512 + 512],
                        sb["mskz"][:, d, :],
                        sb["zc"][:, 0:1].broadcast_to([128, 512])),
                        waits=etoks + [tok_msk, tok_zc])
                mg = b.v(lambda e, pair=pair: e.tensor_tensor(
                    r3(sb["mall"][:, pair * 1024:pair * 1024 + 1024], 128),
                    r3(sb["mall"][:, pair * 1024:pair * 1024 + 1024], 128),
                    pmed[:, 0:128].unsqueeze(1).broadcast_to([128, 8, 128]),
                    ALU.mult),
                    waits=[mzt, gmm])
                badd("med", mg)
                xb = b.v(lambda e, d=d, q=q, rr=rr, h0=h0, pair=pair: e.tensor_tensor(
                    r3(sb["xall"][pair], 64)[:, 0:8, :],
                    r3(sb["xitm"][rr][:, q * 512:(q + 1) * 512], 64),
                    sb["dtb16"][rr][:, h0 + 8 * q:h0 + 8 * q + 8]
                    .unsqueeze(2).broadcast_to([128, 8, 64]), ALU.mult),
                    waits=[S[("xitm", c)], a5, sget(("xall_free", pair))])
                bh = b.v(lambda e, d=d, q=q, rr=rr, h0=h0, pair=pair: e.tensor_tensor(
                    r3(sb["bhat"][pair], 64)[:, 0:8, :],
                    sb["bct"][rr][:, 64 * d:64 * d + 64]
                    .unsqueeze(1).broadcast_to([128, 8, 64]),
                    sb["expdtm"][rr][:, h0 + 8 * q:h0 + 8 * q + 8]
                    .unsqueeze(2).broadcast_to([128, 8, 64]), ALU.mult),
                    waits=[S[("bct", c)], et0, sget(("bhat_free", pair))])
                yield 3
                ydm = None
                for j in range(8):
                    ydm = b.t(lambda e, j=j, pair=pair: e.matmul(
                        pyd[:, j * 64:(j + 1) * 64],
                        sb["mall"][:, pair * 1024 + j * 128:pair * 1024 + j * 128 + 128],
                        sb["xall"][pair][:, j * 64:j * 64 + 64],
                        start=(j == 0), stop=(j == 7)),
                        waits=[mg, xb] + (bw("yd") if j == 0 else []))
                if d == 0:
                    yom = None
                    for j in range(8):
                        hq = 8 * q + j
                        yom = b.t(lambda e, hq=hq, j=j, rr=rr: e.matmul(
                            pyo[:, j * 64:(j + 1) * 64],
                            sb["bcfm"][rr, 0:64, 1, :],
                            sb["stB"][:, hq * 64:hq * 64 + 64],
                            start=(j == 0), stop=(j == 7)),
                            waits=[S[("bcfm_ch", rr, 1)], sget("sbf"), mg]
                            + (bw("yo") if j == 0 else []))
                    ye0 = b.v(lambda e, q=q, rr=rr, h0=h0: e.tensor_tensor(
                        r3(sb["yt"][q % 2], 64), r3(pyo[:, 0:512], 64),
                        sb["expaltm"][rr][:, h0 + 8 * q:h0 + 8 * q + 8]
                        .unsqueeze(2).broadcast_to([128, 8, 64]), ALU.mult),
                        waits=[yom, ae, sget(("yt_free", q % 2))])
                    bset("yo", ye0)
                    ye1 = b.v(lambda e, q=q: e.tensor_tensor(
                        sb["yt"][q % 2], sb["yt"][q % 2], pyd[:, 0:512], ALU.add),
                        waits=[ye0, ydm])
                    bset("yd", ye1)
                    spy = b.dsync(lambda e, c=c, q=q: e.dma_start(
                        dr["yfw_sp"][c * CK + 1:(c + 1) * CK + 1,
                                     q * 512:(q + 1) * 512],
                        sb["yt"][q % 2]), waits=[ye1])
                    S[("yt_free", q % 2)] = spy
                else:
                    ye1 = b.v(lambda e, q=q: e.tensor_copy(
                        sb["yt"][q % 2], pyd[:, 0:512]),
                        waits=[ydm, sget(("yt_free", q % 2))])
                    bset("yd", ye1)
                    spy = b.dsync(lambda e, c=c, q=q: e.dma_start(
                        dr["ybd_sp"][c * CK:(c + 1) * CK, q * 512:(q + 1) * 512],
                        sb["yt"][q % 2]), waits=[ye1])
                    S[("yt_free", q % 2)] = spy
                csm = None
                for j in range(8):
                    csm = b.t(lambda e, j=j, q=q, cb=cb, pair=pair: e.matmul(
                        pbig[0:64, cb + j * 64:cb + j * 64 + 64],
                        sb["bhat"][pair][:, j * 64:j * 64 + 64],
                        sb["xall"][pair][:, j * 64:j * 64 + 64],
                        start=(j == 0), stop=(j == 7)),
                        waits=[bh, xb, mg] + (bw(bkA) if j == 0 else []))
                S[("xall_free", pair)] = csm
                S[("bhat_free", pair)] = csm
                S[("mall_free", pair)] = csm
                if d == 0:
                    j = q
                    pt, bank = ((pyd, "yd") if j % 2 == 0 else (pyo, "yo"))
                    mm = b.t(lambda e, j=j, rr=rr, pt=pt: e.matmul(
                        pt[0:64, 0:512], sb["ones1b"][0:1, 0:64],
                        sb["drb"][rr][0:1, 8 * j:8 * j + 8]
                        .unsqueeze(2).broadcast_to([1, 8, 64]),
                        start=True, stop=True),
                        waits=[dr1, tok_o1b] + bw(bank))
                    evd = b.v(lambda e, j=j, pt=pt: e.tensor_copy(
                        sb["decrep"][:, j * 512:(j + 1) * 512], pt[0:64, 0:512]),
                        waits=[mm, sget("decrep_free")])
                    bset(bank, evd)
                    s_0 = b.v(lambda e, q=q: e.tensor_tensor(
                        sb["stA"][:, q * 512:(q + 1) * 512],
                        sb["stA"][:, q * 512:(q + 1) * 512],
                        sb["decrep"][:, q * 512:(q + 1) * 512], ALU.mult),
                        waits=[evd, sget("sfw"), yom])
                    s_1 = b.v(lambda e, q=q, cb=cb: e.tensor_tensor(
                        sb["stA"][:, q * 512:(q + 1) * 512],
                        sb["stA"][:, q * 512:(q + 1) * 512],
                        pbig[0:64, cb:cb + 512], ALU.add),
                        waits=[s_0, csm])
                    bset(bkA, s_1)
                    S[("sfw_h", q)] = s_1
                    s_2 = b.v(lambda e, q=q: e.tensor_copy(
                        sb["stB"][:, q * 512:(q + 1) * 512],
                        sb["stA"][:, q * 512:(q + 1) * 512]), waits=[s_1])
                    S[("sbf_h", q)] = s_2
                else:
                    cb0 = b.v(lambda e, q=q, cb=cb: e.tensor_copy(
                        sb["csbt"][:, q * 512:(q + 1) * 512],
                        pbig[0:64, cb:cb + 512]),
                        waits=[csm, sget("csbt_free")])
                    bset(bkA, cb0)
                    S[("csbt_h", q)] = cb0
            if d == 0:
                sfb = b.v(lambda e: e.engine_nop(),
                          waits=[S[("sfw_h", q)] for q in range(4)])
                S["sfw"] = sfb
                sbb = b.v(lambda e: e.engine_nop(),
                          waits=[S[("sbf_h", q)] for q in range(4)])
                S["sbf"] = sbb
                S["decrep_free"] = sfb
            else:
                spcs = b.dsync(lambda e, c=c: e.dma_start(dr["csbw_sp"][c],
                                                          sb["csbt"][:]),
                               waits=[S[("csbt_h", q)] for q in range(4)])
                S["csbt_free"] = spcs
                S[("csbw", c)] = spcs
                yfree = spcs
        S[("dts_free", rr)] = alf_tok
        S[("dtb16_free", rr)] = yfree
        S[("alf_free", rr)] = ed0
        S[("poshi_free", rr)] = yfree
        S[("poslo_free", rr)] = yfree
        S[("altm_free", rr)] = yfree
        S[("naltm_free", rr)] = yfree
        S[("expaltm_free", rr)] = yfree
        S[("expdtm_free", rr)] = yfree
        S[("drb_free", rr)] = yfree
        S[("bct_free", rr)] = yfree
        S[("xitm_free", rr)] = yfree
        S[("bcfm_free", rr)] = yfree

    def ffn(st):
        stb = st % 2
        TT = tps * CK
        xn_ready = [S[("xnfm_tile", stb, it)] for it in range(tps)]
        be = None
        for it in range(tps):
            tc = b.t(lambda e, stb=stb, it=it: e.transpose(
                pyd[0:8, 0:128], sb["comb"][stb, :, it, :], sb["idf"][:]),
                waits=[S[("comb_tile", stb, it)]] + bw("yd"))
            ce = b.v(lambda e: e.tensor_copy(sb["combT"][:], pyd[0:8, 0:128]),
                     waits=[tc, sget("combT_free")])
            bset("yd", ce)
            for nn2 in range(2):
                bm = b.t(lambda e, nn2=nn2: e.matmul(
                    pyo[:, 0:512], sb["combT"][:], sb["b2r"][:, nn2 * 512:(nn2 + 1) * 512],
                    start=True, stop=True),
                    waits=[ce, tok_b2r] + bw("yo"))
                be = b.v(lambda e, it=it, nn2=nn2: e.tensor_copy(
                    sb["acc"][:, it, nn2 * 512:(nn2 + 1) * 512], pyo[:, 0:512]),
                    waits=[bm, sget("acc_free")])
                bset("yo", be)
            S["combT_free"] = be
            yield
        acc_init = be
        prev_y = acc_init
        for ei in range(NE):
            for hf in range(2):
                pp = hf
                wtok = b.dgp(lambda e, ei=ei, hf=hf, pp=pp: e.dma_start(
                    sb["wA"][pp],
                    dr["w1_b"][ei][:, hf * 1024:(hf + 1) * 1024]
                    .rearrange("(c k) n -> k c n", k=128)),
                    waits=[sget(("wA_free", pp)), cv_w1[ei]])
                for hcl in range(8):
                    hc = 8 * hf + hcl
                    bank = "b%d" % (hcl % 4)
                    mm = None
                    for kc in range(8):
                        mm = b.t(lambda e, kc=kc, pp=pp, hcl=hcl, TT=TT: e.matmul(
                            pbig[:, (hcl % 4) * 512:(hcl % 4) * 512 + TT],
                            sb["wA"][pp, :, kc, hcl * 128:(hcl + 1) * 128],
                            sb["xnfm"][stb, :, kc, 0:TT],
                            start=(kc == 0), stop=(kc == 7)),
                            waits=[wtok] + xn_ready + (bw(bank) if kc == 0 else []))
                    hv = b.s(lambda e, hc=hc, ei=ei, hcl=hcl, TT=TT: e.activation(
                        sb["hb"][:, hc, 0:TT],
                        pbig[:, (hcl % 4) * 512:(hcl % 4) * 512 + TT], AF.Relu,
                        bias=sb["b1cc"][:, ei, hc:hc + 1]),
                        waits=[mm, tok_b1c, prev_y])
                    bset(bank, hv)
                    S[("hb_hc", hc)] = hv
                    yield
                S[("wA_free", pp)] = mm
            ydone = None
            for chh in range(2):
                pp = chh
                w2tok = b.dgp(lambda e, ei=ei, chh=chh, pp=pp: e.dma_start(
                    sb["wA"][pp].rearrange("p a n -> p (a n)")
                    .rearrange("p (c n) -> p c n", n=512),
                    dr["w2_b"][ei][:, chh * 512:(chh + 1) * 512]
                    .rearrange("(c k) n -> k c n", k=128)),
                    waits=[sget(("wA_free", pp)), cv_w2[ei]])
                for tb in range(tps):
                    pt, bank = ((pyd, "yd") if tb % 2 == 0 else (pyo, "yo"))
                    mm = None
                    for kc in range(16):
                        mm = b.t(lambda e, kc=kc, pp=pp, tb=tb, pt=pt: e.matmul(
                            pt[:, 0:512],
                            sb["hb"][:, kc, tb * 128:(tb + 1) * 128],
                            sb["wA"][pp].rearrange("p a n -> p (a n)")
                            .rearrange("p (c n) -> p c n", n=512)[:, kc, :],
                            start=(kc == 0), stop=(kc == 15)),
                            waits=[w2tok, S[("hb_hc", kc)]] + (bw(bank) if kc == 0 else []))
                    ac = b.v(lambda e, tb=tb, chh=chh, stb=stb, ei=ei, pt=pt:
                             e.scalar_tensor_tensor(
                                 sb["acc"][:, tb, chh * 512:(chh + 1) * 512],
                                 pt[:, 0:512],
                                 sb["comb"][stb, :, tb, ei:ei + 1],
                                 sb["acc"][:, tb, chh * 512:(chh + 1) * 512],
                                 ALU.mult, ALU.add),
                             waits=[mm, S[("comb_tile", stb, tb)], acc_init])
                    bset(bank, ac)
                    ydone = ac
                    yield
                S[("wA_free", pp)] = mm
            prev_y = ydone
        spa = b.dsync(lambda e, st=st, TT=TT: e.dma_start(
            dr["ffn_sp"][st * TT:(st + 1) * TT, :].rearrange("(t p) c -> p t c", p=128),
            sb["acc"][:, 0:tps, :]), waits=[prev_y])
        S["acc_free"] = spa
        S[("comb_free", stb)] = prev_y
        S[("xnfm_free", stb)] = prev_y

    def rl_pre(c):
        rb = c % 2
        t = {}
        t["exp"] = b.dsync(lambda e, c=c, rb=rb: e.dma_start(
            sb["expb"][rb], dr["expal_sp"][c * CK:(c + 1) * CK, 32:64]),
            waits=[sget(("expb_free", rb))])
        t["cbw"] = b.dsync(lambda e, c=c, rb=rb: e.dma_start(
            sb["cbw"][rb], dr["bc_sp"][c * CK:(c + 1) * CK, 192:256]),
            waits=[sget(("cbw_free", rb))])
        t["ybd"] = b.dsync(lambda e, c=c, rb=rb: e.dma_start(
            sb["ybdt"][rb], dr["ybd_sp"][c * CK:(c + 1) * CK, :]),
            waits=[sget(("ybdt_free", rb))])
        t["yfw"] = b.dsync(lambda e, c=c, rb=rb: e.dma_start(
            sb["yfwt"][rb], dr["yfw_sp"][c * CK:(c + 1) * CK, :]),
            waits=[sget(("yfwt_free", rb))])
        t["xi"] = b.dsync(lambda e, c=c, rb=rb: e.dma_start(
            sb["xirl"][rb], dr["xi_sp"][c * CK:(c + 1) * CK, :]),
            waits=[sget(("xirl_free", rb))])
        t["z"] = b.dsync(lambda e, c=c, rb=rb: e.dma_start(
            sb["zrl"][rb], dr["z_sp"][c * CK:(c + 1) * CK, :]),
            waits=[sget(("zrl_free", rb))])
        t["ff"] = b.dsync(lambda e, c=c, rb=rb: e.dma_start(
            sb["ffrl"][rb], dr["ffn_sp"][c * CK:(c + 1) * CK, :]),
            waits=[sget(("ffrl_free", rb)), sget("acc_free")])
        t["x"] = b.dscal(lambda e, c=c, rb=rb: e.dma_start(
            sb["xt"][rb], dr["u"][c * CK:(c + 1) * CK, :]),
            waits=[sget(("xt_free", rb))])
        S[("rlt", c)] = t

    def tile_bc(c):
        rb = c % 2
        rlt = S[("rlt", c)]
        if c < nt - 1:
            d0 = b.v(lambda e, c=c, rb=rb: e.tensor_copy(
                sb["drb"][rb], sb["drows"][:, c + 1, :]),
                waits=[sget(("drb_free", rb))])
            S[("drb_free", rb)] = d0
            dts_ = []
            for j in range(4):
                pt, bank = ((pyd, "yd") if j % 2 == 0 else (pyo, "yo"))
                mm = b.t(lambda e, j=j, rb=rb, pt=pt: e.matmul(
                    pt[0:64, 0:512], sb["ones1b"][0:1, 0:64],
                    sb["drb"][rb][0:1, 32 + 8 * j:32 + 8 * j + 8]
                    .unsqueeze(2).broadcast_to([1, 8, 64]),
                    start=True, stop=True),
                    waits=[d0, tok_o1b] + bw(bank))
                evd = b.s(lambda e, j=j, pt=pt: e.copy(
                    sb["decrep"][:, j * 512:(j + 1) * 512], pt[0:64, 0:512]),
                    waits=[mm, sget("decrep_free")])
                bset(bank, evd)
                dts_.append(evd)
            rl = b.dsync(lambda e, c=c: e.dma_start(
                sb["csbt"][:], dr["csbw_sp"][c + 1]),
                waits=[sget("csrl_free"), S[("csbw", c + 1)]])
            s_0 = b.v(lambda e: e.tensor_tensor(sb["stA"][:],
                                                sb["stA"][:],
                                                sb["decrep"][:], ALU.mult),
                      waits=dts_ + [sget("sbw")])
            s_1 = b.v(lambda e: e.tensor_tensor(
                sb["stA"][:], sb["stA"][:], sb["csbt"][:], ALU.add),
                waits=[s_0, rl])
            S["csrl_free"] = s_1
            S["sbw"] = s_1
            S["decrep_free"] = s_1
            s_2 = b.v(lambda e: e.tensor_copy(sb["stB"][:], sb["stA"][:]),
                      waits=[s_1, sget("sbwbf_rd")])
            S["sbwbf"] = s_2
        rl_exp = rlt["exp"]
        rl_cbw = rlt["cbw"]
        rl_ybd = rlt["ybd"]
        tcb = b.t(lambda e, rb=rb: e.transpose(ptb[0:64, 0:128], sb["cbw"][rb],
                                               sb["idb"][:]),
                  waits=[rl_cbw, sget("cbwfm_free")] + bw("tb"))
        ecb = b.v(lambda e: e.tensor_copy(sb["cbwfm"][:], ptb[0:64, 0:128]),
                  waits=[tcb])
        bset("tb", ecb)
        S[("cbw_free", rb)] = ecb
        ye1 = None
        for q in range(4):
            yom = None
            for j in range(8):
                hq = 8 * q + j
                yom = b.t(lambda e, hq=hq, j=j: e.matmul(
                    pyo[:, j * 64:(j + 1) * 64], sb["cbwfm"][:],
                    sb["stB"][:, hq * 64:hq * 64 + 64],
                    start=(j == 0), stop=(j == 7)),
                    waits=[ecb, sget("sbwbf")] + (bw("yo") if j == 0 else []))
            ye0 = b.v(lambda e, q=q, rb=rb: e.tensor_tensor(
                r3(sb["ybwt"][rb][:, q * 512:(q + 1) * 512], 64),
                r3(pyo[:, 0:512], 64),
                sb["expb"][rb][:, 8 * q:8 * q + 8]
                .unsqueeze(2).broadcast_to([128, 8, 64]), ALU.mult),
                waits=[yom, rl_exp, sget(("ybwt_free", rb))])
            bset("yo", ye0)
            ye1 = b.v(lambda e, q=q, rb=rb: e.tensor_tensor(
                sb["ybwt"][rb][:, q * 512:(q + 1) * 512],
                sb["ybwt"][rb][:, q * 512:(q + 1) * 512],
                sb["ybdt"][rb][:, q * 512:(q + 1) * 512], ALU.add),
                waits=[ye0, rl_ybd])
        S[("ybdt_free", rb)] = ye1
        S[("expb_free", rb)] = ye1
        S["cbwfm_free"] = yom
        S["sbwbf_rd"] = yom
        S[("ybwt_ye", c)] = ye1
        smms = []
        for nb in range(4):
            bank = "b%d" % nb
            sm = b.t(lambda e, nb=nb, rb=rb, c=c: e.matmul(
                pbig[:, nb * 512:(nb + 1) * 512], sb["shm"][:, 0:128],
                sb["ybwt"][rb][:, nb * 512:(nb + 1) * 512],
                start=True, stop=(c == nt - 1)),
                waits=[ye1, tok_shm] + bw(bank))
            if c < nt - 1:
                sm = b.t(lambda e, nb=nb, rb=rb: e.matmul(
                    pbig[:, nb * 512:(nb + 1) * 512], sb["shm"][:, 128:256],
                    sb["ybwt"][rb ^ 1][:, nb * 512:(nb + 1) * 512],
                    start=False, stop=True),
                    waits=[S[("ybwt_ye", c + 1)]])
            smms.append(sm)
        if c < nt - 1:
            S[("ybwt_free", rb ^ 1)] = smms[-1]
        ybse = []
        for nb in range(4):
            ev = b.s(lambda e, nb=nb, rb=rb: e.copy(
                sb["ybshs"][rb][:, nb * 512:(nb + 1) * 512],
                pbig[:, nb * 512:(nb + 1) * 512]),
                waits=[smms[nb], sget(("ybshs_f", rb))])
            bset("b%d" % nb, ev)
            ybse.append(ev)
        rl_yfw = rlt["yfw"]
        rl_xi = rlt["xi"]
        rl_z = rlt["z"]
        rl_ff = rlt["ff"]
        rl_x = rlt["x"]
        if c > 0:
            rl_pre(c - 1)
        y0 = b.v(lambda e, c=c, rb=rb: e.tensor_tensor(
            r3(sb["ycmb"][rb], 64), r3(sb["xirl"][rb], 64),
            sb["fcd"][:, c, :].unsqueeze(2).broadcast_to([128, 32, 64]), ALU.mult),
            waits=[rl_xi, S[("fcd", c)], sget(("ycmb_free", 0)),
                   sget(("ycmb_free", 1))])
        S[("xirl_free", rb)] = y0
        y1 = b.v(lambda e, rb=rb: e.tensor_tensor(sb["ycmb"][rb], sb["ycmb"][rb],
                                                  sb["yfwt"][rb], ALU.add),
                 waits=[y0, rl_yfw])
        y2 = b.v(lambda e, rb=rb: e.tensor_tensor(
            sb["ycmb"][rb], sb["ycmb"][rb], sb["ybshs"][rb], ALU.add),
            waits=[y1] + ybse)
        S[("ybshs_f", rb)] = y2
        S[("yfwt_free", rb)] = y2
        sz = b.s(lambda e, rb=rb: e.activation(sb["zrl"][rb], sb["zrl"][rb], AF.Silu),
                 waits=[rl_z])
        ssq = b.s(lambda e, rb=rb: e.activation(sb["actt"][rb], sb["ycmb"][rb],
                                                AF.Square, accum_out=sb["gsc"][:, 0:1]),
                  waits=[y2, sget("gsc_free"), sget(("actt_free", rb))])
        r0a = b.v(lambda e: e.tensor_scalar(sb["gsc"][:, 1:2], sb["gsc"][:, 0:1],
                                            1.0 / DI, 1e-5, ALU.mult, ALU.add),
                  waits=[ssq])
        r0 = b.s(lambda e: e.activation(sb["gsc"][:, 1:2], sb["gsc"][:, 1:2], AF.Ln),
                 waits=[r0a])
        r1a = b.v(lambda e: e.tensor_scalar_mul(sb["gsc"][:, 2:3], sb["gsc"][:, 1:2],
                                                -0.5), waits=[r0])
        r1 = b.s(lambda e: e.activation(sb["gsc"][:, 2:3], sb["gsc"][:, 2:3], AF.Exp),
                 waits=[r1a])
        v0 = b.v(lambda e, rb=rb: e.tensor_scalar_mul(sb["actt"][rb], sb["ycmb"][rb],
                                                      sb["gsc"][:, 2:3]),
                 waits=[r1, sget(("actt_free", rb))])
        S[("ycmb_free", 0)] = v0
        S[("ycmb_free", 1)] = v0
        S["gsc_free"] = v0
        v1 = b.v(lambda e, rb=rb: e.tensor_tensor(sb["actt"][rb], sb["actt"][rb],
                                                  sb["nwb"][:], ALU.mult),
                 waits=[v0, tok_nwb])
        v2 = b.v(lambda e, rb=rb: e.tensor_tensor(sb["actt"][rb], sb["actt"][rb],
                                                  sb["zrl"][rb], ALU.mult),
                 waits=[v1, sz])
        S[("zrl_free", rb)] = v2
        aevs = []
        for wave in range(2):
            tts2 = []
            for sl in range(8):
                chk = wave * 8 + sl
                tw = [v2, tok_idb]
                tw.extend(bw("tb") if chk == 0 else ([aevs[-1]] if chk == 8 else []))
                tts2.append(b.t(lambda e, chk=chk, sl=sl, rb=rb: e.transpose(
                    ptb[:, sl * 128:(sl + 1) * 128],
                    sb["actt"][rb][:, chk * 128:(chk + 1) * 128], sb["idb"][:]),
                    waits=tw))
            for sl in range(8):
                chk = wave * 8 + sl
                mk = b.v if wave == 0 else b.s
                fn = ((lambda e, chk=chk, sl=sl, rb=rb: e.tensor_copy(
                    sb["actf"][rb][:, chk, :], ptb[:, sl * 128:(sl + 1) * 128]))
                    if wave == 0 else
                    (lambda e, chk=chk, sl=sl, rb=rb: e.copy(
                        sb["actf"][rb][:, chk, :], ptb[:, sl * 128:(sl + 1) * 128])))
                evx = mk(fn, waits=[tts2[-1], sget(("actf_free", rb))])
                aevs.append(evx)
        bset("tb", aevs[-1])
        S[("actt_free", rb)] = aevs[-1]
        o1 = None
        mm = None
        for chh in range(2):
            pt, bank = ((pyd, "yd") if chh == 0 else (pyo, "yo"))
            mm = None
            for kc in range(16):
                mm = b.t(lambda e, kc=kc, chh=chh, rb=rb, pt=pt: e.matmul(
                    pt[:, 0:512], sb["actf"][rb][:, kc, :],
                    sb["woutv"][:, kc, chh * 512:(chh + 1) * 512],
                    start=(kc == 0), stop=(kc == 15)),
                    waits=[aevs[kc], sget("wout_tok")] + (bw(bank) if kc == 0 else []))
            o0 = b.v(lambda e, chh=chh, rb=rb, pt=pt: e.scalar_tensor_tensor(
                sb["outt"][rb][:, chh * 512:(chh + 1) * 512], pt[:, 0:512], 1.0,
                sb["xt"][rb][:, chh * 512:(chh + 1) * 512], ALU.mult, ALU.add),
                waits=[mm, rl_x, sget(("outt_free", rb))])
            bset(bank, o0)
            o1 = b.v(lambda e, chh=chh, rb=rb: e.tensor_tensor(
                sb["outt"][rb][:, chh * 512:(chh + 1) * 512],
                sb["outt"][rb][:, chh * 512:(chh + 1) * 512],
                sb["ffrl"][rb][:, chh * 512:(chh + 1) * 512], ALU.add),
                waits=[o0, rl_ff])
        S[("actf_free", rb)] = mm
        S[("ffrl_free", rb)] = o1
        S[("xt_free", rb)] = o1
        st_out = b.dscal(lambda e, c=c, rb=rb: e.dma_start(
            out[c * CK:(c + 1) * CK, :], sb["outt"][rb]), waits=[o1])
        S[("outt_free", rb)] = st_out

    def drain(gen):
        if gen is not None:
            for _ in gen:
                pass

    def pump(gen, n=1):
        if gen is None:
            return None
        for _ in range(n):
            try:
                next(gen)
            except StopIteration:
                return None
        return gen

    if SCHED != "prepass":
        fg = None
        for i in range(nt):
            st = i // tps
            if i % tps == 0 and st > 0 and SCHED in ("ffn", "full"):
                drain(fg)
                fg = ffn(st - 1)
                fg = pump(fg)
            for h_ in tile_head(i):
                fg = pump(fg, h_)
            if i > 0 and SCHED in ("conv", "ffn", "full"):
                for h_ in conv_ssd(i - 1):
                    fg = pump(fg, h_)
        drain(fg)
        if SCHED in ("conv", "ffn", "full"):
            e2 = b.v(lambda e: e.memset(sb["asm"][(nt - 1) % 2, :, :, 131:134], 0.0),
                     waits=[S[("asm_center", nt - 1)]])
            S[("asm_ready", nt - 1)] = e2
            drain(conv_ssd(nt - 1))
        if SCHED in ("ffn", "full"):
            drain(ffn(nst - 1))
    if SCHED != "full":
        final_waits = []
        for ring in ("dsy", "dsc", "dgp"):
            for ln in range(NLANES):
                sem = "%s%d" % (ring, ln)
                if b.cnt[sem]:
                    final_waits.append((sem, b.cnt[sem]))
        b.op("sync", lambda e: e.nop(), waits=final_waits, inc=None)
        import contextlib
        with contextlib.ExitStack() as stk:
            for name in list(b.cnt):
                b.handles[name] = stk.enter_context(nc.semaphore(name))
            blk = stk.enter_context(nc.Block())
            b.replay_into(blk)
        return nc

    aw = [S[("xnfm_free", 0)], sget(("xnfm_free", 1)), S["acc_free"], S["xmt_free"],
          S["xnt_free"], S["xnf32_free"], S["gsc_free"]]
    aw.append(sget("csbt_free"))
    for r_ in (0, 1):
        for kk in ("xitm_free", "asm_free", "win_free", "bhat_free",
                   "xifm_free", "xall_free", "yt_free", "zt_free", "mall_free"):
            aw.append(sget((kk, r_)))
    bar = b.v(lambda e: e.engine_nop(), waits=aw)
    for r_ in (0, 1):
        for kk in ("yfwt_free", "ybwt_free", "zrl_free",
                   "ybdt_free", "xirl_free", "ycmb_free", "actt_free",
                   "actf_free", "outt_free", "ffrl_free", "expb_free",
                   "cbw_free", "xt_free"):
            S[(kk, r_)] = bar
    S["gsc_free"] = bar
    S["csrl_free"] = bar
    S[("ybshs_f", 0)] = bar
    S[("ybshs_f", 1)] = bar
    S["wout_tok"] = b.dgp(lambda e: e.dma_start(
        sb["woutv"], dr["wout_b"].rearrange("(c k) n -> k c n", k=128)),
        waits=[bar, cv_wout])
    tok_nwb = b.dgp(lambda e: e.dma_start(sb["nwb"], dr["nw"]), waits=[bar])
    S["sbw"] = b.v(lambda e: e.memset(sb["stA"][:], 0.0),
                   waits=[bar, sget("sfw"), sget("sbf")])
    S["sbwbf"] = b.v(lambda e: e.memset(sb["stB"][:], 0.0), waits=[bar, S["sbw"]])
    rl_pre(nt - 1)
    for c in range(nt - 1, -1, -1):
        tile_bc(c)
    final_waits = []
    for ring in ("dsy", "dsc", "dgp"):
        for ln in range(NLANES):
            sem = "%s%d" % (ring, ln)
            if b.cnt[sem]:
                final_waits.append((sem, b.cnt[sem]))
    b.op("sync", lambda e: e.nop(), waits=final_waits, inc=None)

    import contextlib
    with contextlib.ExitStack() as stk:
        for name in list(b.cnt):
            b.handles[name] = stk.enter_context(nc.semaphore(name))
        blk = stk.enter_context(nc.Block())
        b.replay_into(blk)
    return nc


def kernel(**inputs):
    A = host_prep(inputs)
    u = np.ascontiguousarray(inputs["L0_attn_u"], dtype=np.float32)
    nt = NT_OVERRIDE if NT_OVERRIDE else L // CK
    nc = build(nt)
    in_maps = []
    for c in range(8):
        m = {"u": np.ascontiguousarray(u[c][: nt * CK])}
        m.update(A)
        in_maps.append(m)
    res = run_bass_kernel_spmd(nc, in_maps, core_ids=list(range(8)))
    return np.stack([r["out"] for r in res.results])


# version 19
# speedup vs baseline: 1.4450x; 1.0129x over previous
import numpy as np
import ml_dtypes
import concourse.bass as bass
import concourse.mybir as mybir
from concourse.bass_utils import run_bass_kernel_spmd

F32 = mybir.dt.float32
BF16 = mybir.dt.bfloat16
U8 = mybir.dt.uint8
AF = mybir.ActivationFunctionType
ALU = mybir.AluOpType
AX = mybir.AxisListType.X

L, D, DI, DIP = 4096, 1024, 2048, 4416
NE, CK, TPS = 8, 128, 4
ENGS = ("tensor", "vector", "scalar", "sync", "gpsimd")
ZERO = ("pe", 0)
NT_OVERRIDE = 0
SCHED = "full"


NLANES = 8


class Bld:
    def __init__(self, nc):
        self.nc = nc
        self.ops = []
        self.cnt = {s: 0 for s in ("pe", "v", "s")}
        self.lane_ord = {}
        self.ring_ctr = {"dsy": 0, "dsc": 0, "dgp": 0}
        for ring in ("dsy", "dsc", "dgp"):
            for ln in range(NLANES):
                self.cnt["%s%d" % (ring, ln)] = 0
        self.handles = {}

    def dma(self, ring, fn, waits=()):
        ln = self.ring_ctr[ring] % NLANES
        self.ring_ctr[ring] += 1
        sem = "%s%d" % (ring, ln)
        prev = self.cnt[sem]
        w = list(waits) + ([(sem, prev)] if prev else [])
        return self.op("sync" if ring == "dsy" else
                       ("scalar" if ring == "dsc" else "gpsimd"),
                       fn, w, sem, 16)

    def op(self, eng, fn, waits=(), inc=None, incn=1):
        w = [(s, v) for (s, v) in waits if s is not None and v > 0]
        tok = None
        if inc is not None:
            self.cnt[inc] += incn
            tok = (inc, self.cnt[inc])
        self.ops.append((eng, fn, w, inc, incn))
        return tok

    def t(self, fn, waits=()):
        return self.op("tensor", fn, waits, "pe")

    def v(self, fn, waits=()):
        return self.op("vector", fn, waits, "v")

    def s(self, fn, waits=()):
        return self.op("scalar", fn, waits, "s")

    def dsync(self, fn, waits=()):
        return self.dma("dsy", fn, waits)

    def dscal(self, fn, waits=()):
        return self.dma("dsc", fn, waits)

    def dgp(self, fn, waits=()):
        return self.dma("dgp", fn, waits)

    def replay_into(self, blk):
        for name in ENGS:
            getattr(blk, name)(self._mk(name))

    def _mk(self, engname):
        def run(eng):
            waited = {}
            for e, fn, waits, inc, incn in self.ops:
                if e != engname:
                    continue
                for s, v in waits:
                    if waited.get(s, 0) < v:
                        eng.wait_ge(self.handles[s], v)
                        waited[s] = v
                ins = fn(eng)
                if inc is not None:
                    ins.then_inc(self.handles[inc], incn)
        return run


class Ring:

    def __init__(self, handle):
        self.h = handle

    def __getitem__(self, key):
        ap = self.h.ap()
        if isinstance(key, tuple):
            key = (key[1], key[0]) + tuple(key[2:])
            return ap[key]
        return ap[:, key]


class AliasRing:

    def __init__(self, *aps):
        self.aps = list(aps)

    def __getitem__(self, key):
        if isinstance(key, tuple):
            return self.aps[key[0]][tuple(key[1:])]
        return self.aps[key]


BCPERM = np.concatenate([np.arange(2048),
                         2048 + np.arange(64), 2176 + np.arange(64),
                         2112 + np.arange(64), 2240 + np.arange(64)])


def host_prep(inputs):
    A = {}
    wt = inputs["L0_attn_in_proj_w"].T
    colperm = np.concatenate([np.arange(2048), 2048 + BCPERM, np.arange(4352, 4416)])
    A["winT"] = np.ascontiguousarray(wt[:, colperm])
    cw = inputs["L0_attn_conv_w"][:, 0, :][BCPERM]
    cd = np.zeros((128, 18, 7, 32), np.float32)
    cwr = cw.reshape(18, 4, 32, 7)
    for bb in range(4):
        for j in range(32):
            cd[32 * bb + j, :, :, j] = cwr[:, bb, j, :]
    A["convd"] = cd
    A["convb"] = np.ascontiguousarray(
        inputs["L0_attn_conv_b"][BCPERM].reshape(18, 128).T)
    A["dtb"] = np.ascontiguousarray(
        np.broadcast_to(np.concatenate([inputs["L0_attn_dt_bias"]] * 2), (128, 64)))
    A["alog"] = np.ascontiguousarray(
        np.concatenate([inputs["L0_attn_A_log"]] * 2)[:, None])
    A["dpar"] = np.ascontiguousarray(np.broadcast_to(inputs["L0_attn_D_param"], (128, 32)))
    A["fcdT"] = np.ascontiguousarray(inputs["L0_attn_fc_D_w"].T)
    A["nw"] = np.ascontiguousarray(np.broadcast_to(inputs["L0_attn_norm_w"], (128, DI)))
    A["woutT"] = np.ascontiguousarray(inputs["L0_attn_out_proj_w"].T)
    A["w1T"] = np.ascontiguousarray(inputs["L0_ffn_W1"].transpose(0, 2, 1))
    A["b1c"] = np.ascontiguousarray(
        inputs["L0_ffn_b1"].reshape(8, 16, 128).transpose(2, 0, 1))
    A["w2T"] = np.ascontiguousarray(inputs["L0_ffn_W2"].transpose(0, 2, 1))
    A["b2"] = np.ascontiguousarray(inputs["L0_ffn_b2"])
    A["wgT"] = np.ascontiguousarray(inputs["L0_ffn_Wg"].T)
    A["bg"] = np.ascontiguousarray(np.broadcast_to(inputs["L0_ffn_bg"], (128, 8)))
    A["idf"] = np.eye(128, dtype=np.float32)
    idx = np.arange(128)
    mz = np.stack([(idx[None, :] < idx[:, None]),
                   (idx[None, :] > idx[:, None])], axis=0).astype(np.uint8)
    A["mskz"] = np.ascontiguousarray(
        np.tile(mz[None, :, :, :], (1, 1, 1, 4)).reshape(2, 128, 512)
        .transpose(1, 0, 2)).copy()
    A["ones1"] = np.ones((1, 128), np.float32)
    shm = np.zeros((128, 256), np.float32)
    shm[idx[1:], idx[:-1]] = 1.0
    shm[0, 255] = 1.0
    A["shm"] = np.ascontiguousarray(shm.astype(ml_dtypes.bfloat16))
    return A


def build(nt):
    nc = bass.Bass("TRN2", target_bir_lowering=False)
    b = Bld(nc)
    Ltot = nt * CK
    tps = min(TPS, nt)
    nst = max(1, (nt + tps - 1) // tps)

    dr = {}

    def din(name, shape, dt=F32):
        dr[name] = nc.dram_tensor(name, list(shape), dt, kind="ExternalInput").ap()

    din("u", (Ltot, D))
    for name, shape, dt in [
        ("winT", (D, DIP), F32), ("convd", (128, 18, 7, 32), F32),
        ("convb", (128, 18), F32), ("dtb", (128, 64), F32), ("alog", (64, 1), F32),
        ("dpar", (128, 32), F32), ("fcdT", (DI, 32), F32), ("nw", (128, DI), F32),
        ("woutT", (DI, D), F32), ("w1T", (NE, D, DI), F32), ("b1c", (128, NE, 16), F32),
        ("w2T", (NE, DI, D), F32), ("b2", (NE, D), F32), ("wgT", (D, NE), F32),
        ("bg", (128, NE), F32), ("idf", (128, 128), F32),
        ("mskz", (128, 2, 512), U8), ("ones1", (1, 128), F32),
        ("shm", (128, 256), BF16),
    ]:
        dr[name] = nc.dram_tensor(name, list(shape), dt, kind="ExternalInput").ap()
    out = nc.dram_tensor("out", [Ltot, D], F32, kind="ExternalOutput").ap()

    def scr(name, shape, dt=BF16):
        dr[name] = nc.dram_tensor(name, list(shape), dt, kind="Internal").ap()

    scr("win_b", (D, DIP))
    scr("w1_b", (NE, D, DI))
    scr("w2_b", (NE, DI, D))
    scr("wout_b", (DI, D))
    scr("z_sp", (Ltot, DI))
    scr("xi_sp", (Ltot, DI))
    scr("yfw_sp", (Ltot + 1, DI))
    scr("ybd_sp", (Ltot, DI))
    scr("csbw_sp", (nt, 64, 2048))
    scr("bc_sp", (Ltot, 256))
    scr("expal_sp", (Ltot, 64))
    scr("ffn_sp", (Ltot, D))

    sb = {}
    hnd = {}

    def asb(name, shape, dt=BF16):
        hnd[name] = nc.alloc_sbuf_tensor("sb_" + name, list(shape), dt)
        sb[name] = hnd[name]

    def aring(name, shape, dt=BF16):
        R, Pp = shape[0], shape[1]
        hnd[name] = nc.alloc_sbuf_tensor("sb_" + name, [Pp, R] + list(shape[2:]), dt)
        sb[name] = Ring(hnd[name])

    asb("idb", (128, 128))
    asb("shm", (128, 256))
    asb("idf", (128, 128), F32)
    asb("mskz", (128, 2, 512), U8)
    asb("zc", (128, 1))
    asb("ones1b", (1, 128))
    asb("cbc", (128, 18), F32)
    asb("dtbc", (128, 64), F32)
    asb("alogc", (64, 1), F32)
    asb("nega", (64, 1), F32)
    asb("dparc", (128, 32), F32)
    asb("bgc", (128, 8), F32)
    asb("wgc", (128, 8, 8), F32)
    asb("b1cc", (128, NE, 16), F32)
    asb("b2r", (NE, D))
    asb("fcdw", (128, 16, 32))
    asb("cdw", (128, 18, 7, 32))
    asb("drows", (1, nt, 64), F32)
    aring("drb", (2, 1, 64))
    asb("fcd", (128, nt, 32), F32)
    aring("comb", (2, 128, tps, 8), F32)
    asb("combT", (8, 128))
    asb("glt", (128, 8), F32)
    asb("gsc", (128, 12), F32)
    aring("dtr", (2, 128, 64), F32)
    asb("dscr", (128, 64), F32)
    aring("dts", (2, 128, 64), F32)
    aring("dtb16", (2, 128, 64))
    asb("abf", (64, CK), F32)
    aring("alf", (2, 64, CK), F32)
    aring("poshi", (2, 64, CK))
    aring("poslo", (2, 64, CK))
    aring("altm", (2, 128, 64), F32)
    aring("naltm", (2, 128, 64), F32)
    aring("expaltm", (2, 128, 64))
    aring("expdtm", (2, 128, 64))
    asb("expdfm", (64, CK), F32)
    asb("sumc", (64, 1), F32)
    aring("bcfm", (2, 128, 2, 128))
    aring("bct", (2, 128, 256))
    asb("stA", (64, 2048), F32)
    asb("stB", (64, 2048))
    asb("decrep", (64, 2048))
    asb("csbt", (64, 2048))
    aring("xt", (2, 128, D), F32)
    aring("win", (2, 128, 8, 512))
    aring("wA", (2, 128, 8, 1024))
    aring("asm", (2, 128, 18, 134))
    aring("xifm", (2, 128, 16, 128))
    aring("xitm", (2, 128, DI))
    aring("xall", (2, 128, 1024))
    asb("mall", (128, 2048))
    aring("bhat", (2, 128, 1024))
    aring("yt", (2, 128, 512))
    aring("zt", (2, 128, 512))
    asb("xmt", (128, D), F32)
    asb("xnt", (128, D), F32)
    asb("xnf32", (128, 8, 128), F32)
    aring("xnfm", (2, 128, 8, tps * CK))
    asb("acc", (128, tps, D))
    asb("hb", (128, 16, 512))
    aring("expb", (2, 128, 32))
    aring("cbw", (2, 128, 64))
    asb("cbwfm", (64, 128))

    def flat(h):
        ap = h.ap()
        nd = len(ap.shape)
        if nd == 2:
            return ap
        spec = "p " + " ".join("d%d" % k for k in range(nd - 1))
        return ap.rearrange(spec + " -> p (" + spec[2:] + ")")

    hbf = flat(hnd["hb"])
    sb["yfwt"] = AliasRing(hbf[:, 0:2048], hbf[:, 2048:4096])
    sb["ybshs"] = AliasRing(hbf[:, 4096:6144], hbf[:, 6144:8192])
    winf = flat(hnd["win"])
    sb["ybwt"] = AliasRing(winf[:, 0:2048], winf[:, 2048:4096])
    sb["ycmb"] = AliasRing(winf[:, 4096:6144], winf[:, 4096:6144])
    asmf = flat(hnd["asm"])
    sb["xirl"] = AliasRing(asmf[:, 0:2048], asmf[:, 2048:4096])
    xifmf = flat(hnd["xifm"])
    sb["zrl"] = AliasRing(xifmf[:, 0:2048], xifmf[:, 2048:4096])
    sb["ybdt"] = sb["xitm"]
    bhatf = flat(hnd["bhat"])
    sb["actt"] = AliasRing(hnd["mall"].ap(), bhatf)
    xallf = flat(hnd["xall"])
    sb["actf"] = AliasRing(
        xallf.rearrange("p (c n) -> p c n", n=128),
        winf[:, 6144:8192].rearrange("p (c n) -> p c n", n=128))
    sb["outt"] = AliasRing(hnd["xmt"].ap(), hnd["xnt"].ap())
    xnf32f = flat(hnd["xnf32"]).bitcast(BF16)
    sb["ffrl"] = AliasRing(xnf32f[:, 0:1024], xnf32f[:, 1024:2048])
    wAf = flat(hnd["wA"])
    sb["woutv"] = wAf.rearrange("p (c n) -> p c n", n=1024)
    cdwf = flat(hnd["cdw"])
    sb["nwb"] = cdwf[:, 0:2048]

    pbig = nc.alloc_psum_tensor("pbig", [128, 2048], F32)
    pmed = nc.alloc_psum_tensor("pmed", [128, 512], F32)
    pyd = nc.alloc_psum_tensor("pyd", [128, 512], F32)
    pyo = nc.alloc_psum_tensor("pyo", [128, 512], F32)
    ptb = nc.alloc_psum_tensor("ptb", [128, 1024], BF16)

    bank_free = {}

    def bw(key):
        return list(bank_free.get(key, []))

    def bset(key, *toks):
        bank_free[key] = [t for t in toks if t is not None]

    def badd(key, tok):
        bank_free.setdefault(key, []).append(tok)

    S = {}

    def sget(k):
        return S.get(k, ZERO)

    def r3(ap, q):
        return ap.rearrange("p (h q) -> p h q", q=q)

    cv_win = b.dgp(lambda e: e.dma_start(dr["win_b"], dr["winT"]))
    cv_wout = b.dgp(lambda e: e.dma_start(dr["wout_b"], dr["woutT"]))
    cv_w1, cv_w2 = [], []
    for ei in range(NE):
        cv_w1.append(b.dgp(lambda e, ei=ei: e.dma_start(dr["w1_b"][ei], dr["w1T"][ei])))
        cv_w2.append(b.dgp(lambda e, ei=ei: e.dma_start(dr["w2_b"][ei], dr["w2T"][ei])))
    tok_idb = b.dgp(lambda e: e.dma_start(sb["idb"][:], dr["idf"]))
    tok_shm = b.dgp(lambda e: e.dma_start(sb["shm"][:], dr["shm"]))
    tok_b2r = b.dgp(lambda e: e.dma_start(sb["b2r"][:], dr["b2"]))
    tok_fcdw = b.dgp(lambda e: e.dma_start(
        sb["fcdw"][:], dr["fcdT"].rearrange("(c k) n -> k c n", k=128)))
    tok_cdw = b.dgp(lambda e: e.dma_start(sb["cdw"][:], dr["convd"]))
    tok_o1b = b.dgp(lambda e: e.dma_start(sb["ones1b"][:], dr["ones1"]))
    tok_msk = b.dsync(lambda e: e.dma_start(sb["mskz"][:], dr["mskz"]))
    tok_idf = b.dsync(lambda e: e.dma_start(sb["idf"][:], dr["idf"]))
    tok_cbc = b.dsync(lambda e: e.dma_start(sb["cbc"][:], dr["convb"]))
    tok_dtb = b.dsync(lambda e: e.dma_start(sb["dtbc"][:], dr["dtb"]))
    tok_alog = b.dsync(lambda e: e.dma_start(sb["alogc"][:], dr["alog"]))
    tok_dpar = b.dsync(lambda e: e.dma_start(sb["dparc"][:], dr["dpar"]))
    tok_bg = b.dsync(lambda e: e.dma_start(sb["bgc"][:], dr["bg"]))
    tok_wg = b.dsync(lambda e: e.dma_start(
        sb["wgc"][:], dr["wgT"].rearrange("(c k) n -> k c n", k=128)))
    tok_b1c = b.dsync(lambda e: e.dma_start(sb["b1cc"][:], dr["b1c"]))

    t_ne = b.s(lambda e: e.activation(sb["nega"][:], sb["alogc"][:], AF.Exp),
               waits=[tok_alog])
    tok_nega = b.v(lambda e: e.tensor_scalar_mul(sb["nega"][:], sb["nega"][:], -1.0),
                   waits=[t_ne])
    tok_zc = b.v(lambda e: e.memset(sb["zc"][:], 0.0))
    tz = b.v(lambda e: e.memset(sb["outt"][0], 0.0))
    zr1 = b.dsync(lambda e: e.dma_start(
        dr["yfw_sp"][0:1, :], sb["outt"][0, 0:1, :].bitcast(BF16)[:, 0:DI]), waits=[tz])
    zrb = b.v(lambda e: e.engine_nop(), waits=[zr1])
    S[("outt_free", 0)] = zrb
    S["xmt_free"] = zrb
    S["sfw"] = b.v(lambda e: e.memset(sb["stA"][:], 0.0))
    S["sbf"] = b.v(lambda e: e.memset(sb["stB"][:], 0.0))
    S["asm_z0"] = b.v(lambda e: e.memset(sb["asm"][0, :, :, 0:3], 0.0))

    wctr = [0]

    def win_load(g):
        pp = wctr[0] % 2
        wctr[0] += 1
        n = 320 if g == 8 else 512
        tok = b.dsync(lambda e, g=g, pp=pp, n=n: e.dma_start(
            sb["win"][pp, :, :, 0:n],
            dr["win_b"][:, g * 512:g * 512 + n].rearrange("(c k) n -> k c n", k=128)),
            waits=[sget(("win_free", pp)), cv_win])
        return tok, pp

    S["win_next"] = win_load(0)

    def tile_head(i):
        st, it, r = i // tps, i % tps, i % 2
        stb = st % 2
        tok_x = b.dscal(lambda e, i=i, r=r: e.dma_start(
            sb["xt"][r], dr["u"][i * CK:(i + 1) * CK, :]),
            waits=[sget(("xt_free", r))])
        yield 4
        def trunc(tok):
            S[("asm_center", i)] = tok
            S[("dtr", i)] = tok
            S[("comb_tile", i // tps % 2, it)] = tok
            S[("xnfm_tile", i // tps % 2, it)] = tok
        if SCHED == "ln0":
            dmy = b.v(lambda e: e.engine_nop(), waits=[tok_x])
            trunc(dmy)
            return
        w0 = b.v(lambda e, r=r: e.tensor_reduce(sb["gsc"][:, 0:1], sb["xt"][r], AX,
                                                ALU.add),
                 waits=[tok_x, sget("gsc_free")])
        w1_ = b.v(lambda e: e.tensor_scalar_mul(sb["gsc"][:, 1:2], sb["gsc"][:, 0:1],
                                                1.0 / D), waits=[w0])
        w2_ = b.v(lambda e, r=r: e.tensor_scalar(sb["xmt"][:], sb["xt"][r],
                                                 sb["gsc"][:, 1:2], None, ALU.subtract),
                  waits=[w1_, sget("xmt_free")])
        S[("xt_free", r)] = w2_
        if SCHED == "ln1":
            trunc(w2_)
            S["xmt_free"] = w2_
            return
        w3 = b.s(lambda e: e.activation(sb["xnt"][:], sb["xmt"][:], AF.Square,
                                        accum_out=sb["gsc"][:, 2:3]),
                 waits=[w2_, sget("xnt_free")])
        w4a = b.v(lambda e: e.tensor_scalar(sb["gsc"][:, 3:4], sb["gsc"][:, 2:3],
                                            1.0 / D, 1e-5, ALU.mult, ALU.add),
                  waits=[w3])
        w4 = b.s(lambda e: e.activation(sb["gsc"][:, 3:4], sb["gsc"][:, 3:4], AF.Ln),
                 waits=[w4a])
        w5a = b.v(lambda e: e.tensor_scalar_mul(sb["gsc"][:, 4:5], sb["gsc"][:, 3:4],
                                                -0.5), waits=[w4])
        w5 = b.s(lambda e: e.activation(sb["gsc"][:, 4:5], sb["gsc"][:, 4:5], AF.Exp),
                 waits=[w5a])
        if SCHED == "ln2":
            trunc(w5)
            S["xmt_free"] = w5
            S["gsc_free"] = w5
            return
        w6 = b.v(lambda e: e.tensor_scalar_mul(sb["xnt"][:], sb["xmt"][:],
                                               sb["gsc"][:, 4:5]),
                 waits=[w5, sget("xnt_free")])
        S["xmt_free"] = w6
        if SCHED == "ln3":
            trunc(w6)
            S["gsc_free"] = w6
            return
        evb = None
        for kc in range(8):
            bank, pt = (("yd", pyd) if kc % 2 == 0 else ("yo", pyo))
            mmw = [w6, tok_idf] + bw(bank)
            if it == 0 and kc == 0:
                mmw.append(sget(("xnfm_free", stb)))
            if kc < 2:
                mmw.append(sget("xnf32_free"))
            tt = b.t(lambda e, kc=kc, pt=pt: e.transpose(
                pt[:, 0:128], sb["xnt"][:, kc * 128:(kc + 1) * 128], sb["idf"][:]),
                waits=mmw)
            ev1 = b.v(lambda e, kc=kc, it=it, pt=pt, stb=stb: e.tensor_copy(
                sb["xnfm"][stb, :, kc, it * CK:(it + 1) * CK], pt[:, 0:128]),
                waits=[tt])
            ev2 = b.s(lambda e, kc=kc, pt=pt: e.copy(
                sb["xnf32"][:, kc, :], pt[:, 0:128]), waits=[tt, ev1])
            bset(bank, ev1, ev2)
            S[("xnfm_k", kc)] = ev1
            S[("xnf32_k", kc)] = ev2
            evb = ev1
            if kc == 3:
                yield 0
        S["xnt_free"] = evb
        S[("xnfm_tile", stb, it)] = evb
        yield 0
        if SCHED == "ln":
            trunc(evb)
            return
        gm = None
        for kc in range(8):
            gm = b.t(lambda e, kc=kc: e.matmul(
                pmed[:, 128:136], sb["xnf32"][:, kc, :], sb["wgc"][:, kc, :],
                start=(kc == 0), stop=(kc == 7)),
                waits=[tok_wg, S[("xnf32_k", kc)]] + (bw("med") if kc == 0 else []))
        S["xnf32_free"] = gm
        g0 = b.v(lambda e: e.tensor_tensor(sb["glt"][:], pmed[:, 128:136], sb["bgc"][:],
                                           ALU.add),
                 waits=[gm, tok_bg, sget("glt_free")])
        bset("med", g0)
        m1 = b.v(lambda e: e.tensor_reduce(sb["gsc"][:, 5:6], sb["glt"][:], AX, ALU.max),
                 waits=[g0])
        nm1 = b.v(lambda e: e.tensor_reduce(sb["gsc"][:, 6:7], sb["glt"][:], AX, ALU.max,
                                            negate=True), waits=[g0])
        eq = b.v(lambda e: e.tensor_scalar(sb["gsc"][:, 8:12].bitcast(F32) if False
                                           else sb["dscr"][:, 0:8], sb["glt"][:],
                                           sb["gsc"][:, 5:6], None, ALU.is_equal),
                 waits=[m1, sget("dscr_free")])
        mk = b.v(lambda e: e.scalar_tensor_tensor(sb["dscr"][:, 0:8], sb["dscr"][:, 0:8],
                                                  -1e30, sb["glt"][:], ALU.mult, ALU.add),
                 waits=[eq])
        m2 = b.v(lambda e: e.tensor_reduce(sb["gsc"][:, 7:8], sb["dscr"][:, 0:8], AX,
                                           ALU.max), waits=[mk])
        ge = b.v(lambda e: e.tensor_scalar(sb["dscr"][:, 0:8], sb["glt"][:],
                                           sb["gsc"][:, 7:8], None, ALU.is_ge),
                 waits=[m2])
        dm = b.v(lambda e: e.tensor_tensor(sb["gsc"][:, 7:8], sb["gsc"][:, 7:8],
                                           sb["gsc"][:, 6:7], ALU.add),
                 waits=[m2, nm1, ge])
        ew = b.s(lambda e: e.activation(sb["glt"][:], sb["glt"][:], AF.Exp,
                                        bias=sb["gsc"][:, 6:7]), waits=[nm1, ge, mk])
        ed = b.s(lambda e: e.activation(sb["gsc"][:, 7:8], sb["gsc"][:, 7:8], AF.Exp),
                 waits=[dm])
        d1 = b.v(lambda e: e.tensor_scalar_add(sb["gsc"][:, 7:8], sb["gsc"][:, 7:8],
                                               1.0), waits=[ed])
        rc = b.v(lambda e: e.reciprocal(sb["gsc"][:, 7:8], sb["gsc"][:, 7:8]),
                 waits=[d1])
        c1 = b.v(lambda e: e.tensor_tensor(sb["glt"][:], sb["glt"][:], sb["dscr"][:, 0:8],
                                           ALU.mult), waits=[ew])
        c2 = b.v(lambda e, stb=stb, it=it: e.tensor_scalar_mul(
            sb["comb"][stb, :, it, :], sb["glt"][:], sb["gsc"][:, 7:8]),
            waits=[c1, rc, sget(("comb_free", stb))])
        S["glt_free"] = c2
        S["gsc_free"] = c2
        S["dscr_free"] = c2
        S[("comb_tile", stb, it)] = c2
        yield 0
        if SCHED == "lngate":
            S[("asm_center", i)] = c2
            S[("dtr", i)] = c2
            return

        ar = i % 2
        ctrV, ctrS = [None], [None]
        for g in range(9):
            yield 2
            gtok, pp = S["win_next"]
            if g + 1 < 9:
                S["win_next"] = win_load(g + 1)
            elif i + 1 < nt:
                S["win_next"] = win_load(0)
            if g < 4:
                bank = "b%d" % (g % 4)
                mm = None
                for kc in range(8):
                    mm = b.t(lambda e, kc=kc, g=g, pp=pp, it=it: e.matmul(
                        pbig[:, (g % 4) * 512:(g % 4) * 512 + 512],
                        sb["xnfm"][stb, :, kc, it * CK:(it + 1) * CK],
                        sb["win"][pp, :, kc, :],
                        start=(kc == 0), stop=(kc == 7)),
                        waits=[gtok, S[("xnfm_k", kc)]] + (bw(bank) if kc == 0 else []))
                S[("win_free", pp)] = mm
                for hz in range(1):
                    pass
                mke = b.v if g % 2 == 0 else b.s
                fne = ((lambda e, g=g: e.tensor_copy(
                    sb["zt"][g % 2], pbig[:, (g % 4) * 512:(g % 4) * 512 + 512]))
                    if g % 2 == 0 else
                    (lambda e, g=g: e.copy(
                        sb["zt"][g % 2], pbig[:, (g % 4) * 512:(g % 4) * 512 + 512])))
                ev = mke(fne, waits=[mm, sget(("zt_free", g % 2))])
                bset(bank, ev)
                sp = b.dsync(lambda e, i=i, g=g: e.dma_start(
                    dr["z_sp"][i * CK:(i + 1) * CK, g * 512:(g + 1) * 512],
                    sb["zt"][g % 2]), waits=[ev])
                S[("zt_free", g % 2)] = sp
            else:
                nsub = 3 if g == 8 else 4
                for sub in range(nsub):
                    ch_abs = (g - 4) * 4 + sub
                    bank = "b%d" % (ch_abs % 4)
                    mm = None
                    if ch_abs < 18:
                        for kc in range(8):
                            mm = b.t(lambda e, kc=kc, pp=pp, sub=sub, it=it, stb=stb,
                                     ch_abs=ch_abs: e.matmul(
                                pbig[:, (ch_abs % 4) * 512:(ch_abs % 4) * 512 + 128],
                                sb["win"][pp, :, kc, sub * 128:sub * 128 + 128],
                                sb["xnfm"][stb, :, kc, it * CK:(it + 1) * CK],
                                start=(kc == 0), stop=(kc == 7)),
                                waits=[gtok, S[("xnfm_k", kc)]]
                                + (bw(bank) if kc == 0 else []))
                        mke = b.v if ch_abs % 2 == 0 else b.s
                        fne = ((lambda e, ar=ar, ch_abs=ch_abs: e.tensor_copy(
                            sb["asm"][ar, :, ch_abs, 3:131],
                            pbig[:, (ch_abs % 4) * 512:(ch_abs % 4) * 512 + 128]))
                            if ch_abs % 2 == 0 else
                            (lambda e, ar=ar, ch_abs=ch_abs: e.copy(
                                sb["asm"][ar, :, ch_abs, 3:131],
                                pbig[:, (ch_abs % 4) * 512:(ch_abs % 4) * 512 + 128])))
                        ev = mke(fne, waits=[mm, sget(("asm_free", ar))])
                        if ch_abs % 2 == 0:
                            ctrV[0] = ev
                        else:
                            ctrS[0] = ev
                    else:
                        for kc in range(8):
                            mm = b.t(lambda e, kc=kc, pp=pp, it=it, stb=stb,
                                     ch_abs=ch_abs: e.matmul(
                                pbig[:, (ch_abs % 4) * 512:(ch_abs % 4) * 512 + 64],
                                sb["xnfm"][stb, :, kc, it * CK:(it + 1) * CK],
                                sb["win"][pp, :, kc, 256:320],
                                start=(kc == 0), stop=(kc == 7)),
                                waits=[gtok, S[("xnfm_k", kc)]]
                                + (bw(bank) if kc == 0 else []))
                        ev = b.v(lambda e, r=r, ch_abs=ch_abs: e.tensor_copy(
                            sb["dtr"][r],
                            pbig[:, (ch_abs % 4) * 512:(ch_abs % 4) * 512 + 64]),
                            waits=[mm, sget(("dtr_free", r))])
                        S[("dtr", i)] = ev
                    S[("win_free", pp)] = mm
                    bset(bank, ev)
        last_ctr = b.v(lambda e: e.engine_nop(), waits=[ctrV[0], ctrS[0]])
        S[("asm_center", i)] = last_ctr
        if i > 0:
            pv = (i - 1) % 2
            e1 = b.v(lambda e, ar=ar, pv=pv: e.tensor_copy(
                sb["asm"][ar, :, :, 0:3], sb["asm"][pv, :, :, 128:131]),
                waits=[last_ctr, S[("asm_center", i - 1)]])
            e2 = b.v(lambda e, ar=ar, pv=pv: e.tensor_copy(
                sb["asm"][pv, :, :, 131:134], sb["asm"][ar, :, :, 3:6]),
                waits=[last_ctr, S[("asm_center", i - 1)]])
            S[("asm_ready", i - 1)] = e2
            S[("asm_left", i)] = e1
        else:
            S[("asm_left", 0)] = S["asm_z0"]

    def conv_ssd(c):
        rr = c % 2
        ar = c % 2
        asm_ok = [S[("asm_ready", c)], S[("asm_left", c)], tok_cdw, tok_cbc]
        ev = None
        for ch in range(18):
            bank = "b%d" % (ch % 4)
            mm = None
            for bb in range(4):
                for k in range(7):
                    mm = b.t(lambda e, ch=ch, k=k, bb=bb, ar=ar: e.matmul(
                        pbig[32 * bb:32 * bb + 32,
                             (ch % 4) * 512:(ch % 4) * 512 + 128],
                        sb["cdw"][32 * bb:32 * bb + 32, ch, k, :],
                        sb["asm"][ar, 32 * bb:32 * bb + 32, ch, k:k + 128],
                        start=(k == 0), stop=(k == 6),
                        tile_position=(32 * bb, 32 * bb)),
                        waits=asm_ok + (bw(bank) if (k == 0 and bb == 0) else []))
            if ch < 16:
                ev = b.s(lambda e, ch=ch, rr=rr: e.activation(
                    sb["xifm"][rr, :, ch, :],
                    pbig[:, (ch % 4) * 512:(ch % 4) * 512 + 128], AF.Silu,
                    bias=sb["cbc"][:, ch:ch + 1]),
                    waits=[mm, sget(("xifm_free", rr))])
                S[("xifm_ch", rr, ch)] = ev
            else:
                ev = b.s(lambda e, ch=ch, rr=rr: e.activation(
                    sb["bcfm"][rr, :, ch - 16, :],
                    pbig[:, (ch % 4) * 512:(ch % 4) * 512 + 128], AF.Silu,
                    bias=sb["cbc"][:, ch:ch + 1]),
                    waits=[mm, sget(("bcfm_free", rr))])
                S[("bcfm_ch", rr, ch - 16)] = ev
            bset(bank, ev)
            if ch % 3 == 2:
                yield 0
        S[("asm_free", ar)] = ev

        tts = []
        for j in range(2):
            tts.append(b.t(lambda e, j=j, rr=rr: e.transpose(
                ptb[:, j * 128:(j + 1) * 128], sb["bcfm"][rr, :, j, :], sb["idb"][:]),
                waits=[S[("bcfm_ch", rr, j)], tok_idb] + bw("tb")))
        bc2 = None
        for j in range(2):
            bc2 = b.v(lambda e, j=j, rr=rr: e.tensor_copy(
                sb["bct"][rr, :, j * 128:(j + 1) * 128], ptb[:, j * 128:(j + 1) * 128]),
                waits=[tts[-1], sget(("bct_free", rr))])
        bset("tb", bc2)
        S[("bct", c)] = bc2
        yield 0
        spbc = b.dsync(lambda e, c=c, rr=rr: e.dma_start(
            dr["bc_sp"][c * CK:(c + 1) * CK, :], sb["bct"][rr]), waits=[bc2])

        evs = []
        for wave in range(2):
            tts = []
            for sl in range(8):
                ch = wave * 8 + sl
                tw = [S[("xifm_ch", rr, ch)], tok_idb]
                tw.extend(bw("tb") if ch == 0 else ([evs[-1]] if ch == 8 else []))
                tts.append(b.t(lambda e, ch=ch, sl=sl, rr=rr: e.transpose(
                    ptb[:, sl * 128:(sl + 1) * 128], sb["xifm"][rr, :, ch, :],
                    sb["idb"][:]), waits=tw))
            for sl in range(8):
                ch = wave * 8 + sl
                evx = b.v(lambda e, ch=ch, sl=sl, rr=rr: e.tensor_copy(
                    sb["xitm"][rr, :, ch * 128:(ch + 1) * 128],
                    ptb[:, sl * 128:(sl + 1) * 128]),
                    waits=[tts[-1], sget(("xitm_free", rr))])
                evs.append(evx)
            yield 1
        bset("tb", evs[-1])
        xitm_tok = evs[-1]
        S[("xitm", c)] = xitm_tok
        spxi = b.dsync(lambda e, c=c, rr=rr: e.dma_start(
            dr["xi_sp"][c * CK:(c + 1) * CK, :], sb["xitm"][rr]), waits=[xitm_tok])

        fm = None
        for kc in range(16):
            fm = b.t(lambda e, kc=kc, rr=rr: e.matmul(
                pmed[:, 256:288], sb["xifm"][rr, :, kc, :], sb["fcdw"][:, kc, :],
                start=(kc == 0), stop=(kc == 15)),
                waits=[S[("xifm_ch", rr, kc)], tok_fcdw] + (bw("med") if kc == 0 else []))
        S[("xifm_free", rr)] = fm
        fe = b.v(lambda e, c=c: e.tensor_tensor(
            sb["fcd"][:, c, :], pmed[:, 256:288], sb["dparc"][:], ALU.add),
            waits=[fm, tok_dpar])
        bset("med", fe)
        S[("fcd", c)] = fe
        yield 1

        a0 = b.v(lambda e, c=c: e.tensor_tensor(sb["dscr"][:, 0:64], sb["dtr"][c % 2],
                                                sb["dtbc"][:], ALU.add),
                 waits=[S[("dtr", c)], tok_dtb, sget("dscr_free")])
        a1 = b.s(lambda e, rr=rr: e.activation(sb["dts"][rr], sb["dscr"][:, 0:64],
                                               AF.Abs),
                 waits=[a0, sget(("dts_free", rr))])
        a2 = b.s(lambda e, rr=rr: e.activation(sb["dts"][rr], sb["dts"][rr], AF.Exp,
                                               scale=-1.0), waits=[a1])
        a3 = b.s(lambda e, rr=rr: e.activation(sb["dts"][rr], sb["dts"][rr], AF.Ln,
                                               bias=1.0), waits=[a2])
        a4 = b.v(lambda e, rr=rr: e.scalar_tensor_tensor(
            sb["dts"][rr], sb["dscr"][:, 0:64], 0.0, sb["dts"][rr], ALU.max, ALU.add),
            waits=[a3])
        S[("dtr_free", c % 2)] = a4
        S["dscr_free"] = a4
        a5 = b.v(lambda e, rr=rr: e.tensor_copy(sb["dtb16"][rr], sb["dts"][rr]),
                 waits=[a4, sget(("dtb16_free", rr))])
        yield 1
        td = b.t(lambda e, rr=rr: e.transpose(pyd[0:64, 0:128], sb["dts"][rr],
                                              sb["idf"][:]),
                 waits=[a4, tok_idf] + bw("yd"))
        ab0 = b.v(lambda e: e.tensor_scalar_mul(sb["abf"][:], pyd[0:64, 0:128],
                                                sb["nega"][:, 0:1]),
                  waits=[td, tok_nega, sget("abf_free")])
        bset("yd", ab0)
        sc0 = b.v(lambda e, rr=rr: e.tensor_tensor_scan(
            sb["alf"][rr], sb["abf"][:], sb["abf"][:], 0.0, ALU.add, ALU.bypass),
            waits=[ab0, sget(("alf_free", rr))])
        l0 = b.v(lambda e, rr=rr: e.tensor_scalar(
            sb["expdfm"][32:64, :], sb["alf"][rr][32:64, :],
            sb["alf"][rr][32:64, 127:128], -1.0, ALU.subtract, ALU.mult),
            waits=[sc0, sget("expdfm_free")])
        l1 = b.v(lambda e, rr=rr: e.tensor_tensor(
            sb["alf"][rr][32:64, :], sb["expdfm"][32:64, :], sb["abf"][32:64, :],
            ALU.add), waits=[l0])
        S["abf_free"] = l1
        alf_tok = l1
        p0 = b.v(lambda e, rr=rr: e.tensor_copy(sb["poshi"][rr], sb["alf"][rr]),
                 waits=[alf_tok, sget(("poshi_free", rr))])
        p1 = b.v(lambda e, rr=rr: e.scalar_tensor_tensor(
            sb["poslo"][rr], sb["alf"][rr], 1.0, sb["poshi"][rr], ALU.bypass,
            ALU.subtract), waits=[p0, sget(("poslo_free", rr))])
        ta = b.t(lambda e, rr=rr: e.transpose(pyo[0:128, 0:64], sb["alf"][rr],
                                              sb["idf"][0:64, 0:64]),
                 waits=[alf_tok, tok_idf] + bw("yo"))
        at0 = b.v(lambda e, rr=rr: e.tensor_copy(sb["altm"][rr], pyo[0:128, 0:64]),
                  waits=[ta, sget(("altm_free", rr))])
        bset("yo", at0)
        at1 = b.v(lambda e, rr=rr: e.tensor_scalar_mul(sb["naltm"][rr], sb["altm"][rr],
                                                       -1.0),
                  waits=[at0, sget(("naltm_free", rr))])
        ae = b.s(lambda e, rr=rr: e.activation(sb["expaltm"][rr], sb["altm"][rr],
                                               AF.Exp),
                 waits=[at0, sget(("expaltm_free", rr))])
        spal = b.dsync(lambda e, c=c, rr=rr: e.dma_start(
            dr["expal_sp"][c * CK:(c + 1) * CK, :], sb["expaltm"][rr]), waits=[ae])
        sm0 = b.v(lambda e, rr=rr: e.tensor_copy(sb["sumc"][0:32, :],
                                                 sb["alf"][rr][0:32, 127:128]),
                  waits=[sc0, sget("sumc_free")])
        sm1 = b.v(lambda e, rr=rr: e.tensor_copy(sb["sumc"][32:64, :],
                                                 sb["alf"][rr][32:64, 0:1]),
                  waits=[l1, sm0])
        ed0 = b.s(lambda e, rr=rr: e.activation(
            sb["expdfm"][:], sb["alf"][rr], AF.Exp, scale=-1.0,
            bias=sb["sumc"][:, 0:1]), waits=[l1, sm1])
        te = b.t(lambda e: e.transpose(pyd[0:128, 0:64], sb["expdfm"][:],
                                       sb["idf"][0:64, 0:64]),
                 waits=[ed0] + bw("yd"))
        et0 = b.v(lambda e, rr=rr: e.tensor_copy(sb["expdtm"][rr], pyd[0:128, 0:64]),
                  waits=[te, sget(("expdtm_free", rr))])
        bset("yd", et0)
        S["expdfm_free"] = et0
        tdr = b.t(lambda e: e.transpose(pyo[0:1, 0:64], sb["sumc"][:, 0:1],
                                        sb["idf"][0:64, 0:64]),
                  waits=[sm1] + bw("yo"))
        dr0 = b.s(lambda e, c=c: e.activation(sb["drows"][:, c, :], pyo[0:1, 0:64],
                                              AF.Exp), waits=[tdr])
        bset("yo", dr0)
        dr1 = b.v(lambda e, c=c, rr=rr: e.tensor_copy(sb["drb"][rr],
                                                      sb["drows"][:, c, :]),
                  waits=[dr0, sget(("drb_free", rr))])
        S["sumc_free"] = dr0
        yield 1

        yfree = None
        for d in (0, 1):
            h0 = 32 * d
            gmm = b.t(lambda e, d=d, rr=rr: e.matmul(
                pmed[:, 0:128], sb["bcfm"][rr, 64 * d:64 * d + 64, 0, :],
                sb["bcfm"][rr, 64 * d:64 * d + 64, 1, :],
                start=True, stop=True),
                waits=[S[("bcfm_ch", rr, 0)], S[("bcfm_ch", rr, 1)]] + bw("med"))
            yield 1
            for q in range(4):
                yield 0
                pair = q % 2
                cb = pair * 1024
                bkA, bkB = "b%d" % (2 * pair), "b%d" % (2 * pair + 1)
                stops = []
                for h8 in range(8):
                    h = 8 * q + h8
                    colw = bw(bkA) + bw(bkB) if h8 == 0 else []
                    b.t(lambda e, h=h, h8=h8, rr=rr, h0=h0, cb=cb: e.matmul(
                        pbig[:, cb + h8 * 128:cb + h8 * 128 + 128],
                        sb["idb"][0:64, h0 + h:h0 + h + 1].broadcast_to([64, 128]),
                        sb["poshi"][rr],
                        start=(h8 % 4 == 0), stop=False),
                        waits=[p0, tok_idb] + colw)
                    m2 = b.t(lambda e, h=h, h8=h8, rr=rr, h0=h0, cb=cb: e.matmul(
                        pbig[:, cb + h8 * 128:cb + h8 * 128 + 128],
                        sb["idb"][0:64, h0 + h:h0 + h + 1].broadcast_to([64, 128]),
                        sb["poslo"][rr],
                        start=False, stop=(h8 % 4 == 3)),
                        waits=[p1])
                    if h8 % 4 == 3:
                        stops.append(m2)
                etoks = []
                for h8 in range(8):
                    h = 8 * q + h8
                    ex = b.s(lambda e, h=h, h8=h8, rr=rr, h0=h0, cb=cb, pair=pair:
                             e.activation(
                                 sb["mall"][:, pair * 1024 + h8 * 128:
                                            pair * 1024 + h8 * 128 + 128],
                                 pbig[:, cb + h8 * 128:cb + h8 * 128 + 128], AF.Exp,
                                 bias=sb["naltm"][rr][:, h0 + h:h0 + h + 1]),
                             waits=[stops[h8 // 4], at1, sget(("mall_free", pair))])
                    etoks.append(ex)
                bset(bkA, etoks[3])
                bset(bkB, etoks[7])
                mzt = None
                for mq in range(2):
                    mzt = b.v(lambda e, d=d, mq=mq, pair=pair: e.copy_predicated(
                        sb["mall"][:, pair * 1024 + mq * 512:
                                   pair * 1024 + mq * 512 + 512],
                        sb["mskz"][:, d, :],
                        sb["zc"][:, 0:1].broadcast_to([128, 512])),
                        waits=etoks + [tok_msk, tok_zc])
                mg = b.v(lambda e, pair=pair: e.tensor_tensor(
                    r3(sb["mall"][:, pair * 1024:pair * 1024 + 1024], 128),
                    r3(sb["mall"][:, pair * 1024:pair * 1024 + 1024], 128),
                    pmed[:, 0:128].unsqueeze(1).broadcast_to([128, 8, 128]),
                    ALU.mult),
                    waits=[mzt, gmm])
                badd("med", mg)
                xb = b.v(lambda e, d=d, q=q, rr=rr, h0=h0, pair=pair: e.tensor_tensor(
                    r3(sb["xall"][pair], 64)[:, 0:8, :],
                    r3(sb["xitm"][rr][:, q * 512:(q + 1) * 512], 64),
                    sb["dtb16"][rr][:, h0 + 8 * q:h0 + 8 * q + 8]
                    .unsqueeze(2).broadcast_to([128, 8, 64]), ALU.mult),
                    waits=[S[("xitm", c)], a5, sget(("xall_free", pair))])
                bh = b.v(lambda e, d=d, q=q, rr=rr, h0=h0, pair=pair: e.tensor_tensor(
                    r3(sb["bhat"][pair], 64)[:, 0:8, :],
                    sb["bct"][rr][:, 64 * d:64 * d + 64]
                    .unsqueeze(1).broadcast_to([128, 8, 64]),
                    sb["expdtm"][rr][:, h0 + 8 * q:h0 + 8 * q + 8]
                    .unsqueeze(2).broadcast_to([128, 8, 64]), ALU.mult),
                    waits=[S[("bct", c)], et0, sget(("bhat_free", pair))])
                yield 3
                ydm = None
                for j in range(8):
                    ydm = b.t(lambda e, j=j, pair=pair: e.matmul(
                        pyd[:, j * 64:(j + 1) * 64],
                        sb["mall"][:, pair * 1024 + j * 128:pair * 1024 + j * 128 + 128],
                        sb["xall"][pair][:, j * 64:j * 64 + 64],
                        start=(j == 0), stop=(j == 7)),
                        waits=[mg, xb] + (bw("yd") if j == 0 else []))
                if d == 0:
                    yom = None
                    for j in range(8):
                        hq = 8 * q + j
                        yom = b.t(lambda e, hq=hq, j=j, rr=rr: e.matmul(
                            pyo[:, j * 64:(j + 1) * 64],
                            sb["bcfm"][rr, 0:64, 1, :],
                            sb["stB"][:, hq * 64:hq * 64 + 64],
                            start=(j == 0), stop=(j == 7)),
                            waits=[S[("bcfm_ch", rr, 1)], sget("sbf"), mg]
                            + (bw("yo") if j == 0 else []))
                    ye0 = b.v(lambda e, q=q, rr=rr, h0=h0: e.tensor_tensor(
                        r3(sb["yt"][q % 2], 64), r3(pyo[:, 0:512], 64),
                        sb["expaltm"][rr][:, h0 + 8 * q:h0 + 8 * q + 8]
                        .unsqueeze(2).broadcast_to([128, 8, 64]), ALU.mult),
                        waits=[yom, ae, sget(("yt_free", q % 2))])
                    bset("yo", ye0)
                    ye1 = b.v(lambda e, q=q: e.tensor_tensor(
                        sb["yt"][q % 2], sb["yt"][q % 2], pyd[:, 0:512], ALU.add),
                        waits=[ye0, ydm])
                    bset("yd", ye1)
                    spy = b.dsync(lambda e, c=c, q=q: e.dma_start(
                        dr["yfw_sp"][c * CK + 1:(c + 1) * CK + 1,
                                     q * 512:(q + 1) * 512],
                        sb["yt"][q % 2]), waits=[ye1])
                    S[("yt_free", q % 2)] = spy
                else:
                    ye1 = b.v(lambda e, q=q: e.tensor_copy(
                        sb["yt"][q % 2], pyd[:, 0:512]),
                        waits=[ydm, sget(("yt_free", q % 2))])
                    bset("yd", ye1)
                    spy = b.dsync(lambda e, c=c, q=q: e.dma_start(
                        dr["ybd_sp"][c * CK:(c + 1) * CK, q * 512:(q + 1) * 512],
                        sb["yt"][q % 2]), waits=[ye1])
                    S[("yt_free", q % 2)] = spy
                csm = None
                for j in range(8):
                    csm = b.t(lambda e, j=j, q=q, cb=cb, pair=pair: e.matmul(
                        pbig[0:64, cb + j * 64:cb + j * 64 + 64],
                        sb["bhat"][pair][:, j * 64:j * 64 + 64],
                        sb["xall"][pair][:, j * 64:j * 64 + 64],
                        start=(j == 0), stop=(j == 7)),
                        waits=[bh, xb, mg] + (bw(bkA) if j == 0 else []))
                S[("xall_free", pair)] = csm
                S[("bhat_free", pair)] = csm
                S[("mall_free", pair)] = csm
                if d == 0:
                    j = q
                    pt, bank = ((pyd, "yd") if j % 2 == 0 else (pyo, "yo"))
                    mm = b.t(lambda e, j=j, rr=rr, pt=pt: e.matmul(
                        pt[0:64, 0:512], sb["ones1b"][0:1, 0:64],
                        sb["drb"][rr][0:1, 8 * j:8 * j + 8]
                        .unsqueeze(2).broadcast_to([1, 8, 64]),
                        start=True, stop=True),
                        waits=[dr1, tok_o1b] + bw(bank))
                    evd = b.v(lambda e, j=j, pt=pt: e.tensor_copy(
                        sb["decrep"][:, j * 512:(j + 1) * 512], pt[0:64, 0:512]),
                        waits=[mm, sget("decrep_free")])
                    bset(bank, evd)
                    s_0 = b.v(lambda e, q=q: e.tensor_tensor(
                        sb["stA"][:, q * 512:(q + 1) * 512],
                        sb["stA"][:, q * 512:(q + 1) * 512],
                        sb["decrep"][:, q * 512:(q + 1) * 512], ALU.mult),
                        waits=[evd, sget("sfw"), yom])
                    s_1 = b.v(lambda e, q=q, cb=cb: e.tensor_tensor(
                        sb["stA"][:, q * 512:(q + 1) * 512],
                        sb["stA"][:, q * 512:(q + 1) * 512],
                        pbig[0:64, cb:cb + 512], ALU.add),
                        waits=[s_0, csm])
                    bset(bkA, s_1)
                    S[("sfw_h", q)] = s_1
                    s_2 = b.v(lambda e, q=q: e.tensor_copy(
                        sb["stB"][:, q * 512:(q + 1) * 512],
                        sb["stA"][:, q * 512:(q + 1) * 512]), waits=[s_1])
                    S[("sbf_h", q)] = s_2
                else:
                    cb0 = b.v(lambda e, q=q, cb=cb: e.tensor_copy(
                        sb["csbt"][:, q * 512:(q + 1) * 512],
                        pbig[0:64, cb:cb + 512]),
                        waits=[csm, sget("csbt_free")])
                    bset(bkA, cb0)
                    S[("csbt_h", q)] = cb0
            if d == 0:
                sfb = b.v(lambda e: e.engine_nop(),
                          waits=[S[("sfw_h", q)] for q in range(4)])
                S["sfw"] = sfb
                sbb = b.v(lambda e: e.engine_nop(),
                          waits=[S[("sbf_h", q)] for q in range(4)])
                S["sbf"] = sbb
                S["decrep_free"] = sfb
            else:
                spcs = b.dsync(lambda e, c=c: e.dma_start(dr["csbw_sp"][c],
                                                          sb["csbt"][:]),
                               waits=[S[("csbt_h", q)] for q in range(4)])
                S["csbt_free"] = spcs
                S[("csbw", c)] = spcs
                yfree = spcs
        S[("dts_free", rr)] = alf_tok
        S[("dtb16_free", rr)] = yfree
        S[("alf_free", rr)] = ed0
        S[("poshi_free", rr)] = yfree
        S[("poslo_free", rr)] = yfree
        S[("altm_free", rr)] = yfree
        S[("naltm_free", rr)] = yfree
        S[("expaltm_free", rr)] = yfree
        S[("expdtm_free", rr)] = yfree
        S[("drb_free", rr)] = yfree
        S[("bct_free", rr)] = yfree
        S[("xitm_free", rr)] = yfree
        S[("bcfm_free", rr)] = yfree

    def ffn(st):
        stb = st % 2
        TT = tps * CK
        xn_ready = [S[("xnfm_tile", stb, it)] for it in range(tps)]
        be = None
        for it in range(tps):
            tc = b.t(lambda e, stb=stb, it=it: e.transpose(
                pyd[0:8, 0:128], sb["comb"][stb, :, it, :], sb["idf"][:]),
                waits=[S[("comb_tile", stb, it)]] + bw("yd"))
            ce = b.v(lambda e: e.tensor_copy(sb["combT"][:], pyd[0:8, 0:128]),
                     waits=[tc, sget("combT_free")])
            bset("yd", ce)
            for nn2 in range(2):
                bm = b.t(lambda e, nn2=nn2: e.matmul(
                    pyo[:, 0:512], sb["combT"][:], sb["b2r"][:, nn2 * 512:(nn2 + 1) * 512],
                    start=True, stop=True),
                    waits=[ce, tok_b2r] + bw("yo"))
                be = b.v(lambda e, it=it, nn2=nn2: e.tensor_copy(
                    sb["acc"][:, it, nn2 * 512:(nn2 + 1) * 512], pyo[:, 0:512]),
                    waits=[bm, sget("acc_free")])
                bset("yo", be)
            S["combT_free"] = be
            yield
        acc_init = be
        prev_y = acc_init
        for ei in range(NE):
            for hf in range(2):
                pp = hf
                wtok = b.dgp(lambda e, ei=ei, hf=hf, pp=pp: e.dma_start(
                    sb["wA"][pp],
                    dr["w1_b"][ei][:, hf * 1024:(hf + 1) * 1024]
                    .rearrange("(c k) n -> k c n", k=128)),
                    waits=[sget(("wA_free", pp)), cv_w1[ei]])
                for hcl in range(8):
                    hc = 8 * hf + hcl
                    bank = "b%d" % (hcl % 4)
                    mm = None
                    for kc in range(8):
                        mm = b.t(lambda e, kc=kc, pp=pp, hcl=hcl, TT=TT: e.matmul(
                            pbig[:, (hcl % 4) * 512:(hcl % 4) * 512 + TT],
                            sb["wA"][pp, :, kc, hcl * 128:(hcl + 1) * 128],
                            sb["xnfm"][stb, :, kc, 0:TT],
                            start=(kc == 0), stop=(kc == 7)),
                            waits=[wtok] + xn_ready + (bw(bank) if kc == 0 else []))
                    hv = b.s(lambda e, hc=hc, ei=ei, hcl=hcl, TT=TT: e.activation(
                        sb["hb"][:, hc, 0:TT],
                        pbig[:, (hcl % 4) * 512:(hcl % 4) * 512 + TT], AF.Relu,
                        bias=sb["b1cc"][:, ei, hc:hc + 1]),
                        waits=[mm, tok_b1c, prev_y])
                    bset(bank, hv)
                    S[("hb_hc", hc)] = hv
                    yield
                S[("wA_free", pp)] = mm
            ydone = None
            for chh in range(2):
                pp = chh
                w2tok = b.dgp(lambda e, ei=ei, chh=chh, pp=pp: e.dma_start(
                    sb["wA"][pp].rearrange("p a n -> p (a n)")
                    .rearrange("p (c n) -> p c n", n=512),
                    dr["w2_b"][ei][:, chh * 512:(chh + 1) * 512]
                    .rearrange("(c k) n -> k c n", k=128)),
                    waits=[sget(("wA_free", pp)), cv_w2[ei]])
                for tb in range(tps):
                    pt, bank = ((pyd, "yd") if tb % 2 == 0 else (pyo, "yo"))
                    mm = None
                    for kc in range(16):
                        mm = b.t(lambda e, kc=kc, pp=pp, tb=tb, pt=pt: e.matmul(
                            pt[:, 0:512],
                            sb["hb"][:, kc, tb * 128:(tb + 1) * 128],
                            sb["wA"][pp].rearrange("p a n -> p (a n)")
                            .rearrange("p (c n) -> p c n", n=512)[:, kc, :],
                            start=(kc == 0), stop=(kc == 15)),
                            waits=[w2tok, S[("hb_hc", kc)]] + (bw(bank) if kc == 0 else []))
                    ac = b.v(lambda e, tb=tb, chh=chh, stb=stb, ei=ei, pt=pt:
                             e.scalar_tensor_tensor(
                                 sb["acc"][:, tb, chh * 512:(chh + 1) * 512],
                                 pt[:, 0:512],
                                 sb["comb"][stb, :, tb, ei:ei + 1],
                                 sb["acc"][:, tb, chh * 512:(chh + 1) * 512],
                                 ALU.mult, ALU.add),
                             waits=[mm, S[("comb_tile", stb, tb)], acc_init])
                    bset(bank, ac)
                    ydone = ac
                    yield
                S[("wA_free", pp)] = mm
            prev_y = ydone
        spa = b.dsync(lambda e, st=st, TT=TT: e.dma_start(
            dr["ffn_sp"][st * TT:(st + 1) * TT, :].rearrange("(t p) c -> p t c", p=128),
            sb["acc"][:, 0:tps, :]), waits=[prev_y])
        S["acc_free"] = spa
        S[("comb_free", stb)] = prev_y
        S[("xnfm_free", stb)] = prev_y

    def rl_pre(c):
        rb = c % 2
        t = {}
        t["exp"] = b.dsync(lambda e, c=c, rb=rb: e.dma_start(
            sb["expb"][rb], dr["expal_sp"][c * CK:(c + 1) * CK, 32:64]),
            waits=[sget(("expb_free", rb))])
        t["cbw"] = b.dsync(lambda e, c=c, rb=rb: e.dma_start(
            sb["cbw"][rb], dr["bc_sp"][c * CK:(c + 1) * CK, 192:256]),
            waits=[sget(("cbw_free", rb))])
        t["ybd"] = b.dsync(lambda e, c=c, rb=rb: e.dma_start(
            sb["ybdt"][rb], dr["ybd_sp"][c * CK:(c + 1) * CK, :]),
            waits=[sget(("ybdt_free", rb))])
        t["yfw"] = b.dsync(lambda e, c=c, rb=rb: e.dma_start(
            sb["yfwt"][rb], dr["yfw_sp"][c * CK:(c + 1) * CK, :]),
            waits=[sget(("yfwt_free", rb))])
        t["xi"] = b.dsync(lambda e, c=c, rb=rb: e.dma_start(
            sb["xirl"][rb], dr["xi_sp"][c * CK:(c + 1) * CK, :]),
            waits=[sget(("xirl_free", rb))])
        t["z"] = b.dsync(lambda e, c=c, rb=rb: e.dma_start(
            sb["zrl"][rb], dr["z_sp"][c * CK:(c + 1) * CK, :]),
            waits=[sget(("zrl_free", rb))])
        t["ff"] = b.dsync(lambda e, c=c, rb=rb: e.dma_start(
            sb["ffrl"][rb], dr["ffn_sp"][c * CK:(c + 1) * CK, :]),
            waits=[sget(("ffrl_free", rb)), sget("acc_free")])
        t["x"] = b.dscal(lambda e, c=c, rb=rb: e.dma_start(
            sb["xt"][rb], dr["u"][c * CK:(c + 1) * CK, :]),
            waits=[sget(("xt_free", rb))])
        S[("rlt", c)] = t

    def tile_bc(c):
        rb = c % 2
        rlt = S[("rlt", c)]
        if c < nt - 1:
            d0 = b.v(lambda e, c=c, rb=rb: e.tensor_copy(
                sb["drb"][rb], sb["drows"][:, c + 1, :]),
                waits=[sget(("drb_free", rb))])
            S[("drb_free", rb)] = d0
            dts_ = []
            for j in range(4):
                pt, bank = ((pyd, "yd") if j % 2 == 0 else (pyo, "yo"))
                mm = b.t(lambda e, j=j, rb=rb, pt=pt: e.matmul(
                    pt[0:64, 0:512], sb["ones1b"][0:1, 0:64],
                    sb["drb"][rb][0:1, 32 + 8 * j:32 + 8 * j + 8]
                    .unsqueeze(2).broadcast_to([1, 8, 64]),
                    start=True, stop=True),
                    waits=[d0, tok_o1b] + bw(bank))
                evd = b.s(lambda e, j=j, pt=pt: e.copy(
                    sb["decrep"][:, j * 512:(j + 1) * 512], pt[0:64, 0:512]),
                    waits=[mm, sget("decrep_free")])
                bset(bank, evd)
                dts_.append(evd)
            rl = b.dsync(lambda e, c=c: e.dma_start(
                sb["csbt"][:], dr["csbw_sp"][c + 1]),
                waits=[sget("csrl_free"), S[("csbw", c + 1)]])
            s_0 = b.v(lambda e: e.tensor_tensor(sb["stA"][:],
                                                sb["stA"][:],
                                                sb["decrep"][:], ALU.mult),
                      waits=dts_ + [sget("sbw")])
            s_1 = b.v(lambda e: e.tensor_tensor(
                sb["stA"][:], sb["stA"][:], sb["csbt"][:], ALU.add),
                waits=[s_0, rl])
            S["csrl_free"] = s_1
            S["sbw"] = s_1
            S["decrep_free"] = s_1
            s_2 = b.v(lambda e: e.tensor_copy(sb["stB"][:], sb["stA"][:]),
                      waits=[s_1, sget("sbwbf_rd")])
            S["sbwbf"] = s_2
        rl_exp = rlt["exp"]
        rl_cbw = rlt["cbw"]
        rl_ybd = rlt["ybd"]
        tcb = b.t(lambda e, rb=rb: e.transpose(ptb[0:64, 0:128], sb["cbw"][rb],
                                               sb["idb"][:]),
                  waits=[rl_cbw, sget("cbwfm_free")] + bw("tb"))
        ecb = b.v(lambda e: e.tensor_copy(sb["cbwfm"][:], ptb[0:64, 0:128]),
                  waits=[tcb])
        bset("tb", ecb)
        S[("cbw_free", rb)] = ecb
        ye1 = None
        for q in range(4):
            yom = None
            for j in range(8):
                hq = 8 * q + j
                yom = b.t(lambda e, hq=hq, j=j: e.matmul(
                    pyo[:, j * 64:(j + 1) * 64], sb["cbwfm"][:],
                    sb["stB"][:, hq * 64:hq * 64 + 64],
                    start=(j == 0), stop=(j == 7)),
                    waits=[ecb, sget("sbwbf")] + (bw("yo") if j == 0 else []))
            ye0 = b.v(lambda e, q=q, rb=rb: e.tensor_tensor(
                r3(sb["ybwt"][rb][:, q * 512:(q + 1) * 512], 64),
                r3(pyo[:, 0:512], 64),
                sb["expb"][rb][:, 8 * q:8 * q + 8]
                .unsqueeze(2).broadcast_to([128, 8, 64]), ALU.mult),
                waits=[yom, rl_exp, sget(("ybwt_free", rb))])
            bset("yo", ye0)
            ye1 = b.v(lambda e, q=q, rb=rb: e.tensor_tensor(
                sb["ybwt"][rb][:, q * 512:(q + 1) * 512],
                sb["ybwt"][rb][:, q * 512:(q + 1) * 512],
                sb["ybdt"][rb][:, q * 512:(q + 1) * 512], ALU.add),
                waits=[ye0, rl_ybd])
        S[("ybdt_free", rb)] = ye1
        S[("expb_free", rb)] = ye1
        S["cbwfm_free"] = yom
        S["sbwbf_rd"] = yom
        S[("ybwt_ye", c)] = ye1
        smms = []
        for nb in range(4):
            bank = "b%d" % nb
            sm = b.t(lambda e, nb=nb, rb=rb, c=c: e.matmul(
                pbig[:, nb * 512:(nb + 1) * 512], sb["shm"][:, 0:128],
                sb["ybwt"][rb][:, nb * 512:(nb + 1) * 512],
                start=True, stop=(c == nt - 1)),
                waits=[ye1, tok_shm] + bw(bank))
            if c < nt - 1:
                sm = b.t(lambda e, nb=nb, rb=rb: e.matmul(
                    pbig[:, nb * 512:(nb + 1) * 512], sb["shm"][:, 128:256],
                    sb["ybwt"][rb ^ 1][:, nb * 512:(nb + 1) * 512],
                    start=False, stop=True),
                    waits=[S[("ybwt_ye", c + 1)]])
            smms.append(sm)
        if c < nt - 1:
            S[("ybwt_free", rb ^ 1)] = smms[-1]
        ybse = []
        for nb in range(4):
            ev = b.s(lambda e, nb=nb, rb=rb: e.copy(
                sb["ybshs"][rb][:, nb * 512:(nb + 1) * 512],
                pbig[:, nb * 512:(nb + 1) * 512]),
                waits=[smms[nb], sget(("ybshs_f", rb))])
            bset("b%d" % nb, ev)
            ybse.append(ev)
        rl_yfw = rlt["yfw"]
        rl_xi = rlt["xi"]
        rl_z = rlt["z"]
        rl_ff = rlt["ff"]
        rl_x = rlt["x"]
        if c > 0:
            rl_pre(c - 1)
        y0 = b.v(lambda e, c=c, rb=rb: e.tensor_tensor(
            r3(sb["ycmb"][rb], 64), r3(sb["xirl"][rb], 64),
            sb["fcd"][:, c, :].unsqueeze(2).broadcast_to([128, 32, 64]), ALU.mult),
            waits=[rl_xi, S[("fcd", c)], sget(("ycmb_free", 0)),
                   sget(("ycmb_free", 1))])
        S[("xirl_free", rb)] = y0
        y1 = b.v(lambda e, rb=rb: e.tensor_tensor(sb["ycmb"][rb], sb["ycmb"][rb],
                                                  sb["yfwt"][rb], ALU.add),
                 waits=[y0, rl_yfw])
        y2 = b.v(lambda e, rb=rb: e.tensor_tensor(
            sb["ycmb"][rb], sb["ycmb"][rb], sb["ybshs"][rb], ALU.add),
            waits=[y1] + ybse)
        S[("ybshs_f", rb)] = y2
        S[("yfwt_free", rb)] = y2
        sz = b.s(lambda e, rb=rb: e.activation(sb["zrl"][rb], sb["zrl"][rb], AF.Silu),
                 waits=[rl_z])
        ssq = b.s(lambda e, rb=rb: e.activation(sb["actt"][rb], sb["ycmb"][rb],
                                                AF.Square, accum_out=sb["gsc"][:, 0:1]),
                  waits=[y2, sget("gsc_free"), sget(("actt_free", rb))])
        r0a = b.v(lambda e: e.tensor_scalar(sb["gsc"][:, 1:2], sb["gsc"][:, 0:1],
                                            1.0 / DI, 1e-5, ALU.mult, ALU.add),
                  waits=[ssq])
        r0 = b.s(lambda e: e.activation(sb["gsc"][:, 1:2], sb["gsc"][:, 1:2], AF.Ln),
                 waits=[r0a])
        r1a = b.v(lambda e: e.tensor_scalar_mul(sb["gsc"][:, 2:3], sb["gsc"][:, 1:2],
                                                -0.5), waits=[r0])
        r1 = b.s(lambda e: e.activation(sb["gsc"][:, 2:3], sb["gsc"][:, 2:3], AF.Exp),
                 waits=[r1a])
        v0 = b.v(lambda e, rb=rb: e.tensor_scalar_mul(sb["actt"][rb], sb["ycmb"][rb],
                                                      sb["gsc"][:, 2:3]),
                 waits=[r1, sget(("actt_free", rb))])
        S[("ycmb_free", 0)] = v0
        S[("ycmb_free", 1)] = v0
        S["gsc_free"] = v0
        v1 = b.v(lambda e, rb=rb: e.tensor_tensor(sb["actt"][rb], sb["actt"][rb],
                                                  sb["nwb"][:], ALU.mult),
                 waits=[v0, tok_nwb])
        v2 = b.v(lambda e, rb=rb: e.tensor_tensor(sb["actt"][rb], sb["actt"][rb],
                                                  sb["zrl"][rb], ALU.mult),
                 waits=[v1, sz])
        S[("zrl_free", rb)] = v2
        aevs = []
        for wave in range(2):
            tts2 = []
            for sl in range(8):
                chk = wave * 8 + sl
                tw = [v2, tok_idb]
                tw.extend(bw("tb") if chk == 0 else ([aevs[-1]] if chk == 8 else []))
                tts2.append(b.t(lambda e, chk=chk, sl=sl, rb=rb: e.transpose(
                    ptb[:, sl * 128:(sl + 1) * 128],
                    sb["actt"][rb][:, chk * 128:(chk + 1) * 128], sb["idb"][:]),
                    waits=tw))
            for sl in range(8):
                chk = wave * 8 + sl
                mk = b.v if wave == 0 else b.s
                fn = ((lambda e, chk=chk, sl=sl, rb=rb: e.tensor_copy(
                    sb["actf"][rb][:, chk, :], ptb[:, sl * 128:(sl + 1) * 128]))
                    if wave == 0 else
                    (lambda e, chk=chk, sl=sl, rb=rb: e.copy(
                        sb["actf"][rb][:, chk, :], ptb[:, sl * 128:(sl + 1) * 128])))
                evx = mk(fn, waits=[tts2[-1], sget(("actf_free", rb))])
                aevs.append(evx)
        bset("tb", aevs[-1])
        S[("actt_free", rb)] = aevs[-1]
        o1 = None
        mm = None
        for chh in range(2):
            pt, bank = ((pyd, "yd") if chh == 0 else (pyo, "yo"))
            mm = None
            for kc in range(16):
                mm = b.t(lambda e, kc=kc, chh=chh, rb=rb, pt=pt: e.matmul(
                    pt[:, 0:512], sb["actf"][rb][:, kc, :],
                    sb["woutv"][:, kc, chh * 512:(chh + 1) * 512],
                    start=(kc == 0), stop=(kc == 15)),
                    waits=[aevs[kc], sget("wout_tok")] + (bw(bank) if kc == 0 else []))
            o0 = b.v(lambda e, chh=chh, rb=rb, pt=pt: e.scalar_tensor_tensor(
                sb["outt"][rb][:, chh * 512:(chh + 1) * 512], pt[:, 0:512], 1.0,
                sb["xt"][rb][:, chh * 512:(chh + 1) * 512], ALU.mult, ALU.add),
                waits=[mm, rl_x, sget(("outt_free", rb))])
            bset(bank, o0)
            o1 = b.v(lambda e, chh=chh, rb=rb: e.tensor_tensor(
                sb["outt"][rb][:, chh * 512:(chh + 1) * 512],
                sb["outt"][rb][:, chh * 512:(chh + 1) * 512],
                sb["ffrl"][rb][:, chh * 512:(chh + 1) * 512], ALU.add),
                waits=[o0, rl_ff])
        S[("actf_free", rb)] = mm
        S[("ffrl_free", rb)] = o1
        S[("xt_free", rb)] = o1
        st_out = b.dscal(lambda e, c=c, rb=rb: e.dma_start(
            out[c * CK:(c + 1) * CK, :], sb["outt"][rb]), waits=[o1])
        S[("outt_free", rb)] = st_out

    def drain(gen):
        if gen is not None:
            for _ in gen:
                pass

    def pump(gen, n=1):
        if gen is None:
            return None
        for _ in range(n):
            try:
                next(gen)
            except StopIteration:
                return None
        return gen

    if SCHED != "prepass":
        fg = None
        for i in range(nt):
            st = i // tps
            if i % tps == 0 and st > 0 and SCHED in ("ffn", "full"):
                drain(fg)
                fg = ffn(st - 1)
                fg = pump(fg)
            for h_ in tile_head(i):
                fg = pump(fg, h_)
            if i > 0 and SCHED in ("conv", "ffn", "full"):
                for h_ in conv_ssd(i - 1):
                    fg = pump(fg, h_)
        drain(fg)
        if SCHED in ("conv", "ffn", "full"):
            e2 = b.v(lambda e: e.memset(sb["asm"][(nt - 1) % 2, :, :, 131:134], 0.0),
                     waits=[S[("asm_center", nt - 1)]])
            S[("asm_ready", nt - 1)] = e2
            drain(conv_ssd(nt - 1))
        if SCHED in ("ffn", "full"):
            drain(ffn(nst - 1))
    if SCHED != "full":
        final_waits = []
        for ring in ("dsy", "dsc", "dgp"):
            for ln in range(NLANES):
                sem = "%s%d" % (ring, ln)
                if b.cnt[sem]:
                    final_waits.append((sem, b.cnt[sem]))
        b.op("sync", lambda e: e.nop(), waits=final_waits, inc=None)
        import contextlib
        with contextlib.ExitStack() as stk:
            for name in list(b.cnt):
                b.handles[name] = stk.enter_context(nc.semaphore(name))
            blk = stk.enter_context(nc.Block())
            b.replay_into(blk)
        return nc

    aw = [S[("xnfm_free", 0)], sget(("xnfm_free", 1)), S["acc_free"], S["xmt_free"],
          S["xnt_free"], S["xnf32_free"], S["gsc_free"]]
    aw.append(sget("csbt_free"))
    for r_ in (0, 1):
        for kk in ("xitm_free", "asm_free", "win_free", "bhat_free",
                   "xifm_free", "xall_free", "yt_free", "zt_free", "mall_free"):
            aw.append(sget((kk, r_)))
    bar = b.v(lambda e: e.engine_nop(), waits=aw)
    for r_ in (0, 1):
        for kk in ("yfwt_free", "ybwt_free", "zrl_free",
                   "ybdt_free", "xirl_free", "ycmb_free", "actt_free",
                   "actf_free", "outt_free", "ffrl_free", "expb_free",
                   "cbw_free", "xt_free"):
            S[(kk, r_)] = bar
    S["gsc_free"] = bar
    S["csrl_free"] = bar
    S[("ybshs_f", 0)] = bar
    S[("ybshs_f", 1)] = bar
    S["wout_tok"] = b.dgp(lambda e: e.dma_start(
        sb["woutv"], dr["wout_b"].rearrange("(c k) n -> k c n", k=128)),
        waits=[bar, cv_wout])
    tok_nwb = b.dgp(lambda e: e.dma_start(sb["nwb"], dr["nw"]), waits=[bar])
    S["sbw"] = b.v(lambda e: e.memset(sb["stA"][:], 0.0),
                   waits=[bar, sget("sfw"), sget("sbf")])
    S["sbwbf"] = b.v(lambda e: e.memset(sb["stB"][:], 0.0), waits=[bar, S["sbw"]])
    rl_pre(nt - 1)
    for c in range(nt - 1, -1, -1):
        tile_bc(c)
    final_waits = []
    for ring in ("dsy", "dsc", "dgp"):
        for ln in range(NLANES):
            sem = "%s%d" % (ring, ln)
            if b.cnt[sem]:
                final_waits.append((sem, b.cnt[sem]))
    b.op("sync", lambda e: e.nop(), waits=final_waits, inc=None)

    import contextlib
    with contextlib.ExitStack() as stk:
        for name in list(b.cnt):
            b.handles[name] = stk.enter_context(nc.semaphore(name))
        blk = stk.enter_context(nc.Block())
        b.replay_into(blk)
    return nc


def kernel(**inputs):
    A = host_prep(inputs)
    u = np.ascontiguousarray(inputs["L0_attn_u"], dtype=np.float32)
    nt = NT_OVERRIDE if NT_OVERRIDE else L // CK
    nc = build(nt)
    in_maps = []
    for c in range(8):
        m = {"u": np.ascontiguousarray(u[c][: nt * CK])}
        m.update(A)
        in_maps.append(m)
    res = run_bass_kernel_spmd(nc, in_maps, core_ids=list(range(8)))
    return np.stack([r["out"] for r in res.results])
